# version 03
# baseline (speedup 1.0000x reference)
import contextlib
import numpy as np
import ml_dtypes
import concourse.bass as bass
import concourse.mybir as mybir
from concourse.bass_utils import run_bass_kernel_spmd

F32 = mybir.dt.float32
BF16 = mybir.dt.bfloat16
AF = mybir.ActivationFunctionType
ALU = mybir.AluOpType
AX = mybir.AxisListType

NCORES = 8
B, T, HID = 4, 4096, 2048
NH, DQ, DV, CK = 16, 128, 128, 4
TPC = (B * T) // NCORES

BFNP = np.dtype(ml_dtypes.bfloat16)
LN_EPS = 1e-5


def cdiv(a, b):
    return -(-a // b)


class SEng:

    def __init__(self, eng, sem):
        self.e = eng
        self.sem = sem
        self.n = 0
        self.last = None

    def __getattr__(self, name):
        fn = getattr(self.e, name)

        def wrap(*a, **kw):
            inst = fn(*a, **kw)
            self.last = inst
            return inst

        return wrap

    def sync(self):
        if self.last is not None:
            self.n += 1
            self.last.then_inc(self.sem, 1)
            self.e.wait_ge(self.sem, self.n)
            self.last = None


def build_ln(rows, cols):
    nc = bass.Bass("TRN2", target_bir_lowering=False)
    x = nc.dram_tensor("x", [rows, cols], F32, kind="ExternalInput").ap()
    y = nc.dram_tensor("y", [rows, cols], BF16, kind="ExternalOutput").ap()
    nt = rows // 128
    with (
        nc.sbuf_tensor("xt", [128, 2 * cols], F32) as xt,
        nc.sbuf_tensor("ct", [128, 2 * cols], F32) as ct,
        nc.sbuf_tensor("yt", [128, 2 * cols], BF16) as yt,
        nc.sbuf_tensor("sq", [128, cols], F32) as sq,
        nc.sbuf_tensor("mu", [128, 2], F32) as mu,
        nc.sbuf_tensor("mu2", [128, 2], F32) as mu2,
        nc.sbuf_tensor("ssq", [128, 2], F32) as ssq,
        nc.sbuf_tensor("rst", [128, 2], F32) as rst,
        nc.sbuf_tensor("epsc", [128, 1], F32) as epsc,
        nc.semaphore("setup") as setup,
        nc.semaphore("ld") as ld,
        nc.semaphore("vm") as vm,
        nc.semaphore("sm") as sm,
        nc.semaphore("gm") as gm,
        nc.semaphore("vsem") as vsem,
        nc.semaphore("ssem") as ssem,
        nc.semaphore("st") as st,
        nc.Block() as block,
    ):
        @block.sync
        def _(sync):
            for i in range(nt):
                if i > 0:
                    sync.wait_ge(ld, 16 * i)
                if i >= 2:
                    sync.wait_ge(vm, i - 1)
                sync.dma_start(out=xt[:, (i % 2) * cols:(i % 2 + 1) * cols],
                               in_=x[128 * i:128 * (i + 1), :]).then_inc(ld, 16)

        @block.vector
        def _(vector):
            v = SEng(vector, vsem)
            v.memset(epsc[:], LN_EPS).then_inc(setup, 1)
            for i in range(nt):
                b = i % 2
                xv = xt[:, b * cols:(b + 1) * cols]
                cv = ct[:, b * cols:(b + 1) * cols]
                vector.wait_ge(ld, 16 * (i + 1))
                if i >= 2:
                    vector.wait_ge(gm, i - 1)
                v.reduce_sum(mu[:, b:b + 1], xv, axis=AX.X)
                v.sync()
                v.tensor_scalar_mul(mu2[:, b:b + 1], mu[:, b:b + 1], 1.0 / cols)
                v.sync()
                v.tensor_scalar_sub(cv, xv, mu2[:, b:b + 1]).then_inc(vm, 1)

        @block.scalar
        def _(scalar):
            s = SEng(scalar, ssem)
            scalar.wait_ge(setup, 1)
            for i in range(nt):
                b = i % 2
                cv = ct[:, b * cols:(b + 1) * cols]
                scalar.wait_ge(vm, i + 1)
                if i >= 2:
                    scalar.wait_ge(gm, i - 1)
                s.activation(out=sq[:], in_=cv, func=AF.Square,
                             accum_out=ssq[:, b:b + 1])
                s.sync()
                s.activation(out=rst[:, b:b + 1], in_=ssq[:, b:b + 1],
                             func=AF.Ln, scale=1.0 / cols, bias=epsc[:, 0:1])
                s.sync()
                s.activation(out=rst[:, b:b + 1], in_=rst[:, b:b + 1],
                             func=AF.Exp, scale=-0.5).then_inc(sm, 1)

        @block.gpsimd
        def _(gpsimd):
            for i in range(nt):
                b = i % 2
                cv = ct[:, b * cols:(b + 1) * cols]
                yv = yt[:, b * cols:(b + 1) * cols]
                gpsimd.wait_ge(sm, i + 1)
                gpsimd.tensor_scalar_mul(yv, cv, rst[:, b:b + 1]).then_inc(gm, 1)
                gpsimd.wait_ge(gm, i + 1)
                if i > 0:
                    gpsimd.wait_ge(st, 16 * i)
                gpsimd.dma_start(out=y[128 * i:128 * (i + 1), :], in_=yv).then_inc(st, 16)
            gpsimd.wait_ge(st, 16 * nt)

    return nc


_ln_cache = {}


def ln_host(x):
    ntok, C = x.shape
    per = ntok // NCORES
    key = (per, C)
    if key not in _ln_cache:
        _ln_cache[key] = build_ln(per, C)
    nc = _ln_cache[key]
    shards = [np.ascontiguousarray(x[i * per:(i + 1) * per]) for i in range(NCORES)]
    res = run_bass_kernel_spmd(nc, [{"x": s} for s in shards], core_ids=list(range(NCORES)))
    return np.concatenate([r["y"] for r in res.results], axis=0)


def _hadamard(n):
    idx = np.arange(n, dtype=np.uint32)
    par = (np.bitwise_count(np.bitwise_and.outer(idx, idx)) & 1).astype(np.int8)
    return np.where(par == 0, np.float32(1.0), np.float32(-1.0))


def _scan_masks():
    dmask8 = np.zeros((128, 8), np.float32)
    for p in range(128):
        dmask8[p, p % 8] = 1.0
    crmask = np.zeros((128, 128), np.float32)
    for blk in range(4):
        for k_ in range(4):
            for h in range(8):
                for r in range(4):
                    if k_ < r:
                        crmask[blk * 32 + k_ * 8 + h, 40 * r + h] = 1.0
    aqmask = np.zeros((128, 256), np.float32)
    for j in range(16):
        for h in range(8):
            for m in range(4):
                for r in range(4):
                    if j <= 4 * m + r:
                        aqmask[j * 8 + h, m * 64 + 32 * (m % 2) + 8 * r + h] = 1.0
    e8sel = np.zeros((128, 512), np.float32)
    for i in range(16):
        for h in range(8):
            e8sel[i * 8 + h, i * 32 + 8 * (i % 4) + h] = 1.0
    selmask = np.zeros((32, 4), np.float32)
    for r in range(4):
        selmask[8 * r:8 * r + 8, r] = 1.0
    bfe = lambda a: np.ascontiguousarray(a.astype(BFNP))
    return {"dmask8": bfe(dmask8), "sel8": bfe(dmask8.T.copy()),
            "crmask": bfe(crmask), "aqmask": bfe(aqmask),
            "e8sel": e8sel, "selmask": selmask.astype(np.uint8)}


_dbg = {}


def kernel(**inputs):
    ii = {k: np.asarray(v) for k, v in inputs.items()}
    _dbg.clear()
    CC = np.ascontiguousarray
    bfe = lambda a: CC(np.asarray(a).astype(BFNP))
    R8 = list(range(NCORES))

    x = CC(ii["L0_attn_x"].astype(np.float32, copy=False))
    xf = x.reshape(B * T, HID)
    x_cm = [CC(xf[c * TPC:(c + 1) * TPC].T) for c in range(NCORES)]

    xn1 = ln_host(xf)
    _dbg["xn1"] = xn1
    xn1_cm = [CC(xn1[c * TPC:(c + 1) * TPC].T) for c in range(NCORES)]
    xn1_b = [CC(xn1[b * T:(b + 1) * T].T) for b in range(B)]

    TL = 512
    NL = T // TL
    WqT = CC(ii["L0_attn_Wq"].T)
    WkT = CC(ii["L0_attn_Wk"].T)
    WvT = CC(ii["L0_attn_Wv"].T)
    WaT = CC(ii["L0_attn_Wa"].T)
    WbT = CC(ii["L0_attn_Wb"].T)
    ba = ii["L0_attn_ba"].astype(np.float32, copy=False)
    bb = ii["L0_attn_bb"].astype(np.float32, copy=False)
    qcw = ii["L0_attn_qconv_w"].astype(np.float32, copy=False)
    kcw = ii["L0_attn_kconv_w"].astype(np.float32, copy=False)
    vcw = ii["L0_attn_vconv_w"].astype(np.float32, copy=False)
    qcb = ii["L0_attn_qconv_b"].astype(np.float32, copy=False)
    kcb = ii["L0_attn_kconv_b"].astype(np.float32, copy=False)
    vcb = ii["L0_attn_vconv_b"].astype(np.float32, copy=False)
    eyef = np.eye(128, dtype=np.float32)
    eyeb = bfe(np.eye(128, dtype=np.float32))
    masks = _scan_masks()

    l2f_nc = build_l2f(Prog, run_prog, TL)
    l2s_nc = build_l2s(Prog, run_prog, TL)
    lamq = [np.zeros((128, 8), np.float32) for _ in R8]
    kprev_cm = [np.zeros((1024, 16), BFNP) for _ in R8]
    kprev_tm = [np.zeros((16, 1024), BFNP) for _ in R8]
    lam_carry = [np.zeros((17, 1024), np.float32) for _ in R8]
    st_c = [np.zeros((128, 1024), np.float32) for _ in R8]
    w2p_c = [np.zeros((16, 1024), np.float32) for _ in R8]
    o_full = np.zeros((B, T, 16, 128), BFNP)
    for seg in range(NL):
        s0 = seg * TL
        infs = []
        for c in R8:
            b, g = c // 2, c % 2
            xsl = np.zeros((2048, TL + 3), BFNP)
            lo = max(0, s0 - 3)
            xsl[:, 3 - (s0 - lo):] = xn1_b[b][:, lo:s0 + TL]
            infs.append({
                "xnT": xsl,
                "wqT": CC(WqT[:, g * 1024:(g + 1) * 1024]),
                "wkT": CC(WkT[:, g * 1024:(g + 1) * 1024]),
                "wvT": CC(WvT[:, g * 1024:(g + 1) * 1024]),
                "waT": CC(WaT[:, g * 1024:(g + 1) * 1024]),
                "wbT": CC(WbT[:, g * 8:(g + 1) * 8]),
                "bav": CC(ba[g * 1024:(g + 1) * 1024, None]),
                "bbv": CC(bb[g * 8:(g + 1) * 8, None]),
                "qcw": CC(qcw[g * 1024:(g + 1) * 1024]),
                "kcw": CC(kcw[g * 1024:(g + 1) * 1024]),
                "vcw": CC(vcw[g * 1024:(g + 1) * 1024]),
                "qcb": CC(qcb[g * 1024:(g + 1) * 1024, None]),
                "kcb": CC(kcb[g * 1024:(g + 1) * 1024, None]),
                "vcb": CC(vcb[g * 1024:(g + 1) * 1024, None]),
                "kprevcm": kprev_cm[c], "lamq_in": lamq[c],
                "identf": eyef, "identb2": eyeb})
        rf = run_bass_kernel_spmd(l2f_nc, infs, core_ids=R8).results
        inss = []
        for c in R8:
            of = rf[c]
            lam_full = np.zeros((17 + TL, 1024), np.float32)
            lam_full[0:17] = lam_carry[c]
            lam_full[17:] = of["lamtm_o"]
            d = {"kcm": of["kcm_o"], "khcm": of["khcm_o"], "qcm": of["qcm_o"],
                 "ktm": of["ktm_o"], "vtm": of["vtm_o"], "atm": of["atm_o"],
                 "lam_full": lam_full, "A_in": of["A_o"], "AQ_in": of["AQ_o"],
                 "Ap_in": of["Ap_o"], "kprevtm": kprev_tm[c],
                 "st_in": st_c[c], "w2prev_in": w2p_c[c]}
            d.update(masks)
            inss.append(d)
            lamq[c] = of["lamq_o"]
            kprev_cm[c] = CC(of["kcm_o"][:, -16:])
            kprev_tm[c] = CC(of["ktm_o"][-16:])
            lam_carry[c] = CC(of["lamtm_o"][-17:])
        rs = run_bass_kernel_spmd(l2s_nc, inss, core_ids=R8).results
        for c in R8:
            b, g = c // 2, c % 2
            o_full[b, s0:s0 + TL, 8 * g:8 * (g + 1)] = \
                rs[c]["o_out"].reshape(TL, 8, 128)
            st_c[c] = rs[c]["st_out"]
            w2p_c[c] = rs[c]["w2prev_out"]
    o_tm = o_full.reshape(B * T, 2048)
    _dbg["o_tm"] = o_tm

    go_nc = build_gate_oln(TPC)
    WgT0 = CC(ii["L0_attn_Wg"].T.astype(np.float32, copy=False))
    lnw = CC(ii["L0_attn_ln_w"].astype(np.float32, copy=False)[None, :])
    lnb = CC(ii["L0_attn_ln_b"].astype(np.float32, copy=False)[None, :])
    rg = run_bass_kernel_spmd(go_nc, [
        {"xnT": xn1_cm[c], "wgT": WgT0, "o_tm": CC(o_tm[c * TPC:(c + 1) * TPC]),
         "lnw": lnw, "lnb": lnb} for c in R8], core_ids=R8).results
    og_cm = [CC(rg[c]["og_tm"].T) for c in R8]
    _dbg["og"] = [rg[c]["og_tm"] for c in R8]

    m4_nc = build_mm_res_ln(TPC, 16, False, True)
    WoT0 = CC(ii["L0_attn_Wo"].T.astype(np.float32, copy=False))
    r4 = run_bass_kernel_spmd(m4_nc, [
        {"aT": og_cm[c], "wT": WoT0, "xT": x_cm[c]} for c in R8],
        core_ids=R8).results
    x2_cm = [r4[c]["x1"] for c in R8]
    _dbg["x1"] = x2_cm
    xn2_cm = [r4[c]["xn"] for c in R8]
    _dbg["xn2"] = xn2_cm

    dp1_nc = build_dualproj(TPC, 5632)
    gateT = CC(ii["L0_ffn_gate_w"].T.astype(np.float32, copy=False))
    upT = CC(ii["L0_ffn_up_w"].T.astype(np.float32, copy=False))
    gate_s = ii["L0_ffn_gate_s"].astype(np.float32, copy=False)
    up_s = ii["L0_ffn_up_s"].astype(np.float32, copy=False)
    r5 = run_bass_kernel_spmd(dp1_nc, [
        {"aT": xn2_cm[c], "gwT": gateT, "uwT": upT,
         "gsv": gate_s, "usv": up_s} for c in R8], core_ids=R8).results
    m5_nc = build_mm_res_ln(TPC, 44, True, True)
    downT = CC(ii["L0_ffn_down_w"].T.astype(np.float32, copy=False))
    down_s = ii["L0_ffn_down_s"].astype(np.float32, copy=False)
    r5b = run_bass_kernel_spmd(m5_nc, [
        {"aT": r5[c]["h"], "wT": downT, "xT": x2_cm[c], "svec": down_s}
        for c in R8], core_ids=R8).results
    x3_cm = [r5b[c]["x1"] for c in R8]
    _dbg["hdn"] = [r5[c]["h"] for c in R8]
    _dbg["x2"] = x3_cm
    xn3_cm = [r5b[c]["xn"] for c in R8]
    _dbg["xn3"] = xn3_cm

    l1a_nc = build_l1a(TPC)
    wv1 = CC(ii["L1_attn_Wv"].astype(np.float32, copy=False))
    wg1 = CC(ii["L1_attn_Wg"].astype(np.float32, copy=False))
    wq1 = CC(ii["L1_attn_Wq"].astype(np.float32, copy=False))
    bg1 = ii["L1_attn_bg"].astype(np.float32, copy=False)
    bq1 = ii["L1_attn_bq"].astype(np.float32, copy=False)
    r6 = run_bass_kernel_spmd(l1a_nc, [
        {"aT": xn3_cm[c], "wv": wv1, "wg": wg1, "wq": wq1,
         "bg": bg1, "bq": bq1} for c in R8], core_ids=R8).results
    ef_nc = build_emafix_load(TPC)
    decrow = (0.95 ** (np.arange(TPC, dtype=np.float64) + 1)).astype(np.float32)[None, :]
    zcar = np.zeros((1, 2048), np.float32)
    r7 = run_bass_kernel_spmd(ef_nc, [
        {"memloc": r6[c]["memloc"],
         "carryT": (r6[c - 1]["carryT"] if c % 2 == 1 else zcar),
         "que": r6[c]["que"], "decrow": decrow} for c in R8],
        core_ids=R8).results
    _dbg["mem"] = [r6[c]["memloc"] for c in R8]
    _dbg["que"] = [r6[c]["que"] for c in R8]
    _dbg["load"] = [r7[c]["loadT"] for c in R8]
    m7_nc = build_mm_res_ln(TPC, 16, True, True)
    Wo1 = CC(ii["L1_attn_Wo"].astype(np.float32, copy=False))
    sc05 = np.full(2048, 0.05, np.float32)
    r7b = run_bass_kernel_spmd(m7_nc, [
        {"aT": r7[c]["loadT"], "wT": Wo1, "xT": x3_cm[c], "svec": sc05}
        for c in R8], core_ids=R8).results
    x4_cm = [r7b[c]["x1"] for c in R8]
    _dbg["x3"] = x4_cm
    xn4_cm = [r7b[c]["xn"] for c in R8]
    _dbg["xn4"] = xn4_cm

    hi_nc = build_had_in(TPC)
    h2048 = bfe(_hadamard(2048))
    su_ug = ii["L1_ffn_SU_ug"].astype(np.float32, copy=False)
    r8 = run_bass_kernel_spmd(hi_nc, [
        {"xnT": xn4_cm[c], "su": su_ug, "hT": h2048} for c in R8],
        core_ids=R8).results
    _dbg["xw"] = [r8[c]["xw"] for c in R8]
    dp2_nc = build_dualproj(TPC, 8192, post_had=True, sv_scale=64.0, mul_su=True)
    wgate1 = CC(ii["L1_ffn_W_gate"].T.astype(np.float32, copy=False))
    wup1 = CC(ii["L1_ffn_W_up"].T.astype(np.float32, copy=False))
    wsug = ii["L1_ffn_Wscale_ug"].astype(np.float32, copy=False)
    su_dp = ii["L1_ffn_SU_dp"].astype(np.float32, copy=False)
    r8b = run_bass_kernel_spmd(dp2_nc, [
        {"aT": r8[c]["xw"], "gwT": wgate1, "uwT": wup1,
         "gsv": CC(wsug[8192:]), "usv": CC(wsug[:8192]),
         "sudp": su_dp, "h128": bfe(_hadamard(128))} for c in R8],
        core_ids=R8).results
    z1p = [CC(r8b[c]["h"].reshape(64, 64, 2, TPC).transpose(1, 0, 2, 3)
              .reshape(8192, TPC)) for c in R8]
    fh_nc = build_foldH(8192)
    wdT = ii["L1_ffn_W_down"].T.astype(np.float32, copy=False)
    wdperm = CC(wdT.reshape(64, 64, 2, 2048).transpose(1, 0, 2, 3).reshape(8192, 2048))
    h64i2 = bfe(np.kron(_hadamard(64), np.eye(2, dtype=np.float32)))
    rfh = run_bass_kernel_spmd(fh_nc, [{"wT": wdperm, "hmat": h64i2}],
                               core_ids=[0]).results
    w2T = rfh[0]["w2T"]
    m9_nc = build_mm_res_ln(TPC, 64, True, False, svec_scale=64.0)
    wsdp = ii["L1_ffn_Wscale_dp"].astype(np.float32, copy=False)
    r9 = run_bass_kernel_spmd(m9_nc, [
        {"aT": z1p[c], "wT": w2T, "xT": x4_cm[c], "svec": wsdp}
        for c in R8], core_ids=R8).results

    out = np.concatenate([r9[c]["x1"].T for c in R8], axis=0)
    return CC(out.reshape(B, T, HID))


ENGS = ("sync", "tensor", "vector", "scalar", "gpsimd")


class Prog:

    def __init__(self, nc):
        self.nc = nc
        self.q = {e: [] for e in ENGS}
        self.ct = {}
        self.sem_names = []
        self.watermark = {e: {} for e in ENGS}
        self.selfct = {e: 0 for e in ENGS}

    def sem(self, name):
        if name not in self.ct:
            self.ct[name] = 0
            self.sem_names.append(name)
        return name

    def cur(self, name):
        return self.ct[name]

    def add(self, eng, fn, wait=(), inc=()):
        wait = [(s, v) for (s, v) in wait if v > 0]
        inc = list(inc)
        for s, n in inc:
            self.sem(s)
            self.ct[s] += n
        for s, _ in wait:
            self.sem(s)
        self.q[eng].append((fn, wait, inc))

    def dma(self, eng, sem, fn, wait=(), extra_inc=()):
        self.sem(sem)
        w = list(wait)
        w.append((sem, self.cur(sem)))
        self.add(eng, fn, wait=w, inc=[(sem, 16), *extra_inc])

    def selfsync(self, eng):
        name = "_self_" + eng
        self.sem(name)
        if not self.q[eng]:
            return
        fn, wait, inc = self.q[eng][-1]
        if inc:
            self.add(eng, lambda e, s: e.nop(), wait=[], inc=[(name, 1)])
        else:
            inc.append((name, 1))
            self.ct[name] += 1
        self.add(eng, None, wait=[(name, self.ct[name])])

    def emit(self, block, sems):
        for eng in ENGS:
            ops = self.q[eng]
            if not ops:
                continue

            def make(eng, ops):
                def body(e):
                    wm = self.watermark[eng]
                    for fn, wait, inc in ops:
                        for s, v in wait:
                            if wm.get(s, -1) < v:
                                e.wait_ge(sems[s], v)
                                wm[s] = v
                        inst = fn(e, sems) if fn is not None else None
                        first = True
                        for s, n in inc:
                            if inst is None or not first:
                                inst = e.nop()
                            inst.then_inc(sems[s], n)
                            first = False
                    return

                return body

            getattr(block, eng)(make(eng, ops))


def run_prog(nc, prog):
    import contextlib
    with contextlib.ExitStack() as st:
        sems = {}
        for name in prog.sem_names:
            sems[name] = st.enter_context(prog.nc.semaphore(name))
        block = st.enter_context(prog.nc.Block())
        prog.emit(block, sems)


def build_mm_res_ln(tpc, nk, has_scale, do_ln, label="mmres", svec_scale=1.0):
    nc = bass.Bass("TRN2", target_bir_lowering=False)
    CIN = nk * 128
    aT = nc.dram_tensor("aT", [CIN, tpc], BF16, kind="ExternalInput").ap()
    wT = nc.dram_tensor("wT", [CIN, 2048], F32, kind="ExternalInput").ap()
    xT = nc.dram_tensor("xT", [2048, tpc], F32, kind="ExternalInput").ap()
    if has_scale:
        svec = nc.dram_tensor("svec", [2048], F32, kind="ExternalInput").ap()
    x1o = nc.dram_tensor("x1", [2048, tpc], F32, kind="ExternalOutput").ap()
    if do_ln:
        xno = nc.dram_tensor("xn", [2048, tpc], BF16, kind="ExternalOutput").ap()
    wbf = nc.dram_tensor("wbf", [CIN, 2048], BF16).ap()
    NT = 512
    nch = tpc // NT
    ND = 16
    SLAB = 8
    slabs = []
    r = 0
    while r < nk:
        slabs.append((r, min(SLAB, nk - r)))
        r += min(SLAB, nk - r)
    NS = len(slabs)
    import contextlib
    with contextlib.ExitStack() as _stk:
        def sb(name, shape, dt=F32):
            return _stk.enter_context(nc.sbuf_tensor(name, shape, dt))

        def psb(name, shape, dt=F32):
            return _stk.enter_context(nc.psum_tensor(name, shape, dt))

        wsl = sb("wsl", [128, 2 * SLAB * 2048], BF16)
        asl = sb("asl", [128, 2 * SLAB * NT], BF16)
        xb = sb("xb", [128, 2 * NT])
        x2c = sb("x2c", [128, ND * NT])
        x1c = sb("x1c", [128, ND * NT])
        sq = sb("sq", [128, 2 * NT])
        xnt = sb("xnt", [128, 2 * NT], BF16)
        ctmp = sb("ctmp", [128, 2 * NT])
        onescol = sb("onescol", [128, 1])
        onesrow = sb("onesrow", [1, 128])
        epsc = sb("epsc", [1, 1])
        dsb = sb("dsb", [128, 16])
        murow = sb("murow", [1, NT])
        mu2row = sb("mu2row", [1, NT])
        varrow = sb("varrow", [1, NT])
        var2row = sb("var2row", [1, NT])
        rstrow = sb("rstrow", [1, NT])
        ps0 = psb("ps0", [128, NT])
        ps1 = psb("ps1", [128, NT])
        psmu = psb("psmu", [1, NT])
        psvar = psb("psvar", [1, NT])
        psbmu = psb("psbmu", [128, NT])
        psbrst = psb("psbrst", [128, NT])

        P = Prog(nc)
        P.add("vector", lambda e, s: e.memset(onescol[:], 1.0))
        P.add("vector", lambda e, s: e.memset(onesrow[:], 1.0))
        if has_scale:
            P.dma("sync", "lds", lambda e, s: e.dma_start(
                out=dsb[:], in_=bass.AP(svec.tensor, 0, [[1, 128], [128, 16]]),
                allow_slow_non_contiguous=True))
            P.add("vector", None, wait=[("lds", 16)])
            if svec_scale != 1.0:
                P.add("vector", lambda e, s: e.tensor_scalar_mul(
                    dsb[:], dsb[:], float(svec_scale)))
                P.selfsync("vector")
        else:
            P.add("vector", lambda e, s: e.memset(dsb[:], 1.0))
        P.selfsync("vector")
        P.add("vector", None, inc=[("vm", 1)])
        setup_v = P.cur("vm")
        P.add("scalar", None, wait=[("vm", setup_v)])
        P.add("tensor", None, wait=[("vm", setup_v)])
        P.add("vector", lambda e, s: e.memset(epsc[:], LN_EPS))
        P.selfsync("vector")
        for kk in range(nk):
            P.dma("gpsimd", "ldw", lambda e, s, kk=kk: e.dma_start(
                out=wbf[kk * 128:(kk + 1) * 128, :],
                in_=wT[kk * 128:(kk + 1) * 128, :]))
        wcast_done = P.cur("ldw")

        x1dma = {}
        xb_rel = {}
        ps_rel = {}
        sq_rel = {}
        asl_rel = {}
        wsl_rel = {}
        stats_rel = None
        bc_rel = None
        for tc in range(nch):
            accmm = {}
            for si, (s0, slen) in enumerate(slabs):
                ib = (tc * NS + si) % 2
                w = [("ldw", wcast_done)]
                if (tc * NS + si - 2) in wsl_rel:
                    w.append(wsl_rel[tc * NS + si - 2])
                P.dma("sync", "ldwsl", lambda e, s, s0=s0, slen=slen, ib=ib: e.dma_start(
                    out=wsl[:, ib * SLAB * 2048:ib * SLAB * 2048 + slen * 2048],
                    in_=bass.AP(wbf.tensor, s0 * 128 * 2048,
                                [[2048, 128], [128 * 2048, slen], [1, 2048]])), wait=w)
                ldw_done = P.cur("ldwsl")
                w = []
                if (tc * NS + si - 2) in asl_rel:
                    w.append(asl_rel[tc * NS + si - 2])
                P.dma("sync", "lda", lambda e, s, s0=s0, slen=slen, ib=ib, tc=tc: e.dma_start(
                    out=asl[:, ib * SLAB * NT:ib * SLAB * NT + slen * NT],
                    in_=bass.AP(aT.tensor, s0 * 128 * tpc + tc * NT,
                                [[tpc, 128], [128 * tpc, slen], [1, NT]])), wait=w)
                lda_done = P.cur("lda")
                for d in range(ND):
                    ps = ps0 if d % 2 == 0 else ps1
                    w = [("ldwsl", ldw_done), ("lda", lda_done)]
                    if d % 2 in ps_rel:
                        w.append(ps_rel[d % 2])
                    for k in range(slen):
                        P.add("tensor", lambda e, s, k=k, d=d, ib=ib, ps=ps, slen=slen: e.matmul(
                            ps[:],
                            wsl[:, ib * SLAB * 2048 + k * 2048 + d * 128:
                                ib * SLAB * 2048 + k * 2048 + (d + 1) * 128],
                            asl[:, ib * SLAB * NT + k * NT:ib * SLAB * NT + (k + 1) * NT],
                            start=(k == 0), stop=(k == slen - 1)),
                            wait=(w if k == 0 else []),
                            inc=([("mm", 1)] if k == slen - 1 else []))
                    mm_done = P.cur("mm")
                    if d == ND - 1:
                        wsl_rel[tc * NS + si] = ("mm", mm_done)
                        asl_rel[tc * NS + si] = ("mm", mm_done)
                    if si == 0:
                        P.add("vector", lambda e, s, d=d, ps=ps: e.tensor_copy(
                            x2c[:, d * NT:(d + 1) * NT], ps[:]),
                            wait=[("mm", mm_done)])
                    else:
                        P.add("vector", lambda e, s, d=d, ps=ps: e.tensor_add(
                            x2c[:, d * NT:(d + 1) * NT],
                            x2c[:, d * NT:(d + 1) * NT], ps[:]),
                            wait=[("mm", mm_done)])
                    P.selfsync("vector")
                    P.add("vector", None, inc=[("vm", 1)])
                    ps_rel[d % 2] = ("vm", P.cur("vm"))
            for d in range(ND):
                i = tc * ND + d
                w = [xb_rel[i - 2]] if i - 2 in xb_rel else []
                P.dma("sync", "ldx", lambda e, s, d=d, tc=tc, i=i: e.dma_start(
                    out=xb[:, (i % 2) * NT:(i % 2 + 1) * NT],
                    in_=xT[d * 128:(d + 1) * 128, tc * NT:(tc + 1) * NT]), wait=w)
                ldx_done = P.cur("ldx")
                w = [("ldx", ldx_done)]
                if d in x1dma:
                    w.append(("st1", x1dma[d]))
                P.add("vector", lambda e, s, d=d, i=i: e.scalar_tensor_tensor(
                    x1c[:, d * NT:(d + 1) * NT],
                    x2c[:, d * NT:(d + 1) * NT], dsb[:, d:d + 1],
                    xb[:, (i % 2) * NT:(i % 2 + 1) * NT],
                    op0=ALU.mult, op1=ALU.add), wait=w)
                P.selfsync("vector")
                P.add("vector", None, inc=[("vm", 1)])
                vm_done = P.cur("vm")
                xb_rel[i] = ("vm", vm_done)
                P.dma("gpsimd", "st1", lambda e, s, d=d, tc=tc: e.dma_start(
                    out=x1o[d * 128:(d + 1) * 128, tc * NT:(tc + 1) * NT],
                    in_=x1c[:, d * NT:(d + 1) * NT]), wait=[("vm", vm_done)])
                x1dma[d] = P.cur("st1")
                if do_ln:
                    w = [("vm", vm_done)]
                    if d % 2 in sq_rel:
                        w.append(sq_rel[d % 2])
                    P.add("scalar", lambda e, s, d=d: e.activation(
                        out=sq[:, (d % 2) * NT:(d % 2 + 1) * NT],
                        in_=x1c[:, d * NT:(d + 1) * NT], func=AF.Square), wait=w)
                    P.selfsync("scalar")
                    P.add("scalar", None, inc=[("sm", 1)])
                    sm_done = P.cur("sm")
                    w = [("vm", vm_done)]
                    if d == 0 and stats_rel:
                        w.append(stats_rel)
                    P.add("tensor", lambda e, s, d=d: e.matmul(
                        psmu[:], onescol[:], x1c[:, d * NT:(d + 1) * NT],
                        start=(d == 0), stop=(d == ND - 1)), wait=w)
                    P.add("tensor", lambda e, s, d=d: e.matmul(
                        psvar[:], onescol[:], sq[:, (d % 2) * NT:(d % 2 + 1) * NT],
                        start=(d == 0), stop=(d == ND - 1)),
                        wait=[("sm", sm_done)], inc=[("mm", 1)])
                    sq_rel[d % 2] = ("mm", P.cur("mm"))
            if not do_ln:
                continue
            allmm = P.cur("mm")
            P.add("scalar", lambda e, s: e.activation(
                out=murow[:], in_=psmu[:], func=AF.Copy, scale=1.0 / 2048),
                wait=[("mm", allmm)])
            P.selfsync("scalar")
            P.add("scalar", lambda e, s: e.activation(
                out=mu2row[:], in_=murow[:], func=AF.Square))
            P.add("scalar", lambda e, s: e.activation(
                out=varrow[:], in_=psvar[:], func=AF.Copy, scale=1.0 / 2048))
            P.selfsync("scalar")
            P.add("scalar", None, inc=[("sm", 1)])
            P.add("vector", lambda e, s: e.tensor_sub(var2row[:], varrow[:], mu2row[:]),
                  wait=[("sm", P.cur("sm"))])
            P.selfsync("vector")
            P.add("vector", None, inc=[("vm", 1)])
            P.add("scalar", lambda e, s: e.activation(
                out=rstrow[:], in_=var2row[:], func=AF.Ln, bias=epsc[:]),
                wait=[("vm", P.cur("vm"))])
            P.selfsync("scalar")
            P.add("scalar", lambda e, s: e.activation(
                out=rstrow[:], in_=rstrow[:], func=AF.Exp, scale=-0.5))
            P.selfsync("scalar")
            P.add("scalar", None, inc=[("sm", 1)])
            sfin = P.cur("sm")
            stats_rel = ("sm", sfin)
            w = [("sm", sfin)]
            if bc_rel:
                w.append(bc_rel)
            P.add("tensor", lambda e, s: e.matmul(
                psbmu[:], onesrow[:], murow[:], start=True, stop=True), wait=w)
            P.add("tensor", lambda e, s: e.matmul(
                psbrst[:], onesrow[:], rstrow[:], start=True, stop=True),
                inc=[("mm", 1)])
            bc_done = P.cur("mm")
            for d in range(ND):
                bb = d % 2
                P.add("vector", lambda e, s, d=d, bb=bb: e.tensor_sub(
                    ctmp[:, bb * NT:(bb + 1) * NT],
                    x1c[:, d * NT:(d + 1) * NT], psbmu[:]),
                    wait=[("mm", bc_done)])
                P.selfsync("vector")
                w = []
                if "st2" in P.ct and d >= 2:
                    w.append(("st2", P.cur("st2")))
                P.add("vector", lambda e, s, d=d, bb=bb: e.tensor_mul(
                    xnt[:, bb * NT:(bb + 1) * NT],
                    ctmp[:, bb * NT:(bb + 1) * NT], psbrst[:]), wait=w)
                P.selfsync("vector")
                P.add("vector", None, inc=[("vm", 1)])
                P.dma("gpsimd", "st2", lambda e, s, d=d, tc=tc, bb=bb: e.dma_start(
                    out=xno[d * 128:(d + 1) * 128, tc * NT:(tc + 1) * NT],
                    in_=xnt[:, bb * NT:(bb + 1) * NT]),
                    wait=[("vm", P.cur("vm"))])
            bc_rel = ("vm", P.cur("vm"))
        fw = [("st1", P.cur("st1"))]
        if "st2" in P.ct:
            fw.append(("st2", P.cur("st2")))
        P.add("gpsimd", None, wait=fw)
        run_prog(nc, P)
    return nc


def build_dualproj(tpc, ni, post_had=False, sv_scale=1.0, mul_su=False):
    HSCALE = 1.0 / (64.0 * float(np.sqrt(8192.0))) if post_had else 1.0
    nc = bass.Bass("TRN2", target_bir_lowering=False)
    aT = nc.dram_tensor("aT", [2048, tpc], BF16, kind="ExternalInput").ap()
    gwT = nc.dram_tensor("gwT", [2048, ni], F32, kind="ExternalInput").ap()
    uwT = nc.dram_tensor("uwT", [2048, ni], F32, kind="ExternalInput").ap()
    gsv = nc.dram_tensor("gsv", [ni], F32, kind="ExternalInput").ap()
    usv = nc.dram_tensor("usv", [ni], F32, kind="ExternalInput").ap()
    if post_had:
        h128 = nc.dram_tensor("h128", [128, 128], BF16, kind="ExternalInput").ap()
    if mul_su:
        sudp = nc.dram_tensor("sudp", [ni], F32, kind="ExternalInput").ap()
    ho = nc.dram_tensor("h", [ni, tpc], BF16, kind="ExternalOutput").ap()
    NT = 512
    nch = tpc // NT
    NJ = ni // 128
    NK = 16
    import contextlib
    with contextlib.ExitStack() as _stk:
        def sb(name, shape, dt=F32):
            return _stk.enter_context(nc.sbuf_tensor(name, shape, dt))

        def psb(name, shape, dt=F32):
            return _stk.enter_context(nc.psum_tensor(name, shape, dt))

        ab = sb("ab", [128, NK * tpc], BF16)
        gwb = sb("gwb", [128, 2 * NK * 128], BF16)
        uwb = sb("uwb", [128, 2 * NK * 128], BF16)
        gst = sb("gst", [128, NT], BF16)
        ust = sb("ust", [128, NT], BF16)
        ht = sb("ht", [128, 2 * NT], BF16)
        zt = sb("zt", [128, 2 * NT], BF16)
        gsb = sb("gsb", [128, NJ])
        usb = sb("usb", [128, NJ])
        sudpb = sb("sudpb", [128, NJ])
        hadb = sb("hadb", [128, 128], BF16)
        psg = psb("psg", [128, NT])
        psu = psb("psu", [128, NT])
        psz = psb("psz", [128, NT])

        P = Prog(nc)
        P.dma("sync", "lda", lambda e, s: e.dma_start(
            out=ab[:], in_=bass.AP(aT.tensor, 0, [[tpc, 128], [128 * tpc, NK], [1, tpc]])))
        lda_done = P.cur("lda")
        P.dma("sync", "lds", lambda e, s: e.dma_start(
            out=gsb[:], in_=bass.AP(gsv.tensor, 0, [[1, 128], [128, NJ]]),
            allow_slow_non_contiguous=True))
        P.dma("sync", "lds", lambda e, s: e.dma_start(
            out=usb[:], in_=bass.AP(usv.tensor, 0, [[1, 128], [128, NJ]]),
            allow_slow_non_contiguous=True))
        if post_had:
            P.dma("sync", "lds", lambda e, s: e.dma_start(out=hadb[:], in_=h128))
        if mul_su:
            P.dma("sync", "lds", lambda e, s: e.dma_start(
                out=sudpb[:], in_=bass.AP(sudp.tensor, 0, [[1, 128], [128, NJ]]),
                allow_slow_non_contiguous=True))
        lds_done = P.cur("lds")
        if sv_scale != 1.0 or mul_su:
            P.add("vector", None, wait=[("lds", lds_done)])
            if sv_scale != 1.0:
                P.add("vector", lambda e, s: e.tensor_scalar_mul(
                    gsb[:], gsb[:], float(sv_scale)))
                P.selfsync("vector")
                P.add("vector", lambda e, s: e.tensor_scalar_mul(
                    usb[:], usb[:], float(sv_scale)))
                P.selfsync("vector")
            if mul_su:
                P.add("vector", lambda e, s: e.tensor_mul(usb[:], usb[:], sudpb[:]))
                P.selfsync("vector")
            P.add("vector", None, inc=[("vm", 1)])
            P.add("scalar", None, wait=[("lds", lds_done), ("vm", P.cur("vm"))])
        else:
            P.add("scalar", None, wait=[("lds", lds_done)])
        P.add("tensor", None, wait=[("lda", lda_done), ("lds", lds_done)])

        wrel = {}
        evrel = {}
        htrel = {}
        ztrel = {}
        for j in range(NJ):
            jb = j % 2
            w = [wrel[j - 2]] if j - 2 in wrel else []
            P.dma("gpsimd", "ldw", lambda e, s, j=j, jb=jb: e.dma_start(
                out=gwb[:, jb * NK * 128:(jb + 1) * NK * 128],
                in_=bass.AP(gwT.tensor, j * 128, [[ni, 128], [128 * ni, NK], [1, 128]])),
                wait=w)
            P.dma("gpsimd", "ldw2", lambda e, s, j=j, jb=jb: e.dma_start(
                out=uwb[:, jb * NK * 128:(jb + 1) * NK * 128],
                in_=bass.AP(uwT.tensor, j * 128, [[ni, 128], [128 * ni, NK], [1, 128]])),
                wait=([wrel[j - 2]] if j - 2 in wrel else []))
            ldw_done, ldw2_done = P.cur("ldw"), P.cur("ldw2")
            for tc in range(nch):
                w = [("ldw", ldw_done), ("ldw2", ldw2_done)]
                if (j * nch + tc - 1) in evrel:
                    w.append(evrel[j * nch + tc - 1])
                for k in range(NK):
                    P.add("tensor", lambda e, s, k=k, j=j, jb=jb, tc=tc: e.matmul(
                        psg[:], gwb[:, jb * NK * 128 + k * 128:jb * NK * 128 + (k + 1) * 128],
                        ab[:, k * tpc + tc * NT:k * tpc + (tc + 1) * NT],
                        start=(k == 0), stop=(k == NK - 1)),
                        wait=(w if k == 0 else []))
                for k in range(NK):
                    P.add("tensor", lambda e, s, k=k, j=j, jb=jb, tc=tc: e.matmul(
                        psu[:], uwb[:, jb * NK * 128 + k * 128:jb * NK * 128 + (k + 1) * 128],
                        ab[:, k * tpc + tc * NT:k * tpc + (tc + 1) * NT],
                        start=(k == 0), stop=(k == NK - 1)),
                        inc=([("mm", 1)] if k == NK - 1 else []))
                mm_done = P.cur("mm")
                if tc == nch - 1:
                    wrel[j] = ("mm", mm_done)
                it = j * nch + tc
                w = [("mm", mm_done)]
                if it - 1 in evrel:
                    w.append(evrel[it - 1])
                P.add("scalar", lambda e, s, j=j: e.activation(
                    out=gst[:], in_=psg[:], func=AF.Silu, scale=gsb[:, j:j + 1]),
                    wait=w)
                P.selfsync("scalar")
                P.add("scalar", lambda e, s, j=j: e.activation(
                    out=ust[:], in_=psu[:], func=AF.Copy, scale=usb[:, j:j + 1]))
                P.selfsync("scalar")
                P.add("scalar", None, inc=[("sm", 1)])
                sm_done = P.cur("sm")
                hb = it % 2
                w = [("sm", sm_done)]
                if it - 2 in htrel:
                    w.append(htrel[it - 2])
                P.add("vector", lambda e, s, hb=hb: e.tensor_mul(
                    ht[:, hb * NT:(hb + 1) * NT], gst[:], ust[:]), wait=w)
                P.selfsync("vector")
                P.add("vector", None, inc=[("vm", 1)])
                vm_done = P.cur("vm")
                evrel[it] = ("vm", vm_done)
                if post_had:
                    P.add("tensor", lambda e, s, hb=hb: e.matmul(
                        psz[:], hadb[:], ht[:, hb * NT:(hb + 1) * NT],
                        start=True, stop=True), wait=[("vm", vm_done)],
                        inc=[("mm", 1)])
                    htrel[it] = ("mm", P.cur("mm"))
                    w = [("mm", P.cur("mm"))]
                    if it - 2 in ztrel:
                        w.append(ztrel[it - 2])
                    P.add("scalar", lambda e, s, hb=hb: e.activation(
                        out=zt[:, hb * NT:(hb + 1) * NT], in_=psz[:], func=AF.Copy,
                        scale=HSCALE), wait=w)
                    P.selfsync("scalar")
                    P.add("scalar", None, inc=[("sm", 1)])
                    src_t = zt
                    rdy = ("sm", P.cur("sm"))
                else:
                    src_t = ht
                    rdy = ("vm", vm_done)
                P.dma("gpsimd", "sth", lambda e, s, j=j, tc=tc, hb=hb, src_t=src_t: e.dma_start(
                    out=ho[j * 128:(j + 1) * 128, tc * NT:(tc + 1) * NT],
                    in_=src_t[:, hb * NT:(hb + 1) * NT]), wait=[rdy])
                if post_had:
                    ztrel[it] = ("sth", P.cur("sth"))
                else:
                    htrel[it] = ("sth", P.cur("sth"))
        P.add("gpsimd", None, wait=[("sth", P.cur("sth"))])
        run_prog(nc, P)
    return nc


def build_l1a(tpc, decay=0.95):
    nc = bass.Bass("TRN2", target_bir_lowering=False)
    aT = nc.dram_tensor("aT", [2048, tpc], BF16, kind="ExternalInput").ap()
    wv = nc.dram_tensor("wv", [2048, 2048], F32, kind="ExternalInput").ap()
    wg = nc.dram_tensor("wg", [2048, 2048], F32, kind="ExternalInput").ap()
    wq = nc.dram_tensor("wq", [2048, 2048], F32, kind="ExternalInput").ap()
    bg = nc.dram_tensor("bg", [2048], F32, kind="ExternalInput").ap()
    bq = nc.dram_tensor("bq", [2048], F32, kind="ExternalInput").ap()
    memo = nc.dram_tensor("memloc", [2048, tpc], BF16, kind="ExternalOutput").ap()
    caro = nc.dram_tensor("carryT", [1, 2048], F32, kind="ExternalOutput").ap()
    queo = nc.dram_tensor("que", [2048, tpc], BF16, kind="ExternalOutput").ap()
    NT = 512
    nch = tpc // NT
    NK = 16
    import contextlib
    with contextlib.ExitStack() as _stk:
        def sb(name, shape, dt=F32):
            return _stk.enter_context(nc.sbuf_tensor(name, shape, dt))

        def psb(name, shape, dt=F32):
            return _stk.enter_context(nc.psum_tensor(name, shape, dt))

        ab = sb("ab", [128, NK * tpc], BF16)
        wb = sb("wb", [128, 2 * NK * 128], BF16)
        gateb = sb("gateb", [128, 16 * tpc], BF16)
        storet = sb("storet", [128, tpc])
        memt = sb("memt", [128, tpc])
        membf = sb("membf", [128, tpc], BF16)
        quet = sb("quet", [128, 2 * NT], BF16)
        bgb = sb("bgb", [128, 16])
        bqb = sb("bqb", [128, 16])
        decc = sb("decc", [128, 1])
        ps0 = psb("ps0", [128, NT])
        ps1 = psb("ps1", [128, NT])

        P = Prog(nc)
        P.dma("sync", "lda", lambda e, s: e.dma_start(
            out=ab[:], in_=bass.AP(aT.tensor, 0, [[tpc, 128], [128 * tpc, NK], [1, tpc]])))
        lda_done = P.cur("lda")
        P.dma("sync", "ldb", lambda e, s: e.dma_start(
            out=bgb[:], in_=bass.AP(bg.tensor, 0, [[1, 128], [128, 16]]),
            allow_slow_non_contiguous=True))
        P.dma("sync", "ldb", lambda e, s: e.dma_start(
            out=bqb[:], in_=bass.AP(bq.tensor, 0, [[1, 128], [128, 16]]),
            allow_slow_non_contiguous=True))
        ldb_done = P.cur("ldb")
        P.add("vector", lambda e, s: e.memset(decc[:], decay))
        P.selfsync("vector")
        P.add("vector", None, inc=[("vm", 1)])
        P.add("scalar", None, wait=[("ldb", ldb_done)])
        P.add("tensor", None, wait=[("lda", lda_done)])

        def proj_phase(wten, wsem, phase, base):
            for m in range(16):
                mb = m % 2
                w = []
                if ("wrel", base + m - 2) in relmap:
                    w.append(relmap[("wrel", base + m - 2)])
                P.dma("gpsimd", wsem, lambda e, s, m=m, mb=mb, wten=wten: e.dma_start(
                    out=wb[:, mb * NK * 128:(mb + 1) * NK * 128],
                    in_=bass.AP(wten.tensor, m * 128,
                                [[2048, 128], [128 * 2048, NK], [1, 128]])), wait=w)
                ldw_done = P.cur(wsem)
                for tc in range(nch):
                    it = base * nch + m * nch + tc
                    ps = ps0 if it % 2 == 0 else ps1
                    w = [(wsem, ldw_done)]
                    if ("psrel", it - 2) in relmap:
                        w.append(relmap[("psrel", it - 2)])
                    for k in range(NK):
                        P.add("tensor", lambda e, s, k=k, mb=mb, tc=tc, ps=ps: e.matmul(
                            ps[:], wb[:, mb * NK * 128 + k * 128:mb * NK * 128 + (k + 1) * 128],
                            ab[:, k * tpc + tc * NT:k * tpc + (tc + 1) * NT],
                            start=(k == 0), stop=(k == NK - 1)),
                            wait=(w if k == 0 else []),
                            inc=([("mm", 1)] if k == NK - 1 else []))
                    mm_done = P.cur("mm")
                    if tc == nch - 1:
                        relmap[("wrel", base + m)] = ("mm", mm_done)
                    yield m, tc, ps, mm_done

        relmap = {}
        for m, tc, ps, mm_done in proj_phase(wg, "ldwg", "g", 0):
            P.add("scalar", lambda e, s, m=m, tc=tc, ps=ps: e.activation(
                out=gateb[:, m * tpc + tc * NT:m * tpc + (tc + 1) * NT],
                in_=ps[:], func=AF.Sigmoid, bias=bgb[:, m:m + 1]),
                wait=[("mm", mm_done)])
            P.selfsync("scalar")
            P.add("scalar", None, inc=[("sm", 1)])
            relmap[("psrel", 0 * nch + m * nch + tc)] = ("sm", P.cur("sm"))
        for m, tc, ps, mm_done in proj_phase(wv, "ldwv", "v", 16):
            P.add("vector", lambda e, s, m=m, tc=tc, ps=ps: e.tensor_mul(
                storet[:, tc * NT:(tc + 1) * NT], ps[:],
                gateb[:, m * tpc + tc * NT:m * tpc + (tc + 1) * NT]),
                wait=[("mm", mm_done)] + ([relmap[("emad", m - 1)]] if tc == 0 and (("emad", m - 1) in relmap) else []))
            P.selfsync("vector")
            P.add("vector", None, inc=[("vm", 1)])
            relmap[("psrel", 16 * nch + m * nch + tc)] = ("vm", P.cur("vm"))
            if tc == nch - 1:
                w = []
                if "stc" in P.ct:
                    w.append(("stc", P.cur("stc")))
                P.add("vector", lambda e, s: e.tensor_tensor_scan(
                    memt[:], decc[:].broadcast_to([128, tpc]), storet[:], 0.0,
                    op0=ALU.mult, op1=ALU.add), wait=w)
                P.selfsync("vector")
                w = []
                if "stm" in P.ct:
                    w.append(("stm", P.cur("stm")))
                P.add("vector", lambda e, s: e.tensor_copy(membf[:], memt[:]), wait=w)
                P.selfsync("vector")
                P.add("vector", None, inc=[("vm", 1)])
                vm_done = P.cur("vm")
                relmap[("emad", m)] = ("vm", vm_done)
                P.dma("gpsimd", "stm", lambda e, s, m=m: e.dma_start(
                    out=memo[m * 128:(m + 1) * 128, :], in_=membf[:]), wait=[("vm", vm_done)])
                P.dma("gpsimd", "stc", lambda e, s, m=m: e.dma_start(
                    out=bass.AP(caro.tensor, m * 128, [[2048, 1], [1, 128]]),
                    in_=memt[:, tpc - 1:tpc]), wait=[("vm", vm_done)])
        for m, tc, ps, mm_done in proj_phase(wq, "ldwq", "q", 32):
            qb = (m * nch + tc) % 2
            w = [("mm", mm_done)]
            if ("qrel", m * nch + tc - 2) in relmap:
                w.append(relmap[("qrel", m * nch + tc - 2)])
            P.add("scalar", lambda e, s, qb=qb, m=m, ps=ps: e.activation(
                out=quet[:, qb * NT:(qb + 1) * NT], in_=ps[:], func=AF.Sigmoid,
                bias=bqb[:, m:m + 1]), wait=w)
            P.selfsync("scalar")
            P.add("scalar", None, inc=[("sm", 1)])
            relmap[("psrel", 32 * nch + m * nch + tc)] = ("sm", P.cur("sm"))
            P.dma("gpsimd", "stq", lambda e, s, m=m, tc=tc, qb=qb: e.dma_start(
                out=queo[m * 128:(m + 1) * 128, tc * NT:(tc + 1) * NT],
                in_=quet[:, qb * NT:(qb + 1) * NT]), wait=[("sm", P.cur("sm"))])
            relmap[("qrel", m * nch + tc)] = ("stq", P.cur("stq"))
        fw = [("stm", P.cur("stm")), ("stc", P.cur("stc")), ("stq", P.cur("stq"))]
        P.add("gpsimd", None, wait=fw)
        run_prog(nc, P)
    return nc


def build_emafix_load(tpc):
    nc = bass.Bass("TRN2", target_bir_lowering=False)
    memi = nc.dram_tensor("memloc", [2048, tpc], BF16, kind="ExternalInput").ap()
    cari = nc.dram_tensor("carryT", [1, 2048], F32, kind="ExternalInput").ap()
    quei = nc.dram_tensor("que", [2048, tpc], BF16, kind="ExternalInput").ap()
    deci = nc.dram_tensor("decrow", [1, tpc], F32, kind="ExternalInput").ap()
    lo = nc.dram_tensor("loadT", [2048, tpc], BF16, kind="ExternalOutput").ap()
    NT = 512
    nch = tpc // NT
    import contextlib
    with contextlib.ExitStack() as _stk:
        def sb(name, shape, dt=F32):
            return _stk.enter_context(nc.sbuf_tensor(name, shape, dt))

        memb = sb("memb", [128, 2 * NT], BF16)
        queb = sb("queb", [128, 2 * NT], BF16)
        carb = sb("carb", [1, 2048])
        decb = sb("decb", [1, tpc])
        mfix = sb("mfix", [128, 2 * NT])
        lt = sb("lt", [128, 2 * NT], BF16)
        psc = _stk.enter_context(nc.psum_tensor("psc", [128, NT], F32))
        psc2 = _stk.enter_context(nc.psum_tensor("psc2", [128, NT], F32))

        P = Prog(nc)
        P.dma("sync", "lds", lambda e, s: e.dma_start(out=carb[:], in_=cari))
        P.dma("sync", "lds", lambda e, s: e.dma_start(out=decb[:], in_=deci))
        lds_done = P.cur("lds")
        P.add("tensor", None, wait=[("lds", lds_done)])
        rel = {}
        for m in range(16):
            for tc in range(nch):
                it = m * nch + tc
                bb = it % 2
                w = [rel[it - 2]] if it - 2 in rel else []
                P.dma("sync", "ldm", lambda e, s, m=m, tc=tc, bb=bb: e.dma_start(
                    out=memb[:, bb * NT:(bb + 1) * NT],
                    in_=memi[m * 128:(m + 1) * 128, tc * NT:(tc + 1) * NT]), wait=w)
                P.dma("gpsimd", "ldq", lambda e, s, m=m, tc=tc, bb=bb: e.dma_start(
                    out=queb[:, bb * NT:(bb + 1) * NT],
                    in_=quei[m * 128:(m + 1) * 128, tc * NT:(tc + 1) * NT]), wait=w)
                ldm_done, ldq_done = P.cur("ldm"), P.cur("ldq")
                ps = psc if it % 2 == 0 else psc2
                w2 = [("lds", lds_done)]
                if ("ps", it - 2) in rel:
                    w2.append(rel[("ps", it - 2)])
                P.add("tensor", lambda e, s, m=m, tc=tc, ps=ps: e.matmul(
                    ps[:], carb[:, m * 128:(m + 1) * 128],
                    decb[:, tc * NT:(tc + 1) * NT], start=True, stop=True),
                    wait=w2, inc=[("mm", 1)])
                mm_done = P.cur("mm")
                P.add("vector", lambda e, s, bb=bb, ps=ps: e.tensor_add(
                    mfix[:, bb * NT:(bb + 1) * NT], ps[:],
                    memb[:, bb * NT:(bb + 1) * NT]),
                    wait=[("mm", mm_done), ("ldm", ldm_done)])
                P.selfsync("vector")
                P.add("vector", None, inc=[("vm", 1)])
                rel[("ps", it)] = ("vm", P.cur("vm"))
                w3 = [("ldq", ldq_done)]
                if ("lt", it - 2) in rel:
                    w3.append(rel[("lt", it - 2)])
                P.add("vector", lambda e, s, bb=bb: e.tensor_mul(
                    lt[:, bb * NT:(bb + 1) * NT], mfix[:, bb * NT:(bb + 1) * NT],
                    queb[:, bb * NT:(bb + 1) * NT]), wait=w3)
                P.selfsync("vector")
                P.add("vector", None, inc=[("vm", 1)])
                vm_done = P.cur("vm")
                rel[it] = ("vm", vm_done)
                P.dma("gpsimd", "st1", lambda e, s, m=m, tc=tc, bb=bb: e.dma_start(
                    out=lo[m * 128:(m + 1) * 128, tc * NT:(tc + 1) * NT],
                    in_=lt[:, bb * NT:(bb + 1) * NT]), wait=[("vm", vm_done)])
                rel[("lt", it)] = ("st1", P.cur("st1"))
        P.add("gpsimd", None, wait=[("st1", P.cur("st1"))])
        run_prog(nc, P)
    return nc


def build_foldH(ni):
    nc = bass.Bass("TRN2", target_bir_lowering=False)
    wT = nc.dram_tensor("wT", [ni, 2048], F32, kind="ExternalInput").ap()
    hmat = nc.dram_tensor("hmat", [128, 128], BF16, kind="ExternalInput").ap()
    w2 = nc.dram_tensor("w2T", [ni, 2048], F32, kind="ExternalOutput").ap()
    NJ = ni // 128
    import contextlib
    with contextlib.ExitStack() as _stk:
        def sb(name, shape, dt=F32):
            return _stk.enter_context(nc.sbuf_tensor(name, shape, dt))

        wb = sb("wb", [128, 2 * 2048], BF16)
        hb = sb("hb", [128, 128], BF16)
        ob = sb("ob", [128, 2 * 2048])
        ps0 = _stk.enter_context(nc.psum_tensor("ps0", [128, 512], F32))
        ps1 = _stk.enter_context(nc.psum_tensor("ps1", [128, 512], F32))
        P = Prog(nc)
        P.dma("sync", "ldh", lambda e, s: e.dma_start(out=hb[:], in_=hmat))
        P.add("tensor", None, wait=[("ldh", 16)])
        rel = {}
        for j in range(NJ):
            bb = j % 2
            w = [rel[j - 2]] if j - 2 in rel else []
            P.dma("gpsimd", "ldw", lambda e, s, j=j, bb=bb: e.dma_start(
                out=wb[:, bb * 2048:(bb + 1) * 2048],
                in_=wT[j * 128:(j + 1) * 128, :]), wait=w)
            ldw_done = P.cur("ldw")
            for c in range(4):
                ps = ps0 if c % 2 == 0 else ps1
                w2w = [("ldw", ldw_done)]
                if ("ps", j, c - 2) in rel:
                    w2w.append(rel[("ps", j, c - 2)])
                if c < 2 and ("ps", j - 1, c + 2) in rel:
                    w2w.append(rel[("ps", j - 1, c + 2)])
                P.add("tensor", lambda e, s, c=c, bb=bb, ps=ps: e.matmul(
                    ps[:], hb[:], wb[:, bb * 2048 + c * 512:bb * 2048 + (c + 1) * 512],
                    start=True, stop=True), wait=w2w, inc=[("mm", 1)])
                mm_done = P.cur("mm")
                wst = []
                if ("ob", j - 2) in rel:
                    wst.append(rel[("ob", j - 2)])
                P.add("scalar", lambda e, s, c=c, bb=bb, ps=ps: e.activation(
                    out=ob[:, bb * 2048 + c * 512:bb * 2048 + (c + 1) * 512],
                    in_=ps[:], func=AF.Copy), wait=[("mm", mm_done)] + wst)
                P.selfsync("scalar")
                P.add("scalar", None, inc=[("sm", 1)])
                rel[("ps", j, c)] = ("sm", P.cur("sm"))
            rel[j] = ("sm", P.cur("sm"))
            P.dma("gpsimd", "st1", lambda e, s, j=j, bb=bb: e.dma_start(
                out=w2[j * 128:(j + 1) * 128, :], in_=ob[:, bb * 2048:(bb + 1) * 2048]),
                wait=[("sm", P.cur("sm"))])
            rel[("ob", j)] = ("st1", P.cur("st1"))
        P.add("gpsimd", None, wait=[("st1", P.cur("st1"))])
        run_prog(nc, P)
    return nc


def build_had_in(tpc):
    SC = 1.0 / (64.0 * float(np.sqrt(2048.0)))
    nc = bass.Bass("TRN2", target_bir_lowering=False)
    xnT = nc.dram_tensor("xnT", [2048, tpc], BF16, kind="ExternalInput").ap()
    su = nc.dram_tensor("su", [2048], F32, kind="ExternalInput").ap()
    hT = nc.dram_tensor("hT", [2048, 2048], BF16, kind="ExternalInput").ap()
    xwo = nc.dram_tensor("xw", [2048, tpc], BF16, kind="ExternalOutput").ap()
    NT = 512
    nch = tpc // NT
    NK = 16
    import contextlib
    with contextlib.ExitStack() as _stk:
        def sb(name, shape, dt=F32):
            return _stk.enter_context(nc.sbuf_tensor(name, shape, dt))

        xb = sb("xb", [128, NK * tpc], BF16)
        hb = sb("hb", [128, NK * 128 * 2], BF16)
        sub = sb("sub", [128, 16])
        ot = sb("ot", [128, 2 * NT], BF16)
        ps0 = _stk.enter_context(nc.psum_tensor("ps0", [128, NT], F32))
        ps1 = _stk.enter_context(nc.psum_tensor("ps1", [128, NT], F32))
        P = Prog(nc)
        P.dma("sync", "lda", lambda e, s: e.dma_start(
            out=xb[:], in_=bass.AP(xnT.tensor, 0, [[tpc, 128], [128 * tpc, NK], [1, tpc]])))
        P.dma("sync", "lds", lambda e, s: e.dma_start(
            out=sub[:], in_=bass.AP(su.tensor, 0, [[1, 128], [128, 16]]),
            allow_slow_non_contiguous=True))
        P.add("vector", None, wait=[("lda", P.cur("lda")), ("lds", P.cur("lds"))])
        for k in range(NK):
            P.add("vector", lambda e, s, k=k: e.tensor_scalar_mul(
                xb[:, k * tpc:(k + 1) * tpc], xb[:, k * tpc:(k + 1) * tpc],
                sub[:, k:k + 1]))
            P.selfsync("vector")
        P.add("vector", None, inc=[("vm", 1)])
        sc_done = P.cur("vm")
        P.add("tensor", None, wait=[("vm", sc_done)])
        rel = {}
        for d in range(16):
            bb = d % 2
            w = [rel[("h", d - 2)]] if ("h", d - 2) in rel else []
            P.dma("gpsimd", "ldh", lambda e, s, d=d, bb=bb: e.dma_start(
                out=hb[:, bb * NK * 128:(bb + 1) * NK * 128],
                in_=bass.AP(hT.tensor, d * 128, [[2048, 128], [128 * 2048, NK], [1, 128]])),
                wait=w)
            ldh_done = P.cur("ldh")
            for tc in range(nch):
                it = d * nch + tc
                ps = ps0 if it % 2 == 0 else ps1
                w2 = [("ldh", ldh_done)]
                if ("ps", it - 2) in rel:
                    w2.append(rel[("ps", it - 2)])
                for k in range(NK):
                    P.add("tensor", lambda e, s, k=k, d=d, bb=bb, tc=tc, ps=ps: e.matmul(
                        ps[:], hb[:, bb * NK * 128 + k * 128:bb * NK * 128 + (k + 1) * 128],
                        xb[:, k * tpc + tc * NT:k * tpc + (tc + 1) * NT],
                        start=(k == 0), stop=(k == NK - 1)),
                        wait=(w2 if k == 0 else []),
                        inc=([("mm", 1)] if k == NK - 1 else []))
                mm_done = P.cur("mm")
                if tc == nch - 1:
                    rel[("h", d)] = ("mm", mm_done)
                ob = it % 2
                w3 = [("mm", mm_done)]
                if ("ot", it - 2) in rel:
                    w3.append(rel[("ot", it - 2)])
                P.add("scalar", lambda e, s, ob=ob, ps=ps: e.activation(
                    out=ot[:, ob * NT:(ob + 1) * NT], in_=ps[:], func=AF.Copy,
                    scale=SC), wait=w3)
                P.selfsync("scalar")
                P.add("scalar", None, inc=[("sm", 1)])
                rel[("ps", it)] = ("sm", P.cur("sm"))
                P.dma("gpsimd", "st1", lambda e, s, d=d, tc=tc, ob=ob: e.dma_start(
                    out=xwo[d * 128:(d + 1) * 128, tc * NT:(tc + 1) * NT],
                    in_=ot[:, ob * NT:(ob + 1) * NT]), wait=[("sm", P.cur("sm"))])
                rel[("ot", it)] = ("st1", P.cur("st1"))
        P.add("gpsimd", None, wait=[("st1", P.cur("st1"))])
        run_prog(nc, P)
    return nc


def build_gate_oln(tpc):
    nc = bass.Bass("TRN2", target_bir_lowering=False)
    xnT = nc.dram_tensor("xnT", [2048, tpc], BF16, kind="ExternalInput").ap()
    wgT = nc.dram_tensor("wgT", [2048, 2048], F32, kind="ExternalInput").ap()
    o_tm = nc.dram_tensor("o_tm", [tpc, 2048], BF16, kind="ExternalInput").ap()
    lnw = nc.dram_tensor("lnw", [1, 128], F32, kind="ExternalInput").ap()
    lnb = nc.dram_tensor("lnb", [1, 128], F32, kind="ExternalInput").ap()
    ogo = nc.dram_tensor("og_tm", [tpc, 2048], BF16, kind="ExternalOutput").ap()
    NK = 16
    ntt = tpc // 128
    import contextlib
    with contextlib.ExitStack() as _stk:
        def sb(name, shape, dt=F32):
            return _stk.enter_context(nc.sbuf_tensor(name, shape, dt))

        xb = sb("xb", [128, NK * tpc], BF16)
        wgb = sb("wgb", [128, NK * 2048], BF16)
        ob = sb("ob", [128, 2 * 2048], BF16)
        gt = sb("gt", [128, 2 * 2048], BF16)
        sqo = sb("sqo", [128, 2048])
        t1 = sb("t1", [128, 2048])
        ogt = sb("ogt", [128, 2 * 2048], BF16)
        mus = sb("mus", [128, 16])
        msq = sb("msq", [128, 16])
        mu2 = sb("mu2", [128, 16])
        varr = sb("varr", [128, 16])
        rstd = sb("rstd", [128, 16])
        epsc = sb("epsc", [128, 1])
        onesrow = sb("onesrow", [1, 128])
        lnwr = sb("lnwr", [1, 128])
        lnbr = sb("lnbr", [1, 128])
        lnwB = sb("lnwB", [128, 128])
        lnbB = sb("lnbB", [128, 128])
        ps0 = _stk.enter_context(nc.psum_tensor("ps0", [128, 512], F32))
        ps1 = _stk.enter_context(nc.psum_tensor("ps1", [128, 512], F32))
        psb = _stk.enter_context(nc.psum_tensor("psb", [128, 128], F32))

        P = Prog(nc)
        P.dma("sync", "lda", lambda e, s: e.dma_start(
            out=xb[:], in_=bass.AP(xnT.tensor, 0, [[tpc, 128], [128 * tpc, NK], [1, tpc]])))
        P.dma("gpsimd", "ldw", lambda e, s: e.dma_start(
            out=wgb[:], in_=bass.AP(wgT.tensor, 0, [[2048, 128], [128 * 2048, NK], [1, 2048]])))
        P.dma("sync", "lds", lambda e, s: e.dma_start(out=lnwr[:], in_=lnw))
        P.dma("sync", "lds", lambda e, s: e.dma_start(out=lnbr[:], in_=lnb))
        P.add("vector", lambda e, s: e.memset(onesrow[:], 1.0))
        P.add("vector", lambda e, s: e.memset(epsc[:], LN_EPS))
        P.selfsync("vector")
        P.add("vector", None, inc=[("vm", 1)])
        P.add("tensor", lambda e, s: e.matmul(psb[:], onesrow[:], lnwr[:],
                                              start=True, stop=True),
              wait=[("lds", P.cur("lds")), ("vm", P.cur("vm"))], inc=[("mm", 1)])
        P.add("scalar", lambda e, s: e.activation(out=lnwB[:], in_=psb[:], func=AF.Copy),
              wait=[("mm", P.cur("mm"))])
        P.selfsync("scalar")
        P.add("scalar", None, inc=[("sm", 1)])
        P.add("tensor", lambda e, s: e.matmul(psb[:], onesrow[:], lnbr[:],
                                              start=True, stop=True),
              wait=[("sm", P.cur("sm"))], inc=[("mm", 1)])
        P.add("scalar", lambda e, s: e.activation(out=lnbB[:], in_=psb[:], func=AF.Copy),
              wait=[("mm", P.cur("mm"))])
        P.selfsync("scalar")
        P.add("scalar", None, inc=[("sm", 1)])
        setup_sm = P.cur("sm")
        lda_done = P.cur("lda")
        ldw_done = P.cur("ldw")
        P.add("tensor", None, wait=[("lda", lda_done), ("ldw", ldw_done)])
        P.add("vector", None, wait=[("sm", setup_sm)])

        rel = {}
        for tt in range(ntt):
            bb = tt % 2
            w = [rel[("ob", tt - 2)]] if ("ob", tt - 2) in rel else []
            P.dma("sync", "ldo", lambda e, s, tt=tt, bb=bb: e.dma_start(
                out=ob[:, bb * 2048:(bb + 1) * 2048],
                in_=o_tm[tt * 128:(tt + 1) * 128, :]), wait=w)
            ldo_done = P.cur("ldo")
            for co in range(4):
                it = tt * 4 + co
                ps = ps0 if it % 2 == 0 else ps1
                w2 = []
                if ("ps", it - 2) in rel:
                    w2.append(rel[("ps", it - 2)])
                for k in range(NK):
                    P.add("tensor", lambda e, s, k=k, tt=tt, co=co, ps=ps: e.matmul(
                        ps[:], xb[:, k * tpc + tt * 128:k * tpc + (tt + 1) * 128],
                        wgb[:, k * 2048 + co * 512:k * 2048 + (co + 1) * 512],
                        start=(k == 0), stop=(k == NK - 1)),
                        wait=(w2 if k == 0 else []),
                        inc=([("mm", 1)] if k == NK - 1 else []))
                mm_done = P.cur("mm")
                w3 = [("mm", mm_done)]
                if ("gt", tt - 2) in rel and co == 0:
                    w3.append(rel[("gt", tt - 2)])
                P.add("scalar", lambda e, s, bb=bb, co=co, ps=ps: e.activation(
                    out=gt[:, bb * 2048 + co * 512:bb * 2048 + (co + 1) * 512],
                    in_=ps[:], func=AF.Sigmoid), wait=w3)
                P.selfsync("scalar")
                P.add("scalar", None, inc=[("sm", 1)])
                rel[("ps", it)] = ("sm", P.cur("sm"))
            g_done = ("sm", P.cur("sm"))
            ov = ob[:, bb * 2048:(bb + 1) * 2048]
            P.add("vector", None, wait=[("ldo", ldo_done)])
            for h in range(16):
                P.add("vector", lambda e, s, h=h, ov=ov: e.reduce_sum(
                    mus[:, h:h + 1], ov[:, h * 128:(h + 1) * 128], axis=AX.X))
            P.selfsync("vector")
            P.add("vector", None, inc=[("vm", 1)])
            P.add("scalar", lambda e, s, ov=ov: e.activation(
                out=sqo[:], in_=ov, func=AF.Square), wait=[("ldo", ldo_done)])
            P.selfsync("scalar")
            P.add("scalar", None, inc=[("sm", 1)])
            sq_done = P.cur("sm")
            P.add("vector", None, wait=[("sm", sq_done)])
            for h in range(16):
                P.add("vector", lambda e, s, h=h: e.reduce_sum(
                    msq[:, h:h + 1], sqo[:, h * 128:(h + 1) * 128], axis=AX.X))
            P.selfsync("vector")
            P.add("vector", lambda e, s: e.tensor_scalar_mul(mus[:], mus[:], 1.0 / 128))
            P.selfsync("vector")
            P.add("vector", lambda e, s: e.tensor_mul(mu2[:], mus[:], mus[:]))
            P.selfsync("vector")
            P.add("vector", lambda e, s: e.tensor_scalar(
                varr[:], msq[:], 1.0 / 128, None, op0=ALU.mult, op1=ALU.bypass))
            P.selfsync("vector")
            P.add("vector", lambda e, s: e.tensor_sub(varr[:], varr[:], mu2[:]))
            P.selfsync("vector")
            P.add("vector", None, inc=[("vm", 1)])
            st_done = P.cur("vm")
            P.add("scalar", lambda e, s: e.activation(
                out=rstd[:], in_=varr[:], func=AF.Ln, bias=epsc[:]),
                wait=[("vm", st_done)])
            P.selfsync("scalar")
            P.add("scalar", lambda e, s: e.activation(
                out=rstd[:], in_=rstd[:], func=AF.Exp, scale=-0.5))
            P.selfsync("scalar")
            P.add("scalar", None, inc=[("sm", 1)])
            rst_done = P.cur("sm")
            ov3 = ov.rearrange("p (h d) -> p h d", d=128)
            mub = mus[:].unsqueeze(2).broadcast_to([128, 16, 128])
            rsb = rstd[:].unsqueeze(2).broadcast_to([128, 16, 128])
            lnwb3 = lnwB[:].unsqueeze(1).broadcast_to([128, 16, 128])
            lnbb3 = lnbB[:].unsqueeze(1).broadcast_to([128, 16, 128])
            t13 = t1[:].rearrange("p (h d) -> p h d", d=128)
            P.add("vector", lambda e, s, ov3=ov3, mub=mub, t13=t13: e.tensor_tensor(
                t13, ov3, mub, op=ALU.subtract), wait=[("sm", rst_done)])
            P.selfsync("vector")
            P.add("vector", lambda e, s, t13=t13, rsb=rsb: e.tensor_tensor(
                t13, t13, rsb, op=ALU.mult))
            P.selfsync("vector")
            P.add("vector", lambda e, s, t13=t13, lnwb3=lnwb3: e.tensor_tensor(
                t13, t13, lnwb3, op=ALU.mult))
            P.selfsync("vector")
            P.add("vector", lambda e, s, t13=t13, lnbb3=lnbb3: e.tensor_tensor(
                t13, t13, lnbb3, op=ALU.add))
            P.selfsync("vector")
            w4 = [g_done]
            if ("ogt", tt - 2) in rel:
                w4.append(rel[("ogt", tt - 2)])
            P.add("vector", lambda e, s, bb=bb: e.tensor_mul(
                ogt[:, bb * 2048:(bb + 1) * 2048], t1[:],
                gt[:, bb * 2048:(bb + 1) * 2048]), wait=w4)
            P.selfsync("vector")
            P.add("vector", None, inc=[("vm", 1)])
            ogv = P.cur("vm")
            rel[("ob", tt)] = ("vm", ogv)
            rel[("gt", tt)] = ("vm", ogv)
            P.dma("gpsimd", "st1", lambda e, s, tt=tt, bb=bb: e.dma_start(
                out=ogo[tt * 128:(tt + 1) * 128, :],
                in_=ogt[:, bb * 2048:(bb + 1) * 2048]), wait=[("vm", ogv)])
            rel[("ogt", tt)] = ("st1", P.cur("st1"))
        P.add("gpsimd", None, wait=[("st1", P.cur("st1"))])
        run_prog(nc, P)
    return nc



LOGA_MIN = -25.0
C = 16
G = 4
SPAN = 128

def build_l2f(Prog, run_prog, TL=512):
    nc = bass.Bass("TRN2", target_bir_lowering=False)
    TH = TL + 3
    NSP = TL // SPAN

    def inp(name, shape, dt=F32):
        return nc.dram_tensor(name, shape, dt, kind="ExternalInput").ap()

    def outp(name, shape, dt=F32):
        return nc.dram_tensor(name, shape, dt, kind="ExternalOutput").ap()

    xnT = inp("xnT", [2048, TH], BF16)
    wts = {w: inp(w, [2048, 1024]) for w in ("wqT", "wkT", "wvT", "waT")}
    wbT = inp("wbT", [2048, 8])
    bav = inp("bav", [1024, 1]); bbv = inp("bbv", [8, 1])
    cws = {w: inp(w, [1024, 4]) for w in ("qcw", "kcw", "vcw")}
    cbs = {w: inp(w, [1024, 1]) for w in ("qcb", "kcb", "vcb")}
    kprevcm = inp("kprevcm", [1024, 16], BF16)
    lamq_in = inp("lamq_in", [128, 8])
    identf = inp("identf", [128, 128])
    identb2 = inp("identb2", [128, 128], BF16)

    kcm_o = outp("kcm_o", [1024, TL], BF16)
    khcm_o = outp("khcm_o", [1024, TL], BF16)
    qcm_o = outp("qcm_o", [1024, TL], BF16)
    ktm_o = outp("ktm_o", [TL, 1024], BF16)
    vtm_o = outp("vtm_o", [TL, 1024], BF16)
    atm_o = outp("atm_o", [TL, 1024], BF16)
    lamtm_o = outp("lamtm_o", [TL, 1024], F32)
    lamq_o = outp("lamq_o", [128, 8], F32)
    A_o = outp("A_o", [TL, 1024], BF16)
    AQ_o = outp("AQ_o", [TL, 1024], BF16)
    Ap_o = outp("Ap_o", [NSP * C, 1024], BF16)
    beta_sc = nc.dram_tensor("beta_sc", [16, TL], F32).ap()

    with contextlib.ExitStack() as stk:
        def sb(name, shape, dt=F32):
            return stk.enter_context(nc.sbuf_tensor(name, shape, dt))

        def psb(name, shape, dt=F32):
            return stk.enter_context(nc.psum_tensor(name, shape, dt))

        xcmb = sb("xcmb", [128, 16 * TH], BF16)
        wslab = sb("wslab", [128, 2 * 16 * 128], BF16)
        projf = sb("projf", [128, 8 * TH])
        cvt = sb("cvt", [128, TL])
        conb = sb("conb", [128, 4])
        conbias = sb("conbias", [128, 1])
        bab = sb("bab", [128, 8])
        bbb = sb("bbb", [8, 1])
        qcm = sb("qcm", [128, 8 * TL], BF16)
        kcm = sb("kcm", [128, 8 * TL], BF16)
        khcm = sb("khcm", [128, 8 * TL], BF16)
        vcm = sb("vcm", [128, 8 * TL], BF16)
        vbcm = sb("vbcm", [128, 8 * TL], BF16)
        acm = sb("acm", [128, 8 * TL], BF16)
        lamcm = sb("lamcm", [128, 8 * TL])
        zacm = sb("zacm", [128, 8 * TL])
        lamq = sb("lamq", [128, 8])
        logc = sb("logc", [128, 1])
        bcm = sb("bcm", [8, TL])
        bnegcm = sb("bnegcm", [8, TL])
        brow = sb("brow", [1, 2 * TL])
        onesrow = sb("onesrow", [1, 128])
        kprevb = sb("kprevb", [128, 128], BF16)
        identbf = sb("identbf", [128, 128])
        identbb = sb("identbb", [128, 128], BF16)
        tts = sb("tts", [128, 2 * 128], BF16)
        ttf = sb("ttf", [128, 2 * 128])
        ps_a = psb("ps_a", [128, 512])
        ps_a2 = psb("ps_a2", [128, 512])
        ps_bc = psb("ps_bc", [128, 512])
        ps_tb = psb("ps_tb", [128, 128], BF16)
        ps_tf = psb("ps_tf", [128, 128])
        ps_ap = psb("ps_ap", [16, 128])

        P = Prog(nc)
        A = P.add
        DMA = P.dma
        rel = {}

        A("vector", lambda e, s: e.memset(onesrow[:], 1.0))
        A("vector", lambda e, s: e.memset(logc[:], 1.0))
        P.selfsync("vector")
        A("vector", None, inc=[("vm", 1)])
        DMA("sync", "ldx", lambda e, s: e.dma_start(
            out=xcmb[:], in_=bass.AP(xnT.tensor, 0, [[TH, 128], [128 * TH, 16], [1, TH]])))
        DMA("sync", "ldc", lambda e, s: e.dma_start(out=bab[:], in_=bass.AP(
            bav.tensor, 0, [[1, 128], [128, 8]]), allow_slow_non_contiguous=True))
        DMA("sync", "ldc", lambda e, s: e.dma_start(out=bbb[:], in_=bbv))
        DMA("sync", "ldc", lambda e, s: e.dma_start(out=lamq[:], in_=lamq_in))
        DMA("sync", "ldc", lambda e, s: e.dma_start(
            out=kprevb[:], in_=bass.AP(kprevcm.tensor, 0,
                                       [[16, 128], [128 * 16, 8], [1, 16]])))
        DMA("sync", "ldc", lambda e, s: e.dma_start(out=identbf[:], in_=identf))
        DMA("sync", "ldc", lambda e, s: e.dma_start(out=identbb[:], in_=identb2))
        ldx_done = P.cur("ldx")
        ldc_done = P.cur("ldc")
        A("tensor", None, wait=[("ldx", ldx_done), ("ldc", ldc_done), ("vm", P.cur("vm"))])
        A("scalar", None, wait=[("ldc", ldc_done)])
        A("vector", None, wait=[("ldx", ldx_done), ("ldc", ldc_done)])

        gctr = [0]
        wctr = [0]

        def project(wT, wsem, TT, with_bias=False, xoff=0):
            nps = (TT + 511) // 512
            for ht in range(8):
                wi = wctr[0]
                wctr[0] += 1
                sl = wi % 2
                w = [rel[("wsl", wi - 2)]] if ("wsl", wi - 2) in rel else []
                DMA("gpsimd", wsem, lambda e, s, ht=ht, sl=sl, wT=wT: e.dma_start(
                    out=wslab[:, sl * 2048:(sl + 1) * 2048],
                    in_=bass.AP(wT.tensor, ht * 128,
                                [[1024, 128], [128 * 1024, 16], [1, 128]])), wait=w)
                ldw = P.cur(wsem)
                for tc in range(nps):
                    nt = min(512, TT - tc * 512)
                    gi = gctr[0]
                    gctr[0] += 1
                    ps = ps_a if gi % 2 == 0 else ps_a2
                    w2_ = [(wsem, ldw)]
                    if ("psf", gi - 2) in rel:
                        w2_.append(rel[("psf", gi - 2)])
                    for k in range(16):
                        A("tensor", lambda e, s, k=k, sl=sl, tc=tc, nt=nt, ps=ps,
                            xoff=xoff: e.matmul(
                            ps[:, 0:nt],
                            wslab[:, sl * 2048 + k * 128:sl * 2048 + (k + 1) * 128],
                            xcmb[:, k * TH + xoff + tc * 512:
                                 k * TH + xoff + tc * 512 + nt],
                            start=(k == 0), stop=(k == 15)),
                            wait=(w2_ if k == 0 else []),
                            inc=([("mm", 1)] if k == 15 else []))
                    mm_d = P.cur("mm")
                    if tc == nps - 1:
                        rel[("wsl", wi)] = ("mm", mm_d)
                    if with_bias:
                        A("vector", lambda e, s, ht=ht, tc=tc, nt=nt, ps=ps: e.tensor_scalar_add(
                            projf[:, ht * TH + tc * 512:ht * TH + tc * 512 + nt],
                            ps[:, 0:nt], bab[:, ht:ht + 1]),
                            wait=[("mm", mm_d)])
                        P.selfsync("vector")
                        A("vector", None, inc=[("sm", 1)])
                        rel[("psf", gi)] = ("sm", P.cur("sm"))
                    else:
                        A("scalar", lambda e, s, ht=ht, tc=tc, nt=nt, ps=ps: e.activation(
                            out=projf[:, ht * TH + tc * 512:ht * TH + tc * 512 + nt],
                            in_=ps[:, 0:nt], func=AF.Copy), wait=[("mm", mm_d)])
                        P.selfsync("scalar")
                        A("scalar", None, inc=[("sm", 1)])
                        rel[("psf", gi)] = ("sm", P.cur("sm"))
            rel[("pdone", wsem)] = ("sm", P.cur("sm"))

        cctr = [0]

        def conv_silu(cw, cbias, wsem, out_arr, extra_scale, pkey):
            for ht in range(8):
                ci = cctr[0]
                cctr[0] += 1
                w = [rel[("cvw", ci - 1)]] if ("cvw", ci - 1) in rel else []
                DMA("sync", wsem, lambda e, s, ht=ht, cw=cw: e.dma_start(
                    out=conb[:], in_=cw[ht * 128:(ht + 1) * 128, :]), wait=w)
                DMA("sync", wsem, lambda e, s, ht=ht, cbias=cbias: e.dma_start(
                    out=conbias[:], in_=cbias[ht * 128:(ht + 1) * 128, :]))
                ldcw = P.cur(wsem)
                wv = [(wsem, ldcw), rel[("pdone", pkey)]]
                A("vector", lambda e, s, ht=ht: e.tensor_scalar(
                    cvt[:], projf[:, ht * TH:ht * TH + TL], conb[:, 0:1], None,
                    op0=ALU.mult, op1=ALU.bypass), wait=wv)
                P.selfsync("vector")
                for kk in range(1, 4):
                    A("vector", lambda e, s, ht=ht, kk=kk: e.scalar_tensor_tensor(
                        cvt[:], projf[:, ht * TH + kk:ht * TH + kk + TL],
                        conb[:, kk:kk + 1], cvt[:], op0=ALU.mult, op1=ALU.add))
                    P.selfsync("vector")
                A("vector", lambda e, s: e.tensor_scalar_add(
                    cvt[:], cvt[:], conbias[:, 0:1]))
                P.selfsync("vector")
                A("vector", None, inc=[("vm", 1)])
                rel[("cvw", ci)] = ("vm", P.cur("vm"))
                A("scalar", lambda e, s, ht=ht: e.activation(
                    out=zacm[:, ht * TL:(ht + 1) * TL], in_=cvt[:], func=AF.Silu),
                    wait=[("vm", P.cur("vm"))])
                P.selfsync("scalar")
                A("scalar", None, inc=[("sm", 1)])
                A("vector", lambda e, s, ht=ht, out_arr=out_arr, extra_scale=extra_scale:
                    e.tensor_scalar(out_arr[:, ht * TL:(ht + 1) * TL],
                                    zacm[:, ht * TL:(ht + 1) * TL], float(extra_scale),
                                    None, op0=ALU.mult, op1=ALU.bypass),
                    wait=[("sm", P.cur("sm"))])
                P.selfsync("vector")
                A("vector", None, inc=[("vm", 1)])
            rel[("csdone", out_arr.name)] = ("vm", P.cur("vm"))

        project(wts["wqT"], "lwq", TH)
        conv_silu(cws["qcw"], cbs["qcb"], "lcq", qcm, 1.0, "lwq")
        project(wts["wkT"], "lwk", TH)
        conv_silu(cws["kcw"], cbs["kcb"], "lck", kcm, 128.0 ** -0.5, "lwk")
        project(wts["wvT"], "lwv", TH)
        conv_silu(cws["vcw"], cbs["vcb"], "lcv", vcm, 1.0, "lwv")

        project(wts["waT"], "lwa", TL, with_bias=True, xoff=3)
        for ht in range(8):
            pslc = projf[:, ht * TH:ht * TH + TL]
            zslc = zacm[:, ht * TL:(ht + 1) * TL]
            w = [rel[("pdone", "lwa")]]
            A("vector", lambda e, s, pslc=pslc, zslc=zslc: e.tensor_scalar(
                zslc, pslc, -1.0, None, op0=ALU.mult, op1=ALU.bypass), wait=w)
            P.selfsync("vector")
            A("vector", lambda e, s, pslc=pslc, zslc=zslc: e.tensor_tensor(
                zslc, zslc, pslc, op=ALU.max))
            P.selfsync("vector")
            A("vector", None, inc=[("vm", 1)])
            A("scalar", lambda e, s, zslc=zslc: e.activation(
                out=zslc, in_=zslc, func=AF.Exp, scale=-1.0),
                wait=[("vm", P.cur("vm"))])
            P.selfsync("scalar")
            A("scalar", lambda e, s, zslc=zslc: e.activation(
                out=zslc, in_=zslc, func=AF.Ln, bias=logc[:]))
            P.selfsync("scalar")
            A("scalar", None, inc=[("sm", 1)])
            A("vector", lambda e, s, pslc=pslc: e.tensor_scalar_min(
                cvt[:], pslc, 0.0), wait=[("sm", P.cur("sm"))])
            P.selfsync("vector")
            A("vector", lambda e, s, zslc=zslc: e.tensor_sub(zslc, cvt[:], zslc))
            P.selfsync("vector")
            A("vector", lambda e, s, zslc=zslc: e.tensor_scalar_max(
                zslc, zslc, LOGA_MIN))
            P.selfsync("vector")
            A("vector", None, inc=[("vm", 1)])
            A("scalar", lambda e, s, ht=ht, zslc=zslc: e.activation(
                out=acm[:, ht * TL:(ht + 1) * TL], in_=zslc, func=AF.Exp),
                wait=[("vm", P.cur("vm"))])
            P.selfsync("scalar")
            A("scalar", None, inc=[("sm", 1)])
            A("vector", lambda e, s, ht=ht, zslc=zslc: e.tensor_tensor_scan(
                lamcm[:, ht * TL:(ht + 1) * TL],
                logc[:].broadcast_to([128, TL]), zslc, lamq[:, ht:ht + 1],
                op0=ALU.mult, op1=ALU.add),
                wait=[("sm", P.cur("sm"))])
            P.selfsync("vector")
            A("vector", None, inc=[("vm", 1)])
        lam_done = ("vm", P.cur("vm"))
        A("vector", lambda e, s: e.tensor_copy(
            lamq[:, 0:8],
            bass.AP(lamcm, TL - 1, [[8 * TL, 128], [TL, 8]])), wait=[lam_done])
        P.selfsync("vector")
        A("vector", None, inc=[("vm", 1)])
        DMA("gpsimd", "st1", lambda e, s: e.dma_start(out=lamq_o, in_=lamq[:, 0:8]),
            wait=[("vm", P.cur("vm"))])

        wbw = [rel[("wsl", wctr[0] - 2)]] if ("wsl", wctr[0] - 2) in rel else []
        if ("wsl", wctr[0] - 1) in rel:
            wbw.append(rel[("wsl", wctr[0] - 1)])
        DMA("gpsimd", "lwb", lambda e, s: e.dma_start(
            out=wslab[:, 0:128],
            in_=bass.AP(wbT.tensor, 0, [[8, 128], [128 * 8, 16], [1, 8]])), wait=wbw)
        lwb = P.cur("lwb")
        nps = (TL + 511) // 512
        for tc in range(nps):
            nt = min(512, TL - tc * 512)
            w = [("lwb", lwb)]
            if ("psf", gctr[0] - 2) in rel:
                w.append(rel[("psf", gctr[0] - 2)])
            if ("psf", gctr[0] - 1) in rel:
                w.append(rel[("psf", gctr[0] - 1)])
            if ("psb", tc - 1) in rel:
                w.append(rel[("psb", tc - 1)])
            for k in range(16):
                A("tensor", lambda e, s, k=k, tc=tc, nt=nt: e.matmul(
                    ps_a[0:8, 0:nt], wslab[:, k * 8:(k + 1) * 8],
                    xcmb[:, k * TH + 3 + tc * 512:k * TH + 3 + tc * 512 + nt],
                    start=(k == 0), stop=(k == 15)),
                    wait=(w if k == 0 else []),
                    inc=([("mm", 1)] if k == 15 else []))
            mm_d = P.cur("mm")
            A("scalar", lambda e, s, tc=tc, nt=nt: e.activation(
                out=bcm[:, tc * 512:tc * 512 + nt], in_=ps_a[0:8, 0:nt],
                func=AF.Sigmoid, bias=bbb[:]), wait=[("mm", mm_d)])
            P.selfsync("scalar")
            A("scalar", None, inc=[("sm", 1)])
            rel[("psb", tc)] = ("sm", P.cur("sm"))
        A("vector", lambda e, s: e.tensor_scalar_mul(
            bnegcm[:], bcm[:], -1.0), wait=[("sm", P.cur("sm"))])
        P.selfsync("vector")
        A("vector", None, inc=[("vm", 1)])
        DMA("gpsimd", "stb", lambda e, s: e.dma_start(out=beta_sc[0:8, :], in_=bcm[:]),
            wait=[("vm", P.cur("vm"))])
        DMA("gpsimd", "stb", lambda e, s: e.dma_start(out=beta_sc[8:16, :], in_=bnegcm[:]))
        stb_done = ("stb", P.cur("stb"))

        for ht in range(8):
            DMA("sync", "ldbr", lambda e, s, ht=ht: e.dma_start(
                out=brow[:, 0:TL], in_=beta_sc[8 + ht:9 + ht, :]),
                wait=[stb_done, rel.get(("khv", ht - 1), ("vm", 0))])
            DMA("sync", "ldbr", lambda e, s, ht=ht: e.dma_start(
                out=brow[:, TL:2 * TL], in_=beta_sc[ht:ht + 1, :]))
            ldbr = P.cur("ldbr")
            for tc in range(nps):
                nt = min(512, TL - tc * 512)
                A("tensor", lambda e, s, tc=tc, nt=nt: e.matmul(
                    ps_bc[:, 0:nt], onesrow[:], brow[:, tc * 512:tc * 512 + nt],
                    start=True, stop=True), wait=[("ldbr", ldbr)], inc=[("mm", 1)])
                A("vector", lambda e, s, ht=ht, tc=tc, nt=nt: e.tensor_mul(
                    khcm[:, ht * TL + tc * 512:ht * TL + tc * 512 + nt],
                    kcm[:, ht * TL + tc * 512:ht * TL + tc * 512 + nt],
                    ps_bc[:, 0:nt]), wait=[("mm", P.cur("mm"))])
                P.selfsync("vector")
                A("vector", None, inc=[("vm", 1)])
                A("tensor", lambda e, s, tc=tc, nt=nt: e.matmul(
                    ps_bc[:, 0:nt], onesrow[:],
                    brow[:, TL + tc * 512:TL + tc * 512 + nt],
                    start=True, stop=True), wait=[("vm", P.cur("vm"))], inc=[("mm", 1)])
                A("vector", lambda e, s, ht=ht, tc=tc, nt=nt: e.tensor_mul(
                    vbcm[:, ht * TL + tc * 512:ht * TL + tc * 512 + nt],
                    vcm[:, ht * TL + tc * 512:ht * TL + tc * 512 + nt],
                    ps_bc[:, 0:nt]), wait=[("mm", P.cur("mm"))])
                P.selfsync("vector")
                A("vector", None, inc=[("vm", 1)])
            rel[("khv", ht)] = ("vm", P.cur("vm"))
        khv_done = ("vm", P.cur("vm"))

        for arr, dst in ((qcm, qcm_o), (kcm, kcm_o), (khcm, khcm_o)):
            DMA("gpsimd", "st1", lambda e, s, arr=arr, dst=dst: e.dma_start(
                out=bass.AP(dst.tensor, 0, [[TL, 128], [128 * TL, 8], [1, TL]]),
                in_=arr[:]), wait=[khv_done])

        jobs = [(kcm, ktm_o, True), (vbcm, vtm_o, True), (acm, atm_o, True),
                (lamcm, lamtm_o, False)]
        tctr = [0]
        for (arr, dst, isbf) in jobs:
            for ht in range(8):
                for blk in range(TL // 128):
                    ti = tctr[0]
                    tctr[0] += 1
                    idx = ti % 2
                    w = [khv_done]
                    if ("trev", ti - 1) in rel:
                        w.append(rel[("trev", ti - 1)])
                    if ("trdma", ti - 2) in rel:
                        w.append(rel[("trdma", ti - 2)])
                    if isbf:
                        A("tensor", lambda e, s, arr=arr, ht=ht, blk=blk: e.transpose(
                            ps_tb[:], arr[:, ht * TL + blk * 128:ht * TL + (blk + 1) * 128],
                            identbb[:]), wait=w, inc=[("mm", 1)])
                        A("scalar", lambda e, s, idx=idx: e.activation(
                            out=tts[:, idx * 128:(idx + 1) * 128], in_=ps_tb[:],
                            func=AF.Copy), wait=[("mm", P.cur("mm"))])
                    else:
                        A("tensor", lambda e, s, arr=arr, ht=ht, blk=blk: e.transpose(
                            ps_tf[:], arr[:, ht * TL + blk * 128:ht * TL + (blk + 1) * 128],
                            identbf[:]), wait=w, inc=[("mm", 1)])
                        A("scalar", lambda e, s, idx=idx: e.activation(
                            out=ttf[:, idx * 128:(idx + 1) * 128], in_=ps_tf[:],
                            func=AF.Copy), wait=[("mm", P.cur("mm"))])
                    P.selfsync("scalar")
                    A("scalar", None, inc=[("sm", 1)])
                    rel[("trev", ti)] = ("sm", P.cur("sm"))
                    srctile = tts if isbf else ttf
                    DMA("gpsimd", "st1", lambda e, s, dst=dst, ht=ht, blk=blk, idx=idx,
                        srctile=srctile: e.dma_start(
                        out=dst[blk * 128:(blk + 1) * 128, ht * 128:(ht + 1) * 128],
                        in_=srctile[:, idx * 128:(idx + 1) * 128]),
                        wait=[("sm", P.cur("sm"))])
                    rel[("trdma", ti)] = ("st1", P.cur("st1"))

        for sp in range(NSP):
            for ht in range(8):
                ii = sp * 8 + ht
                w = [khv_done]
                if ii < 2:
                    for kk in range(max(0, gctr[0] - 2), gctr[0]):
                        if ("psf", kk) in rel:
                            w.append(rel[("psf", kk)])
                    for tc2 in range(nps):
                        if ("psb", tc2) in rel:
                            w.append(rel[("psb", tc2)])
                if ("aps", ii - 2) in rel:
                    w.append(rel[("aps", ii - 2)])
                ps = ps_a if ii % 2 == 0 else ps_a2
                if ("apev", ii - 1) in rel:
                    w.append(rel[("apev", ii - 1)])
                A("tensor", lambda e, s, sp=sp, ht=ht, ps=ps: e.matmul(
                    ps[:, 0:128], kcm[:, ht * TL + sp * 128:ht * TL + (sp + 1) * 128],
                    khcm[:, ht * TL + sp * 128:ht * TL + (sp + 1) * 128],
                    start=True, stop=True), wait=w)
                A("tensor", lambda e, s, sp=sp, ht=ht, ps=ps: e.matmul(
                    ps[:, 128:256], kcm[:, ht * TL + sp * 128:ht * TL + (sp + 1) * 128],
                    qcm[:, ht * TL + sp * 128:ht * TL + (sp + 1) * 128],
                    start=True, stop=True))
                if sp == 0:
                    A("tensor", lambda e, s, ht=ht: e.matmul(
                        ps_ap[:], kprevb[:, ht * 16:(ht + 1) * 16],
                        khcm[:, ht * TL:ht * TL + 128],
                        start=True, stop=True), inc=[("mm", 1)])
                else:
                    A("tensor", lambda e, s, sp=sp, ht=ht: e.matmul(
                        ps_ap[:],
                        kcm[:, ht * TL + sp * 128 - 16:ht * TL + sp * 128],
                        khcm[:, ht * TL + sp * 128:ht * TL + (sp + 1) * 128],
                        start=True, stop=True), inc=[("mm", 1)])
                mm_d = P.cur("mm")
                wev = [("mm", mm_d)]
                if ("aps", ii - 1) in rel:
                    wev.append(rel[("aps", ii - 1)])
                A("scalar", lambda e, s, ps=ps: e.activation(
                    out=tts[:, 0:128], in_=ps[:, 0:128], func=AF.Copy),
                    wait=wev)
                P.selfsync("scalar")
                A("scalar", lambda e, s, ps=ps: e.activation(
                    out=tts[:, 128:256], in_=ps[:, 128:256], func=AF.Copy))
                P.selfsync("scalar")
                A("scalar", None, inc=[("sm", 1)])
                sm_d = P.cur("sm")
                DMA("gpsimd", "st1", lambda e, s, sp=sp, ht=ht: e.dma_start(
                    out=A_o[sp * 128:(sp + 1) * 128, ht * 128:(ht + 1) * 128],
                    in_=tts[:, 0:128]), wait=[("sm", sm_d)])
                DMA("gpsimd", "st1", lambda e, s, sp=sp, ht=ht: e.dma_start(
                    out=AQ_o[sp * 128:(sp + 1) * 128, ht * 128:(ht + 1) * 128],
                    in_=tts[:, 128:256]), wait=[("sm", sm_d)])
                A("scalar", lambda e, s: e.activation(
                    out=tts[0:16, 0:128], in_=ps_ap[:], func=AF.Copy),
                    wait=[("st1", P.cur("st1")), ("mm", mm_d)])
                P.selfsync("scalar")
                A("scalar", None, inc=[("sm", 1)])
                rel[("apev", ii)] = ("sm", P.cur("sm"))
                DMA("gpsimd", "st1", lambda e, s, sp=sp, ht=ht: e.dma_start(
                    out=Ap_o[sp * 16:(sp + 1) * 16, ht * 128:(ht + 1) * 128],
                    in_=tts[0:16, 0:128]), wait=[("sm", P.cur("sm"))])
                rel[("aps", ii)] = ("st1", P.cur("st1"))
        run_prog(nc, P)
    return nc


def build_l2s(Prog, run_prog, TL=512, dbg=False):
    nc = bass.Bass("TRN2", target_bir_lowering=False)
    NCH = TL // C
    NRB = 3

    def inp(name, shape, dt=F32):
        return nc.dram_tensor(name, shape, dt, kind="ExternalInput").ap()

    kcm_i = inp("kcm", [1024, TL], BF16)
    khcm_i = inp("khcm", [1024, TL], BF16)
    qcm_i = inp("qcm", [1024, TL], BF16)
    ktm_i = inp("ktm", [TL, 1024], BF16)
    vtm_i = inp("vtm", [TL, 1024], BF16)
    atm_i = inp("atm", [TL, 1024], BF16)
    lam_i_d = inp("lam_full", [17 + TL, 1024])
    A_d = inp("A_in", [TL, 1024], BF16)
    AQ_d = inp("AQ_in", [TL, 1024], BF16)
    Ap_d = inp("Ap_in", [(TL // SPAN) * C, 1024], BF16)
    kprevtm = inp("kprevtm", [16, 1024], BF16)
    st_in = inp("st_in", [128, 1024])
    w2prev_in = inp("w2prev_in", [16, 1024])
    dmask8 = inp("dmask8", [128, 8], BF16)
    sel8 = inp("sel8", [8, 128], BF16)
    crmask = inp("crmask", [128, 128], BF16)
    aqmask = inp("aqmask", [128, 256], BF16)
    e8sel = inp("e8sel", [128, 512])
    selmaskd = inp("selmask", [32, 4], mybir.dt.uint8)
    o_out = nc.dram_tensor("o_out", [TL * 8, 128], BF16, kind="ExternalOutput").ap()
    st_out = nc.dram_tensor("st_out", [128, 1024], F32, kind="ExternalOutput").ap()
    w2prev_out = nc.dram_tensor("w2prev_out", [16, 1024], F32, kind="ExternalOutput").ap()
    wtmp_sc = nc.dram_tensor("wtmp_sc", [16, 1024], F32).ap()
    if dbg:
        dbg_outs = {nm: nc.dram_tensor("dbg_" + nm, shp, F32,
                                       kind="ExternalOutput").ap()
                    for nm, shp in [("acols", [128, 512]), ("aqcols", [128, 256]),
                                    ("crosscols", [32, 512]), ("expwf4", [32, 512]),
                                    ("expof", [128, 128]), ("expost", [128, 128]),
                                    ("cb", [128, 256]), ("w2", [128, 256]),
                                    ("expb", [128, 128]), ("ws4dec", [32, 128])]}

    import contextlib
    with contextlib.ExitStack() as stk:
        def sb(name, shape, dt=F32):
            return stk.enter_context(nc.sbuf_tensor(name, shape, dt))

        def psb(name, shape, dt=F32):
            return stk.enter_context(nc.psum_tensor(name, shape, dt))

        St = sb("St", [128, 1024])
        W2 = sb("W2", [128, 256])
        ws4 = sb("ws4", [32, 128])
        ws4dec = sb("ws4dec", [32, 128])
        acols = sb("acols", [128, 512])
        apcols = sb("apcols", [128, 512])
        crosscols = sb("crosscols", [32, 512])
        aqcols = sb("aqcols", [128, 256])
        cb = sb("cb", [128, 2 * 128])
        cbt = sb("cbt", [128, 128])
        kpadb = sb("kpadb", [128, 2 * 1024])
        qpadb = sb("qpadb", [128, 1024])
        expb = sb("expb", [128, 128])
        expwf4 = sb("expwf4", [32, 512])
        expof = sb("expof", [128, 128])
        expost = sb("expost", [128, 128])
        otf = sb("otf", [128, 128])
        otf2 = sb("otf2", [128, 128])
        otileb = sb("otileb", [128, 128], BF16)
        sttmp = sb("sttmp", [128, 1024])
        decrow = sb("decrow", [1, 1024])
        dmask8b = sb("dmask8b", [128, 8], BF16)
        crmaskb = sb("crmaskb", [128, 128], BF16)
        aqmaskb = sb("aqmaskb", [128, 256], BF16)
        e8selb = sb("e8selb", [128, 512])
        selmb = sb("selmb", [32, 4], mybir.dt.uint8)
        onesrow = sb("onesrow", [1, 128])
        w2pv = sb("w2pv", [128, 128])
        aret = sb("aret", [128, NRB * 16], BF16)
        apret = sb("apret", [128, NRB * 16], BF16)
        aqret = sb("aqret", [128, NRB * 16], BF16)
        a8 = sb("a8", [8, NRB * 2048], BF16)
        sel8b = sb("sel8b", [8, 128], BF16)
        lam_i = sb("lam_i", [128, NRB * 128])
        lam_im1 = sb("lam_im1", [128, NRB * 128])
        lrefb = sb("lrefb", [128, NRB * 128])
        lref2 = sb("lref2", [128, NRB * 128])
        lgend = sb("lgend", [128, NRB * 128])
        vb = sb("vb", [128, NRB * 128], BF16)
        ktmb16 = sb("ktmb16", [16, NRB * 1024], BF16)
        declam = sb("declam", [1, NRB * 2048])
        win16 = sb("win16", [16, 1024], BF16)
        ps_wA = psb("ps_wA", [32, 256])
        ps_wB = psb("ps_wB", [32, 256])
        ps_og2 = psb("ps_og2", [64, 256])
        ps_ops = psb("ps_ops", [128, 128])
        ps_b = psb("ps_b", [128, 128])
        ps_a = psb("ps_a", [128, 512])
        ps_a2 = psb("ps_a2", [128, 512])
        ps_ar = psb("ps_ar", [128, 512])

        P = Prog(nc)
        A = P.add
        DMA = P.dma
        rel = {}
        retdone = {}

        A("vector", lambda e, s: e.memset(W2[:], 0.0))
        A("vector", lambda e, s: e.memset(ws4dec[:], 0.0))
        A("vector", lambda e, s: e.memset(acols[:], 0.0))
        A("vector", lambda e, s: e.memset(apcols[:], 0.0))
        A("vector", lambda e, s: e.memset(crosscols[:], 0.0))
        A("vector", lambda e, s: e.memset(aqcols[:], 0.0))
        A("vector", lambda e, s: e.memset(kpadb[:], 0.0))
        A("vector", lambda e, s: e.memset(qpadb[:], 0.0))
        P.selfsync("vector")
        A("vector", lambda e, s: e.memset(onesrow[:], 1.0))
        P.selfsync("vector")
        A("vector", None, inc=[("vm", 1)])
        setup_vm = P.cur("vm")
        DMA("sync", "ldc", lambda e, s: e.dma_start(out=dmask8b[:], in_=dmask8))
        DMA("sync", "ldc", lambda e, s: e.dma_start(out=sel8b[:], in_=sel8))
        DMA("sync", "ldc", lambda e, s: e.dma_start(out=crmaskb[:], in_=crmask))
        DMA("sync", "ldc", lambda e, s: e.dma_start(out=aqmaskb[:], in_=aqmask))
        DMA("sync", "ldc", lambda e, s: e.dma_start(out=e8selb[:], in_=e8sel))
        DMA("sync", "ldc", lambda e, s: e.dma_start(out=selmb[:], in_=selmaskd))
        DMA("sync", "ldc", lambda e, s: e.dma_start(out=St[:], in_=st_in))
        DMA("sync", "ldc", lambda e, s: e.dma_start(
            out=W2[:, 128:256],
            in_=bass.AP(w2prev_in.tensor, 0, [[1024, 16], [128, 8], [1, 128]])),
            wait=[("vm", setup_vm)])
        DMA("gpsimd", "rw", lambda e, s: e.dma_start(out=win16[:], in_=w2prev_in))
        ldc_done = P.cur("ldc")
        A("tensor", None, wait=[("ldc", ldc_done), ("vm", setup_vm)])
        A("vector", None, wait=[("ldc", ldc_done)])
        A("scalar", None, wait=[("ldc", ldc_done)])

        def ret_batch(n):
            slot = n % NRB
            t0 = n * C
            w = [rel[("chunkdone", n - NRB)]] if ("chunkdone", n - NRB) in rel else []
            sp = t0 // SPAN
            il = t0 % SPAN
            DMA("sync", "r1", lambda e, s, sp=sp, il=il, slot=slot: e.dma_start(
                out=aret[:, slot * 16:(slot + 1) * 16],
                in_=bass.AP(A_d.tensor, (sp * SPAN + il) * 1024 + il,
                            [[1024, 16], [128, 8], [1, 16]])), wait=w)
            if il == 0:
                DMA("sync", "r1", lambda e, s, sp=sp, slot=slot: e.dma_start(
                    out=apret[:, slot * 16:(slot + 1) * 16],
                    in_=bass.AP(Ap_d.tensor, sp * 16 * 1024,
                                [[1024, 16], [128, 8], [1, 16]])), wait=w)
            else:
                DMA("sync", "r1", lambda e, s, sp=sp, il=il, slot=slot: e.dma_start(
                    out=apret[:, slot * 16:(slot + 1) * 16],
                    in_=bass.AP(A_d.tensor, (sp * SPAN + il - 16) * 1024 + il,
                                [[1024, 16], [128, 8], [1, 16]])), wait=w)
            DMA("sync", "r1", lambda e, s, sp=sp, il=il, slot=slot: e.dma_start(
                out=aqret[:, slot * 16:(slot + 1) * 16],
                in_=bass.AP(AQ_d.tensor, (sp * SPAN + il) * 1024 + il,
                            [[1024, 16], [128, 8], [1, 16]])), wait=w)
            DMA("sync", "r1", lambda e, s, t0=t0, slot=slot: e.dma_start(
                out=lam_i[:, slot * 128:(slot + 1) * 128],
                in_=bass.AP(lam_i_d.tensor, (17 + t0) * 1024,
                            [[1024, 16], [128, 8], [1, 128]])), wait=w)
            DMA("sync", "r1", lambda e, s, t0=t0, slot=slot: e.dma_start(
                out=lam_im1[:, slot * 128:(slot + 1) * 128],
                in_=bass.AP(lam_i_d.tensor, (16 + t0) * 1024,
                            [[1024, 16], [128, 8], [1, 128]])), wait=w)
            retdone[("r1", n)] = ("r1", P.cur("r1"))
            DMA("scalar", "r2", lambda e, s, t0=t0, slot=slot: e.dma_start(
                out=lrefb[:, slot * 128:(slot + 1) * 128],
                in_=bass.AP(lam_i_d.tensor, t0 * 1024,
                            [[0, 16], [128, 8], [1, 128]])), wait=w)
            DMA("scalar", "r2", lambda e, s, t0=t0, slot=slot: e.dma_start(
                out=lref2[:, slot * 128:(slot + 1) * 128],
                in_=bass.AP(lam_i_d.tensor, (16 + t0) * 1024,
                            [[0, 16], [128, 8], [1, 128]])), wait=w)
            for g in range(4):
                DMA("scalar", "r2", lambda e, s, t0=t0, slot=slot, g=g: e.dma_start(
                    out=lgend[32 * g:32 * (g + 1), slot * 128:(slot + 1) * 128],
                    in_=bass.AP(lam_i_d.tensor, (17 + t0 + 4 * g + 3) * 1024,
                                [[0, 4], [128, 8], [1, 128]])), wait=w)
                w = []
            DMA("scalar", "r2", lambda e, s, t0=t0, slot=slot: e.dma_start(
                out=vb[:, slot * 128:(slot + 1) * 128],
                in_=bass.AP(vtm_i.tensor, t0 * 1024,
                            [[1024, 16], [128, 8], [1, 128]])), wait=w)
            DMA("scalar", "r2", lambda e, s, t0=t0, slot=slot: e.dma_start(
                out=declam[:, slot * 2048:slot * 2048 + 1024],
                in_=lam_i_d[16 + t0:17 + t0, :]), wait=w)
            DMA("scalar", "r2", lambda e, s, t0=t0, slot=slot: e.dma_start(
                out=declam[:, slot * 2048 + 1024:slot * 2048 + 2048],
                in_=lam_i_d[t0:t0 + 1, :]), wait=w)
            retdone[("r2", n)] = ("r2", P.cur("r2"))
            DMA("gpsimd", "r3", lambda e, s, t0=t0, slot=slot: e.dma_start(
                out=a8[:, slot * 2048:(slot + 1) * 2048],
                in_=bass.AP(atm_i.tensor, t0 * 1024,
                            [[128, 8], [1024, 16], [1, 128]])), wait=w)
            if n == 0:
                DMA("gpsimd", "r3", lambda e, s, slot=slot: e.dma_start(
                    out=ktmb16[:, slot * 1024:(slot + 1) * 1024], in_=kprevtm), wait=w)
            else:
                DMA("gpsimd", "r3", lambda e, s, t0=t0, slot=slot: e.dma_start(
                    out=ktmb16[:, slot * 1024:(slot + 1) * 1024],
                    in_=ktm_i[t0 - 16:t0, :]), wait=w)
            retdone[("r3", n)] = ("r3", P.cur("r3"))

        def coef_builds(n):
            slot = n % NRB
            w = [retdone[("r1", n)], ("vm", setup_vm), ("ldc", ldc_done)]
            if dbg and n == 1 and "dbgs" in P.ct:
                w.append(("dbgs", P.cur("dbgs")))
            if ("stepsdone", n - 1) in rel:
                w.append(rel[("stepsdone", n - 1)])
            if ("ogmm", n - 1, 3) in rel:
                w.append(rel[("ogmm", n - 1, 3)])
            for (dst, srcr) in ((acols, aret), (apcols, apret)):
                for r in range(4):
                    A("vector", lambda e, s, dst=dst, srcr=srcr, r=r, slot=slot:
                        e.tensor_tensor(
                            bass.AP(dst, 40 * r, [[512, 128], [128, 4], [1, 8]]),
                            bass.AP(srcr, slot * 16 + r, [[NRB * 16, 128], [4, 4], [0, 8]]),
                            dmask8b[:].unsqueeze(1).broadcast_to([128, 4, 8]),
                            op=ALU.mult), wait=w)
                    w = []
                    P.selfsync("vector")
            for m in range(4):
                A("vector", lambda e, s, m=m, slot=slot: e.tensor_tensor(
                    bass.AP(crosscols, m * 128, [[512, 32], [40, 4], [1, 8]]),
                    bass.AP(aret, (32 * m) * (NRB * 16) + slot * 16 + 4 * m,
                            [[NRB * 16, 32], [1, 4], [0, 8]]),
                    bass.AP(crmaskb, (32 * m) * 128, [[128, 32], [40, 4], [1, 8]]),
                    op=ALU.mult))
                P.selfsync("vector")
            for m in range(4):
                A("vector", lambda e, s, m=m, slot=slot: e.tensor_tensor(
                    bass.AP(aqcols, m * 64 + 32 * (m % 2), [[256, 128], [8, 4], [1, 8]]),
                    bass.AP(aqret, slot * 16 + 4 * m, [[NRB * 16, 128], [1, 4], [0, 8]]),
                    bass.AP(aqmaskb, m * 64 + 32 * (m % 2), [[256, 128], [8, 4], [1, 8]]),
                    op=ALU.mult))
                P.selfsync("vector")
            w2_ = [retdone[("r2", n)], retdone[("r1", n)]]
            A("vector", lambda e, s, slot=slot: e.tensor_sub(
                expof[:], lam_i[:, slot * 128:(slot + 1) * 128],
                lgend[:, slot * 128:(slot + 1) * 128]), wait=w2_)
            P.selfsync("vector")
            A("vector", None, inc=[("vm", 1)])
            A("scalar", lambda e, s: e.activation(out=expof[:], in_=expof[:],
                                                  func=AF.Exp),
              wait=[("vm", P.cur("vm"))])
            P.selfsync("scalar")
            A("vector", lambda e, s, slot=slot: e.tensor_sub(
                expost[:], lam_i[:, slot * 128:(slot + 1) * 128],
                lref2[:, slot * 128:(slot + 1) * 128]))
            P.selfsync("vector")
            A("vector", None, inc=[("vm", 1)])
            A("scalar", lambda e, s: e.activation(out=expost[:], in_=expost[:],
                                                  func=AF.Exp),
              wait=[("vm", P.cur("vm"))])
            P.selfsync("scalar")
            A("scalar", None, inc=[("sm", 1)])
            rel[("coefs", n)] = ("sm", P.cur("sm"))

        def absorb(n):
            slot = n % NRB
            w = [retdone[("r2", n)]]
            if ("absorbdone", n - 1) in rel:
                w.append(rel[("absorbdone", n - 1)])
            A("vector", lambda e, s, slot=slot: e.tensor_sub(
                decrow[:], declam[:, slot * 2048:slot * 2048 + 1024],
                declam[:, slot * 2048 + 1024:slot * 2048 + 2048]), wait=w)
            P.selfsync("vector")
            A("vector", None, inc=[("vm", 1)])
            A("scalar", lambda e, s: e.activation(out=decrow[:], in_=decrow[:],
                                                  func=AF.Exp),
              wait=[("vm", P.cur("vm"))])
            P.selfsync("scalar")
            A("scalar", None, inc=[("sm", 1)])
            dec_done = P.cur("sm")
            A("tensor", lambda e, s: e.matmul(ps_a[:], onesrow[:], decrow[:, 0:512],
                                              start=True, stop=True),
              wait=[("sm", dec_done)] + ([rel[("absorbps", n - 1)]] if ("absorbps", n - 1) in rel else []),
              inc=[("mm", 1)])
            A("tensor", lambda e, s: e.matmul(ps_a2[:], onesrow[:], decrow[:, 512:1024],
                                              start=True, stop=True), inc=[("mm", 1)])
            mm_d = P.cur("mm")
            A("vector", lambda e, s: e.tensor_mul(sttmp[:, 0:512], St[:, 0:512], ps_a[:]),
              wait=[("mm", mm_d)])
            P.selfsync("vector")
            A("vector", lambda e, s: e.tensor_mul(sttmp[:, 512:1024], St[:, 512:1024],
                                                  ps_a2[:]))
            P.selfsync("vector")
            A("vector", None, inc=[("vm", 1)])
            stt_done = P.cur("vm")
            wd = [("vm", stt_done), retdone[("r3", n)], ("rw", P.cur("rw")) if "rw" in P.ct else ("vm", 0)]
            for h in range(8):
                ps = ps_a if h < 4 else ps_a2
                A("tensor", lambda e, s, h=h, ps=ps, slot=slot: e.matmul(
                    ps[:, (h % 4) * 128:(h % 4 + 1) * 128],
                    ktmb16[:, slot * 1024 + h * 128:slot * 1024 + (h + 1) * 128],
                    win16[:, h * 128:(h + 1) * 128],
                    start=True, stop=True),
                    wait=(wd if h == 0 else []),
                    inc=([("mm", 1)] if h == 7 else []))
            mm_d = P.cur("mm")
            A("vector", lambda e, s: e.tensor_add(St[:, 0:512], sttmp[:, 0:512], ps_a[:]),
              wait=[("mm", mm_d)])
            P.selfsync("vector")
            A("vector", lambda e, s: e.tensor_add(St[:, 512:1024], sttmp[:, 512:1024],
                                                  ps_a2[:]))
            P.selfsync("vector")
            A("vector", None, inc=[("vm", 1)])
            rel[("absorbdone", n)] = ("vm", P.cur("vm"))
            rel[("absorbps", n)] = ("vm", P.cur("vm"))

        khcmb = sb("khcmb", [128, 8 * TL], BF16)
        qcmb = sb("qcmb", [128, 8 * TL], BF16)
        DMA("sync", "ldc2", lambda e, s: e.dma_start(
            out=khcmb[:], in_=bass.AP(khcm_i.tensor, 0,
                                      [[TL, 128], [128 * TL, 8], [1, TL]])))
        DMA("sync", "ldc2", lambda e, s: e.dma_start(
            out=qcmb[:], in_=bass.AP(qcm_i.tensor, 0,
                                     [[TL, 128], [128 * TL, 8], [1, TL]])))
        ldc2_done = P.cur("ldc2")
        A("vector", None, wait=[("ldc2", ldc2_done)])

        def bcb(n):
            slot = n % NRB
            par = n % 2
            w = []
            if dbg and n == 1 and "dbgs" in P.ct:
                w.append(("dbgs", P.cur("dbgs")))
            if ("absorbdone", n - 1) in rel:
                w.append(rel[("absorbdone", n - 1)])
            if ("bps", n - 2) in rel:
                w.append(rel[("bps", n - 2)])
            w.append(("ldc2", ldc2_done))
            for h in range(8):
                A("vector", lambda e, s, h=h, par=par, n=n: e.tensor_copy(
                    bass.AP(kpadb, par * 1024 + h * 128 + h, [[2048, 128], [8, 16]]),
                    khcmb[:, h * TL + n * 16:h * TL + (n + 1) * 16]),
                    wait=(w if h == 0 else []))
                P.selfsync("vector")
            A("vector", None, inc=[("vm", 1)])
            kp_done = P.cur("vm")
            wb = [("vm", kp_done)]
            if ("cbdone", n - 1) in rel:
                wb.append(rel[("cbdone", n - 1)])
            for h in range(8):
                A("tensor", lambda e, s, h=h, par=par: e.matmul(
                    ps_b[:], kpadb[:, par * 1024 + h * 128:par * 1024 + (h + 1) * 128],
                    St[:, h * 128:(h + 1) * 128],
                    start=(h == 0), stop=(h == 7)),
                    wait=(wb if h == 0 else []),
                    inc=([("mm", 1)] if h == 7 else []))
            mm_d = P.cur("mm")
            rel[("bps", n)] = ("mm", mm_d)
            A("vector", lambda e, s, slot=slot: e.tensor_sub(
                expb[:], lam_im1[:, slot * 128:(slot + 1) * 128],
                lrefb[:, slot * 128:(slot + 1) * 128]),
                wait=[retdone[("r1", n)], retdone[("r2", n)]])
            P.selfsync("vector")
            A("vector", None, inc=[("vm", 1)])
            A("scalar", lambda e, s: e.activation(out=expb[:], in_=expb[:], func=AF.Exp),
              wait=[("vm", P.cur("vm"))])
            P.selfsync("scalar")
            A("scalar", None, inc=[("sm", 1)])
            wcb = [("sm", P.cur("sm")), ("mm", mm_d)]
            if ("ogmm", n - 2, 3) in rel:
                wcb.append(rel[("ogmm", n - 2, 3)])
            A("vector", lambda e, s: e.tensor_mul(cbt[:], expb[:], ps_b[:]),
              wait=wcb)
            P.selfsync("vector")
            A("vector", lambda e, s, slot=slot, par=par: e.tensor_add(
                cb[:, par * 128:(par + 1) * 128],
                vb[:, slot * 128:(slot + 1) * 128], cbt[:]))
            P.selfsync("vector")
            A("vector", None, inc=[("vm", 1)])
            rel[("cbdone", n)] = ("vm", P.cur("vm"))

        def group_steps(n, m):
            slot = n % NRB
            cur = n % 2
            par = n % 2
            pass
            war = [retdone[("r3", n)], ("ldc", ldc_done)]
            if ("decay", n, 4 * m - 1) in rel:
                war.append(rel[("decay", n, 4 * m - 1)])
            if m == 0 and ("decay", n - 1, 15) in rel:
                war.append(rel[("decay", n - 1, 15)])
            if m == 0 and ("wsc", n - 1, 15) in rel:
                war.append(rel[("wsc", n - 1, 15)])
            A("tensor", lambda e, s, m=m, slot=slot: e.matmul(
                ps_ar[:], sel8b[:],
                a8[:, slot * 2048 + m * 512:slot * 2048 + (m + 1) * 512],
                start=True, stop=True), wait=war, inc=[("mm", 1)])
            rel[("armm", n, m)] = ("mm", P.cur("mm"))
            for r in range(4):
                i = 4 * m + r
                w = [rel[("coefs", n)], rel[("cbdone", n)]]
                if ("flush", n, m - 2) in rel:
                    w.append(rel[("flush", n, m - 2)])
                if m < 2 and ("flush", n - 1, m + 2) in rel:
                    w.append(rel[("flush", n - 1, m + 2)])
                if ("decay", n, i - 1) in rel:
                    w.append(rel[("decay", n, i - 1)])
                if ("memsetw", n) in rel:
                    w.append(rel[("memsetw", n)])
                pw = ps_wA if r % 2 == 0 else ps_wB
                psl = (r // 2) * 128
                A("tensor", lambda e, s, i=i, cur=cur, psl=psl, pw=pw: e.matmul(
                    pw[:, psl:psl + 128], acols[:, i * 32:(i + 1) * 32],
                    W2[:, cur * 128:(cur + 1) * 128],
                    start=True, stop=False), wait=w)
                A("tensor", lambda e, s, i=i, cur=cur, psl=psl, pw=pw: e.matmul(
                    pw[:, psl:psl + 128], apcols[:, i * 32:(i + 1) * 32],
                    W2[:, (1 - cur) * 128:(2 - cur) * 128],
                    start=False, stop=False))
                if r >= 1:
                    A("tensor", lambda e, s, i=i, psl=psl, pw=pw: e.matmul(
                        pw[:, psl:psl + 128], crosscols[:, i * 32:(i + 1) * 32],
                        ws4dec[:], start=False, stop=False),
                        wait=([rel[("wsc", n, i - 1)]]
                              if ("wsc", n, i - 1) in rel else []))
                A("tensor", lambda e, s, i=i, par=par, psl=psl, pw=pw: e.matmul(
                    pw[:, psl:psl + 128], e8selb[:, i * 32:(i + 1) * 32],
                    cb[:, par * 128:(par + 1) * 128],
                    start=False, stop=True), inc=[("mm", 1)])
                mm_d = P.cur("mm")
                wdec = [("mm", mm_d), rel[("armm", n, m)]]
                if i == 0 and ("chunkdone", n - 1) in rel:
                    wdec.append(rel[("chunkdone", n - 1)])
                A("vector", lambda e, s, r=r: e.tensor_tensor(
                    W2[:].rearrange("p (a d) -> p a d", a=2),
                    W2[:].rearrange("p (a d) -> p a d", a=2),
                    bass.AP(ps_ar.tensor if hasattr(ps_ar, "tensor") else ps_ar,
                            r * 128, [[512, 128], [0, 2], [1, 128]]),
                    op=ALU.mult), wait=wdec)
                P.selfsync("vector")
                A("vector", None, inc=[("vm", 1)])
                rel[("decay", n, i)] = ("vm", P.cur("vm"))
                A("vector", lambda e, s, r=r: e.tensor_mul(
                    ws4dec[:], ws4dec[:], ps_ar[0:32, r * 128:(r + 1) * 128]))
                P.selfsync("vector")
                A("vector", lambda e, s, r=r, psl=psl, pw=pw: e.select(
                    ws4dec[:], selmb[:, r:r + 1].broadcast_to([32, 128]),
                    pw[:, psl:psl + 128], ws4dec[:], add_drain=True))
                P.selfsync("vector")
                A("vector", None, inc=[("vm", 1)])
                rel[("wsc", n, i)] = ("vm", P.cur("vm"))
            A("vector", lambda e, s, m=m, cur=cur: e.tensor_copy(
                W2[32 * m:32 * (m + 1), cur * 128:(cur + 1) * 128],
                ws4dec[:]),
                wait=[rel[("decay", n, 4 * m + 3)], rel[("wsc", n, 4 * m + 3)]])
            P.selfsync("vector")
            A("vector", None, inc=[("vm", 1)])
            rel[("flush", n, m)] = ("vm", P.cur("vm"))
            w = [rel[("flush", n, m)]]
            if ("ogps", n - 1) in rel and m < 2:
                w.append(rel[("ogps", n - 1)])
            A("tensor", lambda e, s, m=m, cur=cur: e.matmul(
                ps_og2[:, (m // 2) * 128:(m // 2 + 1) * 128],
                aqcols[:, m * 64:(m + 1) * 64],
                W2[:, cur * 128:(cur + 1) * 128],
                start=(m % 2 == 0), stop=(m % 2 == 1)),
                wait=w, inc=[("mm", 1)])
            rel[("ogmm", n, m)] = ("mm", P.cur("mm"))

        def o_state_final(n):
            slot = n % NRB
            w = [rel[("absorbdone", n)]] if ("absorbdone", n) in rel else []
            w.append(("ldc2", ldc2_done))
            if ("opsps", n - 1) in rel:
                w.append(rel[("opsps", n - 1)])
            for h in range(8):
                A("vector", lambda e, s, h=h, n=n: e.tensor_copy(
                    bass.AP(qpadb, h * 128 + h, [[1024, 128], [8, 16]]),
                    qcmb[:, h * TL + n * 16:h * TL + (n + 1) * 16]),
                    wait=(w if h == 0 else []))
                P.selfsync("vector")
            A("vector", None, inc=[("vm", 1)])
            qp_done = P.cur("vm")
            for h in range(8):
                A("tensor", lambda e, s, h=h: e.matmul(
                    ps_ops[:], qpadb[:, h * 128:(h + 1) * 128],
                    St[:, h * 128:(h + 1) * 128],
                    start=(h == 0), stop=(h == 7)),
                    wait=([("vm", qp_done)] if h == 0 else []),
                    inc=([("mm", 1)] if h == 7 else []))
            mm_d = P.cur("mm")
            w2_ = [rel[("ogmm", n, 1)], rel[("ogmm", n, 3)], rel[("coefs", n)]]
            if ("oadd", n - 1) in rel:
                w2_.append(rel[("oadd", n - 1)])
            A("scalar", lambda e, s: e.activation(out=otf[0:64, :],
                                                  in_=ps_og2[:, 0:128],
                                                  func=AF.Copy), wait=w2_)
            P.selfsync("scalar")
            A("scalar", lambda e, s: e.activation(out=otf[64:128, :],
                                                  in_=ps_og2[:, 128:256],
                                                  func=AF.Copy))
            P.selfsync("scalar")
            A("scalar", None, inc=[("sm", 1)])
            rel[("ogps", n)] = ("sm", P.cur("sm"))
            A("vector", lambda e, s: e.tensor_mul(otf[:], otf[:], expof[:]),
              wait=[("sm", P.cur("sm"))])
            P.selfsync("vector")
            A("vector", lambda e, s: e.tensor_mul(otf2[:], ps_ops[:], expost[:]),
              wait=[("mm", mm_d)])
            P.selfsync("vector")
            A("vector", None, inc=[("vm", 1)])
            rel[("opsps", n)] = ("vm", P.cur("vm"))
            w3 = []
            if ("odma", n - 1) in rel:
                w3.append(rel[("odma", n - 1)])
            A("vector", lambda e, s: e.tensor_add(otileb[:], otf[:], otf2[:]), wait=w3)
            P.selfsync("vector")
            A("vector", None, inc=[("vm", 1)])
            rel[("oadd", n)] = ("vm", P.cur("vm"))
            DMA("gpsimd", "sto", lambda e, s, n=n: e.dma_start(
                out=o_out[n * 128:(n + 1) * 128, :], in_=otileb[:]),
                wait=[("vm", P.cur("vm"))])
            rel[("odma", n)] = ("sto", P.cur("sto"))

        def chunk_end(n):
            cur = n % 2
            w = [rel[("flush", n, 3)], rel[("ogmm", n, 3)]]
            if dbg and n == 0 and "dbgs" in P.ct:
                w.append(("dbgs", P.cur("dbgs")))
            if ("winret", n) in rel:
                w.append(rel[("winret", n)])
            DMA("gpsimd", "wout", lambda e, s, cur=cur: e.dma_start(
                out=bass.AP(wtmp_sc.tensor, 0, [[1024, 16], [128, 8], [1, 128]]),
                in_=W2[:, cur * 128:(cur + 1) * 128]), wait=w)
            wout_d = P.cur("wout")
            if n + 1 < NCH:
                DMA("gpsimd", "rw", lambda e, s: e.dma_start(
                    out=win16[:], in_=wtmp_sc), wait=[("wout", wout_d)] +
                    ([rel[("absorbdone", n)]] if ("absorbdone", n) in rel else []))
                rel[("winret", n + 1)] = ("rw", P.cur("rw"))
                wms = [rel[("steps15", n)]]
                if dbg and n == 0 and "dbgs" in P.ct:
                    wms.append(("dbgs", P.cur("dbgs")))
                A("vector", lambda e, s, cur=cur: e.memset(
                    W2[:, (1 - cur) * 128:(2 - cur) * 128], 0.0),
                    wait=wms)
                P.selfsync("vector")
                A("vector", None, inc=[("vm", 1)])
                rel[("memsetw", n + 1)] = ("vm", P.cur("vm"))
            rel[("chunkdone", n)] = ("wout", wout_d)

        ret_batch(0)
        if NCH > 1:
            ret_batch(1)
        bcb(0)
        for n in range(NCH):
            if n + 2 < NCH:
                ret_batch(n + 2)
            coef_builds(n)
            group_steps(n, 0)
            absorb(n)
            if n + 1 < NCH:
                bcb(n + 1)
            group_steps(n, 1)
            group_steps(n, 2)
            group_steps(n, 3)
            rel[("steps15", n)] = rel[("decay", n, 15)]
            rel[("stepsdone", n)] = rel[("flush", n, 3)]
            o_state_final(n)
            if dbg and n == 0:
                for nm, tens, wk in [
                        ("acols", acols, rel[("coefs", 0)]),
                        ("aqcols", aqcols, rel[("coefs", 0)]),
                        ("crosscols", crosscols, rel[("coefs", 0)]),
                        ("expwf4", expwf4, rel[("coefs", 0)]),
                        ("expof", expof, rel[("coefs", 0)]),
                        ("expost", expost, rel[("coefs", 0)]),
                        ("cb", cb, rel[("cbdone", 0)]),
                        ("expb", expb, rel[("cbdone", 0)]),
                        ("ws4dec", ws4dec, rel[("wsc", 0, 15)]),
                        ("w2", W2, rel[("flush", 0, 3)])]:
                    DMA("gpsimd", "dbgs", lambda e, s, nm=nm, tens=tens:
                        e.dma_start(out=dbg_outs[nm], in_=tens[:]),
                        wait=[wk, rel[("ogmm", 0, 3)], rel[("opsps", 0)]])
            chunk_end(n)
        DMA("gpsimd", "stf", lambda e, s: e.dma_start(out=st_out, in_=St[:]),
            wait=[rel[("absorbdone", NCH - 1)], rel[("opsps", NCH - 1)]])
        DMA("gpsimd", "stf", lambda e, s: e.dma_start(
            out=bass.AP(w2prev_out.tensor, 0, [[1024, 16], [128, 8], [1, 128]]),
            in_=W2[:, (NCH - 1) % 2 * 128:((NCH - 1) % 2 + 1) * 128]),
            wait=[rel[("chunkdone", NCH - 1)]])
        fw = [("stf", P.cur("stf")), ("sto", P.cur("sto"))]
        if dbg and "dbgs" in P.ct:
            fw.append(("dbgs", P.cur("dbgs")))
        A("gpsimd", None, wait=fw)
        run_prog(nc, P)
    return nc


# version 04
# speedup vs baseline: 1.0362x; 1.0362x over previous
import contextlib
import numpy as np
import ml_dtypes
import concourse.bass as bass
import concourse.mybir as mybir
from concourse.bass_utils import run_bass_kernel_spmd

F32 = mybir.dt.float32
BF16 = mybir.dt.bfloat16
AF = mybir.ActivationFunctionType
ALU = mybir.AluOpType
AX = mybir.AxisListType

NCORES = 8
B, T, HID = 4, 4096, 2048
NH, DQ, DV, CK = 16, 128, 128, 4
TPC = (B * T) // NCORES

BFNP = np.dtype(ml_dtypes.bfloat16)
LN_EPS = 1e-5


def cdiv(a, b):
    return -(-a // b)


class SEng:

    def __init__(self, eng, sem):
        self.e = eng
        self.sem = sem
        self.n = 0
        self.last = None

    def __getattr__(self, name):
        fn = getattr(self.e, name)

        def wrap(*a, **kw):
            inst = fn(*a, **kw)
            self.last = inst
            return inst

        return wrap

    def sync(self):
        if self.last is not None:
            self.n += 1
            self.last.then_inc(self.sem, 1)
            self.e.wait_ge(self.sem, self.n)
            self.last = None


def build_ln(rows, cols):
    nc = bass.Bass("TRN2", target_bir_lowering=False)
    x = nc.dram_tensor("x", [rows, cols], F32, kind="ExternalInput").ap()
    y = nc.dram_tensor("y", [rows, cols], BF16, kind="ExternalOutput").ap()
    nt = rows // 128
    with (
        nc.sbuf_tensor("xt", [128, 2 * cols], F32) as xt,
        nc.sbuf_tensor("ct", [128, 2 * cols], F32) as ct,
        nc.sbuf_tensor("yt", [128, 2 * cols], BF16) as yt,
        nc.sbuf_tensor("sq", [128, cols], F32) as sq,
        nc.sbuf_tensor("mu", [128, 2], F32) as mu,
        nc.sbuf_tensor("mu2", [128, 2], F32) as mu2,
        nc.sbuf_tensor("ssq", [128, 2], F32) as ssq,
        nc.sbuf_tensor("rst", [128, 2], F32) as rst,
        nc.sbuf_tensor("epsc", [128, 1], F32) as epsc,
        nc.semaphore("setup") as setup,
        nc.semaphore("ld") as ld,
        nc.semaphore("vm") as vm,
        nc.semaphore("sm") as sm,
        nc.semaphore("gm") as gm,
        nc.semaphore("vsem") as vsem,
        nc.semaphore("ssem") as ssem,
        nc.semaphore("st") as st,
        nc.Block() as block,
    ):
        @block.sync
        def _(sync):
            for i in range(nt):
                if i > 0:
                    sync.wait_ge(ld, 16 * i)
                if i >= 2:
                    sync.wait_ge(vm, i - 1)
                sync.dma_start(out=xt[:, (i % 2) * cols:(i % 2 + 1) * cols],
                               in_=x[128 * i:128 * (i + 1), :]).then_inc(ld, 16)

        @block.vector
        def _(vector):
            v = SEng(vector, vsem)
            v.memset(epsc[:], LN_EPS).then_inc(setup, 1)
            for i in range(nt):
                b = i % 2
                xv = xt[:, b * cols:(b + 1) * cols]
                cv = ct[:, b * cols:(b + 1) * cols]
                vector.wait_ge(ld, 16 * (i + 1))
                if i >= 2:
                    vector.wait_ge(gm, i - 1)
                v.reduce_sum(mu[:, b:b + 1], xv, axis=AX.X)
                v.sync()
                v.tensor_scalar_mul(mu2[:, b:b + 1], mu[:, b:b + 1], 1.0 / cols)
                v.sync()
                v.tensor_scalar_sub(cv, xv, mu2[:, b:b + 1]).then_inc(vm, 1)

        @block.scalar
        def _(scalar):
            s = SEng(scalar, ssem)
            scalar.wait_ge(setup, 1)
            for i in range(nt):
                b = i % 2
                cv = ct[:, b * cols:(b + 1) * cols]
                scalar.wait_ge(vm, i + 1)
                if i >= 2:
                    scalar.wait_ge(gm, i - 1)
                s.activation(out=sq[:], in_=cv, func=AF.Square,
                             accum_out=ssq[:, b:b + 1])
                s.sync()
                s.activation(out=rst[:, b:b + 1], in_=ssq[:, b:b + 1],
                             func=AF.Ln, scale=1.0 / cols, bias=epsc[:, 0:1])
                s.sync()
                s.activation(out=rst[:, b:b + 1], in_=rst[:, b:b + 1],
                             func=AF.Exp, scale=-0.5).then_inc(sm, 1)

        @block.gpsimd
        def _(gpsimd):
            for i in range(nt):
                b = i % 2
                cv = ct[:, b * cols:(b + 1) * cols]
                yv = yt[:, b * cols:(b + 1) * cols]
                gpsimd.wait_ge(sm, i + 1)
                gpsimd.tensor_scalar_mul(yv, cv, rst[:, b:b + 1]).then_inc(gm, 1)
                gpsimd.wait_ge(gm, i + 1)
                if i > 0:
                    gpsimd.wait_ge(st, 16 * i)
                gpsimd.dma_start(out=y[128 * i:128 * (i + 1), :], in_=yv).then_inc(st, 16)
            gpsimd.wait_ge(st, 16 * nt)

    return nc


_ln_cache = {}


def ln_host(x):
    ntok, C = x.shape
    per = ntok // NCORES
    key = (per, C)
    if key not in _ln_cache:
        _ln_cache[key] = build_ln(per, C)
    nc = _ln_cache[key]
    shards = [np.ascontiguousarray(x[i * per:(i + 1) * per]) for i in range(NCORES)]
    res = run_bass_kernel_spmd(nc, [{"x": s} for s in shards], core_ids=list(range(NCORES)))
    return np.concatenate([r["y"] for r in res.results], axis=0)


def _hadamard(n):
    idx = np.arange(n, dtype=np.uint32)
    par = (np.bitwise_count(np.bitwise_and.outer(idx, idx)) & 1).astype(np.int8)
    return np.where(par == 0, np.float32(1.0), np.float32(-1.0))


def _scan_masks():
    dmask8 = np.zeros((128, 8), np.float32)
    for p in range(128):
        dmask8[p, p % 8] = 1.0
    crmask = np.zeros((128, 128), np.float32)
    for blk in range(4):
        for k_ in range(4):
            for h in range(8):
                for r in range(4):
                    if k_ < r:
                        crmask[blk * 32 + k_ * 8 + h, 40 * r + h] = 1.0
    aqmask = np.zeros((128, 256), np.float32)
    for j in range(16):
        for h in range(8):
            for m in range(4):
                for r in range(4):
                    if j <= 4 * m + r:
                        aqmask[j * 8 + h, m * 64 + 32 * (m % 2) + 8 * r + h] = 1.0
    e8sel = np.zeros((128, 512), np.float32)
    for i in range(16):
        for h in range(8):
            e8sel[i * 8 + h, i * 32 + 8 * (i % 4) + h] = 1.0
    selmask = np.zeros((32, 4), np.float32)
    for r in range(4):
        selmask[8 * r:8 * r + 8, r] = 1.0
    bfe = lambda a: np.ascontiguousarray(a.astype(BFNP))
    return {"dmask8": bfe(dmask8), "sel8": bfe(dmask8.T.copy()),
            "crmask": bfe(crmask), "aqmask": bfe(aqmask),
            "e8sel": e8sel, "selmask": selmask.astype(np.uint8)}


_dbg = {}


def kernel(**inputs):
    ii = {k: np.asarray(v) for k, v in inputs.items()}
    _dbg.clear()
    CC = np.ascontiguousarray
    bfe = lambda a: CC(np.asarray(a).astype(BFNP))
    R8 = list(range(NCORES))

    x = CC(ii["L0_attn_x"].astype(np.float32, copy=False))
    xf = x.reshape(B * T, HID)
    x_cm = [CC(xf[c * TPC:(c + 1) * TPC].T) for c in range(NCORES)]

    xn1 = ln_host(xf)
    _dbg["xn1"] = xn1
    xn1_cm = [CC(xn1[c * TPC:(c + 1) * TPC].T) for c in range(NCORES)]
    xn1_b = [CC(xn1[b * T:(b + 1) * T].T) for b in range(B)]

    TL = 512
    NL = T // TL
    WqT = CC(ii["L0_attn_Wq"].T)
    WkT = CC(ii["L0_attn_Wk"].T)
    WvT = CC(ii["L0_attn_Wv"].T)
    WaT = CC(ii["L0_attn_Wa"].T)
    WbT = CC(ii["L0_attn_Wb"].T)
    ba = ii["L0_attn_ba"].astype(np.float32, copy=False)
    bb = ii["L0_attn_bb"].astype(np.float32, copy=False)
    qcw = ii["L0_attn_qconv_w"].astype(np.float32, copy=False)
    kcw = ii["L0_attn_kconv_w"].astype(np.float32, copy=False)
    vcw = ii["L0_attn_vconv_w"].astype(np.float32, copy=False)
    qcb = ii["L0_attn_qconv_b"].astype(np.float32, copy=False)
    kcb = ii["L0_attn_kconv_b"].astype(np.float32, copy=False)
    vcb = ii["L0_attn_vconv_b"].astype(np.float32, copy=False)
    eyef = np.eye(128, dtype=np.float32)
    eyeb = bfe(np.eye(128, dtype=np.float32))
    masks = _scan_masks()

    l2f_nc = build_l2f(Prog, run_prog, TL)
    l2s_nc = build_l2s(Prog, run_prog, TL)
    lamq = [np.zeros((128, 8), np.float32) for _ in R8]
    kprev_cm = [np.zeros((1024, 16), BFNP) for _ in R8]
    kprev_tm = [np.zeros((16, 1024), BFNP) for _ in R8]
    lam_carry = [np.zeros((17, 1024), np.float32) for _ in R8]
    st_c = [np.zeros((128, 1024), np.float32) for _ in R8]
    w2p_c = [np.zeros((16, 1024), np.float32) for _ in R8]
    o_full = np.zeros((B, T, 16, 128), BFNP)
    for seg in range(NL):
        s0 = seg * TL
        infs = []
        for c in R8:
            b, g = c // 2, c % 2
            xsl = np.zeros((2048, TL + 3), BFNP)
            lo = max(0, s0 - 3)
            xsl[:, 3 - (s0 - lo):] = xn1_b[b][:, lo:s0 + TL]
            infs.append({
                "xnT": xsl,
                "wqT": CC(WqT[:, g * 1024:(g + 1) * 1024]),
                "wkT": CC(WkT[:, g * 1024:(g + 1) * 1024]),
                "wvT": CC(WvT[:, g * 1024:(g + 1) * 1024]),
                "waT": CC(WaT[:, g * 1024:(g + 1) * 1024]),
                "wbT": CC(WbT[:, g * 8:(g + 1) * 8]),
                "bav": CC(ba[g * 1024:(g + 1) * 1024, None]),
                "bbv": CC(bb[g * 8:(g + 1) * 8, None]),
                "qcw": CC(qcw[g * 1024:(g + 1) * 1024]),
                "kcw": CC(kcw[g * 1024:(g + 1) * 1024]),
                "vcw": CC(vcw[g * 1024:(g + 1) * 1024]),
                "qcb": CC(qcb[g * 1024:(g + 1) * 1024, None]),
                "kcb": CC(kcb[g * 1024:(g + 1) * 1024, None]),
                "vcb": CC(vcb[g * 1024:(g + 1) * 1024, None]),
                "kprevcm": kprev_cm[c], "lamq_in": lamq[c],
                "identf": eyef, "identb2": eyeb})
        rf = run_bass_kernel_spmd(l2f_nc, infs, core_ids=R8).results
        inss = []
        for c in R8:
            of = rf[c]
            lam_full = np.zeros((17 + TL, 1024), np.float32)
            lam_full[0:17] = lam_carry[c]
            lam_full[17:] = of["lamtm_o"]
            d = {"kcm": of["kcm_o"], "khcm": of["khcm_o"], "qcm": of["qcm_o"],
                 "ktm": of["ktm_o"], "vtm": of["vtm_o"], "atm": of["atm_o"],
                 "lam_full": lam_full, "A_in": of["A_o"], "AQ_in": of["AQ_o"],
                 "Ap_in": of["Ap_o"], "kprevtm": kprev_tm[c],
                 "st_in": st_c[c], "w2prev_in": w2p_c[c]}
            d.update(masks)
            inss.append(d)
            lamq[c] = of["lamq_o"]
            kprev_cm[c] = CC(of["kcm_o"][:, -16:])
            kprev_tm[c] = CC(of["ktm_o"][-16:])
            lam_carry[c] = CC(of["lamtm_o"][-17:])
        rs = run_bass_kernel_spmd(l2s_nc, inss, core_ids=R8).results
        for c in R8:
            b, g = c // 2, c % 2
            o_full[b, s0:s0 + TL, 8 * g:8 * (g + 1)] = \
                rs[c]["o_out"].reshape(TL, 8, 128)
            st_c[c] = rs[c]["st_out"]
            w2p_c[c] = rs[c]["w2prev_out"]
    o_tm = o_full.reshape(B * T, 2048)
    _dbg["o_tm"] = o_tm

    go_nc = build_gate_oln(TPC)
    WgT0 = CC(ii["L0_attn_Wg"].T.astype(np.float32, copy=False))
    lnw = CC(ii["L0_attn_ln_w"].astype(np.float32, copy=False)[None, :])
    lnb = CC(ii["L0_attn_ln_b"].astype(np.float32, copy=False)[None, :])
    rg = run_bass_kernel_spmd(go_nc, [
        {"xnT": xn1_cm[c], "wgT": WgT0, "o_tm": CC(o_tm[c * TPC:(c + 1) * TPC]),
         "lnw": lnw, "lnb": lnb} for c in R8], core_ids=R8).results
    og_cm = [CC(rg[c]["og_tm"].T) for c in R8]
    _dbg["og"] = [rg[c]["og_tm"] for c in R8]

    m4_nc = build_mm_res_ln(TPC, 16, False, True)
    WoT0 = CC(ii["L0_attn_Wo"].T.astype(np.float32, copy=False))
    r4 = run_bass_kernel_spmd(m4_nc, [
        {"aT": og_cm[c], "wT": WoT0, "xT": x_cm[c]} for c in R8],
        core_ids=R8).results
    x2_cm = [r4[c]["x1"] for c in R8]
    _dbg["x1"] = x2_cm
    xn2_cm = [r4[c]["xn"] for c in R8]
    _dbg["xn2"] = xn2_cm

    dp1_nc = build_dualproj(TPC, 5632)
    gateT = CC(ii["L0_ffn_gate_w"].T.astype(np.float32, copy=False))
    upT = CC(ii["L0_ffn_up_w"].T.astype(np.float32, copy=False))
    gate_s = ii["L0_ffn_gate_s"].astype(np.float32, copy=False)
    up_s = ii["L0_ffn_up_s"].astype(np.float32, copy=False)
    r5 = run_bass_kernel_spmd(dp1_nc, [
        {"aT": xn2_cm[c], "gwT": gateT, "uwT": upT,
         "gsv": gate_s, "usv": up_s} for c in R8], core_ids=R8).results
    m5_nc = build_mm_res_ln(TPC, 44, True, True)
    downT = CC(ii["L0_ffn_down_w"].T.astype(np.float32, copy=False))
    down_s = ii["L0_ffn_down_s"].astype(np.float32, copy=False)
    r5b = run_bass_kernel_spmd(m5_nc, [
        {"aT": r5[c]["h"], "wT": downT, "xT": x2_cm[c], "svec": down_s}
        for c in R8], core_ids=R8).results
    x3_cm = [r5b[c]["x1"] for c in R8]
    _dbg["hdn"] = [r5[c]["h"] for c in R8]
    _dbg["x2"] = x3_cm
    xn3_cm = [r5b[c]["xn"] for c in R8]
    _dbg["xn3"] = xn3_cm

    l1a_nc = build_l1a(TPC)
    wv1 = CC(ii["L1_attn_Wv"].astype(np.float32, copy=False))
    wg1 = CC(ii["L1_attn_Wg"].astype(np.float32, copy=False))
    wq1 = CC(ii["L1_attn_Wq"].astype(np.float32, copy=False))
    bg1 = ii["L1_attn_bg"].astype(np.float32, copy=False)
    bq1 = ii["L1_attn_bq"].astype(np.float32, copy=False)
    r6 = run_bass_kernel_spmd(l1a_nc, [
        {"aT": xn3_cm[c], "wv": wv1, "wg": wg1, "wq": wq1,
         "bg": bg1, "bq": bq1} for c in R8], core_ids=R8).results
    ef_nc = build_emafix_load(TPC)
    decrow = (0.95 ** (np.arange(TPC, dtype=np.float64) + 1)).astype(np.float32)[None, :]
    zcar = np.zeros((1, 2048), np.float32)
    r7 = run_bass_kernel_spmd(ef_nc, [
        {"memloc": r6[c]["memloc"],
         "carryT": (r6[c - 1]["carryT"] if c % 2 == 1 else zcar),
         "que": r6[c]["que"], "decrow": decrow} for c in R8],
        core_ids=R8).results
    _dbg["mem"] = [r6[c]["memloc"] for c in R8]
    _dbg["que"] = [r6[c]["que"] for c in R8]
    _dbg["load"] = [r7[c]["loadT"] for c in R8]
    m7_nc = build_mm_res_ln(TPC, 16, True, True)
    Wo1 = CC(ii["L1_attn_Wo"].astype(np.float32, copy=False))
    sc05 = np.full(2048, 0.05, np.float32)
    r7b = run_bass_kernel_spmd(m7_nc, [
        {"aT": r7[c]["loadT"], "wT": Wo1, "xT": x3_cm[c], "svec": sc05}
        for c in R8], core_ids=R8).results
    x4_cm = [r7b[c]["x1"] for c in R8]
    _dbg["x3"] = x4_cm
    xn4_cm = [r7b[c]["xn"] for c in R8]
    _dbg["xn4"] = xn4_cm

    hi_nc = build_had_in(TPC)
    h2048 = bfe(_hadamard(2048))
    su_ug = ii["L1_ffn_SU_ug"].astype(np.float32, copy=False)
    r8 = run_bass_kernel_spmd(hi_nc, [
        {"xnT": xn4_cm[c], "su": su_ug, "hT": h2048} for c in R8],
        core_ids=R8).results
    _dbg["xw"] = [r8[c]["xw"] for c in R8]
    dp2_nc = build_dualproj(TPC, 8192, post_had=True, sv_scale=64.0, mul_su=True)
    wgate1 = CC(ii["L1_ffn_W_gate"].T.astype(np.float32, copy=False))
    wup1 = CC(ii["L1_ffn_W_up"].T.astype(np.float32, copy=False))
    wsug = ii["L1_ffn_Wscale_ug"].astype(np.float32, copy=False)
    su_dp = ii["L1_ffn_SU_dp"].astype(np.float32, copy=False)
    r8b = run_bass_kernel_spmd(dp2_nc, [
        {"aT": r8[c]["xw"], "gwT": wgate1, "uwT": wup1,
         "gsv": CC(wsug[8192:]), "usv": CC(wsug[:8192]),
         "sudp": su_dp, "h128": bfe(_hadamard(128))} for c in R8],
        core_ids=R8).results
    z1p = [CC(r8b[c]["h"].reshape(64, 64, 2, TPC).transpose(1, 0, 2, 3)
              .reshape(8192, TPC)) for c in R8]
    fh_nc = build_foldH(8192)
    wdT = ii["L1_ffn_W_down"].T.astype(np.float32, copy=False)
    wdperm = CC(wdT.reshape(64, 64, 2, 2048).transpose(1, 0, 2, 3).reshape(8192, 2048))
    h64i2 = bfe(np.kron(_hadamard(64), np.eye(2, dtype=np.float32)))
    rfh = run_bass_kernel_spmd(fh_nc, [{"wT": wdperm, "hmat": h64i2}],
                               core_ids=[0]).results
    w2T = rfh[0]["w2T"]
    m9_nc = build_mm_res_ln(TPC, 64, True, False, svec_scale=64.0)
    wsdp = ii["L1_ffn_Wscale_dp"].astype(np.float32, copy=False)
    r9 = run_bass_kernel_spmd(m9_nc, [
        {"aT": z1p[c], "wT": w2T, "xT": x4_cm[c], "svec": wsdp}
        for c in R8], core_ids=R8).results

    out = np.concatenate([r9[c]["x1"].T for c in R8], axis=0)
    return CC(out.reshape(B, T, HID))


ENGS = ("sync", "tensor", "vector", "scalar", "gpsimd")


class Prog:

    def __init__(self, nc):
        self.nc = nc
        self.q = {e: [] for e in ENGS}
        self.ct = {}
        self.sem_names = []
        self.watermark = {e: {} for e in ENGS}
        self.selfct = {e: 0 for e in ENGS}

    def sem(self, name):
        if name not in self.ct:
            self.ct[name] = 0
            self.sem_names.append(name)
        return name

    def cur(self, name):
        return self.ct[name]

    def add(self, eng, fn, wait=(), inc=()):
        wait = [(s, v) for (s, v) in wait if v > 0]
        inc = list(inc)
        for s, n in inc:
            self.sem(s)
            self.ct[s] += n
        for s, _ in wait:
            self.sem(s)
        self.q[eng].append((fn, wait, inc))

    def dma(self, eng, sem, fn, wait=(), extra_inc=()):
        self.sem(sem)
        w = list(wait)
        w.append((sem, self.cur(sem)))
        self.add(eng, fn, wait=w, inc=[(sem, 16), *extra_inc])

    def selfsync(self, eng):
        name = "_self_" + eng
        self.sem(name)
        if not self.q[eng]:
            return
        fn, wait, inc = self.q[eng][-1]
        if inc:
            self.add(eng, lambda e, s: e.nop(), wait=[], inc=[(name, 1)])
        else:
            inc.append((name, 1))
            self.ct[name] += 1
        self.add(eng, None, wait=[(name, self.ct[name])])

    def emit(self, block, sems):
        for eng in ENGS:
            ops = self.q[eng]
            if not ops:
                continue

            def make(eng, ops):
                def body(e):
                    wm = self.watermark[eng]
                    for fn, wait, inc in ops:
                        for s, v in wait:
                            if wm.get(s, -1) < v:
                                e.wait_ge(sems[s], v)
                                wm[s] = v
                        inst = fn(e, sems) if fn is not None else None
                        first = True
                        for s, n in inc:
                            if inst is None or not first:
                                inst = e.nop()
                            inst.then_inc(sems[s], n)
                            first = False
                    return

                return body

            getattr(block, eng)(make(eng, ops))


def run_prog(nc, prog):
    import contextlib
    with contextlib.ExitStack() as st:
        sems = {}
        for name in prog.sem_names:
            sems[name] = st.enter_context(prog.nc.semaphore(name))
        block = st.enter_context(prog.nc.Block())
        prog.emit(block, sems)


def build_mm_res_ln(tpc, nk, has_scale, do_ln, label="mmres", svec_scale=1.0):
    nc = bass.Bass("TRN2", target_bir_lowering=False)
    CIN = nk * 128
    aT = nc.dram_tensor("aT", [CIN, tpc], BF16, kind="ExternalInput").ap()
    wT = nc.dram_tensor("wT", [CIN, 2048], F32, kind="ExternalInput").ap()
    xT = nc.dram_tensor("xT", [2048, tpc], F32, kind="ExternalInput").ap()
    if has_scale:
        svec = nc.dram_tensor("svec", [2048], F32, kind="ExternalInput").ap()
    x1o = nc.dram_tensor("x1", [2048, tpc], F32, kind="ExternalOutput").ap()
    if do_ln:
        xno = nc.dram_tensor("xn", [2048, tpc], BF16, kind="ExternalOutput").ap()
    wbf = nc.dram_tensor("wbf", [CIN, 2048], BF16).ap()
    NT = 512
    nch = tpc // NT
    ND = 16
    SLAB = 8
    slabs = []
    r = 0
    while r < nk:
        slabs.append((r, min(SLAB, nk - r)))
        r += min(SLAB, nk - r)
    NS = len(slabs)
    import contextlib
    with contextlib.ExitStack() as _stk:
        def sb(name, shape, dt=F32):
            return _stk.enter_context(nc.sbuf_tensor(name, shape, dt))

        def psb(name, shape, dt=F32):
            return _stk.enter_context(nc.psum_tensor(name, shape, dt))

        wsl = sb("wsl", [128, 2 * SLAB * 2048], BF16)
        asl = sb("asl", [128, 2 * SLAB * NT], BF16)
        xb = sb("xb", [128, 2 * NT])
        x2c = sb("x2c", [128, ND * NT])
        x1c = sb("x1c", [128, ND * NT])
        sq = sb("sq", [128, 2 * NT])
        xnt = sb("xnt", [128, 2 * NT], BF16)
        ctmp = sb("ctmp", [128, 2 * NT])
        onescol = sb("onescol", [128, 1])
        onesrow = sb("onesrow", [1, 128])
        epsc = sb("epsc", [1, 1])
        dsb = sb("dsb", [128, 16])
        murow = sb("murow", [1, NT])
        mu2row = sb("mu2row", [1, NT])
        varrow = sb("varrow", [1, NT])
        var2row = sb("var2row", [1, NT])
        rstrow = sb("rstrow", [1, NT])
        ps0 = psb("ps0", [128, NT])
        ps1 = psb("ps1", [128, NT])
        psmu = psb("psmu", [1, NT])
        psvar = psb("psvar", [1, NT])
        psbmu = psb("psbmu", [128, NT])
        psbrst = psb("psbrst", [128, NT])

        P = Prog(nc)
        P.add("vector", lambda e, s: e.memset(onescol[:], 1.0))
        P.add("vector", lambda e, s: e.memset(onesrow[:], 1.0))
        if has_scale:
            P.dma("sync", "lds", lambda e, s: e.dma_start(
                out=dsb[:], in_=bass.AP(svec.tensor, 0, [[1, 128], [128, 16]]),
                allow_slow_non_contiguous=True))
            P.add("vector", None, wait=[("lds", 16)])
            if svec_scale != 1.0:
                P.add("vector", lambda e, s: e.tensor_scalar_mul(
                    dsb[:], dsb[:], float(svec_scale)))
                P.selfsync("vector")
        else:
            P.add("vector", lambda e, s: e.memset(dsb[:], 1.0))
        P.selfsync("vector")
        P.add("vector", None, inc=[("vm", 1)])
        setup_v = P.cur("vm")
        P.add("scalar", None, wait=[("vm", setup_v)])
        P.add("tensor", None, wait=[("vm", setup_v)])
        P.add("vector", lambda e, s: e.memset(epsc[:], LN_EPS))
        P.selfsync("vector")
        for kk in range(nk):
            P.dma("gpsimd", "ldw", lambda e, s, kk=kk: e.dma_start(
                out=wbf[kk * 128:(kk + 1) * 128, :],
                in_=wT[kk * 128:(kk + 1) * 128, :]))
        wcast_done = P.cur("ldw")

        x1dma = {}
        xb_rel = {}
        ps_rel = {}
        sq_rel = {}
        asl_rel = {}
        wsl_rel = {}
        stats_rel = None
        bc_rel = None
        for tc in range(nch):
            accmm = {}
            for si, (s0, slen) in enumerate(slabs):
                ib = (tc * NS + si) % 2
                w = [("ldw", wcast_done)]
                if (tc * NS + si - 2) in wsl_rel:
                    w.append(wsl_rel[tc * NS + si - 2])
                P.dma("sync", "ldwsl", lambda e, s, s0=s0, slen=slen, ib=ib: e.dma_start(
                    out=wsl[:, ib * SLAB * 2048:ib * SLAB * 2048 + slen * 2048],
                    in_=bass.AP(wbf.tensor, s0 * 128 * 2048,
                                [[2048, 128], [128 * 2048, slen], [1, 2048]])), wait=w)
                ldw_done = P.cur("ldwsl")
                w = []
                if (tc * NS + si - 2) in asl_rel:
                    w.append(asl_rel[tc * NS + si - 2])
                P.dma("sync", "lda", lambda e, s, s0=s0, slen=slen, ib=ib, tc=tc: e.dma_start(
                    out=asl[:, ib * SLAB * NT:ib * SLAB * NT + slen * NT],
                    in_=bass.AP(aT.tensor, s0 * 128 * tpc + tc * NT,
                                [[tpc, 128], [128 * tpc, slen], [1, NT]])), wait=w)
                lda_done = P.cur("lda")
                for d in range(ND):
                    ps = ps0 if d % 2 == 0 else ps1
                    w = [("ldwsl", ldw_done), ("lda", lda_done)]
                    if d % 2 in ps_rel:
                        w.append(ps_rel[d % 2])
                    for k in range(slen):
                        P.add("tensor", lambda e, s, k=k, d=d, ib=ib, ps=ps, slen=slen: e.matmul(
                            ps[:],
                            wsl[:, ib * SLAB * 2048 + k * 2048 + d * 128:
                                ib * SLAB * 2048 + k * 2048 + (d + 1) * 128],
                            asl[:, ib * SLAB * NT + k * NT:ib * SLAB * NT + (k + 1) * NT],
                            start=(k == 0), stop=(k == slen - 1)),
                            wait=(w if k == 0 else []),
                            inc=([("mm", 1)] if k == slen - 1 else []))
                    mm_done = P.cur("mm")
                    if d == ND - 1:
                        wsl_rel[tc * NS + si] = ("mm", mm_done)
                        asl_rel[tc * NS + si] = ("mm", mm_done)
                    if si == 0:
                        P.add("vector", lambda e, s, d=d, ps=ps: e.tensor_copy(
                            x2c[:, d * NT:(d + 1) * NT], ps[:]),
                            wait=[("mm", mm_done)])
                    else:
                        P.add("vector", lambda e, s, d=d, ps=ps: e.tensor_add(
                            x2c[:, d * NT:(d + 1) * NT],
                            x2c[:, d * NT:(d + 1) * NT], ps[:]),
                            wait=[("mm", mm_done)])
                    P.selfsync("vector")
                    P.add("vector", None, inc=[("vm", 1)])
                    ps_rel[d % 2] = ("vm", P.cur("vm"))
            for d in range(ND):
                i = tc * ND + d
                w = [xb_rel[i - 2]] if i - 2 in xb_rel else []
                P.dma("sync", "ldx", lambda e, s, d=d, tc=tc, i=i: e.dma_start(
                    out=xb[:, (i % 2) * NT:(i % 2 + 1) * NT],
                    in_=xT[d * 128:(d + 1) * 128, tc * NT:(tc + 1) * NT]), wait=w)
                ldx_done = P.cur("ldx")
                w = [("ldx", ldx_done)]
                if d in x1dma:
                    w.append(("st1", x1dma[d]))
                P.add("vector", lambda e, s, d=d, i=i: e.scalar_tensor_tensor(
                    x1c[:, d * NT:(d + 1) * NT],
                    x2c[:, d * NT:(d + 1) * NT], dsb[:, d:d + 1],
                    xb[:, (i % 2) * NT:(i % 2 + 1) * NT],
                    op0=ALU.mult, op1=ALU.add), wait=w)
                P.selfsync("vector")
                P.add("vector", None, inc=[("vm", 1)])
                vm_done = P.cur("vm")
                xb_rel[i] = ("vm", vm_done)
                P.dma("gpsimd", "st1", lambda e, s, d=d, tc=tc: e.dma_start(
                    out=x1o[d * 128:(d + 1) * 128, tc * NT:(tc + 1) * NT],
                    in_=x1c[:, d * NT:(d + 1) * NT]), wait=[("vm", vm_done)])
                x1dma[d] = P.cur("st1")
                if do_ln:
                    w = [("vm", vm_done)]
                    if d % 2 in sq_rel:
                        w.append(sq_rel[d % 2])
                    P.add("scalar", lambda e, s, d=d: e.activation(
                        out=sq[:, (d % 2) * NT:(d % 2 + 1) * NT],
                        in_=x1c[:, d * NT:(d + 1) * NT], func=AF.Square), wait=w)
                    P.selfsync("scalar")
                    P.add("scalar", None, inc=[("sm", 1)])
                    sm_done = P.cur("sm")
                    w = [("vm", vm_done)]
                    if d == 0 and stats_rel:
                        w.append(stats_rel)
                    P.add("tensor", lambda e, s, d=d: e.matmul(
                        psmu[:], onescol[:], x1c[:, d * NT:(d + 1) * NT],
                        start=(d == 0), stop=(d == ND - 1)), wait=w)
                    P.add("tensor", lambda e, s, d=d: e.matmul(
                        psvar[:], onescol[:], sq[:, (d % 2) * NT:(d % 2 + 1) * NT],
                        start=(d == 0), stop=(d == ND - 1)),
                        wait=[("sm", sm_done)], inc=[("mm", 1)])
                    sq_rel[d % 2] = ("mm", P.cur("mm"))
            if not do_ln:
                continue
            allmm = P.cur("mm")
            P.add("scalar", lambda e, s: e.activation(
                out=murow[:], in_=psmu[:], func=AF.Copy, scale=1.0 / 2048),
                wait=[("mm", allmm)])
            P.selfsync("scalar")
            P.add("scalar", lambda e, s: e.activation(
                out=mu2row[:], in_=murow[:], func=AF.Square))
            P.add("scalar", lambda e, s: e.activation(
                out=varrow[:], in_=psvar[:], func=AF.Copy, scale=1.0 / 2048))
            P.selfsync("scalar")
            P.add("scalar", None, inc=[("sm", 1)])
            P.add("vector", lambda e, s: e.tensor_sub(var2row[:], varrow[:], mu2row[:]),
                  wait=[("sm", P.cur("sm"))])
            P.selfsync("vector")
            P.add("vector", None, inc=[("vm", 1)])
            P.add("scalar", lambda e, s: e.activation(
                out=rstrow[:], in_=var2row[:], func=AF.Ln, bias=epsc[:]),
                wait=[("vm", P.cur("vm"))])
            P.selfsync("scalar")
            P.add("scalar", lambda e, s: e.activation(
                out=rstrow[:], in_=rstrow[:], func=AF.Exp, scale=-0.5))
            P.selfsync("scalar")
            P.add("scalar", None, inc=[("sm", 1)])
            sfin = P.cur("sm")
            stats_rel = ("sm", sfin)
            w = [("sm", sfin)]
            if bc_rel:
                w.append(bc_rel)
            P.add("tensor", lambda e, s: e.matmul(
                psbmu[:], onesrow[:], murow[:], start=True, stop=True), wait=w)
            P.add("tensor", lambda e, s: e.matmul(
                psbrst[:], onesrow[:], rstrow[:], start=True, stop=True),
                inc=[("mm", 1)])
            bc_done = P.cur("mm")
            for d in range(ND):
                bb = d % 2
                P.add("vector", lambda e, s, d=d, bb=bb: e.tensor_sub(
                    ctmp[:, bb * NT:(bb + 1) * NT],
                    x1c[:, d * NT:(d + 1) * NT], psbmu[:]),
                    wait=[("mm", bc_done)])
                P.selfsync("vector")
                w = []
                if "st2" in P.ct and d >= 2:
                    w.append(("st2", P.cur("st2")))
                P.add("vector", lambda e, s, d=d, bb=bb: e.tensor_mul(
                    xnt[:, bb * NT:(bb + 1) * NT],
                    ctmp[:, bb * NT:(bb + 1) * NT], psbrst[:]), wait=w)
                P.selfsync("vector")
                P.add("vector", None, inc=[("vm", 1)])
                P.dma("gpsimd", "st2", lambda e, s, d=d, tc=tc, bb=bb: e.dma_start(
                    out=xno[d * 128:(d + 1) * 128, tc * NT:(tc + 1) * NT],
                    in_=xnt[:, bb * NT:(bb + 1) * NT]),
                    wait=[("vm", P.cur("vm"))])
            bc_rel = ("vm", P.cur("vm"))
        fw = [("st1", P.cur("st1"))]
        if "st2" in P.ct:
            fw.append(("st2", P.cur("st2")))
        P.add("gpsimd", None, wait=fw)
        run_prog(nc, P)
    return nc


def build_dualproj(tpc, ni, post_had=False, sv_scale=1.0, mul_su=False):
    HSCALE = 1.0 / (64.0 * float(np.sqrt(8192.0))) if post_had else 1.0
    nc = bass.Bass("TRN2", target_bir_lowering=False)
    aT = nc.dram_tensor("aT", [2048, tpc], BF16, kind="ExternalInput").ap()
    gwT = nc.dram_tensor("gwT", [2048, ni], F32, kind="ExternalInput").ap()
    uwT = nc.dram_tensor("uwT", [2048, ni], F32, kind="ExternalInput").ap()
    gsv = nc.dram_tensor("gsv", [ni], F32, kind="ExternalInput").ap()
    usv = nc.dram_tensor("usv", [ni], F32, kind="ExternalInput").ap()
    if post_had:
        h128 = nc.dram_tensor("h128", [128, 128], BF16, kind="ExternalInput").ap()
    if mul_su:
        sudp = nc.dram_tensor("sudp", [ni], F32, kind="ExternalInput").ap()
    ho = nc.dram_tensor("h", [ni, tpc], BF16, kind="ExternalOutput").ap()
    NT = 512
    nch = tpc // NT
    NJ = ni // 128
    NK = 16
    import contextlib
    with contextlib.ExitStack() as _stk:
        def sb(name, shape, dt=F32):
            return _stk.enter_context(nc.sbuf_tensor(name, shape, dt))

        def psb(name, shape, dt=F32):
            return _stk.enter_context(nc.psum_tensor(name, shape, dt))

        ab = sb("ab", [128, NK * tpc], BF16)
        gwb = sb("gwb", [128, 2 * NK * 128], BF16)
        uwb = sb("uwb", [128, 2 * NK * 128], BF16)
        gst = sb("gst", [128, NT], BF16)
        ust = sb("ust", [128, NT], BF16)
        ht = sb("ht", [128, 2 * NT], BF16)
        zt = sb("zt", [128, 2 * NT], BF16)
        gsb = sb("gsb", [128, NJ])
        usb = sb("usb", [128, NJ])
        sudpb = sb("sudpb", [128, NJ])
        hadb = sb("hadb", [128, 128], BF16)
        psg = psb("psg", [128, NT])
        psu = psb("psu", [128, NT])
        psz = psb("psz", [128, NT])

        P = Prog(nc)
        P.dma("sync", "lda", lambda e, s: e.dma_start(
            out=ab[:], in_=bass.AP(aT.tensor, 0, [[tpc, 128], [128 * tpc, NK], [1, tpc]])))
        lda_done = P.cur("lda")
        P.dma("sync", "lds", lambda e, s: e.dma_start(
            out=gsb[:], in_=bass.AP(gsv.tensor, 0, [[1, 128], [128, NJ]]),
            allow_slow_non_contiguous=True))
        P.dma("sync", "lds", lambda e, s: e.dma_start(
            out=usb[:], in_=bass.AP(usv.tensor, 0, [[1, 128], [128, NJ]]),
            allow_slow_non_contiguous=True))
        if post_had:
            P.dma("sync", "lds", lambda e, s: e.dma_start(out=hadb[:], in_=h128))
        if mul_su:
            P.dma("sync", "lds", lambda e, s: e.dma_start(
                out=sudpb[:], in_=bass.AP(sudp.tensor, 0, [[1, 128], [128, NJ]]),
                allow_slow_non_contiguous=True))
        lds_done = P.cur("lds")
        if sv_scale != 1.0 or mul_su:
            P.add("vector", None, wait=[("lds", lds_done)])
            if sv_scale != 1.0:
                P.add("vector", lambda e, s: e.tensor_scalar_mul(
                    gsb[:], gsb[:], float(sv_scale)))
                P.selfsync("vector")
                P.add("vector", lambda e, s: e.tensor_scalar_mul(
                    usb[:], usb[:], float(sv_scale)))
                P.selfsync("vector")
            if mul_su:
                P.add("vector", lambda e, s: e.tensor_mul(usb[:], usb[:], sudpb[:]))
                P.selfsync("vector")
            P.add("vector", None, inc=[("vm", 1)])
            P.add("scalar", None, wait=[("lds", lds_done), ("vm", P.cur("vm"))])
        else:
            P.add("scalar", None, wait=[("lds", lds_done)])
        P.add("tensor", None, wait=[("lda", lda_done), ("lds", lds_done)])

        wrel = {}
        evrel = {}
        htrel = {}
        ztrel = {}
        for j in range(NJ):
            jb = j % 2
            w = [wrel[j - 2]] if j - 2 in wrel else []
            P.dma("gpsimd", "ldw", lambda e, s, j=j, jb=jb: e.dma_start(
                out=gwb[:, jb * NK * 128:(jb + 1) * NK * 128],
                in_=bass.AP(gwT.tensor, j * 128, [[ni, 128], [128 * ni, NK], [1, 128]])),
                wait=w)
            P.dma("gpsimd", "ldw2", lambda e, s, j=j, jb=jb: e.dma_start(
                out=uwb[:, jb * NK * 128:(jb + 1) * NK * 128],
                in_=bass.AP(uwT.tensor, j * 128, [[ni, 128], [128 * ni, NK], [1, 128]])),
                wait=([wrel[j - 2]] if j - 2 in wrel else []))
            ldw_done, ldw2_done = P.cur("ldw"), P.cur("ldw2")
            for tc in range(nch):
                w = [("ldw", ldw_done), ("ldw2", ldw2_done)]
                if (j * nch + tc - 1) in evrel:
                    w.append(evrel[j * nch + tc - 1])
                for k in range(NK):
                    P.add("tensor", lambda e, s, k=k, j=j, jb=jb, tc=tc: e.matmul(
                        psg[:], gwb[:, jb * NK * 128 + k * 128:jb * NK * 128 + (k + 1) * 128],
                        ab[:, k * tpc + tc * NT:k * tpc + (tc + 1) * NT],
                        start=(k == 0), stop=(k == NK - 1)),
                        wait=(w if k == 0 else []))
                for k in range(NK):
                    P.add("tensor", lambda e, s, k=k, j=j, jb=jb, tc=tc: e.matmul(
                        psu[:], uwb[:, jb * NK * 128 + k * 128:jb * NK * 128 + (k + 1) * 128],
                        ab[:, k * tpc + tc * NT:k * tpc + (tc + 1) * NT],
                        start=(k == 0), stop=(k == NK - 1)),
                        inc=([("mm", 1)] if k == NK - 1 else []))
                mm_done = P.cur("mm")
                if tc == nch - 1:
                    wrel[j] = ("mm", mm_done)
                it = j * nch + tc
                w = [("mm", mm_done)]
                if it - 1 in evrel:
                    w.append(evrel[it - 1])
                P.add("scalar", lambda e, s, j=j: e.activation(
                    out=gst[:], in_=psg[:], func=AF.Silu, scale=gsb[:, j:j + 1]),
                    wait=w)
                P.selfsync("scalar")
                P.add("scalar", lambda e, s, j=j: e.activation(
                    out=ust[:], in_=psu[:], func=AF.Copy, scale=usb[:, j:j + 1]))
                P.selfsync("scalar")
                P.add("scalar", None, inc=[("sm", 1)])
                sm_done = P.cur("sm")
                hb = it % 2
                w = [("sm", sm_done)]
                if it - 2 in htrel:
                    w.append(htrel[it - 2])
                P.add("vector", lambda e, s, hb=hb: e.tensor_mul(
                    ht[:, hb * NT:(hb + 1) * NT], gst[:], ust[:]), wait=w)
                P.selfsync("vector")
                P.add("vector", None, inc=[("vm", 1)])
                vm_done = P.cur("vm")
                evrel[it] = ("vm", vm_done)
                if post_had:
                    P.add("tensor", lambda e, s, hb=hb: e.matmul(
                        psz[:], hadb[:], ht[:, hb * NT:(hb + 1) * NT],
                        start=True, stop=True), wait=[("vm", vm_done)],
                        inc=[("mm", 1)])
                    htrel[it] = ("mm", P.cur("mm"))
                    w = [("mm", P.cur("mm"))]
                    if it - 2 in ztrel:
                        w.append(ztrel[it - 2])
                    P.add("scalar", lambda e, s, hb=hb: e.activation(
                        out=zt[:, hb * NT:(hb + 1) * NT], in_=psz[:], func=AF.Copy,
                        scale=HSCALE), wait=w)
                    P.selfsync("scalar")
                    P.add("scalar", None, inc=[("sm", 1)])
                    src_t = zt
                    rdy = ("sm", P.cur("sm"))
                else:
                    src_t = ht
                    rdy = ("vm", vm_done)
                P.dma("gpsimd", "sth", lambda e, s, j=j, tc=tc, hb=hb, src_t=src_t: e.dma_start(
                    out=ho[j * 128:(j + 1) * 128, tc * NT:(tc + 1) * NT],
                    in_=src_t[:, hb * NT:(hb + 1) * NT]), wait=[rdy])
                if post_had:
                    ztrel[it] = ("sth", P.cur("sth"))
                else:
                    htrel[it] = ("sth", P.cur("sth"))
        P.add("gpsimd", None, wait=[("sth", P.cur("sth"))])
        run_prog(nc, P)
    return nc


def build_l1a(tpc, decay=0.95):
    nc = bass.Bass("TRN2", target_bir_lowering=False)
    aT = nc.dram_tensor("aT", [2048, tpc], BF16, kind="ExternalInput").ap()
    wv = nc.dram_tensor("wv", [2048, 2048], F32, kind="ExternalInput").ap()
    wg = nc.dram_tensor("wg", [2048, 2048], F32, kind="ExternalInput").ap()
    wq = nc.dram_tensor("wq", [2048, 2048], F32, kind="ExternalInput").ap()
    bg = nc.dram_tensor("bg", [2048], F32, kind="ExternalInput").ap()
    bq = nc.dram_tensor("bq", [2048], F32, kind="ExternalInput").ap()
    memo = nc.dram_tensor("memloc", [2048, tpc], BF16, kind="ExternalOutput").ap()
    caro = nc.dram_tensor("carryT", [1, 2048], F32, kind="ExternalOutput").ap()
    queo = nc.dram_tensor("que", [2048, tpc], BF16, kind="ExternalOutput").ap()
    NT = 512
    nch = tpc // NT
    NK = 16
    import contextlib
    with contextlib.ExitStack() as _stk:
        def sb(name, shape, dt=F32):
            return _stk.enter_context(nc.sbuf_tensor(name, shape, dt))

        def psb(name, shape, dt=F32):
            return _stk.enter_context(nc.psum_tensor(name, shape, dt))

        ab = sb("ab", [128, NK * tpc], BF16)
        wb = sb("wb", [128, 2 * NK * 128], BF16)
        gateb = sb("gateb", [128, 16 * tpc], BF16)
        storet = sb("storet", [128, tpc])
        memt = sb("memt", [128, tpc])
        membf = sb("membf", [128, tpc], BF16)
        quet = sb("quet", [128, 2 * NT], BF16)
        bgb = sb("bgb", [128, 16])
        bqb = sb("bqb", [128, 16])
        decc = sb("decc", [128, 1])
        ps0 = psb("ps0", [128, NT])
        ps1 = psb("ps1", [128, NT])

        P = Prog(nc)
        P.dma("sync", "lda", lambda e, s: e.dma_start(
            out=ab[:], in_=bass.AP(aT.tensor, 0, [[tpc, 128], [128 * tpc, NK], [1, tpc]])))
        lda_done = P.cur("lda")
        P.dma("sync", "ldb", lambda e, s: e.dma_start(
            out=bgb[:], in_=bass.AP(bg.tensor, 0, [[1, 128], [128, 16]]),
            allow_slow_non_contiguous=True))
        P.dma("sync", "ldb", lambda e, s: e.dma_start(
            out=bqb[:], in_=bass.AP(bq.tensor, 0, [[1, 128], [128, 16]]),
            allow_slow_non_contiguous=True))
        ldb_done = P.cur("ldb")
        P.add("vector", lambda e, s: e.memset(decc[:], decay))
        P.selfsync("vector")
        P.add("vector", None, inc=[("vm", 1)])
        P.add("scalar", None, wait=[("ldb", ldb_done)])
        P.add("tensor", None, wait=[("lda", lda_done)])

        def proj_phase(wten, wsem, phase, base):
            for m in range(16):
                mb = m % 2
                w = []
                if ("wrel", base + m - 2) in relmap:
                    w.append(relmap[("wrel", base + m - 2)])
                P.dma("gpsimd", wsem, lambda e, s, m=m, mb=mb, wten=wten: e.dma_start(
                    out=wb[:, mb * NK * 128:(mb + 1) * NK * 128],
                    in_=bass.AP(wten.tensor, m * 128,
                                [[2048, 128], [128 * 2048, NK], [1, 128]])), wait=w)
                ldw_done = P.cur(wsem)
                for tc in range(nch):
                    it = base * nch + m * nch + tc
                    ps = ps0 if it % 2 == 0 else ps1
                    w = [(wsem, ldw_done)]
                    if ("psrel", it - 2) in relmap:
                        w.append(relmap[("psrel", it - 2)])
                    for k in range(NK):
                        P.add("tensor", lambda e, s, k=k, mb=mb, tc=tc, ps=ps: e.matmul(
                            ps[:], wb[:, mb * NK * 128 + k * 128:mb * NK * 128 + (k + 1) * 128],
                            ab[:, k * tpc + tc * NT:k * tpc + (tc + 1) * NT],
                            start=(k == 0), stop=(k == NK - 1)),
                            wait=(w if k == 0 else []),
                            inc=([("mm", 1)] if k == NK - 1 else []))
                    mm_done = P.cur("mm")
                    if tc == nch - 1:
                        relmap[("wrel", base + m)] = ("mm", mm_done)
                    yield m, tc, ps, mm_done

        relmap = {}
        for m, tc, ps, mm_done in proj_phase(wg, "ldwg", "g", 0):
            P.add("scalar", lambda e, s, m=m, tc=tc, ps=ps: e.activation(
                out=gateb[:, m * tpc + tc * NT:m * tpc + (tc + 1) * NT],
                in_=ps[:], func=AF.Sigmoid, bias=bgb[:, m:m + 1]),
                wait=[("mm", mm_done)])
            P.selfsync("scalar")
            P.add("scalar", None, inc=[("sm", 1)])
            relmap[("psrel", 0 * nch + m * nch + tc)] = ("sm", P.cur("sm"))
        for m, tc, ps, mm_done in proj_phase(wv, "ldwv", "v", 16):
            P.add("vector", lambda e, s, m=m, tc=tc, ps=ps: e.tensor_mul(
                storet[:, tc * NT:(tc + 1) * NT], ps[:],
                gateb[:, m * tpc + tc * NT:m * tpc + (tc + 1) * NT]),
                wait=[("mm", mm_done)] + ([relmap[("emad", m - 1)]] if tc == 0 and (("emad", m - 1) in relmap) else []))
            P.selfsync("vector")
            P.add("vector", None, inc=[("vm", 1)])
            relmap[("psrel", 16 * nch + m * nch + tc)] = ("vm", P.cur("vm"))
            if tc == nch - 1:
                w = []
                if "stc" in P.ct:
                    w.append(("stc", P.cur("stc")))
                P.add("vector", lambda e, s: e.tensor_tensor_scan(
                    memt[:], decc[:].broadcast_to([128, tpc]), storet[:], 0.0,
                    op0=ALU.mult, op1=ALU.add), wait=w)
                P.selfsync("vector")
                w = []
                if "stm" in P.ct:
                    w.append(("stm", P.cur("stm")))
                P.add("vector", lambda e, s: e.tensor_copy(membf[:], memt[:]), wait=w)
                P.selfsync("vector")
                P.add("vector", None, inc=[("vm", 1)])
                vm_done = P.cur("vm")
                relmap[("emad", m)] = ("vm", vm_done)
                P.dma("gpsimd", "stm", lambda e, s, m=m: e.dma_start(
                    out=memo[m * 128:(m + 1) * 128, :], in_=membf[:]), wait=[("vm", vm_done)])
                P.dma("gpsimd", "stc", lambda e, s, m=m: e.dma_start(
                    out=bass.AP(caro.tensor, m * 128, [[2048, 1], [1, 128]]),
                    in_=memt[:, tpc - 1:tpc]), wait=[("vm", vm_done)])
        for m, tc, ps, mm_done in proj_phase(wq, "ldwq", "q", 32):
            qb = (m * nch + tc) % 2
            w = [("mm", mm_done)]
            if ("qrel", m * nch + tc - 2) in relmap:
                w.append(relmap[("qrel", m * nch + tc - 2)])
            P.add("scalar", lambda e, s, qb=qb, m=m, ps=ps: e.activation(
                out=quet[:, qb * NT:(qb + 1) * NT], in_=ps[:], func=AF.Sigmoid,
                bias=bqb[:, m:m + 1]), wait=w)
            P.selfsync("scalar")
            P.add("scalar", None, inc=[("sm", 1)])
            relmap[("psrel", 32 * nch + m * nch + tc)] = ("sm", P.cur("sm"))
            P.dma("gpsimd", "stq", lambda e, s, m=m, tc=tc, qb=qb: e.dma_start(
                out=queo[m * 128:(m + 1) * 128, tc * NT:(tc + 1) * NT],
                in_=quet[:, qb * NT:(qb + 1) * NT]), wait=[("sm", P.cur("sm"))])
            relmap[("qrel", m * nch + tc)] = ("stq", P.cur("stq"))
        fw = [("stm", P.cur("stm")), ("stc", P.cur("stc")), ("stq", P.cur("stq"))]
        P.add("gpsimd", None, wait=fw)
        run_prog(nc, P)
    return nc


def build_emafix_load(tpc):
    nc = bass.Bass("TRN2", target_bir_lowering=False)
    memi = nc.dram_tensor("memloc", [2048, tpc], BF16, kind="ExternalInput").ap()
    cari = nc.dram_tensor("carryT", [1, 2048], F32, kind="ExternalInput").ap()
    quei = nc.dram_tensor("que", [2048, tpc], BF16, kind="ExternalInput").ap()
    deci = nc.dram_tensor("decrow", [1, tpc], F32, kind="ExternalInput").ap()
    lo = nc.dram_tensor("loadT", [2048, tpc], BF16, kind="ExternalOutput").ap()
    NT = 512
    nch = tpc // NT
    import contextlib
    with contextlib.ExitStack() as _stk:
        def sb(name, shape, dt=F32):
            return _stk.enter_context(nc.sbuf_tensor(name, shape, dt))

        memb = sb("memb", [128, 2 * NT], BF16)
        queb = sb("queb", [128, 2 * NT], BF16)
        carb = sb("carb", [1, 2048])
        decb = sb("decb", [1, tpc])
        mfix = sb("mfix", [128, 2 * NT])
        lt = sb("lt", [128, 2 * NT], BF16)
        psc = _stk.enter_context(nc.psum_tensor("psc", [128, NT], F32))
        psc2 = _stk.enter_context(nc.psum_tensor("psc2", [128, NT], F32))

        P = Prog(nc)
        P.dma("sync", "lds", lambda e, s: e.dma_start(out=carb[:], in_=cari))
        P.dma("sync", "lds", lambda e, s: e.dma_start(out=decb[:], in_=deci))
        lds_done = P.cur("lds")
        P.add("tensor", None, wait=[("lds", lds_done)])
        rel = {}
        for m in range(16):
            for tc in range(nch):
                it = m * nch + tc
                bb = it % 2
                w = [rel[it - 2]] if it - 2 in rel else []
                P.dma("sync", "ldm", lambda e, s, m=m, tc=tc, bb=bb: e.dma_start(
                    out=memb[:, bb * NT:(bb + 1) * NT],
                    in_=memi[m * 128:(m + 1) * 128, tc * NT:(tc + 1) * NT]), wait=w)
                P.dma("gpsimd", "ldq", lambda e, s, m=m, tc=tc, bb=bb: e.dma_start(
                    out=queb[:, bb * NT:(bb + 1) * NT],
                    in_=quei[m * 128:(m + 1) * 128, tc * NT:(tc + 1) * NT]), wait=w)
                ldm_done, ldq_done = P.cur("ldm"), P.cur("ldq")
                ps = psc if it % 2 == 0 else psc2
                w2 = [("lds", lds_done)]
                if ("ps", it - 2) in rel:
                    w2.append(rel[("ps", it - 2)])
                P.add("tensor", lambda e, s, m=m, tc=tc, ps=ps: e.matmul(
                    ps[:], carb[:, m * 128:(m + 1) * 128],
                    decb[:, tc * NT:(tc + 1) * NT], start=True, stop=True),
                    wait=w2, inc=[("mm", 1)])
                mm_done = P.cur("mm")
                P.add("vector", lambda e, s, bb=bb, ps=ps: e.tensor_add(
                    mfix[:, bb * NT:(bb + 1) * NT], ps[:],
                    memb[:, bb * NT:(bb + 1) * NT]),
                    wait=[("mm", mm_done), ("ldm", ldm_done)])
                P.selfsync("vector")
                P.add("vector", None, inc=[("vm", 1)])
                rel[("ps", it)] = ("vm", P.cur("vm"))
                w3 = [("ldq", ldq_done)]
                if ("lt", it - 2) in rel:
                    w3.append(rel[("lt", it - 2)])
                P.add("vector", lambda e, s, bb=bb: e.tensor_mul(
                    lt[:, bb * NT:(bb + 1) * NT], mfix[:, bb * NT:(bb + 1) * NT],
                    queb[:, bb * NT:(bb + 1) * NT]), wait=w3)
                P.selfsync("vector")
                P.add("vector", None, inc=[("vm", 1)])
                vm_done = P.cur("vm")
                rel[it] = ("vm", vm_done)
                P.dma("gpsimd", "st1", lambda e, s, m=m, tc=tc, bb=bb: e.dma_start(
                    out=lo[m * 128:(m + 1) * 128, tc * NT:(tc + 1) * NT],
                    in_=lt[:, bb * NT:(bb + 1) * NT]), wait=[("vm", vm_done)])
                rel[("lt", it)] = ("st1", P.cur("st1"))
        P.add("gpsimd", None, wait=[("st1", P.cur("st1"))])
        run_prog(nc, P)
    return nc


def build_foldH(ni):
    nc = bass.Bass("TRN2", target_bir_lowering=False)
    wT = nc.dram_tensor("wT", [ni, 2048], F32, kind="ExternalInput").ap()
    hmat = nc.dram_tensor("hmat", [128, 128], BF16, kind="ExternalInput").ap()
    w2 = nc.dram_tensor("w2T", [ni, 2048], F32, kind="ExternalOutput").ap()
    NJ = ni // 128
    import contextlib
    with contextlib.ExitStack() as _stk:
        def sb(name, shape, dt=F32):
            return _stk.enter_context(nc.sbuf_tensor(name, shape, dt))

        wb = sb("wb", [128, 2 * 2048], BF16)
        hb = sb("hb", [128, 128], BF16)
        ob = sb("ob", [128, 2 * 2048])
        ps0 = _stk.enter_context(nc.psum_tensor("ps0", [128, 512], F32))
        ps1 = _stk.enter_context(nc.psum_tensor("ps1", [128, 512], F32))
        P = Prog(nc)
        P.dma("sync", "ldh", lambda e, s: e.dma_start(out=hb[:], in_=hmat))
        P.add("tensor", None, wait=[("ldh", 16)])
        rel = {}
        for j in range(NJ):
            bb = j % 2
            w = [rel[j - 2]] if j - 2 in rel else []
            P.dma("gpsimd", "ldw", lambda e, s, j=j, bb=bb: e.dma_start(
                out=wb[:, bb * 2048:(bb + 1) * 2048],
                in_=wT[j * 128:(j + 1) * 128, :]), wait=w)
            ldw_done = P.cur("ldw")
            for c in range(4):
                ps = ps0 if c % 2 == 0 else ps1
                w2w = [("ldw", ldw_done)]
                if ("ps", j, c - 2) in rel:
                    w2w.append(rel[("ps", j, c - 2)])
                if c < 2 and ("ps", j - 1, c + 2) in rel:
                    w2w.append(rel[("ps", j - 1, c + 2)])
                P.add("tensor", lambda e, s, c=c, bb=bb, ps=ps: e.matmul(
                    ps[:], hb[:], wb[:, bb * 2048 + c * 512:bb * 2048 + (c + 1) * 512],
                    start=True, stop=True), wait=w2w, inc=[("mm", 1)])
                mm_done = P.cur("mm")
                wst = []
                if ("ob", j - 2) in rel:
                    wst.append(rel[("ob", j - 2)])
                P.add("scalar", lambda e, s, c=c, bb=bb, ps=ps: e.activation(
                    out=ob[:, bb * 2048 + c * 512:bb * 2048 + (c + 1) * 512],
                    in_=ps[:], func=AF.Copy), wait=[("mm", mm_done)] + wst)
                P.selfsync("scalar")
                P.add("scalar", None, inc=[("sm", 1)])
                rel[("ps", j, c)] = ("sm", P.cur("sm"))
            rel[j] = ("sm", P.cur("sm"))
            P.dma("gpsimd", "st1", lambda e, s, j=j, bb=bb: e.dma_start(
                out=w2[j * 128:(j + 1) * 128, :], in_=ob[:, bb * 2048:(bb + 1) * 2048]),
                wait=[("sm", P.cur("sm"))])
            rel[("ob", j)] = ("st1", P.cur("st1"))
        P.add("gpsimd", None, wait=[("st1", P.cur("st1"))])
        run_prog(nc, P)
    return nc


def build_had_in(tpc):
    SC = 1.0 / (64.0 * float(np.sqrt(2048.0)))
    nc = bass.Bass("TRN2", target_bir_lowering=False)
    xnT = nc.dram_tensor("xnT", [2048, tpc], BF16, kind="ExternalInput").ap()
    su = nc.dram_tensor("su", [2048], F32, kind="ExternalInput").ap()
    hT = nc.dram_tensor("hT", [2048, 2048], BF16, kind="ExternalInput").ap()
    xwo = nc.dram_tensor("xw", [2048, tpc], BF16, kind="ExternalOutput").ap()
    NT = 512
    nch = tpc // NT
    NK = 16
    import contextlib
    with contextlib.ExitStack() as _stk:
        def sb(name, shape, dt=F32):
            return _stk.enter_context(nc.sbuf_tensor(name, shape, dt))

        xb = sb("xb", [128, NK * tpc], BF16)
        hb = sb("hb", [128, NK * 128 * 2], BF16)
        sub = sb("sub", [128, 16])
        ot = sb("ot", [128, 2 * NT], BF16)
        ps0 = _stk.enter_context(nc.psum_tensor("ps0", [128, NT], F32))
        ps1 = _stk.enter_context(nc.psum_tensor("ps1", [128, NT], F32))
        P = Prog(nc)
        P.dma("sync", "lda", lambda e, s: e.dma_start(
            out=xb[:], in_=bass.AP(xnT.tensor, 0, [[tpc, 128], [128 * tpc, NK], [1, tpc]])))
        P.dma("sync", "lds", lambda e, s: e.dma_start(
            out=sub[:], in_=bass.AP(su.tensor, 0, [[1, 128], [128, 16]]),
            allow_slow_non_contiguous=True))
        P.add("vector", None, wait=[("lda", P.cur("lda")), ("lds", P.cur("lds"))])
        for k in range(NK):
            P.add("vector", lambda e, s, k=k: e.tensor_scalar_mul(
                xb[:, k * tpc:(k + 1) * tpc], xb[:, k * tpc:(k + 1) * tpc],
                sub[:, k:k + 1]))
            P.selfsync("vector")
        P.add("vector", None, inc=[("vm", 1)])
        sc_done = P.cur("vm")
        P.add("tensor", None, wait=[("vm", sc_done)])
        rel = {}
        for d in range(16):
            bb = d % 2
            w = [rel[("h", d - 2)]] if ("h", d - 2) in rel else []
            P.dma("gpsimd", "ldh", lambda e, s, d=d, bb=bb: e.dma_start(
                out=hb[:, bb * NK * 128:(bb + 1) * NK * 128],
                in_=bass.AP(hT.tensor, d * 128, [[2048, 128], [128 * 2048, NK], [1, 128]])),
                wait=w)
            ldh_done = P.cur("ldh")
            for tc in range(nch):
                it = d * nch + tc
                ps = ps0 if it % 2 == 0 else ps1
                w2 = [("ldh", ldh_done)]
                if ("ps", it - 2) in rel:
                    w2.append(rel[("ps", it - 2)])
                for k in range(NK):
                    P.add("tensor", lambda e, s, k=k, d=d, bb=bb, tc=tc, ps=ps: e.matmul(
                        ps[:], hb[:, bb * NK * 128 + k * 128:bb * NK * 128 + (k + 1) * 128],
                        xb[:, k * tpc + tc * NT:k * tpc + (tc + 1) * NT],
                        start=(k == 0), stop=(k == NK - 1)),
                        wait=(w2 if k == 0 else []),
                        inc=([("mm", 1)] if k == NK - 1 else []))
                mm_done = P.cur("mm")
                if tc == nch - 1:
                    rel[("h", d)] = ("mm", mm_done)
                ob = it % 2
                w3 = [("mm", mm_done)]
                if ("ot", it - 2) in rel:
                    w3.append(rel[("ot", it - 2)])
                P.add("scalar", lambda e, s, ob=ob, ps=ps: e.activation(
                    out=ot[:, ob * NT:(ob + 1) * NT], in_=ps[:], func=AF.Copy,
                    scale=SC), wait=w3)
                P.selfsync("scalar")
                P.add("scalar", None, inc=[("sm", 1)])
                rel[("ps", it)] = ("sm", P.cur("sm"))
                P.dma("gpsimd", "st1", lambda e, s, d=d, tc=tc, ob=ob: e.dma_start(
                    out=xwo[d * 128:(d + 1) * 128, tc * NT:(tc + 1) * NT],
                    in_=ot[:, ob * NT:(ob + 1) * NT]), wait=[("sm", P.cur("sm"))])
                rel[("ot", it)] = ("st1", P.cur("st1"))
        P.add("gpsimd", None, wait=[("st1", P.cur("st1"))])
        run_prog(nc, P)
    return nc


def build_gate_oln(tpc):
    nc = bass.Bass("TRN2", target_bir_lowering=False)
    xnT = nc.dram_tensor("xnT", [2048, tpc], BF16, kind="ExternalInput").ap()
    wgT = nc.dram_tensor("wgT", [2048, 2048], F32, kind="ExternalInput").ap()
    o_tm = nc.dram_tensor("o_tm", [tpc, 2048], BF16, kind="ExternalInput").ap()
    lnw = nc.dram_tensor("lnw", [1, 128], F32, kind="ExternalInput").ap()
    lnb = nc.dram_tensor("lnb", [1, 128], F32, kind="ExternalInput").ap()
    ogo = nc.dram_tensor("og_tm", [tpc, 2048], BF16, kind="ExternalOutput").ap()
    NK = 16
    ntt = tpc // 128
    import contextlib
    with contextlib.ExitStack() as _stk:
        def sb(name, shape, dt=F32):
            return _stk.enter_context(nc.sbuf_tensor(name, shape, dt))

        xb = sb("xb", [128, NK * tpc], BF16)
        wgb = sb("wgb", [128, NK * 2048], BF16)
        ob = sb("ob", [128, 2 * 2048], BF16)
        gt = sb("gt", [128, 2 * 2048], BF16)
        sqo = sb("sqo", [128, 2048])
        t1 = sb("t1", [128, 2048])
        ogt = sb("ogt", [128, 2 * 2048], BF16)
        mus = sb("mus", [128, 16])
        msq = sb("msq", [128, 16])
        mu2 = sb("mu2", [128, 16])
        varr = sb("varr", [128, 16])
        rstd = sb("rstd", [128, 16])
        epsc = sb("epsc", [128, 1])
        onesrow = sb("onesrow", [1, 128])
        lnwr = sb("lnwr", [1, 128])
        lnbr = sb("lnbr", [1, 128])
        lnwB = sb("lnwB", [128, 128])
        lnbB = sb("lnbB", [128, 128])
        ps0 = _stk.enter_context(nc.psum_tensor("ps0", [128, 512], F32))
        ps1 = _stk.enter_context(nc.psum_tensor("ps1", [128, 512], F32))
        psb = _stk.enter_context(nc.psum_tensor("psb", [128, 128], F32))

        P = Prog(nc)
        P.dma("sync", "lda", lambda e, s: e.dma_start(
            out=xb[:], in_=bass.AP(xnT.tensor, 0, [[tpc, 128], [128 * tpc, NK], [1, tpc]])))
        P.dma("gpsimd", "ldw", lambda e, s: e.dma_start(
            out=wgb[:], in_=bass.AP(wgT.tensor, 0, [[2048, 128], [128 * 2048, NK], [1, 2048]])))
        P.dma("sync", "lds", lambda e, s: e.dma_start(out=lnwr[:], in_=lnw))
        P.dma("sync", "lds", lambda e, s: e.dma_start(out=lnbr[:], in_=lnb))
        P.add("vector", lambda e, s: e.memset(onesrow[:], 1.0))
        P.add("vector", lambda e, s: e.memset(epsc[:], LN_EPS))
        P.selfsync("vector")
        P.add("vector", None, inc=[("vm", 1)])
        P.add("tensor", lambda e, s: e.matmul(psb[:], onesrow[:], lnwr[:],
                                              start=True, stop=True),
              wait=[("lds", P.cur("lds")), ("vm", P.cur("vm"))], inc=[("mm", 1)])
        P.add("scalar", lambda e, s: e.activation(out=lnwB[:], in_=psb[:], func=AF.Copy),
              wait=[("mm", P.cur("mm"))])
        P.selfsync("scalar")
        P.add("scalar", None, inc=[("sm", 1)])
        P.add("tensor", lambda e, s: e.matmul(psb[:], onesrow[:], lnbr[:],
                                              start=True, stop=True),
              wait=[("sm", P.cur("sm"))], inc=[("mm", 1)])
        P.add("scalar", lambda e, s: e.activation(out=lnbB[:], in_=psb[:], func=AF.Copy),
              wait=[("mm", P.cur("mm"))])
        P.selfsync("scalar")
        P.add("scalar", None, inc=[("sm", 1)])
        setup_sm = P.cur("sm")
        lda_done = P.cur("lda")
        ldw_done = P.cur("ldw")
        P.add("tensor", None, wait=[("lda", lda_done), ("ldw", ldw_done)])
        P.add("vector", None, wait=[("sm", setup_sm)])

        rel = {}
        for tt in range(ntt):
            bb = tt % 2
            w = [rel[("ob", tt - 2)]] if ("ob", tt - 2) in rel else []
            P.dma("sync", "ldo", lambda e, s, tt=tt, bb=bb: e.dma_start(
                out=ob[:, bb * 2048:(bb + 1) * 2048],
                in_=o_tm[tt * 128:(tt + 1) * 128, :]), wait=w)
            ldo_done = P.cur("ldo")
            for co in range(4):
                it = tt * 4 + co
                ps = ps0 if it % 2 == 0 else ps1
                w2 = []
                if ("ps", it - 2) in rel:
                    w2.append(rel[("ps", it - 2)])
                for k in range(NK):
                    P.add("tensor", lambda e, s, k=k, tt=tt, co=co, ps=ps: e.matmul(
                        ps[:], xb[:, k * tpc + tt * 128:k * tpc + (tt + 1) * 128],
                        wgb[:, k * 2048 + co * 512:k * 2048 + (co + 1) * 512],
                        start=(k == 0), stop=(k == NK - 1)),
                        wait=(w2 if k == 0 else []),
                        inc=([("mm", 1)] if k == NK - 1 else []))
                mm_done = P.cur("mm")
                w3 = [("mm", mm_done)]
                if ("gt", tt - 2) in rel and co == 0:
                    w3.append(rel[("gt", tt - 2)])
                P.add("scalar", lambda e, s, bb=bb, co=co, ps=ps: e.activation(
                    out=gt[:, bb * 2048 + co * 512:bb * 2048 + (co + 1) * 512],
                    in_=ps[:], func=AF.Sigmoid), wait=w3)
                P.selfsync("scalar")
                P.add("scalar", None, inc=[("sm", 1)])
                rel[("ps", it)] = ("sm", P.cur("sm"))
            g_done = ("sm", P.cur("sm"))
            ov = ob[:, bb * 2048:(bb + 1) * 2048]
            P.add("vector", None, wait=[("ldo", ldo_done)])
            for h in range(16):
                P.add("vector", lambda e, s, h=h, ov=ov: e.reduce_sum(
                    mus[:, h:h + 1], ov[:, h * 128:(h + 1) * 128], axis=AX.X))
            P.selfsync("vector")
            P.add("vector", None, inc=[("vm", 1)])
            P.add("scalar", lambda e, s, ov=ov: e.activation(
                out=sqo[:], in_=ov, func=AF.Square), wait=[("ldo", ldo_done)])
            P.selfsync("scalar")
            P.add("scalar", None, inc=[("sm", 1)])
            sq_done = P.cur("sm")
            P.add("vector", None, wait=[("sm", sq_done)])
            for h in range(16):
                P.add("vector", lambda e, s, h=h: e.reduce_sum(
                    msq[:, h:h + 1], sqo[:, h * 128:(h + 1) * 128], axis=AX.X))
            P.selfsync("vector")
            P.add("vector", lambda e, s: e.tensor_scalar_mul(mus[:], mus[:], 1.0 / 128))
            P.selfsync("vector")
            P.add("vector", lambda e, s: e.tensor_mul(mu2[:], mus[:], mus[:]))
            P.selfsync("vector")
            P.add("vector", lambda e, s: e.tensor_scalar(
                varr[:], msq[:], 1.0 / 128, None, op0=ALU.mult, op1=ALU.bypass))
            P.selfsync("vector")
            P.add("vector", lambda e, s: e.tensor_sub(varr[:], varr[:], mu2[:]))
            P.selfsync("vector")
            P.add("vector", None, inc=[("vm", 1)])
            st_done = P.cur("vm")
            P.add("scalar", lambda e, s: e.activation(
                out=rstd[:], in_=varr[:], func=AF.Ln, bias=epsc[:]),
                wait=[("vm", st_done)])
            P.selfsync("scalar")
            P.add("scalar", lambda e, s: e.activation(
                out=rstd[:], in_=rstd[:], func=AF.Exp, scale=-0.5))
            P.selfsync("scalar")
            P.add("scalar", None, inc=[("sm", 1)])
            rst_done = P.cur("sm")
            ov3 = ov.rearrange("p (h d) -> p h d", d=128)
            mub = mus[:].unsqueeze(2).broadcast_to([128, 16, 128])
            rsb = rstd[:].unsqueeze(2).broadcast_to([128, 16, 128])
            lnwb3 = lnwB[:].unsqueeze(1).broadcast_to([128, 16, 128])
            lnbb3 = lnbB[:].unsqueeze(1).broadcast_to([128, 16, 128])
            t13 = t1[:].rearrange("p (h d) -> p h d", d=128)
            P.add("vector", lambda e, s, ov3=ov3, mub=mub, t13=t13: e.tensor_tensor(
                t13, ov3, mub, op=ALU.subtract), wait=[("sm", rst_done)])
            P.selfsync("vector")
            P.add("vector", lambda e, s, t13=t13, rsb=rsb: e.tensor_tensor(
                t13, t13, rsb, op=ALU.mult))
            P.selfsync("vector")
            P.add("vector", lambda e, s, t13=t13, lnwb3=lnwb3: e.tensor_tensor(
                t13, t13, lnwb3, op=ALU.mult))
            P.selfsync("vector")
            P.add("vector", lambda e, s, t13=t13, lnbb3=lnbb3: e.tensor_tensor(
                t13, t13, lnbb3, op=ALU.add))
            P.selfsync("vector")
            w4 = [g_done]
            if ("ogt", tt - 2) in rel:
                w4.append(rel[("ogt", tt - 2)])
            P.add("vector", lambda e, s, bb=bb: e.tensor_mul(
                ogt[:, bb * 2048:(bb + 1) * 2048], t1[:],
                gt[:, bb * 2048:(bb + 1) * 2048]), wait=w4)
            P.selfsync("vector")
            P.add("vector", None, inc=[("vm", 1)])
            ogv = P.cur("vm")
            rel[("ob", tt)] = ("vm", ogv)
            rel[("gt", tt)] = ("vm", ogv)
            P.dma("gpsimd", "st1", lambda e, s, tt=tt, bb=bb: e.dma_start(
                out=ogo[tt * 128:(tt + 1) * 128, :],
                in_=ogt[:, bb * 2048:(bb + 1) * 2048]), wait=[("vm", ogv)])
            rel[("ogt", tt)] = ("st1", P.cur("st1"))
        P.add("gpsimd", None, wait=[("st1", P.cur("st1"))])
        run_prog(nc, P)
    return nc



LOGA_MIN = -25.0
C = 16
G = 4
SPAN = 128

def build_l2f(Prog, run_prog, TL=512):
    nc = bass.Bass("TRN2", target_bir_lowering=False)
    TH = TL + 3
    NSP = TL // SPAN

    def inp(name, shape, dt=F32):
        return nc.dram_tensor(name, shape, dt, kind="ExternalInput").ap()

    def outp(name, shape, dt=F32):
        return nc.dram_tensor(name, shape, dt, kind="ExternalOutput").ap()

    xnT = inp("xnT", [2048, TH], BF16)
    wts = {w: inp(w, [2048, 1024]) for w in ("wqT", "wkT", "wvT", "waT")}
    wbT = inp("wbT", [2048, 8])
    bav = inp("bav", [1024, 1]); bbv = inp("bbv", [8, 1])
    cws = {w: inp(w, [1024, 4]) for w in ("qcw", "kcw", "vcw")}
    cbs = {w: inp(w, [1024, 1]) for w in ("qcb", "kcb", "vcb")}
    kprevcm = inp("kprevcm", [1024, 16], BF16)
    lamq_in = inp("lamq_in", [128, 8])
    identf = inp("identf", [128, 128])
    identb2 = inp("identb2", [128, 128], BF16)

    kcm_o = outp("kcm_o", [1024, TL], BF16)
    khcm_o = outp("khcm_o", [1024, TL], BF16)
    qcm_o = outp("qcm_o", [1024, TL], BF16)
    ktm_o = outp("ktm_o", [TL, 1024], BF16)
    vtm_o = outp("vtm_o", [TL, 1024], BF16)
    atm_o = outp("atm_o", [TL, 1024], BF16)
    lamtm_o = outp("lamtm_o", [TL, 1024], F32)
    lamq_o = outp("lamq_o", [128, 8], F32)
    A_o = outp("A_o", [TL, 1024], BF16)
    AQ_o = outp("AQ_o", [TL, 1024], BF16)
    Ap_o = outp("Ap_o", [NSP * C, 1024], BF16)
    beta_sc = nc.dram_tensor("beta_sc", [16, TL], F32).ap()

    with contextlib.ExitStack() as stk:
        def sb(name, shape, dt=F32):
            return stk.enter_context(nc.sbuf_tensor(name, shape, dt))

        def psb(name, shape, dt=F32):
            return stk.enter_context(nc.psum_tensor(name, shape, dt))

        xcmb = sb("xcmb", [128, 16 * TH], BF16)
        wslab = sb("wslab", [128, 2 * 16 * 128], BF16)
        projf = sb("projf", [128, 8 * TH])
        cvt = sb("cvt", [128, TL])
        conb = sb("conb", [128, 4])
        conbias = sb("conbias", [128, 1])
        bab = sb("bab", [128, 8])
        bbb = sb("bbb", [8, 1])
        qcm = sb("qcm", [128, 8 * TL], BF16)
        kcm = sb("kcm", [128, 8 * TL], BF16)
        khcm = sb("khcm", [128, 8 * TL], BF16)
        vcm = sb("vcm", [128, 8 * TL], BF16)
        vbcm = sb("vbcm", [128, 8 * TL], BF16)
        acm = sb("acm", [128, 8 * TL], BF16)
        lamcm = sb("lamcm", [128, 8 * TL])
        zacm = sb("zacm", [128, 8 * TL])
        lamq = sb("lamq", [128, 8])
        logc = sb("logc", [128, 1])
        bcm = sb("bcm", [8, TL])
        bnegcm = sb("bnegcm", [8, TL])
        brow = sb("brow", [1, 2 * TL])
        onesrow = sb("onesrow", [1, 128])
        kprevb = sb("kprevb", [128, 128], BF16)
        identbf = sb("identbf", [128, 128])
        identbb = sb("identbb", [128, 128], BF16)
        tts = sb("tts", [128, 2 * 128], BF16)
        ttf = sb("ttf", [128, 2 * 128])
        ps_a = psb("ps_a", [128, 512])
        ps_a2 = psb("ps_a2", [128, 512])
        ps_bc = psb("ps_bc", [128, 512])
        ps_tb = psb("ps_tb", [128, 128], BF16)
        ps_tf = psb("ps_tf", [128, 128])
        ps_ap = psb("ps_ap", [16, 128])

        P = Prog(nc)
        A = P.add
        DMA = P.dma
        rel = {}

        A("vector", lambda e, s: e.memset(onesrow[:], 1.0))
        A("vector", lambda e, s: e.memset(logc[:], 1.0))
        P.selfsync("vector")
        A("vector", None, inc=[("vm", 1)])
        DMA("sync", "ldx", lambda e, s: e.dma_start(
            out=xcmb[:], in_=bass.AP(xnT.tensor, 0, [[TH, 128], [128 * TH, 16], [1, TH]])))
        DMA("sync", "ldc", lambda e, s: e.dma_start(out=bab[:], in_=bass.AP(
            bav.tensor, 0, [[1, 128], [128, 8]]), allow_slow_non_contiguous=True))
        DMA("sync", "ldc", lambda e, s: e.dma_start(out=bbb[:], in_=bbv))
        DMA("sync", "ldc", lambda e, s: e.dma_start(out=lamq[:], in_=lamq_in))
        DMA("sync", "ldc", lambda e, s: e.dma_start(
            out=kprevb[:], in_=bass.AP(kprevcm.tensor, 0,
                                       [[16, 128], [128 * 16, 8], [1, 16]])))
        DMA("sync", "ldc", lambda e, s: e.dma_start(out=identbf[:], in_=identf))
        DMA("sync", "ldc", lambda e, s: e.dma_start(out=identbb[:], in_=identb2))
        ldx_done = P.cur("ldx")
        ldc_done = P.cur("ldc")
        A("tensor", None, wait=[("ldx", ldx_done), ("ldc", ldc_done), ("vm", P.cur("vm"))])
        A("scalar", None, wait=[("ldc", ldc_done)])
        A("vector", None, wait=[("ldx", ldx_done), ("ldc", ldc_done)])

        gctr = [0]
        wctr = [0]

        def project(wT, wsem, TT, with_bias=False, xoff=0):
            nps = (TT + 511) // 512
            for ht in range(8):
                wi = wctr[0]
                wctr[0] += 1
                sl = wi % 2
                w = [rel[("wsl", wi - 2)]] if ("wsl", wi - 2) in rel else []
                DMA("gpsimd", wsem, lambda e, s, ht=ht, sl=sl, wT=wT: e.dma_start(
                    out=wslab[:, sl * 2048:(sl + 1) * 2048],
                    in_=bass.AP(wT.tensor, ht * 128,
                                [[1024, 128], [128 * 1024, 16], [1, 128]])), wait=w)
                ldw = P.cur(wsem)
                for tc in range(nps):
                    nt = min(512, TT - tc * 512)
                    gi = gctr[0]
                    gctr[0] += 1
                    ps = ps_a if gi % 2 == 0 else ps_a2
                    w2_ = [(wsem, ldw)]
                    if ("psf", gi - 2) in rel:
                        w2_.append(rel[("psf", gi - 2)])
                    for k in range(16):
                        A("tensor", lambda e, s, k=k, sl=sl, tc=tc, nt=nt, ps=ps,
                            xoff=xoff: e.matmul(
                            ps[:, 0:nt],
                            wslab[:, sl * 2048 + k * 128:sl * 2048 + (k + 1) * 128],
                            xcmb[:, k * TH + xoff + tc * 512:
                                 k * TH + xoff + tc * 512 + nt],
                            start=(k == 0), stop=(k == 15)),
                            wait=(w2_ if k == 0 else []),
                            inc=([("mm", 1)] if k == 15 else []))
                    mm_d = P.cur("mm")
                    if tc == nps - 1:
                        rel[("wsl", wi)] = ("mm", mm_d)
                    if with_bias:
                        A("vector", lambda e, s, ht=ht, tc=tc, nt=nt, ps=ps: e.tensor_scalar_add(
                            projf[:, ht * TH + tc * 512:ht * TH + tc * 512 + nt],
                            ps[:, 0:nt], bab[:, ht:ht + 1]),
                            wait=[("mm", mm_d)])
                        P.selfsync("vector")
                        A("vector", None, inc=[("sm", 1)])
                        rel[("psf", gi)] = ("sm", P.cur("sm"))
                    else:
                        A("scalar", lambda e, s, ht=ht, tc=tc, nt=nt, ps=ps: e.activation(
                            out=projf[:, ht * TH + tc * 512:ht * TH + tc * 512 + nt],
                            in_=ps[:, 0:nt], func=AF.Copy), wait=[("mm", mm_d)])
                        P.selfsync("scalar")
                        A("scalar", None, inc=[("sm", 1)])
                        rel[("psf", gi)] = ("sm", P.cur("sm"))
            rel[("pdone", wsem)] = ("sm", P.cur("sm"))

        cctr = [0]

        def conv_silu(cw, cbias, wsem, out_arr, extra_scale, pkey):
            for ht in range(8):
                ci = cctr[0]
                cctr[0] += 1
                w = [rel[("cvw", ci - 1)]] if ("cvw", ci - 1) in rel else []
                DMA("sync", wsem, lambda e, s, ht=ht, cw=cw: e.dma_start(
                    out=conb[:], in_=cw[ht * 128:(ht + 1) * 128, :]), wait=w)
                DMA("sync", wsem, lambda e, s, ht=ht, cbias=cbias: e.dma_start(
                    out=conbias[:], in_=cbias[ht * 128:(ht + 1) * 128, :]))
                ldcw = P.cur(wsem)
                wv = [(wsem, ldcw), rel[("pdone", pkey)]]
                A("vector", lambda e, s, ht=ht: e.tensor_scalar(
                    cvt[:], projf[:, ht * TH:ht * TH + TL], conb[:, 0:1], None,
                    op0=ALU.mult, op1=ALU.bypass), wait=wv)
                P.selfsync("vector")
                for kk in range(1, 4):
                    A("vector", lambda e, s, ht=ht, kk=kk: e.scalar_tensor_tensor(
                        cvt[:], projf[:, ht * TH + kk:ht * TH + kk + TL],
                        conb[:, kk:kk + 1], cvt[:], op0=ALU.mult, op1=ALU.add))
                    P.selfsync("vector")
                A("vector", lambda e, s: e.tensor_scalar_add(
                    cvt[:], cvt[:], conbias[:, 0:1]))
                P.selfsync("vector")
                A("vector", None, inc=[("vm", 1)])
                rel[("cvw", ci)] = ("vm", P.cur("vm"))
                A("scalar", lambda e, s, ht=ht: e.activation(
                    out=zacm[:, ht * TL:(ht + 1) * TL], in_=cvt[:], func=AF.Silu),
                    wait=[("vm", P.cur("vm"))])
                P.selfsync("scalar")
                A("scalar", None, inc=[("sm", 1)])
                A("vector", lambda e, s, ht=ht, out_arr=out_arr, extra_scale=extra_scale:
                    e.tensor_scalar(out_arr[:, ht * TL:(ht + 1) * TL],
                                    zacm[:, ht * TL:(ht + 1) * TL], float(extra_scale),
                                    None, op0=ALU.mult, op1=ALU.bypass),
                    wait=[("sm", P.cur("sm"))])
                P.selfsync("vector")
                A("vector", None, inc=[("vm", 1)])
            rel[("csdone", out_arr.name)] = ("vm", P.cur("vm"))

        project(wts["wqT"], "lwq", TH)
        conv_silu(cws["qcw"], cbs["qcb"], "lcq", qcm, 1.0, "lwq")
        project(wts["wkT"], "lwk", TH)
        conv_silu(cws["kcw"], cbs["kcb"], "lck", kcm, 128.0 ** -0.5, "lwk")
        project(wts["wvT"], "lwv", TH)
        conv_silu(cws["vcw"], cbs["vcb"], "lcv", vcm, 1.0, "lwv")

        project(wts["waT"], "lwa", TL, with_bias=True, xoff=3)
        for ht in range(8):
            pslc = projf[:, ht * TH:ht * TH + TL]
            zslc = zacm[:, ht * TL:(ht + 1) * TL]
            w = [rel[("pdone", "lwa")]]
            A("vector", lambda e, s, pslc=pslc, zslc=zslc: e.tensor_scalar(
                zslc, pslc, -1.0, None, op0=ALU.mult, op1=ALU.bypass), wait=w)
            P.selfsync("vector")
            A("vector", lambda e, s, pslc=pslc, zslc=zslc: e.tensor_tensor(
                zslc, zslc, pslc, op=ALU.max))
            P.selfsync("vector")
            A("vector", None, inc=[("vm", 1)])
            A("scalar", lambda e, s, zslc=zslc: e.activation(
                out=zslc, in_=zslc, func=AF.Exp, scale=-1.0),
                wait=[("vm", P.cur("vm"))])
            P.selfsync("scalar")
            A("scalar", lambda e, s, zslc=zslc: e.activation(
                out=zslc, in_=zslc, func=AF.Ln, bias=logc[:]))
            P.selfsync("scalar")
            A("scalar", None, inc=[("sm", 1)])
            A("vector", lambda e, s, pslc=pslc: e.tensor_scalar_min(
                cvt[:], pslc, 0.0), wait=[("sm", P.cur("sm"))])
            P.selfsync("vector")
            A("vector", lambda e, s, zslc=zslc: e.tensor_sub(zslc, cvt[:], zslc))
            P.selfsync("vector")
            A("vector", lambda e, s, zslc=zslc: e.tensor_scalar_max(
                zslc, zslc, LOGA_MIN))
            P.selfsync("vector")
            A("vector", None, inc=[("vm", 1)])
            A("scalar", lambda e, s, ht=ht, zslc=zslc: e.activation(
                out=acm[:, ht * TL:(ht + 1) * TL], in_=zslc, func=AF.Exp),
                wait=[("vm", P.cur("vm"))])
            P.selfsync("scalar")
            A("scalar", None, inc=[("sm", 1)])
            A("vector", lambda e, s, ht=ht, zslc=zslc: e.tensor_tensor_scan(
                lamcm[:, ht * TL:(ht + 1) * TL],
                logc[:].broadcast_to([128, TL]), zslc, lamq[:, ht:ht + 1],
                op0=ALU.mult, op1=ALU.add),
                wait=[("sm", P.cur("sm"))])
            P.selfsync("vector")
            A("vector", None, inc=[("vm", 1)])
        lam_done = ("vm", P.cur("vm"))
        A("vector", lambda e, s: e.tensor_copy(
            lamq[:, 0:8],
            bass.AP(lamcm, TL - 1, [[8 * TL, 128], [TL, 8]])), wait=[lam_done])
        P.selfsync("vector")
        A("vector", None, inc=[("vm", 1)])
        DMA("gpsimd", "st1", lambda e, s: e.dma_start(out=lamq_o, in_=lamq[:, 0:8]),
            wait=[("vm", P.cur("vm"))])

        wbw = [rel[("wsl", wctr[0] - 2)]] if ("wsl", wctr[0] - 2) in rel else []
        if ("wsl", wctr[0] - 1) in rel:
            wbw.append(rel[("wsl", wctr[0] - 1)])
        DMA("gpsimd", "lwb", lambda e, s: e.dma_start(
            out=wslab[:, 0:128],
            in_=bass.AP(wbT.tensor, 0, [[8, 128], [128 * 8, 16], [1, 8]])), wait=wbw)
        lwb = P.cur("lwb")
        nps = (TL + 511) // 512
        for tc in range(nps):
            nt = min(512, TL - tc * 512)
            w = [("lwb", lwb)]
            if ("psf", gctr[0] - 2) in rel:
                w.append(rel[("psf", gctr[0] - 2)])
            if ("psf", gctr[0] - 1) in rel:
                w.append(rel[("psf", gctr[0] - 1)])
            if ("psb", tc - 1) in rel:
                w.append(rel[("psb", tc - 1)])
            for k in range(16):
                A("tensor", lambda e, s, k=k, tc=tc, nt=nt: e.matmul(
                    ps_a[0:8, 0:nt], wslab[:, k * 8:(k + 1) * 8],
                    xcmb[:, k * TH + 3 + tc * 512:k * TH + 3 + tc * 512 + nt],
                    start=(k == 0), stop=(k == 15)),
                    wait=(w if k == 0 else []),
                    inc=([("mm", 1)] if k == 15 else []))
            mm_d = P.cur("mm")
            A("scalar", lambda e, s, tc=tc, nt=nt: e.activation(
                out=bcm[:, tc * 512:tc * 512 + nt], in_=ps_a[0:8, 0:nt],
                func=AF.Sigmoid, bias=bbb[:]), wait=[("mm", mm_d)])
            P.selfsync("scalar")
            A("scalar", None, inc=[("sm", 1)])
            rel[("psb", tc)] = ("sm", P.cur("sm"))
        A("vector", lambda e, s: e.tensor_scalar_mul(
            bnegcm[:], bcm[:], -1.0), wait=[("sm", P.cur("sm"))])
        P.selfsync("vector")
        A("vector", None, inc=[("vm", 1)])
        DMA("gpsimd", "stb", lambda e, s: e.dma_start(out=beta_sc[0:8, :], in_=bcm[:]),
            wait=[("vm", P.cur("vm"))])
        DMA("gpsimd", "stb", lambda e, s: e.dma_start(out=beta_sc[8:16, :], in_=bnegcm[:]))
        stb_done = ("stb", P.cur("stb"))

        for ht in range(8):
            DMA("sync", "ldbr", lambda e, s, ht=ht: e.dma_start(
                out=brow[:, 0:TL], in_=beta_sc[8 + ht:9 + ht, :]),
                wait=[stb_done, rel.get(("khv", ht - 1), ("vm", 0))])
            DMA("sync", "ldbr", lambda e, s, ht=ht: e.dma_start(
                out=brow[:, TL:2 * TL], in_=beta_sc[ht:ht + 1, :]))
            ldbr = P.cur("ldbr")
            for tc in range(nps):
                nt = min(512, TL - tc * 512)
                A("tensor", lambda e, s, tc=tc, nt=nt: e.matmul(
                    ps_bc[:, 0:nt], onesrow[:], brow[:, tc * 512:tc * 512 + nt],
                    start=True, stop=True), wait=[("ldbr", ldbr)], inc=[("mm", 1)])
                A("vector", lambda e, s, ht=ht, tc=tc, nt=nt: e.tensor_mul(
                    khcm[:, ht * TL + tc * 512:ht * TL + tc * 512 + nt],
                    kcm[:, ht * TL + tc * 512:ht * TL + tc * 512 + nt],
                    ps_bc[:, 0:nt]), wait=[("mm", P.cur("mm"))])
                P.selfsync("vector")
                A("vector", None, inc=[("vm", 1)])
                A("tensor", lambda e, s, tc=tc, nt=nt: e.matmul(
                    ps_bc[:, 0:nt], onesrow[:],
                    brow[:, TL + tc * 512:TL + tc * 512 + nt],
                    start=True, stop=True), wait=[("vm", P.cur("vm"))], inc=[("mm", 1)])
                A("vector", lambda e, s, ht=ht, tc=tc, nt=nt: e.tensor_mul(
                    vbcm[:, ht * TL + tc * 512:ht * TL + tc * 512 + nt],
                    vcm[:, ht * TL + tc * 512:ht * TL + tc * 512 + nt],
                    ps_bc[:, 0:nt]), wait=[("mm", P.cur("mm"))])
                P.selfsync("vector")
                A("vector", None, inc=[("vm", 1)])
            rel[("khv", ht)] = ("vm", P.cur("vm"))
        khv_done = ("vm", P.cur("vm"))

        for arr, dst in ((qcm, qcm_o), (kcm, kcm_o), (khcm, khcm_o)):
            DMA("gpsimd", "st1", lambda e, s, arr=arr, dst=dst: e.dma_start(
                out=bass.AP(dst.tensor, 0, [[TL, 128], [128 * TL, 8], [1, TL]]),
                in_=arr[:]), wait=[khv_done])

        jobs = [(kcm, ktm_o, True), (vbcm, vtm_o, True), (acm, atm_o, True),
                (lamcm, lamtm_o, False)]
        tctr = [0]
        for (arr, dst, isbf) in jobs:
            for ht in range(8):
                for blk in range(TL // 128):
                    ti = tctr[0]
                    tctr[0] += 1
                    idx = ti % 2
                    w = [khv_done]
                    if ("trev", ti - 1) in rel:
                        w.append(rel[("trev", ti - 1)])
                    if ("trdma", ti - 2) in rel:
                        w.append(rel[("trdma", ti - 2)])
                    if isbf:
                        A("tensor", lambda e, s, arr=arr, ht=ht, blk=blk: e.transpose(
                            ps_tb[:], arr[:, ht * TL + blk * 128:ht * TL + (blk + 1) * 128],
                            identbb[:]), wait=w, inc=[("mm", 1)])
                        A("scalar", lambda e, s, idx=idx: e.activation(
                            out=tts[:, idx * 128:(idx + 1) * 128], in_=ps_tb[:],
                            func=AF.Copy), wait=[("mm", P.cur("mm"))])
                    else:
                        A("tensor", lambda e, s, arr=arr, ht=ht, blk=blk: e.transpose(
                            ps_tf[:], arr[:, ht * TL + blk * 128:ht * TL + (blk + 1) * 128],
                            identbf[:]), wait=w, inc=[("mm", 1)])
                        A("scalar", lambda e, s, idx=idx: e.activation(
                            out=ttf[:, idx * 128:(idx + 1) * 128], in_=ps_tf[:],
                            func=AF.Copy), wait=[("mm", P.cur("mm"))])
                    P.selfsync("scalar")
                    A("scalar", None, inc=[("sm", 1)])
                    rel[("trev", ti)] = ("sm", P.cur("sm"))
                    srctile = tts if isbf else ttf
                    DMA("gpsimd", "st1", lambda e, s, dst=dst, ht=ht, blk=blk, idx=idx,
                        srctile=srctile: e.dma_start(
                        out=dst[blk * 128:(blk + 1) * 128, ht * 128:(ht + 1) * 128],
                        in_=srctile[:, idx * 128:(idx + 1) * 128]),
                        wait=[("sm", P.cur("sm"))])
                    rel[("trdma", ti)] = ("st1", P.cur("st1"))

        for sp in range(NSP):
            for ht in range(8):
                ii = sp * 8 + ht
                w = [khv_done]
                if ii < 2:
                    for kk in range(max(0, gctr[0] - 2), gctr[0]):
                        if ("psf", kk) in rel:
                            w.append(rel[("psf", kk)])
                    for tc2 in range(nps):
                        if ("psb", tc2) in rel:
                            w.append(rel[("psb", tc2)])
                if ("aps", ii - 2) in rel:
                    w.append(rel[("aps", ii - 2)])
                ps = ps_a if ii % 2 == 0 else ps_a2
                if ("apev", ii - 1) in rel:
                    w.append(rel[("apev", ii - 1)])
                A("tensor", lambda e, s, sp=sp, ht=ht, ps=ps: e.matmul(
                    ps[:, 0:128], kcm[:, ht * TL + sp * 128:ht * TL + (sp + 1) * 128],
                    khcm[:, ht * TL + sp * 128:ht * TL + (sp + 1) * 128],
                    start=True, stop=True), wait=w)
                A("tensor", lambda e, s, sp=sp, ht=ht, ps=ps: e.matmul(
                    ps[:, 128:256], kcm[:, ht * TL + sp * 128:ht * TL + (sp + 1) * 128],
                    qcm[:, ht * TL + sp * 128:ht * TL + (sp + 1) * 128],
                    start=True, stop=True))
                if sp == 0:
                    A("tensor", lambda e, s, ht=ht: e.matmul(
                        ps_ap[:], kprevb[:, ht * 16:(ht + 1) * 16],
                        khcm[:, ht * TL:ht * TL + 128],
                        start=True, stop=True), inc=[("mm", 1)])
                else:
                    A("tensor", lambda e, s, sp=sp, ht=ht: e.matmul(
                        ps_ap[:],
                        kcm[:, ht * TL + sp * 128 - 16:ht * TL + sp * 128],
                        khcm[:, ht * TL + sp * 128:ht * TL + (sp + 1) * 128],
                        start=True, stop=True), inc=[("mm", 1)])
                mm_d = P.cur("mm")
                wev = [("mm", mm_d)]
                if ("aps", ii - 1) in rel:
                    wev.append(rel[("aps", ii - 1)])
                A("scalar", lambda e, s, ps=ps: e.activation(
                    out=tts[:, 0:128], in_=ps[:, 0:128], func=AF.Copy),
                    wait=wev)
                P.selfsync("scalar")
                A("scalar", lambda e, s, ps=ps: e.activation(
                    out=tts[:, 128:256], in_=ps[:, 128:256], func=AF.Copy))
                P.selfsync("scalar")
                A("scalar", None, inc=[("sm", 1)])
                sm_d = P.cur("sm")
                DMA("gpsimd", "st1", lambda e, s, sp=sp, ht=ht: e.dma_start(
                    out=A_o[sp * 128:(sp + 1) * 128, ht * 128:(ht + 1) * 128],
                    in_=tts[:, 0:128]), wait=[("sm", sm_d)])
                DMA("gpsimd", "st1", lambda e, s, sp=sp, ht=ht: e.dma_start(
                    out=AQ_o[sp * 128:(sp + 1) * 128, ht * 128:(ht + 1) * 128],
                    in_=tts[:, 128:256]), wait=[("sm", sm_d)])
                A("scalar", lambda e, s: e.activation(
                    out=tts[0:16, 0:128], in_=ps_ap[:], func=AF.Copy),
                    wait=[("st1", P.cur("st1")), ("mm", mm_d)])
                P.selfsync("scalar")
                A("scalar", None, inc=[("sm", 1)])
                rel[("apev", ii)] = ("sm", P.cur("sm"))
                DMA("gpsimd", "st1", lambda e, s, sp=sp, ht=ht: e.dma_start(
                    out=Ap_o[sp * 16:(sp + 1) * 16, ht * 128:(ht + 1) * 128],
                    in_=tts[0:16, 0:128]), wait=[("sm", P.cur("sm"))])
                rel[("aps", ii)] = ("st1", P.cur("st1"))
        run_prog(nc, P)
    return nc


def build_l2s(Prog, run_prog, TL=512, dbg=False):
    nc = bass.Bass("TRN2", target_bir_lowering=False)
    NCH = TL // C
    NRB = 3

    def inp(name, shape, dt=F32):
        return nc.dram_tensor(name, shape, dt, kind="ExternalInput").ap()

    kcm_i = inp("kcm", [1024, TL], BF16)
    khcm_i = inp("khcm", [1024, TL], BF16)
    qcm_i = inp("qcm", [1024, TL], BF16)
    ktm_i = inp("ktm", [TL, 1024], BF16)
    vtm_i = inp("vtm", [TL, 1024], BF16)
    atm_i = inp("atm", [TL, 1024], BF16)
    lam_i_d = inp("lam_full", [17 + TL, 1024])
    A_d = inp("A_in", [TL, 1024], BF16)
    AQ_d = inp("AQ_in", [TL, 1024], BF16)
    Ap_d = inp("Ap_in", [(TL // SPAN) * C, 1024], BF16)
    kprevtm = inp("kprevtm", [16, 1024], BF16)
    st_in = inp("st_in", [128, 1024])
    w2prev_in = inp("w2prev_in", [16, 1024])
    dmask8 = inp("dmask8", [128, 8], BF16)
    sel8 = inp("sel8", [8, 128], BF16)
    crmask = inp("crmask", [128, 128], BF16)
    aqmask = inp("aqmask", [128, 256], BF16)
    e8sel = inp("e8sel", [128, 512])
    selmaskd = inp("selmask", [32, 4], mybir.dt.uint8)
    o_out = nc.dram_tensor("o_out", [TL * 8, 128], BF16, kind="ExternalOutput").ap()
    st_out = nc.dram_tensor("st_out", [128, 1024], F32, kind="ExternalOutput").ap()
    w2prev_out = nc.dram_tensor("w2prev_out", [16, 1024], F32, kind="ExternalOutput").ap()
    wtmp_sc = nc.dram_tensor("wtmp_sc", [16, 1024], F32).ap()
    if dbg:
        dbg_outs = {nm: nc.dram_tensor("dbg_" + nm, shp, F32,
                                       kind="ExternalOutput").ap()
                    for nm, shp in [("acols", [128, 512]), ("aqcols", [128, 256]),
                                    ("crosscols", [32, 512]), ("expwf4", [32, 512]),
                                    ("expof", [128, 128]), ("expost", [128, 128]),
                                    ("cb", [128, 256]), ("w2", [128, 256]),
                                    ("expb", [128, 128]), ("ws4dec", [32, 128])]}

    import contextlib
    with contextlib.ExitStack() as stk:
        def sb(name, shape, dt=F32):
            return stk.enter_context(nc.sbuf_tensor(name, shape, dt))

        def psb(name, shape, dt=F32):
            return stk.enter_context(nc.psum_tensor(name, shape, dt))

        St = sb("St", [128, 1024])
        W2 = sb("W2", [128, 256])
        ws4 = sb("ws4", [32, 128])
        ws4dec = sb("ws4dec", [32, 128])
        acols = sb("acols", [128, 512])
        apcols = sb("apcols", [128, 512])
        crosscols = sb("crosscols", [32, 512])
        aqcols = sb("aqcols", [128, 256])
        cb = sb("cb", [128, 2 * 128])
        cbt = sb("cbt", [128, 128])
        kpadb = sb("kpadb", [128, 2 * 1024])
        qpadb = sb("qpadb", [128, 1024])
        expb = sb("expb", [128, 128])
        expwf4 = sb("expwf4", [32, 512])
        expof = sb("expof", [128, 128])
        expost = sb("expost", [128, 128])
        otf = sb("otf", [128, 128])
        otf2 = sb("otf2", [128, 128])
        otileb = sb("otileb", [128, 128], BF16)
        sttmp = sb("sttmp", [128, 1024])
        decrow = sb("decrow", [1, 1024])
        dmask8b = sb("dmask8b", [128, 8], BF16)
        crmaskb = sb("crmaskb", [128, 128], BF16)
        aqmaskb = sb("aqmaskb", [128, 256], BF16)
        e8selb = sb("e8selb", [128, 512])
        selmb = sb("selmb", [32, 4], mybir.dt.uint8)
        onesrow = sb("onesrow", [1, 128])
        w2pv = sb("w2pv", [128, 128])
        aret = sb("aret", [128, NRB * 16], BF16)
        apret = sb("apret", [128, NRB * 16], BF16)
        aqret = sb("aqret", [128, NRB * 16], BF16)
        a8 = sb("a8", [8, NRB * 2048], BF16)
        sel8b = sb("sel8b", [8, 128], BF16)
        lam_i = sb("lam_i", [128, NRB * 128])
        lam_im1 = sb("lam_im1", [128, NRB * 128])
        lrefb = sb("lrefb", [128, NRB * 128])
        lref2 = sb("lref2", [128, NRB * 128])
        lgend = sb("lgend", [128, NRB * 128])
        vb = sb("vb", [128, NRB * 128], BF16)
        ktmb16 = sb("ktmb16", [16, NRB * 1024], BF16)
        declam = sb("declam", [1, NRB * 2048])
        win16 = sb("win16", [16, 1024], BF16)
        ps_wA = psb("ps_wA", [32, 256])
        ps_wB = psb("ps_wB", [32, 256])
        ps_og2 = psb("ps_og2", [64, 256])
        ps_ops = psb("ps_ops", [128, 128])
        ps_b = psb("ps_b", [128, 128])
        ps_a = psb("ps_a", [128, 512])
        ps_a2 = psb("ps_a2", [128, 512])
        ps_ar = psb("ps_ar", [128, 512])

        P = Prog(nc)
        A = P.add
        DMA = P.dma
        rel = {}
        retdone = {}

        A("vector", lambda e, s: e.memset(W2[:], 0.0))
        A("vector", lambda e, s: e.memset(ws4dec[:], 0.0))
        A("vector", lambda e, s: e.memset(acols[:], 0.0))
        A("vector", lambda e, s: e.memset(apcols[:], 0.0))
        A("vector", lambda e, s: e.memset(crosscols[:], 0.0))
        A("vector", lambda e, s: e.memset(aqcols[:], 0.0))
        A("vector", lambda e, s: e.memset(kpadb[:], 0.0))
        A("vector", lambda e, s: e.memset(qpadb[:], 0.0))
        P.selfsync("vector")
        A("vector", lambda e, s: e.memset(onesrow[:], 1.0))
        P.selfsync("vector")
        A("vector", None, inc=[("vm", 1)])
        setup_vm = P.cur("vm")
        DMA("sync", "ldc", lambda e, s: e.dma_start(out=dmask8b[:], in_=dmask8))
        DMA("sync", "ldc", lambda e, s: e.dma_start(out=sel8b[:], in_=sel8))
        DMA("sync", "ldc", lambda e, s: e.dma_start(out=crmaskb[:], in_=crmask))
        DMA("sync", "ldc", lambda e, s: e.dma_start(out=aqmaskb[:], in_=aqmask))
        DMA("sync", "ldc", lambda e, s: e.dma_start(out=e8selb[:], in_=e8sel))
        DMA("sync", "ldc", lambda e, s: e.dma_start(out=selmb[:], in_=selmaskd))
        DMA("sync", "ldc", lambda e, s: e.dma_start(out=St[:], in_=st_in))
        DMA("sync", "ldc", lambda e, s: e.dma_start(
            out=W2[:, 128:256],
            in_=bass.AP(w2prev_in.tensor, 0, [[1024, 16], [128, 8], [1, 128]])),
            wait=[("vm", setup_vm)])
        DMA("gpsimd", "rw", lambda e, s: e.dma_start(out=win16[:], in_=w2prev_in))
        ldc_done = P.cur("ldc")
        A("tensor", None, wait=[("ldc", ldc_done), ("vm", setup_vm)])
        A("vector", None, wait=[("ldc", ldc_done)])
        A("scalar", None, wait=[("ldc", ldc_done)])

        def ret_batch(n):
            slot = n % NRB
            t0 = n * C
            w = [rel[("chunkdone", n - NRB)]] if ("chunkdone", n - NRB) in rel else []
            sp = t0 // SPAN
            il = t0 % SPAN
            DMA("sync", "r1", lambda e, s, sp=sp, il=il, slot=slot: e.dma_start(
                out=aret[:, slot * 16:(slot + 1) * 16],
                in_=bass.AP(A_d.tensor, (sp * SPAN + il) * 1024 + il,
                            [[1024, 16], [128, 8], [1, 16]])), wait=w)
            if il == 0:
                DMA("sync", "r1", lambda e, s, sp=sp, slot=slot: e.dma_start(
                    out=apret[:, slot * 16:(slot + 1) * 16],
                    in_=bass.AP(Ap_d.tensor, sp * 16 * 1024,
                                [[1024, 16], [128, 8], [1, 16]])), wait=w)
            else:
                DMA("sync", "r1", lambda e, s, sp=sp, il=il, slot=slot: e.dma_start(
                    out=apret[:, slot * 16:(slot + 1) * 16],
                    in_=bass.AP(A_d.tensor, (sp * SPAN + il - 16) * 1024 + il,
                                [[1024, 16], [128, 8], [1, 16]])), wait=w)
            DMA("sync", "r1", lambda e, s, sp=sp, il=il, slot=slot: e.dma_start(
                out=aqret[:, slot * 16:(slot + 1) * 16],
                in_=bass.AP(AQ_d.tensor, (sp * SPAN + il) * 1024 + il,
                            [[1024, 16], [128, 8], [1, 16]])), wait=w)
            DMA("sync", "r1", lambda e, s, t0=t0, slot=slot: e.dma_start(
                out=lam_i[:, slot * 128:(slot + 1) * 128],
                in_=bass.AP(lam_i_d.tensor, (17 + t0) * 1024,
                            [[1024, 16], [128, 8], [1, 128]])), wait=w)
            DMA("sync", "r1", lambda e, s, t0=t0, slot=slot: e.dma_start(
                out=lam_im1[:, slot * 128:(slot + 1) * 128],
                in_=bass.AP(lam_i_d.tensor, (16 + t0) * 1024,
                            [[1024, 16], [128, 8], [1, 128]])), wait=w)
            retdone[("r1", n)] = ("r1", P.cur("r1"))
            DMA("scalar", "r2", lambda e, s, t0=t0, slot=slot: e.dma_start(
                out=lrefb[:, slot * 128:(slot + 1) * 128],
                in_=bass.AP(lam_i_d.tensor, t0 * 1024,
                            [[0, 16], [128, 8], [1, 128]])), wait=w)
            DMA("scalar", "r2", lambda e, s, t0=t0, slot=slot: e.dma_start(
                out=lref2[:, slot * 128:(slot + 1) * 128],
                in_=bass.AP(lam_i_d.tensor, (16 + t0) * 1024,
                            [[0, 16], [128, 8], [1, 128]])), wait=w)
            for g in range(4):
                DMA("scalar", "r2", lambda e, s, t0=t0, slot=slot, g=g: e.dma_start(
                    out=lgend[32 * g:32 * (g + 1), slot * 128:(slot + 1) * 128],
                    in_=bass.AP(lam_i_d.tensor, (17 + t0 + 4 * g + 3) * 1024,
                                [[0, 4], [128, 8], [1, 128]])), wait=w)
                w = []
            DMA("scalar", "r2", lambda e, s, t0=t0, slot=slot: e.dma_start(
                out=vb[:, slot * 128:(slot + 1) * 128],
                in_=bass.AP(vtm_i.tensor, t0 * 1024,
                            [[1024, 16], [128, 8], [1, 128]])), wait=w)
            DMA("scalar", "r2", lambda e, s, t0=t0, slot=slot: e.dma_start(
                out=declam[:, slot * 2048:slot * 2048 + 1024],
                in_=lam_i_d[16 + t0:17 + t0, :]), wait=w)
            DMA("scalar", "r2", lambda e, s, t0=t0, slot=slot: e.dma_start(
                out=declam[:, slot * 2048 + 1024:slot * 2048 + 2048],
                in_=lam_i_d[t0:t0 + 1, :]), wait=w)
            retdone[("r2", n)] = ("r2", P.cur("r2"))
            DMA("gpsimd", "r3", lambda e, s, t0=t0, slot=slot: e.dma_start(
                out=a8[:, slot * 2048:(slot + 1) * 2048],
                in_=bass.AP(atm_i.tensor, t0 * 1024,
                            [[128, 8], [1024, 16], [1, 128]])), wait=w)
            if n == 0:
                DMA("gpsimd", "r3", lambda e, s, slot=slot: e.dma_start(
                    out=ktmb16[:, slot * 1024:(slot + 1) * 1024], in_=kprevtm), wait=w)
            else:
                DMA("gpsimd", "r3", lambda e, s, t0=t0, slot=slot: e.dma_start(
                    out=ktmb16[:, slot * 1024:(slot + 1) * 1024],
                    in_=ktm_i[t0 - 16:t0, :]), wait=w)
            retdone[("r3", n)] = ("r3", P.cur("r3"))

        def coef_builds(n):
            slot = n % NRB
            w = [retdone[("r1", n)], ("vm", setup_vm), ("ldc", ldc_done)]
            if dbg and n == 1 and "dbgs" in P.ct:
                w.append(("dbgs", P.cur("dbgs")))
            if ("stepsdone", n - 1) in rel:
                w.append(rel[("stepsdone", n - 1)])
            if ("ogmm", n - 1, 3) in rel:
                w.append(rel[("ogmm", n - 1, 3)])
            for (dst, srcr) in ((acols, aret), (apcols, apret)):
                for r in range(4):
                    A("vector", lambda e, s, dst=dst, srcr=srcr, r=r, slot=slot:
                        e.tensor_tensor(
                            bass.AP(dst, 40 * r, [[512, 128], [128, 4], [1, 8]]),
                            bass.AP(srcr, slot * 16 + r, [[NRB * 16, 128], [4, 4], [0, 8]]),
                            dmask8b[:].unsqueeze(1).broadcast_to([128, 4, 8]),
                            op=ALU.mult), wait=w)
                    w = []
                    P.selfsync("vector")
            for m in range(4):
                A("vector", lambda e, s, m=m, slot=slot: e.tensor_tensor(
                    bass.AP(crosscols, m * 128, [[512, 32], [40, 4], [1, 8]]),
                    bass.AP(aret, (32 * m) * (NRB * 16) + slot * 16 + 4 * m,
                            [[NRB * 16, 32], [1, 4], [0, 8]]),
                    bass.AP(crmaskb, (32 * m) * 128, [[128, 32], [40, 4], [1, 8]]),
                    op=ALU.mult))
                P.selfsync("vector")
            for m in range(4):
                A("vector", lambda e, s, m=m, slot=slot: e.tensor_tensor(
                    bass.AP(aqcols, m * 64 + 32 * (m % 2), [[256, 128], [8, 4], [1, 8]]),
                    bass.AP(aqret, slot * 16 + 4 * m, [[NRB * 16, 128], [1, 4], [0, 8]]),
                    bass.AP(aqmaskb, m * 64 + 32 * (m % 2), [[256, 128], [8, 4], [1, 8]]),
                    op=ALU.mult))
                P.selfsync("vector")
            w2_ = [retdone[("r2", n)], retdone[("r1", n)]]
            A("vector", lambda e, s, slot=slot: e.tensor_sub(
                expof[:], lam_i[:, slot * 128:(slot + 1) * 128],
                lgend[:, slot * 128:(slot + 1) * 128]), wait=w2_)
            P.selfsync("vector")
            A("vector", None, inc=[("vm", 1)])
            A("scalar", lambda e, s: e.activation(out=expof[:], in_=expof[:],
                                                  func=AF.Exp),
              wait=[("vm", P.cur("vm"))])
            P.selfsync("scalar")
            A("vector", lambda e, s, slot=slot: e.tensor_sub(
                expost[:], lam_i[:, slot * 128:(slot + 1) * 128],
                lref2[:, slot * 128:(slot + 1) * 128]))
            P.selfsync("vector")
            A("vector", None, inc=[("vm", 1)])
            A("scalar", lambda e, s: e.activation(out=expost[:], in_=expost[:],
                                                  func=AF.Exp),
              wait=[("vm", P.cur("vm"))])
            P.selfsync("scalar")
            A("scalar", None, inc=[("sm", 1)])
            rel[("coefs", n)] = ("sm", P.cur("sm"))

        def absorb(n):
            slot = n % NRB
            w = [retdone[("r2", n)]]
            if ("absorbdone", n - 1) in rel:
                w.append(rel[("absorbdone", n - 1)])
            A("vector", lambda e, s, slot=slot: e.tensor_sub(
                decrow[:], declam[:, slot * 2048:slot * 2048 + 1024],
                declam[:, slot * 2048 + 1024:slot * 2048 + 2048]), wait=w)
            P.selfsync("vector")
            A("vector", None, inc=[("vm", 1)])
            A("scalar", lambda e, s: e.activation(out=decrow[:], in_=decrow[:],
                                                  func=AF.Exp),
              wait=[("vm", P.cur("vm"))])
            P.selfsync("scalar")
            A("scalar", None, inc=[("sm", 1)])
            dec_done = P.cur("sm")
            A("tensor", lambda e, s: e.matmul(ps_a[:], onesrow[:], decrow[:, 0:512],
                                              start=True, stop=True),
              wait=[("sm", dec_done)] + ([rel[("absorbps", n - 1)]] if ("absorbps", n - 1) in rel else []),
              inc=[("mm", 1)])
            A("tensor", lambda e, s: e.matmul(ps_a2[:], onesrow[:], decrow[:, 512:1024],
                                              start=True, stop=True), inc=[("mm", 1)])
            mm_d = P.cur("mm")
            A("vector", lambda e, s: e.tensor_mul(sttmp[:, 0:512], St[:, 0:512], ps_a[:]),
              wait=[("mm", mm_d)])
            P.selfsync("vector")
            A("vector", lambda e, s: e.tensor_mul(sttmp[:, 512:1024], St[:, 512:1024],
                                                  ps_a2[:]))
            P.selfsync("vector")
            A("vector", None, inc=[("vm", 1)])
            stt_done = P.cur("vm")
            wd = [("vm", stt_done), retdone[("r3", n)], ("rw", P.cur("rw")) if "rw" in P.ct else ("vm", 0)]
            for h in range(8):
                ps = ps_a if h < 4 else ps_a2
                A("tensor", lambda e, s, h=h, ps=ps, slot=slot: e.matmul(
                    ps[:, (h % 4) * 128:(h % 4 + 1) * 128],
                    ktmb16[:, slot * 1024 + h * 128:slot * 1024 + (h + 1) * 128],
                    win16[:, h * 128:(h + 1) * 128],
                    start=True, stop=True),
                    wait=(wd if h == 0 else []),
                    inc=([("mm", 1)] if h == 7 else []))
            mm_d = P.cur("mm")
            A("vector", lambda e, s: e.tensor_add(St[:, 0:512], sttmp[:, 0:512], ps_a[:]),
              wait=[("mm", mm_d)])
            P.selfsync("vector")
            A("vector", lambda e, s: e.tensor_add(St[:, 512:1024], sttmp[:, 512:1024],
                                                  ps_a2[:]))
            P.selfsync("vector")
            A("vector", None, inc=[("vm", 1)])
            rel[("absorbdone", n)] = ("vm", P.cur("vm"))
            rel[("absorbps", n)] = ("vm", P.cur("vm"))

        khcmb = sb("khcmb", [128, 8 * TL], BF16)
        qcmb = sb("qcmb", [128, 8 * TL], BF16)
        DMA("sync", "ldc2", lambda e, s: e.dma_start(
            out=khcmb[:], in_=bass.AP(khcm_i.tensor, 0,
                                      [[TL, 128], [128 * TL, 8], [1, TL]])))
        DMA("sync", "ldc2", lambda e, s: e.dma_start(
            out=qcmb[:], in_=bass.AP(qcm_i.tensor, 0,
                                     [[TL, 128], [128 * TL, 8], [1, TL]])))
        ldc2_done = P.cur("ldc2")
        A("vector", None, wait=[("ldc2", ldc2_done)])

        def bcb(n):
            slot = n % NRB
            par = n % 2
            w = []
            if dbg and n == 1 and "dbgs" in P.ct:
                w.append(("dbgs", P.cur("dbgs")))
            if ("absorbdone", n - 1) in rel:
                w.append(rel[("absorbdone", n - 1)])
            if ("bps", n - 2) in rel:
                w.append(rel[("bps", n - 2)])
            w.append(("ldc2", ldc2_done))
            for h in range(8):
                A("vector", lambda e, s, h=h, par=par, n=n: e.tensor_copy(
                    bass.AP(kpadb, par * 1024 + h * 128 + h, [[2048, 128], [8, 16]]),
                    khcmb[:, h * TL + n * 16:h * TL + (n + 1) * 16]),
                    wait=(w if h == 0 else []))
                P.selfsync("vector")
            A("vector", None, inc=[("vm", 1)])
            kp_done = P.cur("vm")
            wb = [("vm", kp_done)]
            if ("cbdone", n - 1) in rel:
                wb.append(rel[("cbdone", n - 1)])
            for h in range(8):
                A("tensor", lambda e, s, h=h, par=par: e.matmul(
                    ps_b[:], kpadb[:, par * 1024 + h * 128:par * 1024 + (h + 1) * 128],
                    St[:, h * 128:(h + 1) * 128],
                    start=(h == 0), stop=(h == 7)),
                    wait=(wb if h == 0 else []),
                    inc=([("mm", 1)] if h == 7 else []))
            mm_d = P.cur("mm")
            rel[("bps", n)] = ("mm", mm_d)
            A("vector", lambda e, s, slot=slot: e.tensor_sub(
                expb[:], lam_im1[:, slot * 128:(slot + 1) * 128],
                lrefb[:, slot * 128:(slot + 1) * 128]),
                wait=[retdone[("r1", n)], retdone[("r2", n)]])
            P.selfsync("vector")
            A("vector", None, inc=[("vm", 1)])
            A("scalar", lambda e, s: e.activation(out=expb[:], in_=expb[:], func=AF.Exp),
              wait=[("vm", P.cur("vm"))])
            P.selfsync("scalar")
            A("scalar", None, inc=[("sm", 1)])
            wcb = [("sm", P.cur("sm")), ("mm", mm_d)]
            if ("ogmm", n - 2, 3) in rel:
                wcb.append(rel[("ogmm", n - 2, 3)])
            A("vector", lambda e, s: e.tensor_mul(cbt[:], expb[:], ps_b[:]),
              wait=wcb)
            P.selfsync("vector")
            A("vector", lambda e, s, slot=slot, par=par: e.tensor_add(
                cb[:, par * 128:(par + 1) * 128],
                vb[:, slot * 128:(slot + 1) * 128], cbt[:]))
            P.selfsync("vector")
            A("vector", None, inc=[("vm", 1)])
            rel[("cbdone", n)] = ("vm", P.cur("vm"))

        def group_steps(n, m):
            slot = n % NRB
            cur = n % 2
            par = n % 2
            pass
            war = [retdone[("r3", n)], ("ldc", ldc_done)]
            if ("decay", n, 4 * m - 1) in rel:
                war.append(rel[("decay", n, 4 * m - 1)])
            if m == 0 and ("decay", n - 1, 15) in rel:
                war.append(rel[("decay", n - 1, 15)])
            if m == 0 and ("wsc", n - 1, 15) in rel:
                war.append(rel[("wsc", n - 1, 15)])
            A("tensor", lambda e, s, m=m, slot=slot: e.matmul(
                ps_ar[:], sel8b[:],
                a8[:, slot * 2048 + m * 512:slot * 2048 + (m + 1) * 512],
                start=True, stop=True), wait=war, inc=[("mm", 1)])
            rel[("armm", n, m)] = ("mm", P.cur("mm"))
            for r in range(4):
                i = 4 * m + r
                w = [rel[("coefs", n)], rel[("cbdone", n)]]
                if ("flush", n, m - 2) in rel:
                    w.append(rel[("flush", n, m - 2)])
                if m < 2 and ("flush", n - 1, m + 2) in rel:
                    w.append(rel[("flush", n - 1, m + 2)])
                if ("decay", n, i - 1) in rel:
                    w.append(rel[("decay", n, i - 1)])
                if ("memsetw", n) in rel:
                    w.append(rel[("memsetw", n)])
                pw = ps_wA if r % 2 == 0 else ps_wB
                psl = (r // 2) * 128
                A("tensor", lambda e, s, i=i, cur=cur, psl=psl, pw=pw: e.matmul(
                    pw[:, psl:psl + 128], acols[:, i * 32:(i + 1) * 32],
                    W2[:, cur * 128:(cur + 1) * 128],
                    start=True, stop=False), wait=w)
                A("tensor", lambda e, s, i=i, cur=cur, psl=psl, pw=pw: e.matmul(
                    pw[:, psl:psl + 128], apcols[:, i * 32:(i + 1) * 32],
                    W2[:, (1 - cur) * 128:(2 - cur) * 128],
                    start=False, stop=False))
                if r >= 1:
                    A("tensor", lambda e, s, i=i, psl=psl, pw=pw: e.matmul(
                        pw[:, psl:psl + 128], crosscols[:, i * 32:(i + 1) * 32],
                        ws4dec[:], start=False, stop=False),
                        wait=([rel[("wsc", n, i - 1)]]
                              if ("wsc", n, i - 1) in rel else []))
                A("tensor", lambda e, s, i=i, par=par, psl=psl, pw=pw: e.matmul(
                    pw[:, psl:psl + 128], e8selb[:, i * 32:(i + 1) * 32],
                    cb[:, par * 128:(par + 1) * 128],
                    start=False, stop=True), inc=[("mm", 1)])
                mm_d = P.cur("mm")
                wdec = [("mm", mm_d), rel[("armm", n, m)]]
                if i == 0 and ("chunkdone", n - 1) in rel:
                    wdec.append(rel[("chunkdone", n - 1)])
                A("vector", lambda e, s, r=r: e.tensor_tensor(
                    W2[:].rearrange("p (a d) -> p a d", a=2),
                    W2[:].rearrange("p (a d) -> p a d", a=2),
                    bass.AP(ps_ar.tensor if hasattr(ps_ar, "tensor") else ps_ar,
                            r * 128, [[512, 128], [0, 2], [1, 128]]),
                    op=ALU.mult), wait=wdec, inc=[("vm", 1)])
                rel[("decay", n, i)] = ("vm", P.cur("vm"))
                wmu = [rel[("wsc", n, i - 1)]] if ("wsc", n, i - 1) in rel else []
                A("vector", lambda e, s, r=r: e.tensor_mul(
                    ws4dec[:], ws4dec[:], ps_ar[0:32, r * 128:(r + 1) * 128]),
                    wait=wmu, inc=[("vm", 1)])
                mu_v = P.cur("vm")
                A("vector", lambda e, s, r=r, psl=psl, pw=pw: e.select(
                    ws4dec[:], selmb[:, r:r + 1].broadcast_to([32, 128]),
                    pw[:, psl:psl + 128], ws4dec[:], add_drain=True),
                    wait=[("vm", mu_v)], inc=[("vm", 1)])
                rel[("wsc", n, i)] = ("vm", P.cur("vm"))
            A("vector", lambda e, s, m=m, cur=cur: e.tensor_copy(
                W2[32 * m:32 * (m + 1), cur * 128:(cur + 1) * 128],
                ws4dec[:]),
                wait=[rel[("decay", n, 4 * m + 3)], rel[("wsc", n, 4 * m + 3)]])
            P.selfsync("vector")
            A("vector", None, inc=[("vm", 1)])
            rel[("flush", n, m)] = ("vm", P.cur("vm"))
            w = [rel[("flush", n, m)]]
            if ("ogps", n - 1) in rel and m < 2:
                w.append(rel[("ogps", n - 1)])
            A("tensor", lambda e, s, m=m, cur=cur: e.matmul(
                ps_og2[:, (m // 2) * 128:(m // 2 + 1) * 128],
                aqcols[:, m * 64:(m + 1) * 64],
                W2[:, cur * 128:(cur + 1) * 128],
                start=(m % 2 == 0), stop=(m % 2 == 1)),
                wait=w, inc=[("mm", 1)])
            rel[("ogmm", n, m)] = ("mm", P.cur("mm"))

        def o_state_final(n):
            slot = n % NRB
            w = [rel[("absorbdone", n)]] if ("absorbdone", n) in rel else []
            w.append(("ldc2", ldc2_done))
            if ("opsps", n - 1) in rel:
                w.append(rel[("opsps", n - 1)])
            for h in range(8):
                A("vector", lambda e, s, h=h, n=n: e.tensor_copy(
                    bass.AP(qpadb, h * 128 + h, [[1024, 128], [8, 16]]),
                    qcmb[:, h * TL + n * 16:h * TL + (n + 1) * 16]),
                    wait=(w if h == 0 else []))
                P.selfsync("vector")
            A("vector", None, inc=[("vm", 1)])
            qp_done = P.cur("vm")
            for h in range(8):
                A("tensor", lambda e, s, h=h: e.matmul(
                    ps_ops[:], qpadb[:, h * 128:(h + 1) * 128],
                    St[:, h * 128:(h + 1) * 128],
                    start=(h == 0), stop=(h == 7)),
                    wait=([("vm", qp_done)] if h == 0 else []),
                    inc=([("mm", 1)] if h == 7 else []))
            mm_d = P.cur("mm")
            w2_ = [rel[("ogmm", n, 1)], rel[("ogmm", n, 3)], rel[("coefs", n)]]
            if ("oadd", n - 1) in rel:
                w2_.append(rel[("oadd", n - 1)])
            A("scalar", lambda e, s: e.activation(out=otf[0:64, :],
                                                  in_=ps_og2[:, 0:128],
                                                  func=AF.Copy), wait=w2_)
            P.selfsync("scalar")
            A("scalar", lambda e, s: e.activation(out=otf[64:128, :],
                                                  in_=ps_og2[:, 128:256],
                                                  func=AF.Copy))
            P.selfsync("scalar")
            A("scalar", None, inc=[("sm", 1)])
            rel[("ogps", n)] = ("sm", P.cur("sm"))
            A("vector", lambda e, s: e.tensor_mul(otf[:], otf[:], expof[:]),
              wait=[("sm", P.cur("sm"))])
            P.selfsync("vector")
            A("vector", lambda e, s: e.tensor_mul(otf2[:], ps_ops[:], expost[:]),
              wait=[("mm", mm_d)])
            P.selfsync("vector")
            A("vector", None, inc=[("vm", 1)])
            rel[("opsps", n)] = ("vm", P.cur("vm"))
            w3 = []
            if ("odma", n - 1) in rel:
                w3.append(rel[("odma", n - 1)])
            A("vector", lambda e, s: e.tensor_add(otileb[:], otf[:], otf2[:]), wait=w3)
            P.selfsync("vector")
            A("vector", None, inc=[("vm", 1)])
            rel[("oadd", n)] = ("vm", P.cur("vm"))
            DMA("gpsimd", "sto", lambda e, s, n=n: e.dma_start(
                out=o_out[n * 128:(n + 1) * 128, :], in_=otileb[:]),
                wait=[("vm", P.cur("vm"))])
            rel[("odma", n)] = ("sto", P.cur("sto"))

        def chunk_end(n):
            cur = n % 2
            w = [rel[("flush", n, 3)], rel[("ogmm", n, 3)]]
            if dbg and n == 0 and "dbgs" in P.ct:
                w.append(("dbgs", P.cur("dbgs")))
            if ("winret", n) in rel:
                w.append(rel[("winret", n)])
            DMA("gpsimd", "wout", lambda e, s, cur=cur: e.dma_start(
                out=bass.AP(wtmp_sc.tensor, 0, [[1024, 16], [128, 8], [1, 128]]),
                in_=W2[:, cur * 128:(cur + 1) * 128]), wait=w)
            wout_d = P.cur("wout")
            if n + 1 < NCH:
                DMA("gpsimd", "rw", lambda e, s: e.dma_start(
                    out=win16[:], in_=wtmp_sc), wait=[("wout", wout_d)] +
                    ([rel[("absorbdone", n)]] if ("absorbdone", n) in rel else []))
                rel[("winret", n + 1)] = ("rw", P.cur("rw"))
                wms = [rel[("steps15", n)]]
                if dbg and n == 0 and "dbgs" in P.ct:
                    wms.append(("dbgs", P.cur("dbgs")))
                A("vector", lambda e, s, cur=cur: e.memset(
                    W2[:, (1 - cur) * 128:(2 - cur) * 128], 0.0),
                    wait=wms)
                P.selfsync("vector")
                A("vector", None, inc=[("vm", 1)])
                rel[("memsetw", n + 1)] = ("vm", P.cur("vm"))
            rel[("chunkdone", n)] = ("wout", wout_d)

        ret_batch(0)
        if NCH > 1:
            ret_batch(1)
        bcb(0)
        for n in range(NCH):
            if n + 2 < NCH:
                ret_batch(n + 2)
            coef_builds(n)
            group_steps(n, 0)
            absorb(n)
            if n + 1 < NCH:
                bcb(n + 1)
            group_steps(n, 1)
            group_steps(n, 2)
            group_steps(n, 3)
            rel[("steps15", n)] = rel[("decay", n, 15)]
            rel[("stepsdone", n)] = rel[("flush", n, 3)]
            o_state_final(n)
            if dbg and n == 0:
                for nm, tens, wk in [
                        ("acols", acols, rel[("coefs", 0)]),
                        ("aqcols", aqcols, rel[("coefs", 0)]),
                        ("crosscols", crosscols, rel[("coefs", 0)]),
                        ("expwf4", expwf4, rel[("coefs", 0)]),
                        ("expof", expof, rel[("coefs", 0)]),
                        ("expost", expost, rel[("coefs", 0)]),
                        ("cb", cb, rel[("cbdone", 0)]),
                        ("expb", expb, rel[("cbdone", 0)]),
                        ("ws4dec", ws4dec, rel[("wsc", 0, 15)]),
                        ("w2", W2, rel[("flush", 0, 3)])]:
                    DMA("gpsimd", "dbgs", lambda e, s, nm=nm, tens=tens:
                        e.dma_start(out=dbg_outs[nm], in_=tens[:]),
                        wait=[wk, rel[("ogmm", 0, 3)], rel[("opsps", 0)]])
            chunk_end(n)
        DMA("gpsimd", "stf", lambda e, s: e.dma_start(out=st_out, in_=St[:]),
            wait=[rel[("absorbdone", NCH - 1)], rel[("opsps", NCH - 1)]])
        DMA("gpsimd", "stf", lambda e, s: e.dma_start(
            out=bass.AP(w2prev_out.tensor, 0, [[1024, 16], [128, 8], [1, 128]]),
            in_=W2[:, (NCH - 1) % 2 * 128:((NCH - 1) % 2 + 1) * 128]),
            wait=[rel[("chunkdone", NCH - 1)]])
        fw = [("stf", P.cur("stf")), ("sto", P.cur("sto"))]
        if dbg and "dbgs" in P.ct:
            fw.append(("dbgs", P.cur("dbgs")))
        A("gpsimd", None, wait=fw)
        run_prog(nc, P)
    return nc


# version 05
# speedup vs baseline: 1.1301x; 1.0906x over previous
import contextlib
import numpy as np
import ml_dtypes
import concourse.bass as bass
import concourse.mybir as mybir
from concourse.bass_utils import run_bass_kernel_spmd

F32 = mybir.dt.float32
BF16 = mybir.dt.bfloat16
AF = mybir.ActivationFunctionType
ALU = mybir.AluOpType
AX = mybir.AxisListType

NCORES = 8
B, T, HID = 4, 4096, 2048
NH, DQ, DV, CK = 16, 128, 128, 4
TPC = (B * T) // NCORES

BFNP = np.dtype(ml_dtypes.bfloat16)
LN_EPS = 1e-5


def cdiv(a, b):
    return -(-a // b)


class SEng:

    def __init__(self, eng, sem):
        self.e = eng
        self.sem = sem
        self.n = 0
        self.last = None

    def __getattr__(self, name):
        fn = getattr(self.e, name)

        def wrap(*a, **kw):
            inst = fn(*a, **kw)
            self.last = inst
            return inst

        return wrap

    def sync(self):
        if self.last is not None:
            self.n += 1
            self.last.then_inc(self.sem, 1)
            self.e.wait_ge(self.sem, self.n)
            self.last = None


def build_ln(rows, cols):
    nc = bass.Bass("TRN2", target_bir_lowering=False)
    x = nc.dram_tensor("x", [rows, cols], F32, kind="ExternalInput").ap()
    y = nc.dram_tensor("y", [rows, cols], BF16, kind="ExternalOutput").ap()
    nt = rows // 128
    with (
        nc.sbuf_tensor("xt", [128, 2 * cols], F32) as xt,
        nc.sbuf_tensor("ct", [128, 2 * cols], F32) as ct,
        nc.sbuf_tensor("yt", [128, 2 * cols], BF16) as yt,
        nc.sbuf_tensor("sq", [128, cols], F32) as sq,
        nc.sbuf_tensor("mu", [128, 2], F32) as mu,
        nc.sbuf_tensor("mu2", [128, 2], F32) as mu2,
        nc.sbuf_tensor("ssq", [128, 2], F32) as ssq,
        nc.sbuf_tensor("rst", [128, 2], F32) as rst,
        nc.sbuf_tensor("epsc", [128, 1], F32) as epsc,
        nc.semaphore("setup") as setup,
        nc.semaphore("ld") as ld,
        nc.semaphore("vm") as vm,
        nc.semaphore("sm") as sm,
        nc.semaphore("gm") as gm,
        nc.semaphore("vsem") as vsem,
        nc.semaphore("ssem") as ssem,
        nc.semaphore("st") as st,
        nc.Block() as block,
    ):
        @block.sync
        def _(sync):
            for i in range(nt):
                if i > 0:
                    sync.wait_ge(ld, 16 * i)
                if i >= 2:
                    sync.wait_ge(vm, i - 1)
                sync.dma_start(out=xt[:, (i % 2) * cols:(i % 2 + 1) * cols],
                               in_=x[128 * i:128 * (i + 1), :]).then_inc(ld, 16)

        @block.vector
        def _(vector):
            v = SEng(vector, vsem)
            v.memset(epsc[:], LN_EPS).then_inc(setup, 1)
            for i in range(nt):
                b = i % 2
                xv = xt[:, b * cols:(b + 1) * cols]
                cv = ct[:, b * cols:(b + 1) * cols]
                vector.wait_ge(ld, 16 * (i + 1))
                if i >= 2:
                    vector.wait_ge(gm, i - 1)
                v.reduce_sum(mu[:, b:b + 1], xv, axis=AX.X)
                v.sync()
                v.tensor_scalar_mul(mu2[:, b:b + 1], mu[:, b:b + 1], 1.0 / cols)
                v.sync()
                v.tensor_scalar_sub(cv, xv, mu2[:, b:b + 1]).then_inc(vm, 1)

        @block.scalar
        def _(scalar):
            s = SEng(scalar, ssem)
            scalar.wait_ge(setup, 1)
            for i in range(nt):
                b = i % 2
                cv = ct[:, b * cols:(b + 1) * cols]
                scalar.wait_ge(vm, i + 1)
                if i >= 2:
                    scalar.wait_ge(gm, i - 1)
                s.activation(out=sq[:], in_=cv, func=AF.Square,
                             accum_out=ssq[:, b:b + 1])
                s.sync()
                s.activation(out=rst[:, b:b + 1], in_=ssq[:, b:b + 1],
                             func=AF.Ln, scale=1.0 / cols, bias=epsc[:, 0:1])
                s.sync()
                s.activation(out=rst[:, b:b + 1], in_=rst[:, b:b + 1],
                             func=AF.Exp, scale=-0.5).then_inc(sm, 1)

        @block.gpsimd
        def _(gpsimd):
            for i in range(nt):
                b = i % 2
                cv = ct[:, b * cols:(b + 1) * cols]
                yv = yt[:, b * cols:(b + 1) * cols]
                gpsimd.wait_ge(sm, i + 1)
                gpsimd.tensor_scalar_mul(yv, cv, rst[:, b:b + 1]).then_inc(gm, 1)
                gpsimd.wait_ge(gm, i + 1)
                if i > 0:
                    gpsimd.wait_ge(st, 16 * i)
                gpsimd.dma_start(out=y[128 * i:128 * (i + 1), :], in_=yv).then_inc(st, 16)
            gpsimd.wait_ge(st, 16 * nt)

    return nc


_ln_cache = {}


def ln_host(x):
    ntok, C = x.shape
    per = ntok // NCORES
    key = (per, C)
    if key not in _ln_cache:
        _ln_cache[key] = build_ln(per, C)
    nc = _ln_cache[key]
    shards = [np.ascontiguousarray(x[i * per:(i + 1) * per]) for i in range(NCORES)]
    res = run_bass_kernel_spmd(nc, [{"x": s} for s in shards], core_ids=list(range(NCORES)))
    return np.concatenate([r["y"] for r in res.results], axis=0)


def _hadamard(n):
    idx = np.arange(n, dtype=np.uint32)
    par = (np.bitwise_count(np.bitwise_and.outer(idx, idx)) & 1).astype(np.int8)
    return np.where(par == 0, np.float32(1.0), np.float32(-1.0))


def _scan_masks():
    dmask8 = np.zeros((128, 8), np.float32)
    for p in range(128):
        dmask8[p, p % 8] = 1.0
    crmask = np.zeros((128, 128), np.float32)
    for blk in range(4):
        for k_ in range(4):
            for h in range(8):
                for r in range(4):
                    if k_ < r:
                        crmask[blk * 32 + k_ * 8 + h, 40 * r + h] = 1.0
    aqmask = np.zeros((128, 256), np.float32)
    for j in range(16):
        for h in range(8):
            for m in range(4):
                for r in range(4):
                    if j <= 4 * m + r:
                        aqmask[j * 8 + h, m * 64 + 32 * (m % 2) + 8 * r + h] = 1.0
    e8sel = np.zeros((128, 512), np.float32)
    for i in range(16):
        for h in range(8):
            e8sel[i * 8 + h, i * 32 + 8 * (i % 4) + h] = 1.0
    selmask = np.zeros((32, 4), np.float32)
    for r in range(4):
        selmask[8 * r:8 * r + 8, r] = 1.0
    bfe = lambda a: np.ascontiguousarray(a.astype(BFNP))
    return {"dmask8": bfe(dmask8), "sel8": bfe(dmask8.T.copy()),
            "crmask": bfe(crmask), "aqmask": bfe(aqmask),
            "e8sel": bfe(e8sel), "selmask": selmask.astype(np.uint8)}


_dbg = {}


def kernel(**inputs):
    ii = {k: np.asarray(v) for k, v in inputs.items()}
    _dbg.clear()
    CC = np.ascontiguousarray
    bfe = lambda a: CC(np.asarray(a).astype(BFNP))
    R8 = list(range(NCORES))

    x = CC(ii["L0_attn_x"].astype(np.float32, copy=False))
    xf = x.reshape(B * T, HID)
    x_cm = [CC(xf[c * TPC:(c + 1) * TPC].T) for c in range(NCORES)]

    xn1 = ln_host(xf)
    _dbg["xn1"] = xn1
    xn1_cm = [CC(xn1[c * TPC:(c + 1) * TPC].T) for c in range(NCORES)]
    xn1_b = [CC(xn1[b * T:(b + 1) * T].T) for b in range(B)]

    TL = 512
    NL = T // TL
    WqT = CC(ii["L0_attn_Wq"].T)
    WkT = CC(ii["L0_attn_Wk"].T)
    WvT = CC(ii["L0_attn_Wv"].T)
    WaT = CC(ii["L0_attn_Wa"].T)
    WbT = CC(ii["L0_attn_Wb"].T)
    ba = ii["L0_attn_ba"].astype(np.float32, copy=False)
    bb = ii["L0_attn_bb"].astype(np.float32, copy=False)
    qcw = ii["L0_attn_qconv_w"].astype(np.float32, copy=False)
    kcw = ii["L0_attn_kconv_w"].astype(np.float32, copy=False)
    vcw = ii["L0_attn_vconv_w"].astype(np.float32, copy=False)
    qcb = ii["L0_attn_qconv_b"].astype(np.float32, copy=False)
    kcb = ii["L0_attn_kconv_b"].astype(np.float32, copy=False)
    vcb = ii["L0_attn_vconv_b"].astype(np.float32, copy=False)
    eyef = np.eye(128, dtype=np.float32)
    eyeb = bfe(np.eye(128, dtype=np.float32))
    masks = _scan_masks()

    l2f_nc = build_l2f(Prog, run_prog, TL)
    l2s_nc = build_l2s(Prog, run_prog, TL)
    lamq = [np.zeros((128, 8), np.float32) for _ in R8]
    kprev_cm = [np.zeros((1024, 16), BFNP) for _ in R8]
    kprev_tm = [np.zeros((16, 1024), BFNP) for _ in R8]
    lam_carry = [np.zeros((17, 1024), np.float32) for _ in R8]
    st_c = [np.zeros((128, 1024), np.float32) for _ in R8]
    w2p_c = [np.zeros((16, 1024), BFNP) for _ in R8]
    o_full = np.zeros((B, T, 16, 128), BFNP)
    for seg in range(NL):
        s0 = seg * TL
        infs = []
        for c in R8:
            b, g = c // 2, c % 2
            xsl = np.zeros((2048, TL + 3), BFNP)
            lo = max(0, s0 - 3)
            xsl[:, 3 - (s0 - lo):] = xn1_b[b][:, lo:s0 + TL]
            infs.append({
                "xnT": xsl,
                "wqT": CC(WqT[:, g * 1024:(g + 1) * 1024]),
                "wkT": CC(WkT[:, g * 1024:(g + 1) * 1024]),
                "wvT": CC(WvT[:, g * 1024:(g + 1) * 1024]),
                "waT": CC(WaT[:, g * 1024:(g + 1) * 1024]),
                "wbT": CC(WbT[:, g * 8:(g + 1) * 8]),
                "bav": CC(ba[g * 1024:(g + 1) * 1024, None]),
                "bbv": CC(bb[g * 8:(g + 1) * 8, None]),
                "qcw": CC(qcw[g * 1024:(g + 1) * 1024]),
                "kcw": CC(kcw[g * 1024:(g + 1) * 1024]),
                "vcw": CC(vcw[g * 1024:(g + 1) * 1024]),
                "qcb": CC(qcb[g * 1024:(g + 1) * 1024, None]),
                "kcb": CC(kcb[g * 1024:(g + 1) * 1024, None]),
                "vcb": CC(vcb[g * 1024:(g + 1) * 1024, None]),
                "kprevcm": kprev_cm[c], "lamq_in": lamq[c],
                "identf": eyef, "identb2": eyeb})
        rf = run_bass_kernel_spmd(l2f_nc, infs, core_ids=R8).results
        inss = []
        for c in R8:
            of = rf[c]
            lam_full = np.zeros((17 + TL, 1024), np.float32)
            lam_full[0:17] = lam_carry[c]
            lam_full[17:] = of["lamtm_o"]
            d = {"kcm": of["kcm_o"], "khcm": of["khcm_o"], "qcm": of["qcm_o"],
                 "ktm": of["ktm_o"], "vtm": of["vtm_o"], "atm": of["atm_o"],
                 "lam_full": lam_full, "A_in": of["A_o"], "AQ_in": of["AQ_o"],
                 "Ap_in": of["Ap_o"], "kprevtm": kprev_tm[c],
                 "st_in": st_c[c], "w2prev_in": w2p_c[c]}
            d.update(masks)
            inss.append(d)
            lamq[c] = of["lamq_o"]
            kprev_cm[c] = CC(of["kcm_o"][:, -16:])
            kprev_tm[c] = CC(of["ktm_o"][-16:])
            lam_carry[c] = CC(of["lamtm_o"][-17:])
        rs = run_bass_kernel_spmd(l2s_nc, inss, core_ids=R8).results
        for c in R8:
            b, g = c // 2, c % 2
            o_full[b, s0:s0 + TL, 8 * g:8 * (g + 1)] = \
                rs[c]["o_out"].reshape(TL, 8, 128)
            st_c[c] = rs[c]["st_out"]
            w2p_c[c] = rs[c]["w2prev_out"]
    o_tm = o_full.reshape(B * T, 2048)
    _dbg["o_tm"] = o_tm

    go_nc = build_gate_oln(TPC)
    WgT0 = CC(ii["L0_attn_Wg"].T.astype(np.float32, copy=False))
    lnw = CC(ii["L0_attn_ln_w"].astype(np.float32, copy=False)[None, :])
    lnb = CC(ii["L0_attn_ln_b"].astype(np.float32, copy=False)[None, :])
    rg = run_bass_kernel_spmd(go_nc, [
        {"xnT": xn1_cm[c], "wgT": WgT0, "o_tm": CC(o_tm[c * TPC:(c + 1) * TPC]),
         "lnw": lnw, "lnb": lnb} for c in R8], core_ids=R8).results
    og_cm = [CC(rg[c]["og_tm"].T) for c in R8]
    _dbg["og"] = [rg[c]["og_tm"] for c in R8]

    m4_nc = build_mm_res_ln(TPC, 16, False, True)
    WoT0 = CC(ii["L0_attn_Wo"].T.astype(np.float32, copy=False))
    r4 = run_bass_kernel_spmd(m4_nc, [
        {"aT": og_cm[c], "wT": WoT0, "xT": x_cm[c]} for c in R8],
        core_ids=R8).results
    x2_cm = [r4[c]["x1"] for c in R8]
    _dbg["x1"] = x2_cm
    xn2_cm = [r4[c]["xn"] for c in R8]
    _dbg["xn2"] = xn2_cm

    dp1_nc = build_dualproj(TPC, 5632)
    gateT = CC(ii["L0_ffn_gate_w"].T.astype(np.float32, copy=False))
    upT = CC(ii["L0_ffn_up_w"].T.astype(np.float32, copy=False))
    gate_s = ii["L0_ffn_gate_s"].astype(np.float32, copy=False)
    up_s = ii["L0_ffn_up_s"].astype(np.float32, copy=False)
    r5 = run_bass_kernel_spmd(dp1_nc, [
        {"aT": xn2_cm[c], "gwT": gateT, "uwT": upT,
         "gsv": gate_s, "usv": up_s} for c in R8], core_ids=R8).results
    m5_nc = build_mm_res_ln(TPC, 44, True, True)
    downT = CC(ii["L0_ffn_down_w"].T.astype(np.float32, copy=False))
    down_s = ii["L0_ffn_down_s"].astype(np.float32, copy=False)
    r5b = run_bass_kernel_spmd(m5_nc, [
        {"aT": r5[c]["h"], "wT": downT, "xT": x2_cm[c], "svec": down_s}
        for c in R8], core_ids=R8).results
    x3_cm = [r5b[c]["x1"] for c in R8]
    _dbg["hdn"] = [r5[c]["h"] for c in R8]
    _dbg["x2"] = x3_cm
    xn3_cm = [r5b[c]["xn"] for c in R8]
    _dbg["xn3"] = xn3_cm

    l1a_nc = build_l1a(TPC)
    wv1 = CC(ii["L1_attn_Wv"].astype(np.float32, copy=False))
    wg1 = CC(ii["L1_attn_Wg"].astype(np.float32, copy=False))
    wq1 = CC(ii["L1_attn_Wq"].astype(np.float32, copy=False))
    bg1 = ii["L1_attn_bg"].astype(np.float32, copy=False)
    bq1 = ii["L1_attn_bq"].astype(np.float32, copy=False)
    r6 = run_bass_kernel_spmd(l1a_nc, [
        {"aT": xn3_cm[c], "wv": wv1, "wg": wg1, "wq": wq1,
         "bg": bg1, "bq": bq1} for c in R8], core_ids=R8).results
    ef_nc = build_emafix_load(TPC)
    decrow = (0.95 ** (np.arange(TPC, dtype=np.float64) + 1)).astype(np.float32)[None, :]
    zcar = np.zeros((1, 2048), np.float32)
    r7 = run_bass_kernel_spmd(ef_nc, [
        {"memloc": r6[c]["memloc"],
         "carryT": (r6[c - 1]["carryT"] if c % 2 == 1 else zcar),
         "que": r6[c]["que"], "decrow": decrow} for c in R8],
        core_ids=R8).results
    _dbg["mem"] = [r6[c]["memloc"] for c in R8]
    _dbg["que"] = [r6[c]["que"] for c in R8]
    _dbg["load"] = [r7[c]["loadT"] for c in R8]
    m7_nc = build_mm_res_ln(TPC, 16, True, True)
    Wo1 = CC(ii["L1_attn_Wo"].astype(np.float32, copy=False))
    sc05 = np.full(2048, 0.05, np.float32)
    r7b = run_bass_kernel_spmd(m7_nc, [
        {"aT": r7[c]["loadT"], "wT": Wo1, "xT": x3_cm[c], "svec": sc05}
        for c in R8], core_ids=R8).results
    x4_cm = [r7b[c]["x1"] for c in R8]
    _dbg["x3"] = x4_cm
    xn4_cm = [r7b[c]["xn"] for c in R8]
    _dbg["xn4"] = xn4_cm

    hi_nc = build_had_in(TPC)
    h2048 = bfe(_hadamard(2048))
    su_ug = ii["L1_ffn_SU_ug"].astype(np.float32, copy=False)
    r8 = run_bass_kernel_spmd(hi_nc, [
        {"xnT": xn4_cm[c], "su": su_ug, "hT": h2048} for c in R8],
        core_ids=R8).results
    _dbg["xw"] = [r8[c]["xw"] for c in R8]
    dp2_nc = build_dualproj(TPC, 8192, post_had=True, sv_scale=64.0, mul_su=True)
    wgate1 = CC(ii["L1_ffn_W_gate"].T.astype(np.float32, copy=False))
    wup1 = CC(ii["L1_ffn_W_up"].T.astype(np.float32, copy=False))
    wsug = ii["L1_ffn_Wscale_ug"].astype(np.float32, copy=False)
    su_dp = ii["L1_ffn_SU_dp"].astype(np.float32, copy=False)
    r8b = run_bass_kernel_spmd(dp2_nc, [
        {"aT": r8[c]["xw"], "gwT": wgate1, "uwT": wup1,
         "gsv": CC(wsug[8192:]), "usv": CC(wsug[:8192]),
         "sudp": su_dp, "h128": bfe(_hadamard(128))} for c in R8],
        core_ids=R8).results
    z1p = [CC(r8b[c]["h"].reshape(64, 64, 2, TPC).transpose(1, 0, 2, 3)
              .reshape(8192, TPC)) for c in R8]
    fh_nc = build_foldH(8192)
    wdT = ii["L1_ffn_W_down"].T.astype(np.float32, copy=False)
    wdperm = CC(wdT.reshape(64, 64, 2, 2048).transpose(1, 0, 2, 3).reshape(8192, 2048))
    h64i2 = bfe(np.kron(_hadamard(64), np.eye(2, dtype=np.float32)))
    rfh = run_bass_kernel_spmd(fh_nc, [{"wT": wdperm, "hmat": h64i2}],
                               core_ids=[0]).results
    w2T = rfh[0]["w2T"]
    m9_nc = build_mm_res_ln(TPC, 64, True, False, svec_scale=64.0)
    wsdp = ii["L1_ffn_Wscale_dp"].astype(np.float32, copy=False)
    r9 = run_bass_kernel_spmd(m9_nc, [
        {"aT": z1p[c], "wT": w2T, "xT": x4_cm[c], "svec": wsdp}
        for c in R8], core_ids=R8).results

    out = np.concatenate([r9[c]["x1"].T for c in R8], axis=0)
    return CC(out.reshape(B, T, HID))


ENGS = ("sync", "tensor", "vector", "scalar", "gpsimd")


class Prog:

    def __init__(self, nc):
        self.nc = nc
        self.q = {e: [] for e in ENGS}
        self.ct = {}
        self.sem_names = []
        self.watermark = {e: {} for e in ENGS}
        self.selfct = {e: 0 for e in ENGS}

    def sem(self, name):
        if name not in self.ct:
            self.ct[name] = 0
            self.sem_names.append(name)
        return name

    def cur(self, name):
        return self.ct[name]

    def add(self, eng, fn, wait=(), inc=()):
        wait = [(s, v) for (s, v) in wait if v > 0]
        inc = list(inc)
        for s, n in inc:
            self.sem(s)
            self.ct[s] += n
        for s, _ in wait:
            self.sem(s)
        self.q[eng].append((fn, wait, inc))

    def dma(self, eng, sem, fn, wait=(), extra_inc=()):
        self.sem(sem)
        w = list(wait)
        w.append((sem, self.cur(sem)))
        self.add(eng, fn, wait=w, inc=[(sem, 16), *extra_inc])

    def selfsync(self, eng):
        name = "_self_" + eng
        self.sem(name)
        if not self.q[eng]:
            return
        fn, wait, inc = self.q[eng][-1]
        if inc:
            self.add(eng, lambda e, s: e.nop(), wait=[], inc=[(name, 1)])
        else:
            inc.append((name, 1))
            self.ct[name] += 1
        self.add(eng, None, wait=[(name, self.ct[name])])

    def emit(self, block, sems):
        for eng in ENGS:
            ops = self.q[eng]
            if not ops:
                continue

            def make(eng, ops):
                def body(e):
                    wm = self.watermark[eng]
                    for fn, wait, inc in ops:
                        for s, v in wait:
                            if wm.get(s, -1) < v:
                                e.wait_ge(sems[s], v)
                                wm[s] = v
                        inst = fn(e, sems) if fn is not None else None
                        first = True
                        for s, n in inc:
                            if inst is None or not first:
                                inst = e.nop()
                            inst.then_inc(sems[s], n)
                            first = False
                    return

                return body

            getattr(block, eng)(make(eng, ops))


def run_prog(nc, prog):
    import contextlib
    with contextlib.ExitStack() as st:
        sems = {}
        for name in prog.sem_names:
            sems[name] = st.enter_context(prog.nc.semaphore(name))
        block = st.enter_context(prog.nc.Block())
        prog.emit(block, sems)


def build_mm_res_ln(tpc, nk, has_scale, do_ln, label="mmres", svec_scale=1.0):
    nc = bass.Bass("TRN2", target_bir_lowering=False)
    CIN = nk * 128
    aT = nc.dram_tensor("aT", [CIN, tpc], BF16, kind="ExternalInput").ap()
    wT = nc.dram_tensor("wT", [CIN, 2048], F32, kind="ExternalInput").ap()
    xT = nc.dram_tensor("xT", [2048, tpc], F32, kind="ExternalInput").ap()
    if has_scale:
        svec = nc.dram_tensor("svec", [2048], F32, kind="ExternalInput").ap()
    x1o = nc.dram_tensor("x1", [2048, tpc], F32, kind="ExternalOutput").ap()
    if do_ln:
        xno = nc.dram_tensor("xn", [2048, tpc], BF16, kind="ExternalOutput").ap()
    wbf = nc.dram_tensor("wbf", [CIN, 2048], BF16).ap()
    NT = 512
    nch = tpc // NT
    ND = 16
    SLAB = 8
    slabs = []
    r = 0
    while r < nk:
        slabs.append((r, min(SLAB, nk - r)))
        r += min(SLAB, nk - r)
    NS = len(slabs)
    import contextlib
    with contextlib.ExitStack() as _stk:
        def sb(name, shape, dt=F32):
            return _stk.enter_context(nc.sbuf_tensor(name, shape, dt))

        def psb(name, shape, dt=F32):
            return _stk.enter_context(nc.psum_tensor(name, shape, dt))

        wsl = sb("wsl", [128, 2 * SLAB * 2048], BF16)
        asl = sb("asl", [128, 2 * SLAB * NT], BF16)
        xb = sb("xb", [128, 2 * NT])
        x2c = sb("x2c", [128, ND * NT])
        x1c = sb("x1c", [128, ND * NT])
        sq = sb("sq", [128, 2 * NT])
        xnt = sb("xnt", [128, 2 * NT], BF16)
        ctmp = sb("ctmp", [128, 2 * NT])
        onescol = sb("onescol", [128, 1])
        onesrow = sb("onesrow", [1, 128])
        epsc = sb("epsc", [1, 1])
        dsb = sb("dsb", [128, 16])
        murow = sb("murow", [1, NT])
        mu2row = sb("mu2row", [1, NT])
        varrow = sb("varrow", [1, NT])
        var2row = sb("var2row", [1, NT])
        rstrow = sb("rstrow", [1, NT])
        ps0 = psb("ps0", [128, NT])
        ps1 = psb("ps1", [128, NT])
        psmu = psb("psmu", [1, NT])
        psvar = psb("psvar", [1, NT])
        psbmu = psb("psbmu", [128, NT])
        psbrst = psb("psbrst", [128, NT])

        P = Prog(nc)
        P.add("vector", lambda e, s: e.memset(onescol[:], 1.0))
        P.add("vector", lambda e, s: e.memset(onesrow[:], 1.0))
        if has_scale:
            P.dma("sync", "lds", lambda e, s: e.dma_start(
                out=dsb[:], in_=bass.AP(svec.tensor, 0, [[1, 128], [128, 16]]),
                allow_slow_non_contiguous=True))
            P.add("vector", None, wait=[("lds", 16)])
            if svec_scale != 1.0:
                P.add("vector", lambda e, s: e.tensor_scalar_mul(
                    dsb[:], dsb[:], float(svec_scale)))
                P.selfsync("vector")
        else:
            P.add("vector", lambda e, s: e.memset(dsb[:], 1.0))
        P.selfsync("vector")
        P.add("vector", None, inc=[("vm", 1)])
        setup_v = P.cur("vm")
        P.add("scalar", None, wait=[("vm", setup_v)])
        P.add("tensor", None, wait=[("vm", setup_v)])
        P.add("vector", lambda e, s: e.memset(epsc[:], LN_EPS))
        P.selfsync("vector")
        for kk in range(nk):
            P.dma("gpsimd", "ldw", lambda e, s, kk=kk: e.dma_start(
                out=wbf[kk * 128:(kk + 1) * 128, :],
                in_=wT[kk * 128:(kk + 1) * 128, :]))
        wcast_done = P.cur("ldw")

        x1dma = {}
        xb_rel = {}
        ps_rel = {}
        sq_rel = {}
        asl_rel = {}
        wsl_rel = {}
        stats_rel = None
        bc_rel = None
        for tc in range(nch):
            accmm = {}
            for si, (s0, slen) in enumerate(slabs):
                ib = (tc * NS + si) % 2
                w = [("ldw", wcast_done)]
                if (tc * NS + si - 2) in wsl_rel:
                    w.append(wsl_rel[tc * NS + si - 2])
                P.dma("sync", "ldwsl", lambda e, s, s0=s0, slen=slen, ib=ib: e.dma_start(
                    out=wsl[:, ib * SLAB * 2048:ib * SLAB * 2048 + slen * 2048],
                    in_=bass.AP(wbf.tensor, s0 * 128 * 2048,
                                [[2048, 128], [128 * 2048, slen], [1, 2048]])), wait=w)
                ldw_done = P.cur("ldwsl")
                w = []
                if (tc * NS + si - 2) in asl_rel:
                    w.append(asl_rel[tc * NS + si - 2])
                P.dma("sync", "lda", lambda e, s, s0=s0, slen=slen, ib=ib, tc=tc: e.dma_start(
                    out=asl[:, ib * SLAB * NT:ib * SLAB * NT + slen * NT],
                    in_=bass.AP(aT.tensor, s0 * 128 * tpc + tc * NT,
                                [[tpc, 128], [128 * tpc, slen], [1, NT]])), wait=w)
                lda_done = P.cur("lda")
                for d in range(ND):
                    ps = ps0 if d % 2 == 0 else ps1
                    w = [("ldwsl", ldw_done), ("lda", lda_done)]
                    if d % 2 in ps_rel:
                        w.append(ps_rel[d % 2])
                    for k in range(slen):
                        P.add("tensor", lambda e, s, k=k, d=d, ib=ib, ps=ps, slen=slen: e.matmul(
                            ps[:],
                            wsl[:, ib * SLAB * 2048 + k * 2048 + d * 128:
                                ib * SLAB * 2048 + k * 2048 + (d + 1) * 128],
                            asl[:, ib * SLAB * NT + k * NT:ib * SLAB * NT + (k + 1) * NT],
                            start=(k == 0), stop=(k == slen - 1)),
                            wait=(w if k == 0 else []),
                            inc=([("mm", 1)] if k == slen - 1 else []))
                    mm_done = P.cur("mm")
                    if d == ND - 1:
                        wsl_rel[tc * NS + si] = ("mm", mm_done)
                        asl_rel[tc * NS + si] = ("mm", mm_done)
                    if si == 0:
                        P.add("vector", lambda e, s, d=d, ps=ps: e.tensor_copy(
                            x2c[:, d * NT:(d + 1) * NT], ps[:]),
                            wait=[("mm", mm_done)])
                    else:
                        P.add("vector", lambda e, s, d=d, ps=ps: e.tensor_add(
                            x2c[:, d * NT:(d + 1) * NT],
                            x2c[:, d * NT:(d + 1) * NT], ps[:]),
                            wait=[("mm", mm_done)])
                    P.selfsync("vector")
                    P.add("vector", None, inc=[("vm", 1)])
                    ps_rel[d % 2] = ("vm", P.cur("vm"))
            for d in range(ND):
                i = tc * ND + d
                w = [xb_rel[i - 2]] if i - 2 in xb_rel else []
                P.dma("sync", "ldx", lambda e, s, d=d, tc=tc, i=i: e.dma_start(
                    out=xb[:, (i % 2) * NT:(i % 2 + 1) * NT],
                    in_=xT[d * 128:(d + 1) * 128, tc * NT:(tc + 1) * NT]), wait=w)
                ldx_done = P.cur("ldx")
                w = [("ldx", ldx_done)]
                if d in x1dma:
                    w.append(("st1", x1dma[d]))
                P.add("vector", lambda e, s, d=d, i=i: e.scalar_tensor_tensor(
                    x1c[:, d * NT:(d + 1) * NT],
                    x2c[:, d * NT:(d + 1) * NT], dsb[:, d:d + 1],
                    xb[:, (i % 2) * NT:(i % 2 + 1) * NT],
                    op0=ALU.mult, op1=ALU.add), wait=w)
                P.selfsync("vector")
                P.add("vector", None, inc=[("vm", 1)])
                vm_done = P.cur("vm")
                xb_rel[i] = ("vm", vm_done)
                P.dma("gpsimd", "st1", lambda e, s, d=d, tc=tc: e.dma_start(
                    out=x1o[d * 128:(d + 1) * 128, tc * NT:(tc + 1) * NT],
                    in_=x1c[:, d * NT:(d + 1) * NT]), wait=[("vm", vm_done)])
                x1dma[d] = P.cur("st1")
                if do_ln:
                    w = [("vm", vm_done)]
                    if d % 2 in sq_rel:
                        w.append(sq_rel[d % 2])
                    P.add("scalar", lambda e, s, d=d: e.activation(
                        out=sq[:, (d % 2) * NT:(d % 2 + 1) * NT],
                        in_=x1c[:, d * NT:(d + 1) * NT], func=AF.Square), wait=w)
                    P.selfsync("scalar")
                    P.add("scalar", None, inc=[("sm", 1)])
                    sm_done = P.cur("sm")
                    w = [("vm", vm_done)]
                    if d == 0 and stats_rel:
                        w.append(stats_rel)
                    P.add("tensor", lambda e, s, d=d: e.matmul(
                        psmu[:], onescol[:], x1c[:, d * NT:(d + 1) * NT],
                        start=(d == 0), stop=(d == ND - 1)), wait=w)
                    P.add("tensor", lambda e, s, d=d: e.matmul(
                        psvar[:], onescol[:], sq[:, (d % 2) * NT:(d % 2 + 1) * NT],
                        start=(d == 0), stop=(d == ND - 1)),
                        wait=[("sm", sm_done)], inc=[("mm", 1)])
                    sq_rel[d % 2] = ("mm", P.cur("mm"))
            if not do_ln:
                continue
            allmm = P.cur("mm")
            P.add("scalar", lambda e, s: e.activation(
                out=murow[:], in_=psmu[:], func=AF.Copy, scale=1.0 / 2048),
                wait=[("mm", allmm)])
            P.selfsync("scalar")
            P.add("scalar", lambda e, s: e.activation(
                out=mu2row[:], in_=murow[:], func=AF.Square))
            P.add("scalar", lambda e, s: e.activation(
                out=varrow[:], in_=psvar[:], func=AF.Copy, scale=1.0 / 2048))
            P.selfsync("scalar")
            P.add("scalar", None, inc=[("sm", 1)])
            P.add("vector", lambda e, s: e.tensor_sub(var2row[:], varrow[:], mu2row[:]),
                  wait=[("sm", P.cur("sm"))])
            P.selfsync("vector")
            P.add("vector", None, inc=[("vm", 1)])
            P.add("scalar", lambda e, s: e.activation(
                out=rstrow[:], in_=var2row[:], func=AF.Ln, bias=epsc[:]),
                wait=[("vm", P.cur("vm"))])
            P.selfsync("scalar")
            P.add("scalar", lambda e, s: e.activation(
                out=rstrow[:], in_=rstrow[:], func=AF.Exp, scale=-0.5))
            P.selfsync("scalar")
            P.add("scalar", None, inc=[("sm", 1)])
            sfin = P.cur("sm")
            stats_rel = ("sm", sfin)
            w = [("sm", sfin)]
            if bc_rel:
                w.append(bc_rel)
            P.add("tensor", lambda e, s: e.matmul(
                psbmu[:], onesrow[:], murow[:], start=True, stop=True), wait=w)
            P.add("tensor", lambda e, s: e.matmul(
                psbrst[:], onesrow[:], rstrow[:], start=True, stop=True),
                inc=[("mm", 1)])
            bc_done = P.cur("mm")
            for d in range(ND):
                bb = d % 2
                P.add("vector", lambda e, s, d=d, bb=bb: e.tensor_sub(
                    ctmp[:, bb * NT:(bb + 1) * NT],
                    x1c[:, d * NT:(d + 1) * NT], psbmu[:]),
                    wait=[("mm", bc_done)])
                P.selfsync("vector")
                w = []
                if "st2" in P.ct and d >= 2:
                    w.append(("st2", P.cur("st2")))
                P.add("vector", lambda e, s, d=d, bb=bb: e.tensor_mul(
                    xnt[:, bb * NT:(bb + 1) * NT],
                    ctmp[:, bb * NT:(bb + 1) * NT], psbrst[:]), wait=w)
                P.selfsync("vector")
                P.add("vector", None, inc=[("vm", 1)])
                P.dma("gpsimd", "st2", lambda e, s, d=d, tc=tc, bb=bb: e.dma_start(
                    out=xno[d * 128:(d + 1) * 128, tc * NT:(tc + 1) * NT],
                    in_=xnt[:, bb * NT:(bb + 1) * NT]),
                    wait=[("vm", P.cur("vm"))])
            bc_rel = ("vm", P.cur("vm"))
        fw = [("st1", P.cur("st1"))]
        if "st2" in P.ct:
            fw.append(("st2", P.cur("st2")))
        P.add("gpsimd", None, wait=fw)
        run_prog(nc, P)
    return nc


def build_dualproj(tpc, ni, post_had=False, sv_scale=1.0, mul_su=False):
    HSCALE = 1.0 / (64.0 * float(np.sqrt(8192.0))) if post_had else 1.0
    nc = bass.Bass("TRN2", target_bir_lowering=False)
    aT = nc.dram_tensor("aT", [2048, tpc], BF16, kind="ExternalInput").ap()
    gwT = nc.dram_tensor("gwT", [2048, ni], F32, kind="ExternalInput").ap()
    uwT = nc.dram_tensor("uwT", [2048, ni], F32, kind="ExternalInput").ap()
    gsv = nc.dram_tensor("gsv", [ni], F32, kind="ExternalInput").ap()
    usv = nc.dram_tensor("usv", [ni], F32, kind="ExternalInput").ap()
    if post_had:
        h128 = nc.dram_tensor("h128", [128, 128], BF16, kind="ExternalInput").ap()
    if mul_su:
        sudp = nc.dram_tensor("sudp", [ni], F32, kind="ExternalInput").ap()
    ho = nc.dram_tensor("h", [ni, tpc], BF16, kind="ExternalOutput").ap()
    NT = 512
    nch = tpc // NT
    NJ = ni // 128
    NK = 16
    import contextlib
    with contextlib.ExitStack() as _stk:
        def sb(name, shape, dt=F32):
            return _stk.enter_context(nc.sbuf_tensor(name, shape, dt))

        def psb(name, shape, dt=F32):
            return _stk.enter_context(nc.psum_tensor(name, shape, dt))

        ab = sb("ab", [128, NK * tpc], BF16)
        gwb = sb("gwb", [128, 2 * NK * 128], BF16)
        uwb = sb("uwb", [128, 2 * NK * 128], BF16)
        gst = sb("gst", [128, NT], BF16)
        ust = sb("ust", [128, NT], BF16)
        ht = sb("ht", [128, 2 * NT], BF16)
        zt = sb("zt", [128, 2 * NT], BF16)
        gsb = sb("gsb", [128, NJ])
        usb = sb("usb", [128, NJ])
        sudpb = sb("sudpb", [128, NJ])
        hadb = sb("hadb", [128, 128], BF16)
        psg = psb("psg", [128, NT])
        psu = psb("psu", [128, NT])
        psz = psb("psz", [128, NT])

        P = Prog(nc)
        P.dma("sync", "lda", lambda e, s: e.dma_start(
            out=ab[:], in_=bass.AP(aT.tensor, 0, [[tpc, 128], [128 * tpc, NK], [1, tpc]])))
        lda_done = P.cur("lda")
        P.dma("sync", "lds", lambda e, s: e.dma_start(
            out=gsb[:], in_=bass.AP(gsv.tensor, 0, [[1, 128], [128, NJ]]),
            allow_slow_non_contiguous=True))
        P.dma("sync", "lds", lambda e, s: e.dma_start(
            out=usb[:], in_=bass.AP(usv.tensor, 0, [[1, 128], [128, NJ]]),
            allow_slow_non_contiguous=True))
        if post_had:
            P.dma("sync", "lds", lambda e, s: e.dma_start(out=hadb[:], in_=h128))
        if mul_su:
            P.dma("sync", "lds", lambda e, s: e.dma_start(
                out=sudpb[:], in_=bass.AP(sudp.tensor, 0, [[1, 128], [128, NJ]]),
                allow_slow_non_contiguous=True))
        lds_done = P.cur("lds")
        if sv_scale != 1.0 or mul_su:
            P.add("vector", None, wait=[("lds", lds_done)])
            if sv_scale != 1.0:
                P.add("vector", lambda e, s: e.tensor_scalar_mul(
                    gsb[:], gsb[:], float(sv_scale)))
                P.selfsync("vector")
                P.add("vector", lambda e, s: e.tensor_scalar_mul(
                    usb[:], usb[:], float(sv_scale)))
                P.selfsync("vector")
            if mul_su:
                P.add("vector", lambda e, s: e.tensor_mul(usb[:], usb[:], sudpb[:]))
                P.selfsync("vector")
            P.add("vector", None, inc=[("vm", 1)])
            P.add("scalar", None, wait=[("lds", lds_done), ("vm", P.cur("vm"))])
        else:
            P.add("scalar", None, wait=[("lds", lds_done)])
        P.add("tensor", None, wait=[("lda", lda_done), ("lds", lds_done)])

        wrel = {}
        evrel = {}
        htrel = {}
        ztrel = {}
        for j in range(NJ):
            jb = j % 2
            w = [wrel[j - 2]] if j - 2 in wrel else []
            P.dma("gpsimd", "ldw", lambda e, s, j=j, jb=jb: e.dma_start(
                out=gwb[:, jb * NK * 128:(jb + 1) * NK * 128],
                in_=bass.AP(gwT.tensor, j * 128, [[ni, 128], [128 * ni, NK], [1, 128]])),
                wait=w)
            P.dma("gpsimd", "ldw2", lambda e, s, j=j, jb=jb: e.dma_start(
                out=uwb[:, jb * NK * 128:(jb + 1) * NK * 128],
                in_=bass.AP(uwT.tensor, j * 128, [[ni, 128], [128 * ni, NK], [1, 128]])),
                wait=([wrel[j - 2]] if j - 2 in wrel else []))
            ldw_done, ldw2_done = P.cur("ldw"), P.cur("ldw2")
            for tc in range(nch):
                w = [("ldw", ldw_done), ("ldw2", ldw2_done)]
                if (j * nch + tc - 1) in evrel:
                    w.append(evrel[j * nch + tc - 1])
                for k in range(NK):
                    P.add("tensor", lambda e, s, k=k, j=j, jb=jb, tc=tc: e.matmul(
                        psg[:], gwb[:, jb * NK * 128 + k * 128:jb * NK * 128 + (k + 1) * 128],
                        ab[:, k * tpc + tc * NT:k * tpc + (tc + 1) * NT],
                        start=(k == 0), stop=(k == NK - 1)),
                        wait=(w if k == 0 else []))
                for k in range(NK):
                    P.add("tensor", lambda e, s, k=k, j=j, jb=jb, tc=tc: e.matmul(
                        psu[:], uwb[:, jb * NK * 128 + k * 128:jb * NK * 128 + (k + 1) * 128],
                        ab[:, k * tpc + tc * NT:k * tpc + (tc + 1) * NT],
                        start=(k == 0), stop=(k == NK - 1)),
                        inc=([("mm", 1)] if k == NK - 1 else []))
                mm_done = P.cur("mm")
                if tc == nch - 1:
                    wrel[j] = ("mm", mm_done)
                it = j * nch + tc
                w = [("mm", mm_done)]
                if it - 1 in evrel:
                    w.append(evrel[it - 1])
                P.add("scalar", lambda e, s, j=j: e.activation(
                    out=gst[:], in_=psg[:], func=AF.Silu, scale=gsb[:, j:j + 1]),
                    wait=w)
                P.selfsync("scalar")
                P.add("scalar", lambda e, s, j=j: e.activation(
                    out=ust[:], in_=psu[:], func=AF.Copy, scale=usb[:, j:j + 1]))
                P.selfsync("scalar")
                P.add("scalar", None, inc=[("sm", 1)])
                sm_done = P.cur("sm")
                hb = it % 2
                w = [("sm", sm_done)]
                if it - 2 in htrel:
                    w.append(htrel[it - 2])
                P.add("vector", lambda e, s, hb=hb: e.tensor_mul(
                    ht[:, hb * NT:(hb + 1) * NT], gst[:], ust[:]), wait=w)
                P.selfsync("vector")
                P.add("vector", None, inc=[("vm", 1)])
                vm_done = P.cur("vm")
                evrel[it] = ("vm", vm_done)
                if post_had:
                    P.add("tensor", lambda e, s, hb=hb: e.matmul(
                        psz[:], hadb[:], ht[:, hb * NT:(hb + 1) * NT],
                        start=True, stop=True), wait=[("vm", vm_done)],
                        inc=[("mm", 1)])
                    htrel[it] = ("mm", P.cur("mm"))
                    w = [("mm", P.cur("mm"))]
                    if it - 2 in ztrel:
                        w.append(ztrel[it - 2])
                    P.add("scalar", lambda e, s, hb=hb: e.activation(
                        out=zt[:, hb * NT:(hb + 1) * NT], in_=psz[:], func=AF.Copy,
                        scale=HSCALE), wait=w)
                    P.selfsync("scalar")
                    P.add("scalar", None, inc=[("sm", 1)])
                    src_t = zt
                    rdy = ("sm", P.cur("sm"))
                else:
                    src_t = ht
                    rdy = ("vm", vm_done)
                P.dma("gpsimd", "sth", lambda e, s, j=j, tc=tc, hb=hb, src_t=src_t: e.dma_start(
                    out=ho[j * 128:(j + 1) * 128, tc * NT:(tc + 1) * NT],
                    in_=src_t[:, hb * NT:(hb + 1) * NT]), wait=[rdy])
                if post_had:
                    ztrel[it] = ("sth", P.cur("sth"))
                else:
                    htrel[it] = ("sth", P.cur("sth"))
        P.add("gpsimd", None, wait=[("sth", P.cur("sth"))])
        run_prog(nc, P)
    return nc


def build_l1a(tpc, decay=0.95):
    nc = bass.Bass("TRN2", target_bir_lowering=False)
    aT = nc.dram_tensor("aT", [2048, tpc], BF16, kind="ExternalInput").ap()
    wv = nc.dram_tensor("wv", [2048, 2048], F32, kind="ExternalInput").ap()
    wg = nc.dram_tensor("wg", [2048, 2048], F32, kind="ExternalInput").ap()
    wq = nc.dram_tensor("wq", [2048, 2048], F32, kind="ExternalInput").ap()
    bg = nc.dram_tensor("bg", [2048], F32, kind="ExternalInput").ap()
    bq = nc.dram_tensor("bq", [2048], F32, kind="ExternalInput").ap()
    memo = nc.dram_tensor("memloc", [2048, tpc], BF16, kind="ExternalOutput").ap()
    caro = nc.dram_tensor("carryT", [1, 2048], F32, kind="ExternalOutput").ap()
    queo = nc.dram_tensor("que", [2048, tpc], BF16, kind="ExternalOutput").ap()
    NT = 512
    nch = tpc // NT
    NK = 16
    import contextlib
    with contextlib.ExitStack() as _stk:
        def sb(name, shape, dt=F32):
            return _stk.enter_context(nc.sbuf_tensor(name, shape, dt))

        def psb(name, shape, dt=F32):
            return _stk.enter_context(nc.psum_tensor(name, shape, dt))

        ab = sb("ab", [128, NK * tpc], BF16)
        wb = sb("wb", [128, 2 * NK * 128], BF16)
        gateb = sb("gateb", [128, 16 * tpc], BF16)
        storet = sb("storet", [128, tpc])
        memt = sb("memt", [128, tpc])
        membf = sb("membf", [128, tpc], BF16)
        quet = sb("quet", [128, 2 * NT], BF16)
        bgb = sb("bgb", [128, 16])
        bqb = sb("bqb", [128, 16])
        decc = sb("decc", [128, 1])
        ps0 = psb("ps0", [128, NT])
        ps1 = psb("ps1", [128, NT])

        P = Prog(nc)
        P.dma("sync", "lda", lambda e, s: e.dma_start(
            out=ab[:], in_=bass.AP(aT.tensor, 0, [[tpc, 128], [128 * tpc, NK], [1, tpc]])))
        lda_done = P.cur("lda")
        P.dma("sync", "ldb", lambda e, s: e.dma_start(
            out=bgb[:], in_=bass.AP(bg.tensor, 0, [[1, 128], [128, 16]]),
            allow_slow_non_contiguous=True))
        P.dma("sync", "ldb", lambda e, s: e.dma_start(
            out=bqb[:], in_=bass.AP(bq.tensor, 0, [[1, 128], [128, 16]]),
            allow_slow_non_contiguous=True))
        ldb_done = P.cur("ldb")
        P.add("vector", lambda e, s: e.memset(decc[:], decay))
        P.selfsync("vector")
        P.add("vector", None, inc=[("vm", 1)])
        P.add("scalar", None, wait=[("ldb", ldb_done)])
        P.add("tensor", None, wait=[("lda", lda_done)])

        def proj_phase(wten, wsem, phase, base):
            for m in range(16):
                mb = m % 2
                w = []
                if ("wrel", base + m - 2) in relmap:
                    w.append(relmap[("wrel", base + m - 2)])
                P.dma("gpsimd", wsem, lambda e, s, m=m, mb=mb, wten=wten: e.dma_start(
                    out=wb[:, mb * NK * 128:(mb + 1) * NK * 128],
                    in_=bass.AP(wten.tensor, m * 128,
                                [[2048, 128], [128 * 2048, NK], [1, 128]])), wait=w)
                ldw_done = P.cur(wsem)
                for tc in range(nch):
                    it = base * nch + m * nch + tc
                    ps = ps0 if it % 2 == 0 else ps1
                    w = [(wsem, ldw_done)]
                    if ("psrel", it - 2) in relmap:
                        w.append(relmap[("psrel", it - 2)])
                    for k in range(NK):
                        P.add("tensor", lambda e, s, k=k, mb=mb, tc=tc, ps=ps: e.matmul(
                            ps[:], wb[:, mb * NK * 128 + k * 128:mb * NK * 128 + (k + 1) * 128],
                            ab[:, k * tpc + tc * NT:k * tpc + (tc + 1) * NT],
                            start=(k == 0), stop=(k == NK - 1)),
                            wait=(w if k == 0 else []),
                            inc=([("mm", 1)] if k == NK - 1 else []))
                    mm_done = P.cur("mm")
                    if tc == nch - 1:
                        relmap[("wrel", base + m)] = ("mm", mm_done)
                    yield m, tc, ps, mm_done

        relmap = {}
        for m, tc, ps, mm_done in proj_phase(wg, "ldwg", "g", 0):
            P.add("scalar", lambda e, s, m=m, tc=tc, ps=ps: e.activation(
                out=gateb[:, m * tpc + tc * NT:m * tpc + (tc + 1) * NT],
                in_=ps[:], func=AF.Sigmoid, bias=bgb[:, m:m + 1]),
                wait=[("mm", mm_done)])
            P.selfsync("scalar")
            P.add("scalar", None, inc=[("sm", 1)])
            relmap[("psrel", 0 * nch + m * nch + tc)] = ("sm", P.cur("sm"))
        for m, tc, ps, mm_done in proj_phase(wv, "ldwv", "v", 16):
            P.add("vector", lambda e, s, m=m, tc=tc, ps=ps: e.tensor_mul(
                storet[:, tc * NT:(tc + 1) * NT], ps[:],
                gateb[:, m * tpc + tc * NT:m * tpc + (tc + 1) * NT]),
                wait=[("mm", mm_done)] + ([relmap[("emad", m - 1)]] if tc == 0 and (("emad", m - 1) in relmap) else []))
            P.selfsync("vector")
            P.add("vector", None, inc=[("vm", 1)])
            relmap[("psrel", 16 * nch + m * nch + tc)] = ("vm", P.cur("vm"))
            if tc == nch - 1:
                w = []
                if "stc" in P.ct:
                    w.append(("stc", P.cur("stc")))
                P.add("vector", lambda e, s: e.tensor_tensor_scan(
                    memt[:], decc[:].broadcast_to([128, tpc]), storet[:], 0.0,
                    op0=ALU.mult, op1=ALU.add), wait=w)
                P.selfsync("vector")
                w = []
                if "stm" in P.ct:
                    w.append(("stm", P.cur("stm")))
                P.add("vector", lambda e, s: e.tensor_copy(membf[:], memt[:]), wait=w)
                P.selfsync("vector")
                P.add("vector", None, inc=[("vm", 1)])
                vm_done = P.cur("vm")
                relmap[("emad", m)] = ("vm", vm_done)
                P.dma("gpsimd", "stm", lambda e, s, m=m: e.dma_start(
                    out=memo[m * 128:(m + 1) * 128, :], in_=membf[:]), wait=[("vm", vm_done)])
                P.dma("gpsimd", "stc", lambda e, s, m=m: e.dma_start(
                    out=bass.AP(caro.tensor, m * 128, [[2048, 1], [1, 128]]),
                    in_=memt[:, tpc - 1:tpc]), wait=[("vm", vm_done)])
        for m, tc, ps, mm_done in proj_phase(wq, "ldwq", "q", 32):
            qb = (m * nch + tc) % 2
            w = [("mm", mm_done)]
            if ("qrel", m * nch + tc - 2) in relmap:
                w.append(relmap[("qrel", m * nch + tc - 2)])
            P.add("scalar", lambda e, s, qb=qb, m=m, ps=ps: e.activation(
                out=quet[:, qb * NT:(qb + 1) * NT], in_=ps[:], func=AF.Sigmoid,
                bias=bqb[:, m:m + 1]), wait=w)
            P.selfsync("scalar")
            P.add("scalar", None, inc=[("sm", 1)])
            relmap[("psrel", 32 * nch + m * nch + tc)] = ("sm", P.cur("sm"))
            P.dma("gpsimd", "stq", lambda e, s, m=m, tc=tc, qb=qb: e.dma_start(
                out=queo[m * 128:(m + 1) * 128, tc * NT:(tc + 1) * NT],
                in_=quet[:, qb * NT:(qb + 1) * NT]), wait=[("sm", P.cur("sm"))])
            relmap[("qrel", m * nch + tc)] = ("stq", P.cur("stq"))
        fw = [("stm", P.cur("stm")), ("stc", P.cur("stc")), ("stq", P.cur("stq"))]
        P.add("gpsimd", None, wait=fw)
        run_prog(nc, P)
    return nc


def build_emafix_load(tpc):
    nc = bass.Bass("TRN2", target_bir_lowering=False)
    memi = nc.dram_tensor("memloc", [2048, tpc], BF16, kind="ExternalInput").ap()
    cari = nc.dram_tensor("carryT", [1, 2048], F32, kind="ExternalInput").ap()
    quei = nc.dram_tensor("que", [2048, tpc], BF16, kind="ExternalInput").ap()
    deci = nc.dram_tensor("decrow", [1, tpc], F32, kind="ExternalInput").ap()
    lo = nc.dram_tensor("loadT", [2048, tpc], BF16, kind="ExternalOutput").ap()
    NT = 512
    nch = tpc // NT
    import contextlib
    with contextlib.ExitStack() as _stk:
        def sb(name, shape, dt=F32):
            return _stk.enter_context(nc.sbuf_tensor(name, shape, dt))

        memb = sb("memb", [128, 2 * NT], BF16)
        queb = sb("queb", [128, 2 * NT], BF16)
        carb = sb("carb", [1, 2048])
        decb = sb("decb", [1, tpc])
        mfix = sb("mfix", [128, 2 * NT])
        lt = sb("lt", [128, 2 * NT], BF16)
        psc = _stk.enter_context(nc.psum_tensor("psc", [128, NT], F32))
        psc2 = _stk.enter_context(nc.psum_tensor("psc2", [128, NT], F32))

        P = Prog(nc)
        P.dma("sync", "lds", lambda e, s: e.dma_start(out=carb[:], in_=cari))
        P.dma("sync", "lds", lambda e, s: e.dma_start(out=decb[:], in_=deci))
        lds_done = P.cur("lds")
        P.add("tensor", None, wait=[("lds", lds_done)])
        rel = {}
        for m in range(16):
            for tc in range(nch):
                it = m * nch + tc
                bb = it % 2
                w = [rel[it - 2]] if it - 2 in rel else []
                P.dma("sync", "ldm", lambda e, s, m=m, tc=tc, bb=bb: e.dma_start(
                    out=memb[:, bb * NT:(bb + 1) * NT],
                    in_=memi[m * 128:(m + 1) * 128, tc * NT:(tc + 1) * NT]), wait=w)
                P.dma("gpsimd", "ldq", lambda e, s, m=m, tc=tc, bb=bb: e.dma_start(
                    out=queb[:, bb * NT:(bb + 1) * NT],
                    in_=quei[m * 128:(m + 1) * 128, tc * NT:(tc + 1) * NT]), wait=w)
                ldm_done, ldq_done = P.cur("ldm"), P.cur("ldq")
                ps = psc if it % 2 == 0 else psc2
                w2 = [("lds", lds_done)]
                if ("ps", it - 2) in rel:
                    w2.append(rel[("ps", it - 2)])
                P.add("tensor", lambda e, s, m=m, tc=tc, ps=ps: e.matmul(
                    ps[:], carb[:, m * 128:(m + 1) * 128],
                    decb[:, tc * NT:(tc + 1) * NT], start=True, stop=True),
                    wait=w2, inc=[("mm", 1)])
                mm_done = P.cur("mm")
                P.add("vector", lambda e, s, bb=bb, ps=ps: e.tensor_add(
                    mfix[:, bb * NT:(bb + 1) * NT], ps[:],
                    memb[:, bb * NT:(bb + 1) * NT]),
                    wait=[("mm", mm_done), ("ldm", ldm_done)])
                P.selfsync("vector")
                P.add("vector", None, inc=[("vm", 1)])
                rel[("ps", it)] = ("vm", P.cur("vm"))
                w3 = [("ldq", ldq_done)]
                if ("lt", it - 2) in rel:
                    w3.append(rel[("lt", it - 2)])
                P.add("vector", lambda e, s, bb=bb: e.tensor_mul(
                    lt[:, bb * NT:(bb + 1) * NT], mfix[:, bb * NT:(bb + 1) * NT],
                    queb[:, bb * NT:(bb + 1) * NT]), wait=w3)
                P.selfsync("vector")
                P.add("vector", None, inc=[("vm", 1)])
                vm_done = P.cur("vm")
                rel[it] = ("vm", vm_done)
                P.dma("gpsimd", "st1", lambda e, s, m=m, tc=tc, bb=bb: e.dma_start(
                    out=lo[m * 128:(m + 1) * 128, tc * NT:(tc + 1) * NT],
                    in_=lt[:, bb * NT:(bb + 1) * NT]), wait=[("vm", vm_done)])
                rel[("lt", it)] = ("st1", P.cur("st1"))
        P.add("gpsimd", None, wait=[("st1", P.cur("st1"))])
        run_prog(nc, P)
    return nc


def build_foldH(ni):
    nc = bass.Bass("TRN2", target_bir_lowering=False)
    wT = nc.dram_tensor("wT", [ni, 2048], F32, kind="ExternalInput").ap()
    hmat = nc.dram_tensor("hmat", [128, 128], BF16, kind="ExternalInput").ap()
    w2 = nc.dram_tensor("w2T", [ni, 2048], F32, kind="ExternalOutput").ap()
    NJ = ni // 128
    import contextlib
    with contextlib.ExitStack() as _stk:
        def sb(name, shape, dt=F32):
            return _stk.enter_context(nc.sbuf_tensor(name, shape, dt))

        wb = sb("wb", [128, 2 * 2048], BF16)
        hb = sb("hb", [128, 128], BF16)
        ob = sb("ob", [128, 2 * 2048])
        ps0 = _stk.enter_context(nc.psum_tensor("ps0", [128, 512], F32))
        ps1 = _stk.enter_context(nc.psum_tensor("ps1", [128, 512], F32))
        P = Prog(nc)
        P.dma("sync", "ldh", lambda e, s: e.dma_start(out=hb[:], in_=hmat))
        P.add("tensor", None, wait=[("ldh", 16)])
        rel = {}
        for j in range(NJ):
            bb = j % 2
            w = [rel[j - 2]] if j - 2 in rel else []
            P.dma("gpsimd", "ldw", lambda e, s, j=j, bb=bb: e.dma_start(
                out=wb[:, bb * 2048:(bb + 1) * 2048],
                in_=wT[j * 128:(j + 1) * 128, :]), wait=w)
            ldw_done = P.cur("ldw")
            for c in range(4):
                ps = ps0 if c % 2 == 0 else ps1
                w2w = [("ldw", ldw_done)]
                if ("ps", j, c - 2) in rel:
                    w2w.append(rel[("ps", j, c - 2)])
                if c < 2 and ("ps", j - 1, c + 2) in rel:
                    w2w.append(rel[("ps", j - 1, c + 2)])
                P.add("tensor", lambda e, s, c=c, bb=bb, ps=ps: e.matmul(
                    ps[:], hb[:], wb[:, bb * 2048 + c * 512:bb * 2048 + (c + 1) * 512],
                    start=True, stop=True), wait=w2w, inc=[("mm", 1)])
                mm_done = P.cur("mm")
                wst = []
                if ("ob", j - 2) in rel:
                    wst.append(rel[("ob", j - 2)])
                P.add("scalar", lambda e, s, c=c, bb=bb, ps=ps: e.activation(
                    out=ob[:, bb * 2048 + c * 512:bb * 2048 + (c + 1) * 512],
                    in_=ps[:], func=AF.Copy), wait=[("mm", mm_done)] + wst)
                P.selfsync("scalar")
                P.add("scalar", None, inc=[("sm", 1)])
                rel[("ps", j, c)] = ("sm", P.cur("sm"))
            rel[j] = ("sm", P.cur("sm"))
            P.dma("gpsimd", "st1", lambda e, s, j=j, bb=bb: e.dma_start(
                out=w2[j * 128:(j + 1) * 128, :], in_=ob[:, bb * 2048:(bb + 1) * 2048]),
                wait=[("sm", P.cur("sm"))])
            rel[("ob", j)] = ("st1", P.cur("st1"))
        P.add("gpsimd", None, wait=[("st1", P.cur("st1"))])
        run_prog(nc, P)
    return nc


def build_had_in(tpc):
    SC = 1.0 / (64.0 * float(np.sqrt(2048.0)))
    nc = bass.Bass("TRN2", target_bir_lowering=False)
    xnT = nc.dram_tensor("xnT", [2048, tpc], BF16, kind="ExternalInput").ap()
    su = nc.dram_tensor("su", [2048], F32, kind="ExternalInput").ap()
    hT = nc.dram_tensor("hT", [2048, 2048], BF16, kind="ExternalInput").ap()
    xwo = nc.dram_tensor("xw", [2048, tpc], BF16, kind="ExternalOutput").ap()
    NT = 512
    nch = tpc // NT
    NK = 16
    import contextlib
    with contextlib.ExitStack() as _stk:
        def sb(name, shape, dt=F32):
            return _stk.enter_context(nc.sbuf_tensor(name, shape, dt))

        xb = sb("xb", [128, NK * tpc], BF16)
        hb = sb("hb", [128, NK * 128 * 2], BF16)
        sub = sb("sub", [128, 16])
        ot = sb("ot", [128, 2 * NT], BF16)
        ps0 = _stk.enter_context(nc.psum_tensor("ps0", [128, NT], F32))
        ps1 = _stk.enter_context(nc.psum_tensor("ps1", [128, NT], F32))
        P = Prog(nc)
        P.dma("sync", "lda", lambda e, s: e.dma_start(
            out=xb[:], in_=bass.AP(xnT.tensor, 0, [[tpc, 128], [128 * tpc, NK], [1, tpc]])))
        P.dma("sync", "lds", lambda e, s: e.dma_start(
            out=sub[:], in_=bass.AP(su.tensor, 0, [[1, 128], [128, 16]]),
            allow_slow_non_contiguous=True))
        P.add("vector", None, wait=[("lda", P.cur("lda")), ("lds", P.cur("lds"))])
        for k in range(NK):
            P.add("vector", lambda e, s, k=k: e.tensor_scalar_mul(
                xb[:, k * tpc:(k + 1) * tpc], xb[:, k * tpc:(k + 1) * tpc],
                sub[:, k:k + 1]))
            P.selfsync("vector")
        P.add("vector", None, inc=[("vm", 1)])
        sc_done = P.cur("vm")
        P.add("tensor", None, wait=[("vm", sc_done)])
        rel = {}
        for d in range(16):
            bb = d % 2
            w = [rel[("h", d - 2)]] if ("h", d - 2) in rel else []
            P.dma("gpsimd", "ldh", lambda e, s, d=d, bb=bb: e.dma_start(
                out=hb[:, bb * NK * 128:(bb + 1) * NK * 128],
                in_=bass.AP(hT.tensor, d * 128, [[2048, 128], [128 * 2048, NK], [1, 128]])),
                wait=w)
            ldh_done = P.cur("ldh")
            for tc in range(nch):
                it = d * nch + tc
                ps = ps0 if it % 2 == 0 else ps1
                w2 = [("ldh", ldh_done)]
                if ("ps", it - 2) in rel:
                    w2.append(rel[("ps", it - 2)])
                for k in range(NK):
                    P.add("tensor", lambda e, s, k=k, d=d, bb=bb, tc=tc, ps=ps: e.matmul(
                        ps[:], hb[:, bb * NK * 128 + k * 128:bb * NK * 128 + (k + 1) * 128],
                        xb[:, k * tpc + tc * NT:k * tpc + (tc + 1) * NT],
                        start=(k == 0), stop=(k == NK - 1)),
                        wait=(w2 if k == 0 else []),
                        inc=([("mm", 1)] if k == NK - 1 else []))
                mm_done = P.cur("mm")
                if tc == nch - 1:
                    rel[("h", d)] = ("mm", mm_done)
                ob = it % 2
                w3 = [("mm", mm_done)]
                if ("ot", it - 2) in rel:
                    w3.append(rel[("ot", it - 2)])
                P.add("scalar", lambda e, s, ob=ob, ps=ps: e.activation(
                    out=ot[:, ob * NT:(ob + 1) * NT], in_=ps[:], func=AF.Copy,
                    scale=SC), wait=w3)
                P.selfsync("scalar")
                P.add("scalar", None, inc=[("sm", 1)])
                rel[("ps", it)] = ("sm", P.cur("sm"))
                P.dma("gpsimd", "st1", lambda e, s, d=d, tc=tc, ob=ob: e.dma_start(
                    out=xwo[d * 128:(d + 1) * 128, tc * NT:(tc + 1) * NT],
                    in_=ot[:, ob * NT:(ob + 1) * NT]), wait=[("sm", P.cur("sm"))])
                rel[("ot", it)] = ("st1", P.cur("st1"))
        P.add("gpsimd", None, wait=[("st1", P.cur("st1"))])
        run_prog(nc, P)
    return nc


def build_gate_oln(tpc):
    nc = bass.Bass("TRN2", target_bir_lowering=False)
    xnT = nc.dram_tensor("xnT", [2048, tpc], BF16, kind="ExternalInput").ap()
    wgT = nc.dram_tensor("wgT", [2048, 2048], F32, kind="ExternalInput").ap()
    o_tm = nc.dram_tensor("o_tm", [tpc, 2048], BF16, kind="ExternalInput").ap()
    lnw = nc.dram_tensor("lnw", [1, 128], F32, kind="ExternalInput").ap()
    lnb = nc.dram_tensor("lnb", [1, 128], F32, kind="ExternalInput").ap()
    ogo = nc.dram_tensor("og_tm", [tpc, 2048], BF16, kind="ExternalOutput").ap()
    NK = 16
    ntt = tpc // 128
    import contextlib
    with contextlib.ExitStack() as _stk:
        def sb(name, shape, dt=F32):
            return _stk.enter_context(nc.sbuf_tensor(name, shape, dt))

        xb = sb("xb", [128, NK * tpc], BF16)
        wgb = sb("wgb", [128, NK * 2048], BF16)
        ob = sb("ob", [128, 2 * 2048], BF16)
        gt = sb("gt", [128, 2 * 2048], BF16)
        sqo = sb("sqo", [128, 2048])
        t1 = sb("t1", [128, 2048])
        ogt = sb("ogt", [128, 2 * 2048], BF16)
        mus = sb("mus", [128, 16])
        msq = sb("msq", [128, 16])
        mu2 = sb("mu2", [128, 16])
        varr = sb("varr", [128, 16])
        rstd = sb("rstd", [128, 16])
        epsc = sb("epsc", [128, 1])
        onesrow = sb("onesrow", [1, 128])
        lnwr = sb("lnwr", [1, 128])
        lnbr = sb("lnbr", [1, 128])
        lnwB = sb("lnwB", [128, 128])
        lnbB = sb("lnbB", [128, 128])
        ps0 = _stk.enter_context(nc.psum_tensor("ps0", [128, 512], F32))
        ps1 = _stk.enter_context(nc.psum_tensor("ps1", [128, 512], F32))
        psb = _stk.enter_context(nc.psum_tensor("psb", [128, 128], F32))

        P = Prog(nc)
        P.dma("sync", "lda", lambda e, s: e.dma_start(
            out=xb[:], in_=bass.AP(xnT.tensor, 0, [[tpc, 128], [128 * tpc, NK], [1, tpc]])))
        P.dma("gpsimd", "ldw", lambda e, s: e.dma_start(
            out=wgb[:], in_=bass.AP(wgT.tensor, 0, [[2048, 128], [128 * 2048, NK], [1, 2048]])))
        P.dma("sync", "lds", lambda e, s: e.dma_start(out=lnwr[:], in_=lnw))
        P.dma("sync", "lds", lambda e, s: e.dma_start(out=lnbr[:], in_=lnb))
        P.add("vector", lambda e, s: e.memset(onesrow[:], 1.0))
        P.add("vector", lambda e, s: e.memset(epsc[:], LN_EPS))
        P.selfsync("vector")
        P.add("vector", None, inc=[("vm", 1)])
        P.add("tensor", lambda e, s: e.matmul(psb[:], onesrow[:], lnwr[:],
                                              start=True, stop=True),
              wait=[("lds", P.cur("lds")), ("vm", P.cur("vm"))], inc=[("mm", 1)])
        P.add("scalar", lambda e, s: e.activation(out=lnwB[:], in_=psb[:], func=AF.Copy),
              wait=[("mm", P.cur("mm"))])
        P.selfsync("scalar")
        P.add("scalar", None, inc=[("sm", 1)])
        P.add("tensor", lambda e, s: e.matmul(psb[:], onesrow[:], lnbr[:],
                                              start=True, stop=True),
              wait=[("sm", P.cur("sm"))], inc=[("mm", 1)])
        P.add("scalar", lambda e, s: e.activation(out=lnbB[:], in_=psb[:], func=AF.Copy),
              wait=[("mm", P.cur("mm"))])
        P.selfsync("scalar")
        P.add("scalar", None, inc=[("sm", 1)])
        setup_sm = P.cur("sm")
        lda_done = P.cur("lda")
        ldw_done = P.cur("ldw")
        P.add("tensor", None, wait=[("lda", lda_done), ("ldw", ldw_done)])
        P.add("vector", None, wait=[("sm", setup_sm)])

        rel = {}
        for tt in range(ntt):
            bb = tt % 2
            w = [rel[("ob", tt - 2)]] if ("ob", tt - 2) in rel else []
            P.dma("sync", "ldo", lambda e, s, tt=tt, bb=bb: e.dma_start(
                out=ob[:, bb * 2048:(bb + 1) * 2048],
                in_=o_tm[tt * 128:(tt + 1) * 128, :]), wait=w)
            ldo_done = P.cur("ldo")
            for co in range(4):
                it = tt * 4 + co
                ps = ps0 if it % 2 == 0 else ps1
                w2 = []
                if ("ps", it - 2) in rel:
                    w2.append(rel[("ps", it - 2)])
                for k in range(NK):
                    P.add("tensor", lambda e, s, k=k, tt=tt, co=co, ps=ps: e.matmul(
                        ps[:], xb[:, k * tpc + tt * 128:k * tpc + (tt + 1) * 128],
                        wgb[:, k * 2048 + co * 512:k * 2048 + (co + 1) * 512],
                        start=(k == 0), stop=(k == NK - 1)),
                        wait=(w2 if k == 0 else []),
                        inc=([("mm", 1)] if k == NK - 1 else []))
                mm_done = P.cur("mm")
                w3 = [("mm", mm_done)]
                if ("gt", tt - 2) in rel and co == 0:
                    w3.append(rel[("gt", tt - 2)])
                P.add("scalar", lambda e, s, bb=bb, co=co, ps=ps: e.activation(
                    out=gt[:, bb * 2048 + co * 512:bb * 2048 + (co + 1) * 512],
                    in_=ps[:], func=AF.Sigmoid), wait=w3)
                P.selfsync("scalar")
                P.add("scalar", None, inc=[("sm", 1)])
                rel[("ps", it)] = ("sm", P.cur("sm"))
            g_done = ("sm", P.cur("sm"))
            ov = ob[:, bb * 2048:(bb + 1) * 2048]
            P.add("vector", None, wait=[("ldo", ldo_done)])
            for h in range(16):
                P.add("vector", lambda e, s, h=h, ov=ov: e.reduce_sum(
                    mus[:, h:h + 1], ov[:, h * 128:(h + 1) * 128], axis=AX.X))
            P.selfsync("vector")
            P.add("vector", None, inc=[("vm", 1)])
            P.add("scalar", lambda e, s, ov=ov: e.activation(
                out=sqo[:], in_=ov, func=AF.Square), wait=[("ldo", ldo_done)])
            P.selfsync("scalar")
            P.add("scalar", None, inc=[("sm", 1)])
            sq_done = P.cur("sm")
            P.add("vector", None, wait=[("sm", sq_done)])
            for h in range(16):
                P.add("vector", lambda e, s, h=h: e.reduce_sum(
                    msq[:, h:h + 1], sqo[:, h * 128:(h + 1) * 128], axis=AX.X))
            P.selfsync("vector")
            P.add("vector", lambda e, s: e.tensor_scalar_mul(mus[:], mus[:], 1.0 / 128))
            P.selfsync("vector")
            P.add("vector", lambda e, s: e.tensor_mul(mu2[:], mus[:], mus[:]))
            P.selfsync("vector")
            P.add("vector", lambda e, s: e.tensor_scalar(
                varr[:], msq[:], 1.0 / 128, None, op0=ALU.mult, op1=ALU.bypass))
            P.selfsync("vector")
            P.add("vector", lambda e, s: e.tensor_sub(varr[:], varr[:], mu2[:]))
            P.selfsync("vector")
            P.add("vector", None, inc=[("vm", 1)])
            st_done = P.cur("vm")
            P.add("scalar", lambda e, s: e.activation(
                out=rstd[:], in_=varr[:], func=AF.Ln, bias=epsc[:]),
                wait=[("vm", st_done)])
            P.selfsync("scalar")
            P.add("scalar", lambda e, s: e.activation(
                out=rstd[:], in_=rstd[:], func=AF.Exp, scale=-0.5))
            P.selfsync("scalar")
            P.add("scalar", None, inc=[("sm", 1)])
            rst_done = P.cur("sm")
            ov3 = ov.rearrange("p (h d) -> p h d", d=128)
            mub = mus[:].unsqueeze(2).broadcast_to([128, 16, 128])
            rsb = rstd[:].unsqueeze(2).broadcast_to([128, 16, 128])
            lnwb3 = lnwB[:].unsqueeze(1).broadcast_to([128, 16, 128])
            lnbb3 = lnbB[:].unsqueeze(1).broadcast_to([128, 16, 128])
            t13 = t1[:].rearrange("p (h d) -> p h d", d=128)
            P.add("vector", lambda e, s, ov3=ov3, mub=mub, t13=t13: e.tensor_tensor(
                t13, ov3, mub, op=ALU.subtract), wait=[("sm", rst_done)])
            P.selfsync("vector")
            P.add("vector", lambda e, s, t13=t13, rsb=rsb: e.tensor_tensor(
                t13, t13, rsb, op=ALU.mult))
            P.selfsync("vector")
            P.add("vector", lambda e, s, t13=t13, lnwb3=lnwb3: e.tensor_tensor(
                t13, t13, lnwb3, op=ALU.mult))
            P.selfsync("vector")
            P.add("vector", lambda e, s, t13=t13, lnbb3=lnbb3: e.tensor_tensor(
                t13, t13, lnbb3, op=ALU.add))
            P.selfsync("vector")
            w4 = [g_done]
            if ("ogt", tt - 2) in rel:
                w4.append(rel[("ogt", tt - 2)])
            P.add("vector", lambda e, s, bb=bb: e.tensor_mul(
                ogt[:, bb * 2048:(bb + 1) * 2048], t1[:],
                gt[:, bb * 2048:(bb + 1) * 2048]), wait=w4)
            P.selfsync("vector")
            P.add("vector", None, inc=[("vm", 1)])
            ogv = P.cur("vm")
            rel[("ob", tt)] = ("vm", ogv)
            rel[("gt", tt)] = ("vm", ogv)
            P.dma("gpsimd", "st1", lambda e, s, tt=tt, bb=bb: e.dma_start(
                out=ogo[tt * 128:(tt + 1) * 128, :],
                in_=ogt[:, bb * 2048:(bb + 1) * 2048]), wait=[("vm", ogv)])
            rel[("ogt", tt)] = ("st1", P.cur("st1"))
        P.add("gpsimd", None, wait=[("st1", P.cur("st1"))])
        run_prog(nc, P)
    return nc



LOGA_MIN = -25.0
C = 16
G = 4
SPAN = 128

def build_l2f(Prog, run_prog, TL=512):
    nc = bass.Bass("TRN2", target_bir_lowering=False)
    TH = TL + 3
    NSP = TL // SPAN

    def inp(name, shape, dt=F32):
        return nc.dram_tensor(name, shape, dt, kind="ExternalInput").ap()

    def outp(name, shape, dt=F32):
        return nc.dram_tensor(name, shape, dt, kind="ExternalOutput").ap()

    xnT = inp("xnT", [2048, TH], BF16)
    wts = {w: inp(w, [2048, 1024]) for w in ("wqT", "wkT", "wvT", "waT")}
    wbT = inp("wbT", [2048, 8])
    bav = inp("bav", [1024, 1]); bbv = inp("bbv", [8, 1])
    cws = {w: inp(w, [1024, 4]) for w in ("qcw", "kcw", "vcw")}
    cbs = {w: inp(w, [1024, 1]) for w in ("qcb", "kcb", "vcb")}
    kprevcm = inp("kprevcm", [1024, 16], BF16)
    lamq_in = inp("lamq_in", [128, 8])
    identf = inp("identf", [128, 128])
    identb2 = inp("identb2", [128, 128], BF16)

    kcm_o = outp("kcm_o", [1024, TL], BF16)
    khcm_o = outp("khcm_o", [1024, TL], BF16)
    qcm_o = outp("qcm_o", [1024, TL], BF16)
    ktm_o = outp("ktm_o", [TL, 1024], BF16)
    vtm_o = outp("vtm_o", [TL, 1024], BF16)
    atm_o = outp("atm_o", [TL, 1024], BF16)
    lamtm_o = outp("lamtm_o", [TL, 1024], F32)
    lamq_o = outp("lamq_o", [128, 8], F32)
    A_o = outp("A_o", [TL, 1024], BF16)
    AQ_o = outp("AQ_o", [TL, 1024], BF16)
    Ap_o = outp("Ap_o", [NSP * C, 1024], BF16)
    beta_sc = nc.dram_tensor("beta_sc", [16, TL], F32).ap()

    with contextlib.ExitStack() as stk:
        def sb(name, shape, dt=F32):
            return stk.enter_context(nc.sbuf_tensor(name, shape, dt))

        def psb(name, shape, dt=F32):
            return stk.enter_context(nc.psum_tensor(name, shape, dt))

        xcmb = sb("xcmb", [128, 16 * TH], BF16)
        wslab = sb("wslab", [128, 2 * 16 * 128], BF16)
        projf = sb("projf", [128, 8 * TH])
        cvt = sb("cvt", [128, TL])
        conb = sb("conb", [128, 4])
        conbias = sb("conbias", [128, 1])
        bab = sb("bab", [128, 8])
        bbb = sb("bbb", [8, 1])
        qcm = sb("qcm", [128, 8 * TL], BF16)
        kcm = sb("kcm", [128, 8 * TL], BF16)
        khcm = sb("khcm", [128, 8 * TL], BF16)
        vcm = sb("vcm", [128, 8 * TL], BF16)
        vbcm = sb("vbcm", [128, 8 * TL], BF16)
        acm = sb("acm", [128, 8 * TL], BF16)
        lamcm = sb("lamcm", [128, 8 * TL])
        zacm = sb("zacm", [128, 8 * TL])
        lamq = sb("lamq", [128, 8])
        logc = sb("logc", [128, 1])
        bcm = sb("bcm", [8, TL])
        bnegcm = sb("bnegcm", [8, TL])
        brow = sb("brow", [1, 2 * TL])
        onesrow = sb("onesrow", [1, 128])
        kprevb = sb("kprevb", [128, 128], BF16)
        identbf = sb("identbf", [128, 128])
        identbb = sb("identbb", [128, 128], BF16)
        tts = sb("tts", [128, 2 * 128], BF16)
        ttf = sb("ttf", [128, 2 * 128])
        ps_a = psb("ps_a", [128, 512])
        ps_a2 = psb("ps_a2", [128, 512])
        ps_bc = psb("ps_bc", [128, 512])
        ps_tb = psb("ps_tb", [128, 128], BF16)
        ps_tf = psb("ps_tf", [128, 128])
        ps_ap = psb("ps_ap", [16, 128])

        P = Prog(nc)
        A = P.add
        DMA = P.dma
        rel = {}

        A("vector", lambda e, s: e.memset(onesrow[:], 1.0))
        A("vector", lambda e, s: e.memset(logc[:], 1.0))
        P.selfsync("vector")
        A("vector", None, inc=[("vm", 1)])
        DMA("sync", "ldx", lambda e, s: e.dma_start(
            out=xcmb[:], in_=bass.AP(xnT.tensor, 0, [[TH, 128], [128 * TH, 16], [1, TH]])))
        DMA("sync", "ldc", lambda e, s: e.dma_start(out=bab[:], in_=bass.AP(
            bav.tensor, 0, [[1, 128], [128, 8]]), allow_slow_non_contiguous=True))
        DMA("sync", "ldc", lambda e, s: e.dma_start(out=bbb[:], in_=bbv))
        DMA("sync", "ldc", lambda e, s: e.dma_start(out=lamq[:], in_=lamq_in))
        DMA("sync", "ldc", lambda e, s: e.dma_start(
            out=kprevb[:], in_=bass.AP(kprevcm.tensor, 0,
                                       [[16, 128], [128 * 16, 8], [1, 16]])))
        DMA("sync", "ldc", lambda e, s: e.dma_start(out=identbf[:], in_=identf))
        DMA("sync", "ldc", lambda e, s: e.dma_start(out=identbb[:], in_=identb2))
        ldx_done = P.cur("ldx")
        ldc_done = P.cur("ldc")
        A("tensor", None, wait=[("ldx", ldx_done), ("ldc", ldc_done), ("vm", P.cur("vm"))])
        A("scalar", None, wait=[("ldc", ldc_done)])
        A("vector", None, wait=[("ldx", ldx_done), ("ldc", ldc_done)])

        gctr = [0]
        wctr = [0]

        def project(wT, wsem, TT, with_bias=False, xoff=0):
            nps = (TT + 511) // 512
            for ht in range(8):
                wi = wctr[0]
                wctr[0] += 1
                sl = wi % 2
                w = [rel[("wsl", wi - 2)]] if ("wsl", wi - 2) in rel else []
                DMA("gpsimd", wsem, lambda e, s, ht=ht, sl=sl, wT=wT: e.dma_start(
                    out=wslab[:, sl * 2048:(sl + 1) * 2048],
                    in_=bass.AP(wT.tensor, ht * 128,
                                [[1024, 128], [128 * 1024, 16], [1, 128]])), wait=w)
                ldw = P.cur(wsem)
                for tc in range(nps):
                    nt = min(512, TT - tc * 512)
                    gi = gctr[0]
                    gctr[0] += 1
                    ps = ps_a if gi % 2 == 0 else ps_a2
                    w2_ = [(wsem, ldw)]
                    if ("psf", gi - 2) in rel:
                        w2_.append(rel[("psf", gi - 2)])
                    for k in range(16):
                        A("tensor", lambda e, s, k=k, sl=sl, tc=tc, nt=nt, ps=ps,
                            xoff=xoff: e.matmul(
                            ps[:, 0:nt],
                            wslab[:, sl * 2048 + k * 128:sl * 2048 + (k + 1) * 128],
                            xcmb[:, k * TH + xoff + tc * 512:
                                 k * TH + xoff + tc * 512 + nt],
                            start=(k == 0), stop=(k == 15)),
                            wait=(w2_ if k == 0 else []),
                            inc=([("mm", 1)] if k == 15 else []))
                    mm_d = P.cur("mm")
                    if tc == nps - 1:
                        rel[("wsl", wi)] = ("mm", mm_d)
                    if with_bias:
                        A("vector", lambda e, s, ht=ht, tc=tc, nt=nt, ps=ps: e.tensor_scalar_add(
                            projf[:, ht * TH + tc * 512:ht * TH + tc * 512 + nt],
                            ps[:, 0:nt], bab[:, ht:ht + 1]),
                            wait=[("mm", mm_d)])
                        P.selfsync("vector")
                        A("vector", None, inc=[("sm", 1)])
                        rel[("psf", gi)] = ("sm", P.cur("sm"))
                    else:
                        A("scalar", lambda e, s, ht=ht, tc=tc, nt=nt, ps=ps: e.activation(
                            out=projf[:, ht * TH + tc * 512:ht * TH + tc * 512 + nt],
                            in_=ps[:, 0:nt], func=AF.Copy), wait=[("mm", mm_d)])
                        P.selfsync("scalar")
                        A("scalar", None, inc=[("sm", 1)])
                        rel[("psf", gi)] = ("sm", P.cur("sm"))
            rel[("pdone", wsem)] = ("sm", P.cur("sm"))

        cctr = [0]

        def conv_silu(cw, cbias, wsem, out_arr, extra_scale, pkey):
            for ht in range(8):
                ci = cctr[0]
                cctr[0] += 1
                w = [rel[("cvw", ci - 1)]] if ("cvw", ci - 1) in rel else []
                DMA("sync", wsem, lambda e, s, ht=ht, cw=cw: e.dma_start(
                    out=conb[:], in_=cw[ht * 128:(ht + 1) * 128, :]), wait=w)
                DMA("sync", wsem, lambda e, s, ht=ht, cbias=cbias: e.dma_start(
                    out=conbias[:], in_=cbias[ht * 128:(ht + 1) * 128, :]))
                ldcw = P.cur(wsem)
                wv = [(wsem, ldcw), rel[("pdone", pkey)]]
                A("vector", lambda e, s, ht=ht: e.tensor_scalar(
                    cvt[:], projf[:, ht * TH:ht * TH + TL], conb[:, 0:1], None,
                    op0=ALU.mult, op1=ALU.bypass), wait=wv)
                P.selfsync("vector")
                for kk in range(1, 4):
                    A("vector", lambda e, s, ht=ht, kk=kk: e.scalar_tensor_tensor(
                        cvt[:], projf[:, ht * TH + kk:ht * TH + kk + TL],
                        conb[:, kk:kk + 1], cvt[:], op0=ALU.mult, op1=ALU.add))
                    P.selfsync("vector")
                A("vector", lambda e, s: e.tensor_scalar_add(
                    cvt[:], cvt[:], conbias[:, 0:1]))
                P.selfsync("vector")
                A("vector", None, inc=[("vm", 1)])
                rel[("cvw", ci)] = ("vm", P.cur("vm"))
                A("scalar", lambda e, s, ht=ht: e.activation(
                    out=zacm[:, ht * TL:(ht + 1) * TL], in_=cvt[:], func=AF.Silu),
                    wait=[("vm", P.cur("vm"))])
                P.selfsync("scalar")
                A("scalar", None, inc=[("sm", 1)])
                A("vector", lambda e, s, ht=ht, out_arr=out_arr, extra_scale=extra_scale:
                    e.tensor_scalar(out_arr[:, ht * TL:(ht + 1) * TL],
                                    zacm[:, ht * TL:(ht + 1) * TL], float(extra_scale),
                                    None, op0=ALU.mult, op1=ALU.bypass),
                    wait=[("sm", P.cur("sm"))])
                P.selfsync("vector")
                A("vector", None, inc=[("vm", 1)])
            rel[("csdone", out_arr.name)] = ("vm", P.cur("vm"))

        project(wts["wqT"], "lwq", TH)
        conv_silu(cws["qcw"], cbs["qcb"], "lcq", qcm, 1.0, "lwq")
        project(wts["wkT"], "lwk", TH)
        conv_silu(cws["kcw"], cbs["kcb"], "lck", kcm, 128.0 ** -0.5, "lwk")
        project(wts["wvT"], "lwv", TH)
        conv_silu(cws["vcw"], cbs["vcb"], "lcv", vcm, 1.0, "lwv")

        project(wts["waT"], "lwa", TL, with_bias=True, xoff=3)
        for ht in range(8):
            pslc = projf[:, ht * TH:ht * TH + TL]
            zslc = zacm[:, ht * TL:(ht + 1) * TL]
            w = [rel[("pdone", "lwa")]]
            A("vector", lambda e, s, pslc=pslc, zslc=zslc: e.tensor_scalar(
                zslc, pslc, -1.0, None, op0=ALU.mult, op1=ALU.bypass), wait=w)
            P.selfsync("vector")
            A("vector", lambda e, s, pslc=pslc, zslc=zslc: e.tensor_tensor(
                zslc, zslc, pslc, op=ALU.max))
            P.selfsync("vector")
            A("vector", None, inc=[("vm", 1)])
            A("scalar", lambda e, s, zslc=zslc: e.activation(
                out=zslc, in_=zslc, func=AF.Exp, scale=-1.0),
                wait=[("vm", P.cur("vm"))])
            P.selfsync("scalar")
            A("scalar", lambda e, s, zslc=zslc: e.activation(
                out=zslc, in_=zslc, func=AF.Ln, bias=logc[:]))
            P.selfsync("scalar")
            A("scalar", None, inc=[("sm", 1)])
            A("vector", lambda e, s, pslc=pslc: e.tensor_scalar_min(
                cvt[:], pslc, 0.0), wait=[("sm", P.cur("sm"))])
            P.selfsync("vector")
            A("vector", lambda e, s, zslc=zslc: e.tensor_sub(zslc, cvt[:], zslc))
            P.selfsync("vector")
            A("vector", lambda e, s, zslc=zslc: e.tensor_scalar_max(
                zslc, zslc, LOGA_MIN))
            P.selfsync("vector")
            A("vector", None, inc=[("vm", 1)])
            A("scalar", lambda e, s, ht=ht, zslc=zslc: e.activation(
                out=acm[:, ht * TL:(ht + 1) * TL], in_=zslc, func=AF.Exp),
                wait=[("vm", P.cur("vm"))])
            P.selfsync("scalar")
            A("scalar", None, inc=[("sm", 1)])
            A("vector", lambda e, s, ht=ht, zslc=zslc: e.tensor_tensor_scan(
                lamcm[:, ht * TL:(ht + 1) * TL],
                logc[:].broadcast_to([128, TL]), zslc, lamq[:, ht:ht + 1],
                op0=ALU.mult, op1=ALU.add),
                wait=[("sm", P.cur("sm"))])
            P.selfsync("vector")
            A("vector", None, inc=[("vm", 1)])
        lam_done = ("vm", P.cur("vm"))
        A("vector", lambda e, s: e.tensor_copy(
            lamq[:, 0:8],
            bass.AP(lamcm, TL - 1, [[8 * TL, 128], [TL, 8]])), wait=[lam_done])
        P.selfsync("vector")
        A("vector", None, inc=[("vm", 1)])
        DMA("gpsimd", "st1", lambda e, s: e.dma_start(out=lamq_o, in_=lamq[:, 0:8]),
            wait=[("vm", P.cur("vm"))])

        wbw = [rel[("wsl", wctr[0] - 2)]] if ("wsl", wctr[0] - 2) in rel else []
        if ("wsl", wctr[0] - 1) in rel:
            wbw.append(rel[("wsl", wctr[0] - 1)])
        DMA("gpsimd", "lwb", lambda e, s: e.dma_start(
            out=wslab[:, 0:128],
            in_=bass.AP(wbT.tensor, 0, [[8, 128], [128 * 8, 16], [1, 8]])), wait=wbw)
        lwb = P.cur("lwb")
        nps = (TL + 511) // 512
        for tc in range(nps):
            nt = min(512, TL - tc * 512)
            w = [("lwb", lwb)]
            if ("psf", gctr[0] - 2) in rel:
                w.append(rel[("psf", gctr[0] - 2)])
            if ("psf", gctr[0] - 1) in rel:
                w.append(rel[("psf", gctr[0] - 1)])
            if ("psb", tc - 1) in rel:
                w.append(rel[("psb", tc - 1)])
            for k in range(16):
                A("tensor", lambda e, s, k=k, tc=tc, nt=nt: e.matmul(
                    ps_a[0:8, 0:nt], wslab[:, k * 8:(k + 1) * 8],
                    xcmb[:, k * TH + 3 + tc * 512:k * TH + 3 + tc * 512 + nt],
                    start=(k == 0), stop=(k == 15)),
                    wait=(w if k == 0 else []),
                    inc=([("mm", 1)] if k == 15 else []))
            mm_d = P.cur("mm")
            A("scalar", lambda e, s, tc=tc, nt=nt: e.activation(
                out=bcm[:, tc * 512:tc * 512 + nt], in_=ps_a[0:8, 0:nt],
                func=AF.Sigmoid, bias=bbb[:]), wait=[("mm", mm_d)])
            P.selfsync("scalar")
            A("scalar", None, inc=[("sm", 1)])
            rel[("psb", tc)] = ("sm", P.cur("sm"))
        A("vector", lambda e, s: e.tensor_scalar_mul(
            bnegcm[:], bcm[:], -1.0), wait=[("sm", P.cur("sm"))])
        P.selfsync("vector")
        A("vector", None, inc=[("vm", 1)])
        DMA("gpsimd", "stb", lambda e, s: e.dma_start(out=beta_sc[0:8, :], in_=bcm[:]),
            wait=[("vm", P.cur("vm"))])
        DMA("gpsimd", "stb", lambda e, s: e.dma_start(out=beta_sc[8:16, :], in_=bnegcm[:]))
        stb_done = ("stb", P.cur("stb"))

        for ht in range(8):
            DMA("sync", "ldbr", lambda e, s, ht=ht: e.dma_start(
                out=brow[:, 0:TL], in_=beta_sc[8 + ht:9 + ht, :]),
                wait=[stb_done, rel.get(("khv", ht - 1), ("vm", 0))])
            DMA("sync", "ldbr", lambda e, s, ht=ht: e.dma_start(
                out=brow[:, TL:2 * TL], in_=beta_sc[ht:ht + 1, :]))
            ldbr = P.cur("ldbr")
            for tc in range(nps):
                nt = min(512, TL - tc * 512)
                A("tensor", lambda e, s, tc=tc, nt=nt: e.matmul(
                    ps_bc[:, 0:nt], onesrow[:], brow[:, tc * 512:tc * 512 + nt],
                    start=True, stop=True), wait=[("ldbr", ldbr)], inc=[("mm", 1)])
                A("vector", lambda e, s, ht=ht, tc=tc, nt=nt: e.tensor_mul(
                    khcm[:, ht * TL + tc * 512:ht * TL + tc * 512 + nt],
                    kcm[:, ht * TL + tc * 512:ht * TL + tc * 512 + nt],
                    ps_bc[:, 0:nt]), wait=[("mm", P.cur("mm"))])
                P.selfsync("vector")
                A("vector", None, inc=[("vm", 1)])
                A("tensor", lambda e, s, tc=tc, nt=nt: e.matmul(
                    ps_bc[:, 0:nt], onesrow[:],
                    brow[:, TL + tc * 512:TL + tc * 512 + nt],
                    start=True, stop=True), wait=[("vm", P.cur("vm"))], inc=[("mm", 1)])
                A("vector", lambda e, s, ht=ht, tc=tc, nt=nt: e.tensor_mul(
                    vbcm[:, ht * TL + tc * 512:ht * TL + tc * 512 + nt],
                    vcm[:, ht * TL + tc * 512:ht * TL + tc * 512 + nt],
                    ps_bc[:, 0:nt]), wait=[("mm", P.cur("mm"))])
                P.selfsync("vector")
                A("vector", None, inc=[("vm", 1)])
            rel[("khv", ht)] = ("vm", P.cur("vm"))
        khv_done = ("vm", P.cur("vm"))

        for arr, dst in ((qcm, qcm_o), (kcm, kcm_o), (khcm, khcm_o)):
            DMA("gpsimd", "st1", lambda e, s, arr=arr, dst=dst: e.dma_start(
                out=bass.AP(dst.tensor, 0, [[TL, 128], [128 * TL, 8], [1, TL]]),
                in_=arr[:]), wait=[khv_done])

        jobs = [(kcm, ktm_o, True), (vbcm, vtm_o, True), (acm, atm_o, True),
                (lamcm, lamtm_o, False)]
        tctr = [0]
        for (arr, dst, isbf) in jobs:
            for ht in range(8):
                for blk in range(TL // 128):
                    ti = tctr[0]
                    tctr[0] += 1
                    idx = ti % 2
                    w = [khv_done]
                    if ("trev", ti - 1) in rel:
                        w.append(rel[("trev", ti - 1)])
                    if ("trdma", ti - 2) in rel:
                        w.append(rel[("trdma", ti - 2)])
                    if isbf:
                        A("tensor", lambda e, s, arr=arr, ht=ht, blk=blk: e.transpose(
                            ps_tb[:], arr[:, ht * TL + blk * 128:ht * TL + (blk + 1) * 128],
                            identbb[:]), wait=w, inc=[("mm", 1)])
                        A("scalar", lambda e, s, idx=idx: e.activation(
                            out=tts[:, idx * 128:(idx + 1) * 128], in_=ps_tb[:],
                            func=AF.Copy), wait=[("mm", P.cur("mm"))])
                    else:
                        A("tensor", lambda e, s, arr=arr, ht=ht, blk=blk: e.transpose(
                            ps_tf[:], arr[:, ht * TL + blk * 128:ht * TL + (blk + 1) * 128],
                            identbf[:]), wait=w, inc=[("mm", 1)])
                        A("scalar", lambda e, s, idx=idx: e.activation(
                            out=ttf[:, idx * 128:(idx + 1) * 128], in_=ps_tf[:],
                            func=AF.Copy), wait=[("mm", P.cur("mm"))])
                    P.selfsync("scalar")
                    A("scalar", None, inc=[("sm", 1)])
                    rel[("trev", ti)] = ("sm", P.cur("sm"))
                    srctile = tts if isbf else ttf
                    DMA("gpsimd", "st1", lambda e, s, dst=dst, ht=ht, blk=blk, idx=idx,
                        srctile=srctile: e.dma_start(
                        out=dst[blk * 128:(blk + 1) * 128, ht * 128:(ht + 1) * 128],
                        in_=srctile[:, idx * 128:(idx + 1) * 128]),
                        wait=[("sm", P.cur("sm"))])
                    rel[("trdma", ti)] = ("st1", P.cur("st1"))

        for sp in range(NSP):
            for ht in range(8):
                ii = sp * 8 + ht
                w = [khv_done]
                if ii < 2:
                    for kk in range(max(0, gctr[0] - 2), gctr[0]):
                        if ("psf", kk) in rel:
                            w.append(rel[("psf", kk)])
                    for tc2 in range(nps):
                        if ("psb", tc2) in rel:
                            w.append(rel[("psb", tc2)])
                if ("aps", ii - 2) in rel:
                    w.append(rel[("aps", ii - 2)])
                ps = ps_a if ii % 2 == 0 else ps_a2
                if ("apev", ii - 1) in rel:
                    w.append(rel[("apev", ii - 1)])
                A("tensor", lambda e, s, sp=sp, ht=ht, ps=ps: e.matmul(
                    ps[:, 0:128], kcm[:, ht * TL + sp * 128:ht * TL + (sp + 1) * 128],
                    khcm[:, ht * TL + sp * 128:ht * TL + (sp + 1) * 128],
                    start=True, stop=True), wait=w)
                A("tensor", lambda e, s, sp=sp, ht=ht, ps=ps: e.matmul(
                    ps[:, 128:256], kcm[:, ht * TL + sp * 128:ht * TL + (sp + 1) * 128],
                    qcm[:, ht * TL + sp * 128:ht * TL + (sp + 1) * 128],
                    start=True, stop=True))
                if sp == 0:
                    A("tensor", lambda e, s, ht=ht: e.matmul(
                        ps_ap[:], kprevb[:, ht * 16:(ht + 1) * 16],
                        khcm[:, ht * TL:ht * TL + 128],
                        start=True, stop=True), inc=[("mm", 1)])
                else:
                    A("tensor", lambda e, s, sp=sp, ht=ht: e.matmul(
                        ps_ap[:],
                        kcm[:, ht * TL + sp * 128 - 16:ht * TL + sp * 128],
                        khcm[:, ht * TL + sp * 128:ht * TL + (sp + 1) * 128],
                        start=True, stop=True), inc=[("mm", 1)])
                mm_d = P.cur("mm")
                wev = [("mm", mm_d)]
                if ("aps", ii - 1) in rel:
                    wev.append(rel[("aps", ii - 1)])
                A("scalar", lambda e, s, ps=ps: e.activation(
                    out=tts[:, 0:128], in_=ps[:, 0:128], func=AF.Copy),
                    wait=wev)
                P.selfsync("scalar")
                A("scalar", lambda e, s, ps=ps: e.activation(
                    out=tts[:, 128:256], in_=ps[:, 128:256], func=AF.Copy))
                P.selfsync("scalar")
                A("scalar", None, inc=[("sm", 1)])
                sm_d = P.cur("sm")
                DMA("gpsimd", "st1", lambda e, s, sp=sp, ht=ht: e.dma_start(
                    out=A_o[sp * 128:(sp + 1) * 128, ht * 128:(ht + 1) * 128],
                    in_=tts[:, 0:128]), wait=[("sm", sm_d)])
                DMA("gpsimd", "st1", lambda e, s, sp=sp, ht=ht: e.dma_start(
                    out=AQ_o[sp * 128:(sp + 1) * 128, ht * 128:(ht + 1) * 128],
                    in_=tts[:, 128:256]), wait=[("sm", sm_d)])
                A("scalar", lambda e, s: e.activation(
                    out=tts[0:16, 0:128], in_=ps_ap[:], func=AF.Copy),
                    wait=[("st1", P.cur("st1")), ("mm", mm_d)])
                P.selfsync("scalar")
                A("scalar", None, inc=[("sm", 1)])
                rel[("apev", ii)] = ("sm", P.cur("sm"))
                DMA("gpsimd", "st1", lambda e, s, sp=sp, ht=ht: e.dma_start(
                    out=Ap_o[sp * 16:(sp + 1) * 16, ht * 128:(ht + 1) * 128],
                    in_=tts[0:16, 0:128]), wait=[("sm", P.cur("sm"))])
                rel[("aps", ii)] = ("st1", P.cur("st1"))
        run_prog(nc, P)
    return nc


def build_l2s(Prog, run_prog, TL=512, dbg=False):
    nc = bass.Bass("TRN2", target_bir_lowering=False)
    NCH = TL // C
    NRB = 3

    def inp(name, shape, dt=F32):
        return nc.dram_tensor(name, shape, dt, kind="ExternalInput").ap()

    kcm_i = inp("kcm", [1024, TL], BF16)
    khcm_i = inp("khcm", [1024, TL], BF16)
    qcm_i = inp("qcm", [1024, TL], BF16)
    ktm_i = inp("ktm", [TL, 1024], BF16)
    vtm_i = inp("vtm", [TL, 1024], BF16)
    atm_i = inp("atm", [TL, 1024], BF16)
    lam_i_d = inp("lam_full", [17 + TL, 1024])
    A_d = inp("A_in", [TL, 1024], BF16)
    AQ_d = inp("AQ_in", [TL, 1024], BF16)
    Ap_d = inp("Ap_in", [(TL // SPAN) * C, 1024], BF16)
    kprevtm = inp("kprevtm", [16, 1024], BF16)
    st_in = inp("st_in", [128, 1024])
    w2prev_in = inp("w2prev_in", [16, 1024], BF16)
    dmask8 = inp("dmask8", [128, 8], BF16)
    sel8 = inp("sel8", [8, 128], BF16)
    crmask = inp("crmask", [128, 128], BF16)
    aqmask = inp("aqmask", [128, 256], BF16)
    e8sel = inp("e8sel", [128, 512], BF16)
    selmaskd = inp("selmask", [32, 4], mybir.dt.uint8)
    o_out = nc.dram_tensor("o_out", [TL * 8, 128], BF16, kind="ExternalOutput").ap()
    st_out = nc.dram_tensor("st_out", [128, 1024], F32, kind="ExternalOutput").ap()
    w2prev_out = nc.dram_tensor("w2prev_out", [16, 1024], BF16, kind="ExternalOutput").ap()
    wtmp_sc = nc.dram_tensor("wtmp_sc", [16, 1024], BF16).ap()
    if dbg:
        dbg_outs = {nm: nc.dram_tensor("dbg_" + nm, shp, F32,
                                       kind="ExternalOutput").ap()
                    for nm, shp in [("acols", [128, 512]), ("aqcols", [128, 256]),
                                    ("crosscols", [32, 512]), ("expwf4", [32, 512]),
                                    ("expof", [128, 128]), ("expost", [128, 128]),
                                    ("cb", [128, 256]), ("w2", [128, 256]),
                                    ("expb", [128, 128]), ("ws4dec", [32, 128])]}

    import contextlib
    with contextlib.ExitStack() as stk:
        def sb(name, shape, dt=F32):
            return stk.enter_context(nc.sbuf_tensor(name, shape, dt))

        def psb(name, shape, dt=F32):
            return stk.enter_context(nc.psum_tensor(name, shape, dt))

        St = sb("St", [128, 1024])
        W2 = sb("W2", [128, 256], BF16)
        ws4 = sb("ws4", [32, 128])
        ws4dec = sb("ws4dec", [32, 128], BF16)
        acols = sb("acols", [128, 512], BF16)
        apcols = sb("apcols", [128, 512], BF16)
        crosscols = sb("crosscols", [32, 512], BF16)
        aqcols = sb("aqcols", [128, 256], BF16)
        cb = sb("cb", [128, 2 * 128], BF16)
        cbt = sb("cbt", [128, 128])
        kpadb = sb("kpadb", [128, 2 * 1024], BF16)
        qpadb = sb("qpadb", [128, 1024], BF16)
        expb = sb("expb", [128, 128])
        expwf4 = sb("expwf4", [32, 512])
        expof = sb("expof", [128, 128])
        expost = sb("expost", [128, 128])
        otf = sb("otf", [128, 128])
        otf2 = sb("otf2", [128, 128])
        otileb = sb("otileb", [128, 128], BF16)
        sttmp = sb("sttmp", [128, 1024])
        stbf = sb("stbf", [128, 1024], BF16)
        decrow = sb("decrow", [1, 1024])
        dmask8b = sb("dmask8b", [128, 8], BF16)
        crmaskb = sb("crmaskb", [128, 128], BF16)
        aqmaskb = sb("aqmaskb", [128, 256], BF16)
        e8selb = sb("e8selb", [128, 512], BF16)
        selmb = sb("selmb", [32, 4], mybir.dt.uint8)
        onesrow = sb("onesrow", [1, 128])
        w2pv = sb("w2pv", [128, 128])
        aret = sb("aret", [128, NRB * 16], BF16)
        apret = sb("apret", [128, NRB * 16], BF16)
        aqret = sb("aqret", [128, NRB * 16], BF16)
        a8 = sb("a8", [8, NRB * 2048], BF16)
        sel8b = sb("sel8b", [8, 128], BF16)
        lam_i = sb("lam_i", [128, NRB * 128])
        lam_im1 = sb("lam_im1", [128, NRB * 128])
        lrefb = sb("lrefb", [128, NRB * 128])
        lref2 = sb("lref2", [128, NRB * 128])
        lgend = sb("lgend", [128, NRB * 128])
        vb = sb("vb", [128, NRB * 128], BF16)
        ktmb16 = sb("ktmb16", [16, NRB * 1024], BF16)
        declam = sb("declam", [1, NRB * 2048])
        win16 = sb("win16", [16, 1024], BF16)
        ps_wA = psb("ps_wA", [32, 256])
        ps_wB = psb("ps_wB", [32, 256])
        ps_og2 = psb("ps_og2", [64, 256])
        ps_ops = psb("ps_ops", [128, 128])
        ps_b = psb("ps_b", [128, 128])
        ps_a = psb("ps_a", [128, 512])
        ps_a2 = psb("ps_a2", [128, 512])
        ps_ar = psb("ps_ar", [128, 512])

        P = Prog(nc)
        A = P.add
        DMA = P.dma
        rel = {}
        retdone = {}

        A("vector", lambda e, s: e.memset(W2[:], 0.0))
        A("vector", lambda e, s: e.memset(ws4dec[:], 0.0))
        A("vector", lambda e, s: e.memset(acols[:], 0.0))
        A("vector", lambda e, s: e.memset(apcols[:], 0.0))
        A("vector", lambda e, s: e.memset(crosscols[:], 0.0))
        A("vector", lambda e, s: e.memset(aqcols[:], 0.0))
        A("vector", lambda e, s: e.memset(kpadb[:], 0.0))
        A("vector", lambda e, s: e.memset(qpadb[:], 0.0))
        P.selfsync("vector")
        A("vector", lambda e, s: e.memset(onesrow[:], 1.0))
        P.selfsync("vector")
        A("vector", None, inc=[("vm", 1)])
        setup_vm = P.cur("vm")
        DMA("sync", "ldc", lambda e, s: e.dma_start(out=dmask8b[:], in_=dmask8))
        DMA("sync", "ldc", lambda e, s: e.dma_start(out=sel8b[:], in_=sel8))
        DMA("sync", "ldc", lambda e, s: e.dma_start(out=crmaskb[:], in_=crmask))
        DMA("sync", "ldc", lambda e, s: e.dma_start(out=aqmaskb[:], in_=aqmask))
        DMA("sync", "ldc", lambda e, s: e.dma_start(out=e8selb[:], in_=e8sel))
        DMA("sync", "ldc", lambda e, s: e.dma_start(out=selmb[:], in_=selmaskd))
        DMA("sync", "ldc", lambda e, s: e.dma_start(out=St[:], in_=st_in))
        DMA("sync", "ldc", lambda e, s: e.dma_start(
            out=W2[:, 128:256],
            in_=bass.AP(w2prev_in.tensor, 0, [[1024, 16], [128, 8], [1, 128]])),
            wait=[("vm", setup_vm)])
        DMA("gpsimd", "rw", lambda e, s: e.dma_start(out=win16[:], in_=w2prev_in))
        DMA("gpsimd", "rw", lambda e, s: e.dma_start(out=stbf[:], in_=st_in))
        ldc_done = P.cur("ldc")
        A("tensor", None, wait=[("ldc", ldc_done), ("vm", setup_vm),
                                ("rw", P.cur("rw"))])
        A("vector", None, wait=[("ldc", ldc_done)])
        A("scalar", None, wait=[("ldc", ldc_done)])

        def ret_batch(n):
            slot = n % NRB
            t0 = n * C
            w = [rel[("chunkdone", n - NRB)]] if ("chunkdone", n - NRB) in rel else []
            sp = t0 // SPAN
            il = t0 % SPAN
            DMA("sync", "r1", lambda e, s, sp=sp, il=il, slot=slot: e.dma_start(
                out=aret[:, slot * 16:(slot + 1) * 16],
                in_=bass.AP(A_d.tensor, (sp * SPAN + il) * 1024 + il,
                            [[1024, 16], [128, 8], [1, 16]])), wait=w)
            if il == 0:
                DMA("sync", "r1", lambda e, s, sp=sp, slot=slot: e.dma_start(
                    out=apret[:, slot * 16:(slot + 1) * 16],
                    in_=bass.AP(Ap_d.tensor, sp * 16 * 1024,
                                [[1024, 16], [128, 8], [1, 16]])), wait=w)
            else:
                DMA("sync", "r1", lambda e, s, sp=sp, il=il, slot=slot: e.dma_start(
                    out=apret[:, slot * 16:(slot + 1) * 16],
                    in_=bass.AP(A_d.tensor, (sp * SPAN + il - 16) * 1024 + il,
                                [[1024, 16], [128, 8], [1, 16]])), wait=w)
            DMA("sync", "r1", lambda e, s, sp=sp, il=il, slot=slot: e.dma_start(
                out=aqret[:, slot * 16:(slot + 1) * 16],
                in_=bass.AP(AQ_d.tensor, (sp * SPAN + il) * 1024 + il,
                            [[1024, 16], [128, 8], [1, 16]])), wait=w)
            DMA("sync", "r1", lambda e, s, t0=t0, slot=slot: e.dma_start(
                out=lam_i[:, slot * 128:(slot + 1) * 128],
                in_=bass.AP(lam_i_d.tensor, (17 + t0) * 1024,
                            [[1024, 16], [128, 8], [1, 128]])), wait=w)
            DMA("sync", "r1", lambda e, s, t0=t0, slot=slot: e.dma_start(
                out=lam_im1[:, slot * 128:(slot + 1) * 128],
                in_=bass.AP(lam_i_d.tensor, (16 + t0) * 1024,
                            [[1024, 16], [128, 8], [1, 128]])), wait=w)
            retdone[("r1", n)] = ("r1", P.cur("r1"))
            DMA("scalar", "r2", lambda e, s, t0=t0, slot=slot: e.dma_start(
                out=lrefb[:, slot * 128:(slot + 1) * 128],
                in_=bass.AP(lam_i_d.tensor, t0 * 1024,
                            [[0, 16], [128, 8], [1, 128]])), wait=w)
            DMA("scalar", "r2", lambda e, s, t0=t0, slot=slot: e.dma_start(
                out=lref2[:, slot * 128:(slot + 1) * 128],
                in_=bass.AP(lam_i_d.tensor, (16 + t0) * 1024,
                            [[0, 16], [128, 8], [1, 128]])), wait=w)
            for g in range(4):
                DMA("scalar", "r2", lambda e, s, t0=t0, slot=slot, g=g: e.dma_start(
                    out=lgend[32 * g:32 * (g + 1), slot * 128:(slot + 1) * 128],
                    in_=bass.AP(lam_i_d.tensor, (17 + t0 + 4 * g + 3) * 1024,
                                [[0, 4], [128, 8], [1, 128]])), wait=w)
                w = []
            DMA("scalar", "r2", lambda e, s, t0=t0, slot=slot: e.dma_start(
                out=vb[:, slot * 128:(slot + 1) * 128],
                in_=bass.AP(vtm_i.tensor, t0 * 1024,
                            [[1024, 16], [128, 8], [1, 128]])), wait=w)
            DMA("scalar", "r2", lambda e, s, t0=t0, slot=slot: e.dma_start(
                out=declam[:, slot * 2048:slot * 2048 + 1024],
                in_=lam_i_d[16 + t0:17 + t0, :]), wait=w)
            DMA("scalar", "r2", lambda e, s, t0=t0, slot=slot: e.dma_start(
                out=declam[:, slot * 2048 + 1024:slot * 2048 + 2048],
                in_=lam_i_d[t0:t0 + 1, :]), wait=w)
            retdone[("r2", n)] = ("r2", P.cur("r2"))
            DMA("gpsimd", "r3", lambda e, s, t0=t0, slot=slot: e.dma_start(
                out=a8[:, slot * 2048:(slot + 1) * 2048],
                in_=bass.AP(atm_i.tensor, t0 * 1024,
                            [[128, 8], [1024, 16], [1, 128]])), wait=w)
            if n == 0:
                DMA("gpsimd", "r3", lambda e, s, slot=slot: e.dma_start(
                    out=ktmb16[:, slot * 1024:(slot + 1) * 1024], in_=kprevtm), wait=w)
            else:
                DMA("gpsimd", "r3", lambda e, s, t0=t0, slot=slot: e.dma_start(
                    out=ktmb16[:, slot * 1024:(slot + 1) * 1024],
                    in_=ktm_i[t0 - 16:t0, :]), wait=w)
            retdone[("r3", n)] = ("r3", P.cur("r3"))

        def coef_builds(n):
            slot = n % NRB
            w = [retdone[("r1", n)], ("vm", setup_vm), ("ldc", ldc_done)]
            if dbg and n == 1 and "dbgs" in P.ct:
                w.append(("dbgs", P.cur("dbgs")))
            if ("stepsdone", n - 1) in rel:
                w.append(rel[("stepsdone", n - 1)])
            if ("ogmm", n - 1, 3) in rel:
                w.append(rel[("ogmm", n - 1, 3)])
            for (dst, srcr) in ((acols, aret), (apcols, apret)):
                for r in range(4):
                    A("vector", lambda e, s, dst=dst, srcr=srcr, r=r, slot=slot:
                        e.tensor_tensor(
                            bass.AP(dst, 40 * r, [[512, 128], [128, 4], [1, 8]]),
                            bass.AP(srcr, slot * 16 + r, [[NRB * 16, 128], [4, 4], [0, 8]]),
                            dmask8b[:].unsqueeze(1).broadcast_to([128, 4, 8]),
                            op=ALU.mult), wait=w)
                    w = []
                    P.selfsync("vector")
            for m in range(4):
                A("vector", lambda e, s, m=m, slot=slot: e.tensor_tensor(
                    bass.AP(crosscols, m * 128, [[512, 32], [40, 4], [1, 8]]),
                    bass.AP(aret, (32 * m) * (NRB * 16) + slot * 16 + 4 * m,
                            [[NRB * 16, 32], [1, 4], [0, 8]]),
                    bass.AP(crmaskb, (32 * m) * 128, [[128, 32], [40, 4], [1, 8]]),
                    op=ALU.mult))
                P.selfsync("vector")
            for m in range(4):
                A("vector", lambda e, s, m=m, slot=slot: e.tensor_tensor(
                    bass.AP(aqcols, m * 64 + 32 * (m % 2), [[256, 128], [8, 4], [1, 8]]),
                    bass.AP(aqret, slot * 16 + 4 * m, [[NRB * 16, 128], [1, 4], [0, 8]]),
                    bass.AP(aqmaskb, m * 64 + 32 * (m % 2), [[256, 128], [8, 4], [1, 8]]),
                    op=ALU.mult))
                P.selfsync("vector")
            w2_ = [retdone[("r2", n)], retdone[("r1", n)]]
            A("vector", lambda e, s, slot=slot: e.tensor_sub(
                expof[:], lam_i[:, slot * 128:(slot + 1) * 128],
                lgend[:, slot * 128:(slot + 1) * 128]), wait=w2_)
            P.selfsync("vector")
            A("vector", None, inc=[("vm", 1)])
            A("scalar", lambda e, s: e.activation(out=expof[:], in_=expof[:],
                                                  func=AF.Exp),
              wait=[("vm", P.cur("vm"))])
            P.selfsync("scalar")
            A("vector", lambda e, s, slot=slot: e.tensor_sub(
                expost[:], lam_i[:, slot * 128:(slot + 1) * 128],
                lref2[:, slot * 128:(slot + 1) * 128]))
            P.selfsync("vector")
            A("vector", None, inc=[("vm", 1)])
            A("scalar", lambda e, s: e.activation(out=expost[:], in_=expost[:],
                                                  func=AF.Exp),
              wait=[("vm", P.cur("vm"))])
            P.selfsync("scalar")
            A("scalar", None, inc=[("sm", 1)])
            rel[("coefs", n)] = ("sm", P.cur("sm"))

        def absorb(n):
            slot = n % NRB
            w = [retdone[("r2", n)]]
            if ("absorbdone", n - 1) in rel:
                w.append(rel[("absorbdone", n - 1)])
            A("vector", lambda e, s, slot=slot: e.tensor_sub(
                decrow[:], declam[:, slot * 2048:slot * 2048 + 1024],
                declam[:, slot * 2048 + 1024:slot * 2048 + 2048]), wait=w)
            P.selfsync("vector")
            A("vector", None, inc=[("vm", 1)])
            A("scalar", lambda e, s: e.activation(out=decrow[:], in_=decrow[:],
                                                  func=AF.Exp),
              wait=[("vm", P.cur("vm"))])
            P.selfsync("scalar")
            A("scalar", None, inc=[("sm", 1)])
            dec_done = P.cur("sm")
            A("tensor", lambda e, s: e.matmul(ps_a[:], onesrow[:], decrow[:, 0:512],
                                              start=True, stop=True),
              wait=[("sm", dec_done)] + ([rel[("absorbps", n - 1)]] if ("absorbps", n - 1) in rel else []),
              inc=[("mm", 1)])
            A("tensor", lambda e, s: e.matmul(ps_a2[:], onesrow[:], decrow[:, 512:1024],
                                              start=True, stop=True), inc=[("mm", 1)])
            mm_d = P.cur("mm")
            A("vector", lambda e, s: e.tensor_mul(sttmp[:, 0:512], St[:, 0:512], ps_a[:]),
              wait=[("mm", mm_d)])
            P.selfsync("vector")
            A("vector", lambda e, s: e.tensor_mul(sttmp[:, 512:1024], St[:, 512:1024],
                                                  ps_a2[:]))
            P.selfsync("vector")
            A("vector", None, inc=[("vm", 1)])
            stt_done = P.cur("vm")
            wd = [("vm", stt_done), retdone[("r3", n)], ("rw", P.cur("rw")) if "rw" in P.ct else ("vm", 0)]
            for h in range(8):
                ps = ps_a if h < 4 else ps_a2
                A("tensor", lambda e, s, h=h, ps=ps, slot=slot: e.matmul(
                    ps[:, (h % 4) * 128:(h % 4 + 1) * 128],
                    ktmb16[:, slot * 1024 + h * 128:slot * 1024 + (h + 1) * 128],
                    win16[:, h * 128:(h + 1) * 128],
                    start=True, stop=True),
                    wait=(wd if h == 0 else []),
                    inc=([("mm", 1)] if h == 7 else []))
            mm_d = P.cur("mm")
            A("vector", lambda e, s: e.tensor_add(St[:, 0:512], sttmp[:, 0:512], ps_a[:]),
              wait=[("mm", mm_d)])
            P.selfsync("vector")
            A("vector", lambda e, s: e.tensor_add(St[:, 512:1024], sttmp[:, 512:1024],
                                                  ps_a2[:]))
            P.selfsync("vector")
            A("vector", lambda e, s: e.tensor_copy(stbf[:], St[:]))
            P.selfsync("vector")
            A("vector", None, inc=[("vm", 1)])
            rel[("absorbdone", n)] = ("vm", P.cur("vm"))
            rel[("absorbps", n)] = ("vm", P.cur("vm"))

        khcmb = sb("khcmb", [128, 8 * TL], BF16)
        qcmb = sb("qcmb", [128, 8 * TL], BF16)
        DMA("sync", "ldc2", lambda e, s: e.dma_start(
            out=khcmb[:], in_=bass.AP(khcm_i.tensor, 0,
                                      [[TL, 128], [128 * TL, 8], [1, TL]])))
        DMA("sync", "ldc2", lambda e, s: e.dma_start(
            out=qcmb[:], in_=bass.AP(qcm_i.tensor, 0,
                                     [[TL, 128], [128 * TL, 8], [1, TL]])))
        ldc2_done = P.cur("ldc2")
        A("vector", None, wait=[("ldc2", ldc2_done)])

        def bcb(n):
            slot = n % NRB
            par = n % 2
            w = []
            if dbg and n == 1 and "dbgs" in P.ct:
                w.append(("dbgs", P.cur("dbgs")))
            if ("absorbdone", n - 1) in rel:
                w.append(rel[("absorbdone", n - 1)])
            if ("bps", n - 2) in rel:
                w.append(rel[("bps", n - 2)])
            w.append(("ldc2", ldc2_done))
            for h in range(8):
                A("vector", lambda e, s, h=h, par=par, n=n: e.tensor_copy(
                    bass.AP(kpadb, par * 1024 + h * 128 + h, [[2048, 128], [8, 16]]),
                    khcmb[:, h * TL + n * 16:h * TL + (n + 1) * 16]),
                    wait=(w if h == 0 else []))
                P.selfsync("vector")
            A("vector", None, inc=[("vm", 1)])
            kp_done = P.cur("vm")
            wb = [("vm", kp_done)]
            if ("cbdone", n - 1) in rel:
                wb.append(rel[("cbdone", n - 1)])
            for h in range(8):
                A("tensor", lambda e, s, h=h, par=par: e.matmul(
                    ps_b[:], kpadb[:, par * 1024 + h * 128:par * 1024 + (h + 1) * 128],
                    stbf[:, h * 128:(h + 1) * 128],
                    start=(h == 0), stop=(h == 7)),
                    wait=(wb if h == 0 else []),
                    inc=([("mm", 1)] if h == 7 else []))
            mm_d = P.cur("mm")
            rel[("bps", n)] = ("mm", mm_d)
            A("vector", lambda e, s, slot=slot: e.tensor_sub(
                expb[:], lam_im1[:, slot * 128:(slot + 1) * 128],
                lrefb[:, slot * 128:(slot + 1) * 128]),
                wait=[retdone[("r1", n)], retdone[("r2", n)]])
            P.selfsync("vector")
            A("vector", None, inc=[("vm", 1)])
            A("scalar", lambda e, s: e.activation(out=expb[:], in_=expb[:], func=AF.Exp),
              wait=[("vm", P.cur("vm"))])
            P.selfsync("scalar")
            A("scalar", None, inc=[("sm", 1)])
            wcb = [("sm", P.cur("sm")), ("mm", mm_d)]
            if ("ogmm", n - 2, 3) in rel:
                wcb.append(rel[("ogmm", n - 2, 3)])
            A("vector", lambda e, s: e.tensor_mul(cbt[:], expb[:], ps_b[:]),
              wait=wcb)
            P.selfsync("vector")
            A("vector", lambda e, s, slot=slot, par=par: e.tensor_add(
                cb[:, par * 128:(par + 1) * 128],
                vb[:, slot * 128:(slot + 1) * 128], cbt[:]))
            P.selfsync("vector")
            A("vector", None, inc=[("vm", 1)])
            rel[("cbdone", n)] = ("vm", P.cur("vm"))

        def group_steps(n, m):
            slot = n % NRB
            cur = n % 2
            par = n % 2
            pass
            war = [retdone[("r3", n)], ("ldc", ldc_done)]
            if ("decay", n, 4 * m - 1) in rel:
                war.append(rel[("decay", n, 4 * m - 1)])
            if m == 0 and ("decay", n - 1, 15) in rel:
                war.append(rel[("decay", n - 1, 15)])
            if m == 0 and ("wsc", n - 1, 15) in rel:
                war.append(rel[("wsc", n - 1, 15)])
            A("tensor", lambda e, s, m=m, slot=slot: e.matmul(
                ps_ar[:], sel8b[:],
                a8[:, slot * 2048 + m * 512:slot * 2048 + (m + 1) * 512],
                start=True, stop=True), wait=war, inc=[("mm", 1)])
            rel[("armm", n, m)] = ("mm", P.cur("mm"))
            for r in range(4):
                i = 4 * m + r
                w = [rel[("coefs", n)], rel[("cbdone", n)]]
                if ("flush", n, m - 2) in rel:
                    w.append(rel[("flush", n, m - 2)])
                if m < 2 and ("flush", n - 1, m + 2) in rel:
                    w.append(rel[("flush", n - 1, m + 2)])
                if ("decay", n, i - 1) in rel:
                    w.append(rel[("decay", n, i - 1)])
                if ("memsetw", n) in rel:
                    w.append(rel[("memsetw", n)])
                pw = ps_wA if r % 2 == 0 else ps_wB
                psl = (r // 2) * 128
                A("tensor", lambda e, s, i=i, cur=cur, psl=psl, pw=pw: e.matmul(
                    pw[:, psl:psl + 128], acols[:, i * 32:(i + 1) * 32],
                    W2[:, cur * 128:(cur + 1) * 128],
                    start=True, stop=False), wait=w)
                A("tensor", lambda e, s, i=i, cur=cur, psl=psl, pw=pw: e.matmul(
                    pw[:, psl:psl + 128], apcols[:, i * 32:(i + 1) * 32],
                    W2[:, (1 - cur) * 128:(2 - cur) * 128],
                    start=False, stop=False))
                if r >= 1:
                    A("tensor", lambda e, s, i=i, psl=psl, pw=pw: e.matmul(
                        pw[:, psl:psl + 128], crosscols[:, i * 32:(i + 1) * 32],
                        ws4dec[:], start=False, stop=False),
                        wait=([rel[("wsc", n, i - 1)]]
                              if ("wsc", n, i - 1) in rel else []))
                A("tensor", lambda e, s, i=i, par=par, psl=psl, pw=pw: e.matmul(
                    pw[:, psl:psl + 128], e8selb[:, i * 32:(i + 1) * 32],
                    cb[:, par * 128:(par + 1) * 128],
                    start=False, stop=True), inc=[("mm", 1)])
                mm_d = P.cur("mm")
                wdec = [("mm", mm_d), rel[("armm", n, m)]]
                if i == 0 and ("chunkdone", n - 1) in rel:
                    wdec.append(rel[("chunkdone", n - 1)])
                A("vector", lambda e, s, r=r: e.tensor_tensor(
                    W2[:].rearrange("p (a d) -> p a d", a=2),
                    W2[:].rearrange("p (a d) -> p a d", a=2),
                    bass.AP(ps_ar.tensor if hasattr(ps_ar, "tensor") else ps_ar,
                            r * 128, [[512, 128], [0, 2], [1, 128]]),
                    op=ALU.mult), wait=wdec, inc=[("vm", 1)])
                rel[("decay", n, i)] = ("vm", P.cur("vm"))
                wmu = [rel[("wsc", n, i - 1)]] if ("wsc", n, i - 1) in rel else []
                A("vector", lambda e, s, r=r: e.tensor_mul(
                    ws4dec[:], ws4dec[:], ps_ar[0:32, r * 128:(r + 1) * 128]),
                    wait=wmu, inc=[("vm", 1)])
                mu_v = P.cur("vm")
                A("vector", lambda e, s, r=r, psl=psl, pw=pw: e.select(
                    ws4dec[:], selmb[:, r:r + 1].broadcast_to([32, 128]),
                    pw[:, psl:psl + 128], ws4dec[:], add_drain=True),
                    wait=[("vm", mu_v)], inc=[("vm", 1)])
                rel[("wsc", n, i)] = ("vm", P.cur("vm"))
            A("vector", lambda e, s, m=m, cur=cur: e.tensor_copy(
                W2[32 * m:32 * (m + 1), cur * 128:(cur + 1) * 128],
                ws4dec[:]),
                wait=[rel[("decay", n, 4 * m + 3)], rel[("wsc", n, 4 * m + 3)]])
            P.selfsync("vector")
            A("vector", None, inc=[("vm", 1)])
            rel[("flush", n, m)] = ("vm", P.cur("vm"))
            w = [rel[("flush", n, m)]]
            if ("ogps", n - 1) in rel and m < 2:
                w.append(rel[("ogps", n - 1)])
            A("tensor", lambda e, s, m=m, cur=cur: e.matmul(
                ps_og2[:, (m // 2) * 128:(m // 2 + 1) * 128],
                aqcols[:, m * 64:(m + 1) * 64],
                W2[:, cur * 128:(cur + 1) * 128],
                start=(m % 2 == 0), stop=(m % 2 == 1)),
                wait=w, inc=[("mm", 1)])
            rel[("ogmm", n, m)] = ("mm", P.cur("mm"))

        def o_state_final(n):
            slot = n % NRB
            w = [rel[("absorbdone", n)]] if ("absorbdone", n) in rel else []
            w.append(("ldc2", ldc2_done))
            if ("opsps", n - 1) in rel:
                w.append(rel[("opsps", n - 1)])
            for h in range(8):
                A("vector", lambda e, s, h=h, n=n: e.tensor_copy(
                    bass.AP(qpadb, h * 128 + h, [[1024, 128], [8, 16]]),
                    qcmb[:, h * TL + n * 16:h * TL + (n + 1) * 16]),
                    wait=(w if h == 0 else []))
                P.selfsync("vector")
            A("vector", None, inc=[("vm", 1)])
            qp_done = P.cur("vm")
            for h in range(8):
                A("tensor", lambda e, s, h=h: e.matmul(
                    ps_ops[:], qpadb[:, h * 128:(h + 1) * 128],
                    stbf[:, h * 128:(h + 1) * 128],
                    start=(h == 0), stop=(h == 7)),
                    wait=([("vm", qp_done)] if h == 0 else []),
                    inc=([("mm", 1)] if h == 7 else []))
            mm_d = P.cur("mm")
            w2_ = [rel[("ogmm", n, 1)], rel[("ogmm", n, 3)], rel[("coefs", n)]]
            if ("oadd", n - 1) in rel:
                w2_.append(rel[("oadd", n - 1)])
            A("scalar", lambda e, s: e.activation(out=otf[0:64, :],
                                                  in_=ps_og2[:, 0:128],
                                                  func=AF.Copy), wait=w2_)
            P.selfsync("scalar")
            A("scalar", lambda e, s: e.activation(out=otf[64:128, :],
                                                  in_=ps_og2[:, 128:256],
                                                  func=AF.Copy))
            P.selfsync("scalar")
            A("scalar", None, inc=[("sm", 1)])
            rel[("ogps", n)] = ("sm", P.cur("sm"))
            A("vector", lambda e, s: e.tensor_mul(otf[:], otf[:], expof[:]),
              wait=[("sm", P.cur("sm"))])
            P.selfsync("vector")
            A("vector", lambda e, s: e.tensor_mul(otf2[:], ps_ops[:], expost[:]),
              wait=[("mm", mm_d)])
            P.selfsync("vector")
            A("vector", None, inc=[("vm", 1)])
            rel[("opsps", n)] = ("vm", P.cur("vm"))
            w3 = []
            if ("odma", n - 1) in rel:
                w3.append(rel[("odma", n - 1)])
            A("vector", lambda e, s: e.tensor_add(otileb[:], otf[:], otf2[:]), wait=w3)
            P.selfsync("vector")
            A("vector", None, inc=[("vm", 1)])
            rel[("oadd", n)] = ("vm", P.cur("vm"))
            DMA("gpsimd", "sto", lambda e, s, n=n: e.dma_start(
                out=o_out[n * 128:(n + 1) * 128, :], in_=otileb[:]),
                wait=[("vm", P.cur("vm"))])
            rel[("odma", n)] = ("sto", P.cur("sto"))

        def chunk_end(n):
            cur = n % 2
            w = [rel[("flush", n, 3)], rel[("ogmm", n, 3)]]
            if dbg and n == 0 and "dbgs" in P.ct:
                w.append(("dbgs", P.cur("dbgs")))
            if ("winret", n) in rel:
                w.append(rel[("winret", n)])
            DMA("gpsimd", "wout", lambda e, s, cur=cur: e.dma_start(
                out=bass.AP(wtmp_sc.tensor, 0, [[1024, 16], [128, 8], [1, 128]]),
                in_=W2[:, cur * 128:(cur + 1) * 128]), wait=w)
            wout_d = P.cur("wout")
            if n + 1 < NCH:
                DMA("gpsimd", "rw", lambda e, s: e.dma_start(
                    out=win16[:], in_=wtmp_sc), wait=[("wout", wout_d)] +
                    ([rel[("absorbdone", n)]] if ("absorbdone", n) in rel else []))
                rel[("winret", n + 1)] = ("rw", P.cur("rw"))
                wms = [rel[("steps15", n)]]
                if dbg and n == 0 and "dbgs" in P.ct:
                    wms.append(("dbgs", P.cur("dbgs")))
                A("vector", lambda e, s, cur=cur: e.memset(
                    W2[:, (1 - cur) * 128:(2 - cur) * 128], 0.0),
                    wait=wms)
                P.selfsync("vector")
                A("vector", None, inc=[("vm", 1)])
                rel[("memsetw", n + 1)] = ("vm", P.cur("vm"))
            rel[("chunkdone", n)] = ("wout", wout_d)

        ret_batch(0)
        if NCH > 1:
            ret_batch(1)
        bcb(0)
        for n in range(NCH):
            if n + 2 < NCH:
                ret_batch(n + 2)
            coef_builds(n)
            group_steps(n, 0)
            absorb(n)
            if n + 1 < NCH:
                bcb(n + 1)
            group_steps(n, 1)
            group_steps(n, 2)
            group_steps(n, 3)
            rel[("steps15", n)] = rel[("decay", n, 15)]
            rel[("stepsdone", n)] = rel[("flush", n, 3)]
            o_state_final(n)
            if dbg and n == 0:
                for nm, tens, wk in [
                        ("acols", acols, rel[("coefs", 0)]),
                        ("aqcols", aqcols, rel[("coefs", 0)]),
                        ("crosscols", crosscols, rel[("coefs", 0)]),
                        ("expwf4", expwf4, rel[("coefs", 0)]),
                        ("expof", expof, rel[("coefs", 0)]),
                        ("expost", expost, rel[("coefs", 0)]),
                        ("cb", cb, rel[("cbdone", 0)]),
                        ("expb", expb, rel[("cbdone", 0)]),
                        ("ws4dec", ws4dec, rel[("wsc", 0, 15)]),
                        ("w2", W2, rel[("flush", 0, 3)])]:
                    DMA("gpsimd", "dbgs", lambda e, s, nm=nm, tens=tens:
                        e.dma_start(out=dbg_outs[nm], in_=tens[:]),
                        wait=[wk, rel[("ogmm", 0, 3)], rel[("opsps", 0)]])
            chunk_end(n)
        DMA("gpsimd", "stf", lambda e, s: e.dma_start(out=st_out, in_=St[:]),
            wait=[rel[("absorbdone", NCH - 1)], rel[("opsps", NCH - 1)]])
        DMA("gpsimd", "stf", lambda e, s: e.dma_start(
            out=bass.AP(w2prev_out.tensor, 0, [[1024, 16], [128, 8], [1, 128]]),
            in_=W2[:, (NCH - 1) % 2 * 128:((NCH - 1) % 2 + 1) * 128]),
            wait=[rel[("chunkdone", NCH - 1)]])
        fw = [("stf", P.cur("stf")), ("sto", P.cur("sto"))]
        if dbg and "dbgs" in P.ct:
            fw.append(("dbgs", P.cur("dbgs")))
        A("gpsimd", None, wait=fw)
        run_prog(nc, P)
    return nc
